# version 02
# baseline (speedup 1.0000x reference)
import contextlib
import numpy as np
import ml_dtypes
import concourse.bass as bass
import concourse.mybir as mybir

F32 = mybir.dt.float32
BF16 = mybir.dt.bfloat16
AF = mybir.ActivationFunctionType
ALU = mybir.AluOpType
AX = mybir.AxisListType
bf16 = ml_dtypes.bfloat16

ENGINES = ("tensor", "vector", "scalar", "gpsimd", "sync")


class Cfg:
    def __init__(self, D=4096, S=128, NH=64, HD=64, K=4, T=1024, HALO=256,
                 E=16, R=16, debug=False, phases=6):
        self.D, self.S, self.NH, self.HD, self.K = D, S, NH, HD, K
        self.T, self.HALO, self.E, self.R = T, HALO, E, R
        self.debug = debug
        self.phases = phases
        self.CONV = D + 2 * S
        self.PROJ = 2 * D + 2 * S + NH
        self.KT = D // 128
        self.ZOT = D // 128
        self.XOT = self.CONV // 128
        self.NC = T // 128
        self.HC = HALO // 128
        self.TH = T + HALO
        self.NG = NH // 8
        self.ER = E * R
        self.NOT = (self.PROJ + 127) // 128
        self.DTW = self.PROJ - 128 * (self.NOT - 1)
        self.KQ = max(1, self.KT // 4)
        self.NKQ = (self.KT + self.KQ - 1) // self.KQ
        self.ERT = self.ER // 128
        assert D % 128 == 0 and self.CONV % 128 == 0 and S == 128 and HD == 64
        assert T % 128 == 0 and HALO % 128 == 0 and NH % 8 == 0 and self.ER % 128 == 0


class Sem:
    def __init__(self, h, step=1):
        self.h, self.step, self.n = h, step, 0

    def inc(self):
        self.n += self.step
        return self.n


class Build:
    def __init__(self, cfg):
        self.cfg = cfg
        self.nc = bass.Bass("TRN2", target_bir_lowering=False,
                            detect_race_conditions=False)
        self.ops = {e: [] for e in ENGINES}
        self._ctx = contextlib.ExitStack()

    def sem(self, name, step=1):
        sm = Sem(self._ctx.enter_context(self.nc.semaphore(name)), step)
        if not hasattr(self, "_allsems"):
            self._allsems = {}
        self._allsems[name] = sm
        return sm

    def sbuf(self, name, shape, dtype):
        return self._ctx.enter_context(self.nc.sbuf_tensor(name, shape, dtype))

    def psum(self, name, shape, dtype=F32):
        return self._ctx.enter_context(self.nc.psum_tensor(name, shape, dtype))

    def wait(self, e, sem, val):
        if val > 0:
            self.ops[e].append(lambda eng, s=sem.h, v=val: eng.wait_ge(s, v))

    def run(self, e, f, inc=None):
        incs = inc if isinstance(inc, (list, tuple)) else ([inc] if inc else [])
        tgts = [s.inc() for s in incs]

        def g(eng, f=f, incs=incs):
            ins = f(eng)
            for s in incs:
                ins = ins.then_inc(s.h, s.step)
        self.ops[e].append(g)
        return tgts[0] if len(tgts) == 1 else (tgts if tgts else None)


def _finalize(b, nc, c, dram):
    for name, sem_ in list(b._allsems.items()):
        b.wait("sync", sem_, sem_.n)
    with b._ctx:
        with nc.Block() as block:
            @block.tensor
            def _(eng):
                for f in b.ops["tensor"]:
                    f(eng)

            @block.vector
            def _(eng):
                for f in b.ops["vector"]:
                    f(eng)

            @block.scalar
            def _(eng):
                for f in b.ops["scalar"]:
                    f(eng)

            @block.gpsimd
            def _(eng):
                for f in b.ops["gpsimd"]:
                    f(eng)

            @block.sync
            def _(eng):
                for f in b.ops["sync"]:
                    f(eng)
    return nc


def build_program(cfg: Cfg):
    b = Build(cfg)
    nc, c = b.nc, cfg
    D_, T_, H_, TH, KT, NH = c.D, c.T, c.HALO, c.TH, c.KT, c.NH
    NT, HT = T_ // 128, H_ // 128
    tsp = [(i * 512, min(512, T_ - i * 512)) for i in range((T_ + 511) // 512)]
    KQ, NKQ, ERT = c.KQ, c.NKQ, c.ERT
    dram = {}

    def din(name, shape, dtype):
        dram[name] = nc.dram_tensor(name, shape, dtype, kind="ExternalInput").ap()
        return dram[name]

    def dout(name, shape, dtype):
        dram[name] = nc.dram_tensor(name, shape, dtype, kind="ExternalOutput").ap()
        return dram[name]

    def dint(name, shape, dtype):
        dram[name] = nc.dram_tensor(name, shape, dtype, kind="Internal").ap()
        return dram[name]

    x_in = din("x_in", [T_, D_], F32)
    prev_in = din("prev_in", [H_, D_], F32)
    w_inproj = din("w_inproj", [D_, c.PROJ], F32)
    w_convc = din("w_convc", [128, c.XOT * c.K], F32)
    w_outproj = din("w_outproj", [D_, D_], F32)
    w_ffnT = din("w_ffnT", [D_, D_], F32)
    w_lAT = din("w_lAT", [D_, c.ER], F32)
    w_lBt = din("w_lBt", [c.ER, D_], F32)
    probs_col = din("probs_col", [128, c.ER // 128], F32)
    mup_cols = din("mup_cols", [128, c.NOT], F32)
    rows3_in = din("rows3_in", [65, D_], F32)
    alog_row = din("alog_row", [1, NH], F32)
    dtb_col = din("dtb_col", [NH, 1], F32)
    d_cols = din("d_cols", [128, NH], F32)
    ident_bf = din("ident_bf", [128, 128], BF16)
    ident_f = din("ident_f", [64, 64], F32)
    ut_bf = din("ut_bf", [128, 128], BF16)
    tri_bf = din("tri_bf", [128, 128], BF16)
    ones_bf = din("ones_bf", [1, 128], BF16)
    ones65_bf = din("ones65_bf", [65, 128], BF16)
    ones_f = din("ones_f", [1, 128], F32)
    fconsts = din("fconsts", [128, 8], F32)
    out = dout("out", [T_, D_], F32)

    ydump = dint("ydump", [T_, D_], F32)
    lnspill = dint("lnspill", [T_, D_], BF16)
    scspill = dint("scspill", [TH, D_], BF16)
    zspill = dint("zspill", [2 * D_ + 2 * c.S, TH], BF16)
    ffnspill = dint("ffnspill", [T_, D_], BF16)
    yspill = dint("yspill", [T_, D_], BF16)

    sb = b.sbuf
    poolA = sb("poolA", [128, KT * T_], BF16)
    PB = max(KT * H_ + 512 + 3 + TH,
             KT * ERT * 128 + 2 * ERT * 512 + ERT * T_,
             NH * 128 + 2 * D_ + 512,
             2 * 512)
    poolB = sb("poolB", [128, PB], BF16)
    poolF = sb("poolF", [128, max(D_, 2560, 2 * TH)], F32)
    wq = [sb(f"wq{i}", [128, KQ * 512], BF16) for i in range(2)]
    xs_tm = [sb(f"xs_tm{i}", [128, D_], BF16) for i in range(2)]
    sz_tm = sb("sz_tm", [128, D_], BF16)
    bcc = [sb(f"bcc{i}", [128, 2 * 128], BF16) for i in range(2)]

    ln_bf = poolA[:, 0:D_]
    sc_bf = poolA[:, D_:2 * D_]
    lnT = poolA
    lnTh = poolB
    diag_cv = poolB[:, KT * H_:KT * H_ + 512]
    xbc_tmp = poolB[:, KT * H_ + 512:KT * H_ + 512 + 3 + TH]
    lAT = poolB[:, 0:KT * ERT * 128]
    lB0 = KT * ERT * 128
    lBtq = [[poolB[:, lB0 + (p * ERT + et) * 512: lB0 + (p * ERT + et) * 512 + 512]
             for et in range(ERT)] for p in range(2)]
    hprime = poolB[:, lB0 + 2 * ERT * 512:lB0 + 2 * ERT * 512 + ERT * T_]
    diagD = poolB[:, 0:NH * 128]
    nw_bc = poolB[:, NH * 128:NH * 128 + D_]
    yhat_bf = poolB[:, NH * 128 + D_:NH * 128 + 2 * D_]
    xw8 = poolB[:, NH * 128 + 2 * D_:NH * 128 + 2 * D_ + 512]
    gbufs = [poolB[:, i * 512:(i + 1) * 512] for i in range(2)]
    xa = poolF[:, 0:D_]
    dtscr_f = poolF[0:NH, 0:TH]
    dt_t2 = poolF[0:64, TH:2 * TH]
    yfull = poolF[:, 0:D_]
    xrb = [poolF[:, i * 512:(i + 1) * 512] for i in range(2)]
    ob = [poolF[:, 1024 + i * 512:1024 + (i + 1) * 512] for i in range(2)]
    gsil = poolF[:, 2048:2560]
    sqjunkA = xs_tm[0]
    lnw_bc = xs_tm[1]

    ev_bf = sb("ev_bf", [128, max(TH, 512)], BF16)
    convw = sb("convw", [128, c.XOT * c.K], F32)
    mupc = sb("mupc", [128, c.NOT], F32)
    dt_sp = sb("dt_sp", [NH, TH], F32)
    H = sb("H", [128, NH * 64], F32)
    H_bf = sb("H_bf", [128, NH * 64], BF16)
    dtT = sb("dtT", [128, NH], F32)
    logdtT = sb("logdtT", [128, NH], F32)
    dtAT = sb("dtAT", [128, NH], BF16)
    colst = sb("colst", [128, NH], F32)
    expcT = sb("expcT", [128, NH], F32)
    a_bc = sb("a_bc", [128, NH], F32)
    a_col = sb("a_col", [NH, 1], F32)
    atot_bc = sb("atot_bc", [128, NH], F32)
    wtilT = sb("wtilT", [128, NH], F32)
    rowbuf = sb("rowbuf", [1, 128], F32)
    dtsum = sb("dtsum", [NH, 2], F32)
    gsb = sb("gsb", [128, 128], F32)
    btile = sb("btile", [128, 128], BF16)
    e8 = sb("e8", [128, 8 * 128], BF16)
    mstat = [sb(f"mstat{i}", [128, 128], BF16) for i in range(2)]
    t1buf = sb("t1buf", [128, 512], F32)
    dcol_all = sb("dcol_all", [128, NH], F32)
    ident_s = sb("ident_s", [128, 128], BF16)
    identf_s = sb("identf_s", [64, 64], F32)
    ut_s = sb("ut_s", [128, 128], BF16)
    tri_s = sb("tri_s", [128, 128], BF16)
    ones_s = sb("ones_s", [1, 128], BF16)
    ones65 = sb("ones65", [65, 128], BF16)
    onesf_s = sb("onesf_s", [1, 128], F32)
    dtb_s = sb("dtb_s", [NH, 1], F32)
    arow_s = sb("arow_s", [1, NH], F32)
    arow_bf = sb("arow_bf", [1, NH], BF16)
    rows3 = sb("rows3", [65, D_], BF16)
    probsc = sb("probsc", [128, max(ERT, 1)], F32)
    lncol = sb("lncol", [128, 8], F32)
    fconst = sb("fconst", [128, 8], F32)
    scr = sb("scr", [128, 8], F32)

    pb = [b.psum(f"pb{i}", [128, 512], F32) for i in range(7)]
    pbf = b.psum("pbf", [128, 1024], BF16)

    s_te = b.sem("s_te")
    s_dve = b.sem("s_dve")
    s_act = b.sem("s_act")
    s_ldx = b.sem("s_ldx", 16)
    s_ldw = [b.sem(f"s_ldw{i}", 16) for i in range(2)]
    s_st = b.sem("s_st", 16)
    s_misc = b.sem("s_misc", 16)
    s_lb = [b.sem(f"s_lb{i}", 16) for i in range(2)]
    s_xa = b.sem("s_xa", 16)
    s_lnt = b.sem("s_lnt", 16)
    s_rel2 = [b.sem(f"s_rel2{i}", 16) for i in range(2)]
    s_sz = b.sem("s_sz", 16)
    s_gx = [b.sem(f"s_gx{i}", 16) for i in range(2)]

    def ld_sync(dst, src, transpose=False, sem=None):
        return b.run("sync", lambda e, d=dst, s=src, t=transpose:
                     e.dma_start(out=d, in_=s, transpose=t), inc=(sem or s_ldx))

    def ld_w(dst, src, wi):
        return b.run("gpsimd", lambda e, d=dst, s=src: e.dma_start(out=d, in_=s),
                     inc=s_ldw[wi])

    def store(dst, src, gate_sem, gate_val):
        b.wait("scalar", gate_sem, gate_val)
        return b.run("scalar", lambda e, d=dst, s=src: e.dma_start(out=d, in_=s), inc=s_st)

    def rb_act(tile):
        p = tile.shape[0]
        b.wait("scalar", s_act, s_act.n)
        return b.run("scalar", lambda e, t=tile, p=p: e.activation(scr[0:p, 6:7], t, AF.Copy), inc=s_act)

    def rb_dve(tile):
        p = tile.shape[0]
        b.wait("vector", s_dve, s_dve.n)
        return b.run("vector", lambda e, t=tile, p=p: e.tensor_copy(scr[0:p, 7:8], t), inc=s_dve)

    def barrier():
        te0, dv0, ac0 = s_te.n, s_dve.n, s_act.n
        b.wait("tensor", s_dve, dv0)
        b.wait("tensor", s_act, ac0)
        b.wait("vector", s_te, te0)
        b.wait("vector", s_act, ac0)
        b.wait("scalar", s_te, te0)
        b.wait("scalar", s_dve, dv0)

    ld_sync(fconst[:], fconsts)
    for _i, _v in enumerate([1e-5, 1.0 / D_, -0.5, -1.0, 1.0, 0.0, 3.0e37, 0.5]):
        nc.const_aps.register(F32, float(_v), fconst[:, _i:_i + 1])
    ld_sync(ident_s[:], ident_bf)
    ld_sync(identf_s[:], ident_f)
    ld_sync(ut_s[:], ut_bf)
    ld_sync(tri_s[:], tri_bf)
    ld_sync(ones_s[:], ones_bf)
    ld_sync(ones65[:], ones65_bf)
    ld_sync(onesf_s[:], ones_f)
    ld_sync(convw[:], w_convc)
    ld_sync(mupc[:], mup_cols)
    ld_sync(dtb_s[:], dtb_col)
    ld_sync(arow_s[:], alog_row)
    ld_sync(dcol_all[:], d_cols)
    cst = ld_sync(probsc[:], probs_col)
    gw0 = b.run("gpsimd", lambda e: e.dma_start(out=rows3[:], in_=rows3_in), inc=s_misc)

    b.wait("vector", s_ldx, cst)
    b.run("vector", lambda e: e.memset(H[:, 0:NH * 64], 0.0))
    b.run("vector", lambda e: e.memset(H_bf[:, 0:NH * 64], 0.0), inc=s_dve)

    b.wait("tensor", s_ldx, cst)
    b.wait("tensor", s_misc, gw0)
    lnwev = {0: 0, 1: 0}
    for j in range(D_ // 512):
        b.wait("tensor", s_dve, lnwev[j % 2])
        tt = b.run("tensor", lambda e, j=j: e.matmul(
            pb[j % 2][:], ones_s[0:1, :], rows3[0:1, j * 512:(j + 1) * 512],
            start=True, stop=True), inc=s_te)
        b.wait("vector", s_te, tt)
        lnwev[j % 2] = b.run("vector", lambda e, j=j: e.tensor_copy(
            lnw_bc[:, j * 512:(j + 1) * 512], pb[j % 2][:]), inc=s_dve)
    b.wait("scalar", s_ldx, cst)
    b.run("scalar", lambda e: e.activation(arow_s[:], arow_s[:], AF.Exp))
    b.run("scalar", lambda e: e.mul(arow_s[:], arow_s[:], -1.0))
    aa = b.run("scalar", lambda e: e.copy(arow_bf[:], arow_s[:]), inc=s_act)
    b.wait("tensor", s_act, aa)
    tt = b.run("tensor", lambda e: e.matmul(pb[2][:, 0:NH], ones_s[0:1, :],
                                            arow_bf[0:1, :], start=True, stop=True), inc=s_te)
    t2 = b.run("tensor", lambda e: e.transpose(pb[3][0:NH, 0:1], arow_s[0:1, 0:NH],
                                               identf_s[0:1, 0:1]), inc=s_te)
    b.wait("vector", s_te, t2)
    b.run("vector", lambda e: e.tensor_copy(a_bc[:], pb[2][:, 0:NH]))
    b.run("vector", lambda e: e.tensor_copy(a_col[:], pb[3][0:NH, 0:1]), inc=s_dve)

    inv_d = 1.0 / D_
    a5_hist = 0
    st_hist = 0
    for i in range(NT + HT):
        halo = i >= NT
        src = prev_in[(i - NT) * 128:(i - NT + 1) * 128, :] if halo else \
            x_in[i * 128:(i + 1) * 128, :]
        b.wait("sync", s_act, a5_hist)
        lt = ld_sync(xa, src, sem=s_xa)
        b.wait("vector", s_xa, lt)
        vrd = b.run("vector", lambda e: e.tensor_reduce(lncol[:, 0:1], xa, AX.X, ALU.add), inc=s_dve)
        b.wait("vector", s_dve, vrd)
        v1 = b.run("vector", lambda e: e.tensor_scalar(lncol[:, 1:2], lncol[:, 0:1], inv_d, None, ALU.mult), inc=s_dve)
        b.wait("scalar", s_xa, lt)
        b.wait("scalar", s_dve, v1)
        vz0 = b.run("vector", lambda e: e.memset(lncol[:, 2:3], 0.0), inc=s_dve)
        b.wait("scalar", s_dve, vz0)
        a1 = b.run("scalar", lambda e: e.activation(
            sqjunkA[:], xa, AF.Square, accum_out=lncol[:, 2:3]), inc=s_act)
        b.wait("vector", s_act, a1)
        b.run("vector", lambda e: e.tensor_tensor(lncol[:, 3:4], lncol[:, 1:2], lncol[:, 1:2], ALU.mult))
        vq = b.run("vector", lambda e: e.tensor_scalar(lncol[:, 2:3], lncol[:, 2:3], inv_d, None, ALU.mult), inc=s_dve)
        b.wait("vector", s_dve, vq)
        v2 = b.run("vector", lambda e: e.tensor_tensor(lncol[:, 2:3], lncol[:, 2:3], lncol[:, 3:4], ALU.subtract), inc=s_dve)
        b.wait("scalar", s_dve, v2)
        b.run("scalar", lambda e: e.activation(lncol[:, 3:4], lncol[:, 2:3], AF.Ln, bias=1e-5))
        a2 = b.run("scalar", lambda e: e.activation(lncol[:, 3:4], lncol[:, 3:4], AF.Exp, scale=-0.5), inc=s_act)
        b.wait("vector", s_act, a2)
        vq = b.run("vector", lambda e: e.tensor_tensor(lncol[:, 4:5], lncol[:, 2:3], lncol[:, 3:4], ALU.mult), inc=s_dve)
        b.wait("vector", s_dve, vq)
        v3 = b.run("vector", lambda e: e.tensor_tensor(lncol[:, 4:5], lncol[:, 4:5], lncol[:, 3:4], ALU.mult), inc=s_dve)
        b.wait("scalar", s_dve, v3)
        b.run("scalar", lambda e: e.activation(lncol[:, 4:5], lncol[:, 4:5], AF.Ln, bias=1e-5))
        a3 = b.run("scalar", lambda e: e.activation(lncol[:, 4:5], lncol[:, 4:5], AF.Exp, scale=-0.5), inc=s_act)
        b.wait("vector", s_act, a3)
        vq = b.run("vector", lambda e: e.tensor_tensor(lncol[:, 5:6], lncol[:, 1:2], lncol[:, 3:4], ALU.mult), inc=s_dve)
        b.wait("vector", s_dve, vq)
        vq = b.run("vector", lambda e: e.tensor_scalar(lncol[:, 5:6], lncol[:, 5:6], -1.0, None, ALU.mult), inc=s_dve)
        b.run("vector", lambda e: e.tensor_tensor(lncol[:, 6:7], lncol[:, 3:4], lncol[:, 4:5], ALU.mult))
        b.wait("vector", s_dve, vq)
        v4 = b.run("vector", lambda e: e.tensor_tensor(lncol[:, 7:8], lncol[:, 5:6], lncol[:, 4:5], ALU.mult), inc=s_dve)
        b.wait("scalar", s_dve, v4)
        b.wait("scalar", s_st, st_hist)
        if not halo:
            b.run("scalar", lambda e: e.activation(
                ln_bf, xa, AF.Identity, bias=lncol[:, 5:6], scale=lncol[:, 3:4]))
        a5 = b.run("scalar", lambda e: e.activation(
            sc_bf, xa, AF.Identity, bias=lncol[:, 7:8], scale=lncol[:, 6:7]), inc=s_act)
        a5_hist = a5
        b.wait("vector", s_act, a5)
        v5 = b.run("vector", lambda e: e.tensor_tensor(sc_bf, sc_bf, lnw_bc[:], ALU.mult), inc=s_dve)
        g1 = rb_dve(sc_bf[:, 0:1])
        row0 = (T_ + (i - NT) * 128) if halo else i * 128
        if not halo:
            ga = rb_act(ln_bf[:, 0:1])
            store(lnspill[i * 128:(i + 1) * 128, :], ln_bf, s_act, ga)
        st_hist = store(scspill[row0:row0 + 128, :], sc_bf, s_dve, g1)
    a1_spills = s_st.n
    barrier()

    b.wait("sync", s_st, a1_spills)
    b.wait("sync", s_dve, s_dve.n)
    b.wait("sync", s_act, s_act.n)
    for k in range(KT):
        ld_sync(lnT[:, k * T_:(k + 1) * T_],
                scspill[0:T_, k * 128:(k + 1) * 128], transpose=True, sem=s_lnt)
        ld_sync(lnTh[:, k * H_:(k + 1) * H_],
                scspill[T_:TH, k * 128:(k + 1) * 128], transpose=True, sem=s_lnt)
    lnT_ready = s_lnt.n

    if c.phases < 2:
        return _finalize(b, nc, c, dram)
    n_blk = (c.NOT + 1) // 2
    pfree = {0: (None, 0), 1: (None, 0)}
    wfree = {0: 0, 1: 0}
    evst = 0
    conv_prev = 0
    qi = 0
    b.wait("tensor", s_lnt, lnT_ready)
    for blk in range(n_blk):
        o0 = blk * 2
        ntile = min(2, c.NOT - o0)
        cols0 = o0 * 128
        cols1 = min((o0 + 2) * 128, c.PROJ)
        bw = cols1 - cols0
        psets = [[pb[0], pb[1], pb[2]], [pb[3], pb[4], pb[5]]]
        for oj in range(ntile):
            sfree, vfree = pfree[oj]
            if sfree is not None:
                b.wait("tensor", sfree, vfree)
        tdone = [0] * ntile
        for kq in range(NKQ):
            wi = qi % 2
            qi += 1
            b.wait("gpsimd", s_te, wfree[wi])
            lw = ld_w(wq[wi][:, 0:KQ * bw].rearrange("p (k o) -> p k o", k=KQ),
                      w_inproj[kq * KQ * 128:min((kq + 1) * KQ * 128, D_), cols0:cols1]
                      .rearrange("(k p) o -> p k o", p=128), wi)
            b.wait("tensor", s_ldw[wi], lw)
            for oj in range(ntile):
                o = o0 + oj
                is_dt = (o == c.NOT - 1)
                owid = c.DTW if is_dt else 128
                is_z = o < c.ZOT
                ps = psets[oj]

                def mms(eng, wi=wi, kq=kq, oj=oj, owid=owid, is_z=is_z, ps=ps, bw=bw):
                    ins = None
                    for kk in range(KQ):
                        k = kq * KQ + kk
                        if k >= KT:
                            break
                        st = wq[wi][:, kk * bw + oj * 128: kk * bw + oj * 128 + owid]
                        first = (k == 0)
                        last = (k == KT - 1)
                        for hi, (t0, tw) in enumerate(tsp):
                            ins = eng.matmul(ps[hi][0:owid, 0:tw], st,
                                             lnT[:, k * T_ + t0:k * T_ + t0 + tw],
                                             start=first, stop=last)
                        if not is_z:
                            ins = eng.matmul(ps[2][0:owid, 0:H_], st,
                                             lnTh[:, k * H_:(k + 1) * H_],
                                             start=first, stop=last)
                    return ins
                tdone[oj] = b.run("tensor", mms, inc=s_te)
            wfree[wi] = s_te.n
        for oj in range(ntile):
            o = o0 + oj
            is_dt = (o == c.NOT - 1)
            owid = c.DTW if is_dt else 128
            is_z = o < c.ZOT
            ps = psets[oj]
            mupcol = mupc[:, o:o + 1]
            if is_z:
                b.wait("scalar", s_te, tdone[oj])
                b.wait("scalar", s_st, evst)
                for hi, (t0, tw) in enumerate(tsp):
                    b.run("scalar", lambda e, hi=hi, t0=t0, tw=tw, ps=ps, mc=mupcol: e.activation(
                        ev_bf[:, t0:t0 + tw], ps[hi][:, 0:tw], AF.Silu, scale=mc))
                ga = rb_act(ev_bf[:, 0:1])
                pfree[oj] = (s_act, ga)
                evst = store(zspill[o * 128:(o + 1) * 128, H_:TH], ev_bf[:, 0:T_], s_act, ga)
            else:
                b.wait("vector", s_te, tdone[oj])
                b.wait("vector", s_te, conv_prev)
                for hi, (t0, tw) in enumerate(tsp):
                    b.run("vector", lambda e, hi=hi, t0=t0, tw=tw, ps=ps, mc=mupcol, w=owid: e.tensor_scalar(
                        xbc_tmp[0:w, 3 + H_ + t0:3 + H_ + t0 + tw], ps[hi][0:w, 0:tw],
                        mc[0:w, :], None, ALU.mult))
                vv1 = b.run("vector", lambda e, ps=ps, mc=mupcol, w=owid: e.tensor_scalar(
                    xbc_tmp[0:w, 3:3 + H_], ps[2][0:w, 0:H_], mc[0:w, :], None, ALU.mult), inc=s_dve)
                if is_dt:
                    b.run("vector", lambda e, w=owid: e.tensor_copy(
                        dtscr_f[0:w, 0:TH], xbc_tmp[0:w, 3:3 + TH]))
                    vd = rb_dve(dtscr_f[0:NH, 0:1])
                    pfree[oj] = (s_dve, vd)
                else:
                    xo = o - c.ZOT
                    b.run("vector", lambda e: e.memset(xbc_tmp[:, 0:3], 0.0))
                    for kk in range(c.K):
                        b.run("vector", lambda e, xo=xo, kk=kk: e.tensor_scalar(
                            diag_cv[:, kk * 128:(kk + 1) * 128], ident_s[:],
                            convw[:, (xo * c.K + kk):(xo * c.K + kk + 1)], None, ALU.mult))
                    vd = b.run("vector", lambda e: e.tensor_copy(scr[:, 1:2], diag_cv[:, 0:1]), inc=s_dve)
                    b.wait("tensor", s_dve, vd)

                    def conv_mms(eng, ps=ps):
                        ins = None
                        for kk in range(c.K):
                            dg = diag_cv[:, kk * 128:(kk + 1) * 128]
                            ins = eng.matmul(ps[2][:, 0:H_], dg, xbc_tmp[:, kk:kk + H_],
                                             start=(kk == 0), stop=(kk == c.K - 1))
                            for hi, (t0, tw) in enumerate(tsp):
                                ins = eng.matmul(ps[hi][:, 0:tw], dg,
                                                 xbc_tmp[:, H_ + t0 + kk:H_ + t0 + kk + tw],
                                                 start=(kk == 0), stop=(kk == c.K - 1))
                        return ins
                    tc = b.run("tensor", conv_mms, inc=s_te)
                    conv_prev = tc
                    b.wait("scalar", s_te, tc)
                    b.wait("scalar", s_st, evst)
                    b.run("scalar", lambda e, ps=ps: e.activation(
                        ev_bf[:, 0:H_], ps[2][:, 0:H_], AF.Silu))
                    for hi, (t0, tw) in enumerate(tsp):
                        b.run("scalar", lambda e, hi=hi, t0=t0, tw=tw, ps=ps: e.activation(
                            ev_bf[:, H_ + t0:H_ + t0 + tw], ps[hi][:, 0:tw], AF.Silu))
                    ga = rb_act(ev_bf[:, 0:1])
                    pfree[oj] = (s_act, ga)
                    evst = store(zspill[D_ + xo * 128:D_ + (xo + 1) * 128, :],
                                 ev_bf[:, 0:TH], s_act, ga)
    a2_spills = s_st.n
    barrier()

    b.run("vector", lambda e: e.tensor_scalar(dtscr_f[0:NH, 0:TH], dtscr_f[0:NH, 0:TH],
                                              dtb_s[0:NH, :], None, ALU.add))
    vr = b.run("vector", lambda e: e.tensor_scalar(dt_sp[0:NH, :], dtscr_f[0:NH, 0:TH],
                                                   0.0, None, ALU.max), inc=s_dve)
    b.wait("scalar", s_dve, vr)
    b.run("scalar", lambda e: e.activation(dt_t2[0:NH, 0:TH], dtscr_f[0:NH, 0:TH], AF.Abs))
    b.run("scalar", lambda e: e.activation(dt_t2[0:NH, 0:TH], dt_t2[0:NH, 0:TH], AF.Exp, scale=-1.0))
    asp = b.run("scalar", lambda e: e.activation(dt_t2[0:NH, 0:TH], dt_t2[0:NH, 0:TH], AF.Ln, bias=1.0), inc=s_act)
    b.wait("vector", s_act, asp)
    vsp = b.run("vector", lambda e: e.tensor_tensor(dt_sp[0:NH, :], dt_sp[0:NH, :],
                                                    dt_t2[0:NH, 0:TH], ALU.add), inc=s_dve)

    def emit_reload(ci, slot, main):
        col0 = (H_ + ci * 128) if main else ci * 128
        b.wait("sync", s_st, a2_spills)
        b.wait("sync", s_te, s_te.n)
        b.wait("sync", s_dve, s_dve.n)
        b.wait("sync", s_act, s_act.n)
        ld_sync(xs_tm[slot][:], zspill[D_:2 * D_, col0:col0 + 128], transpose=True,
                sem=s_rel2[slot])
        return (slot, ld_sync(bcc[slot].rearrange("p (a t) -> p a t", a=2),
                              zspill[2 * D_:2 * D_ + 2 * c.S, col0:col0 + 128]
                              .rearrange("(a p) t -> p a t", p=128), sem=s_rel2[slot]))

    def chunk_prep(ci, main, slot, rel_tgt):
        mp = getattr(c, "microphase", 9)
        barrier()
        col0 = (H_ + ci * 128) if main else ci * 128
        if mp < 0.5:
            return dict(vcs=0, aat=0, aex=0, vbt=0, awt=0, vg=0)
        b.wait("tensor", s_dve, vsp)
        ta = b.run("tensor", lambda e, c0=col0: e.transpose(
            pb[2][:, 0:NH], dt_sp[0:NH, c0:c0 + 128], identf_s[0:NH, 0:NH]), inc=s_te)
        if mp < 1:
            tq = b.run("tensor", lambda e: e.matmul(
                pb[3][:, 0:128], ident_s[:], ut_s[:], start=True, stop=True), inc=s_te)
            b.wait("vector", s_te, tq)
            b.run("vector", lambda e: e.tensor_copy(gsb[:], pb[3][:, 0:128]), inc=s_dve)
            return dict(vcs=0, aat=0, aex=0, vbt=0, awt=0, vg=0)
        b.wait("vector", s_te, ta)
        vda = b.run("vector", lambda e: e.tensor_copy(dtT[:], pb[2][:, 0:NH]), inc=s_dve)
        if mp < 2:
            return dict(vcs=0, aat=0, aex=0, vbt=0, awt=0, vg=0)
        b.wait("scalar", s_dve, vda)
        ala = b.run("scalar", lambda e: e.activation(logdtT[:], dtT[:], AF.Ln), inc=s_act)
        if mp < 2.2:
            return dict(vcs=0, aat=0, aex=0, vbt=0, awt=0, vg=0)
        b.wait("vector", s_dve, vda)
        vab = b.run("vector", lambda e: e.tensor_tensor(dtAT[:], dtT[:], a_bc[:], ALU.mult), inc=s_dve)
        if mp < 2.4:
            return dict(vcs=0, aat=0, aex=0, vbt=0, awt=0, vg=0)
        b.wait("tensor", s_dve, vab)
        tb_ = b.run("tensor", lambda e: e.matmul(pb[2][:, 0:NH], ut_s[:], dtAT[:],
                                                 start=True, stop=True), inc=s_te)
        if mp < 2.6:
            return dict(vcs=0, aat=0, aex=0, vbt=0, awt=0, vg=0)
        b.wait("vector", s_te, tb_)
        b.wait("vector", s_act, ala)
        vcs = b.run("vector", lambda e: e.tensor_tensor(colst[:], logdtT[:], pb[2][:, 0:NH], ALU.subtract), inc=s_dve)
        if mp < 2.8:
            return dict(vcs=0, aat=0, aex=0, vbt=0, awt=0, vg=0)
        b.wait("scalar", s_te, tb_)
        b.wait("scalar", s_dve, vcs)
        aex = b.run("scalar", lambda e: e.activation(expcT[:], pb[2][:, 0:NH], AF.Exp), inc=s_act)
        if mp < 3:
            return dict(vcs=vcs, aat=0, aex=aex, vbt=0, awt=0, vg=0)
        vq2 = b.run("vector", lambda e, c0=col0: e.tensor_reduce(
            dtsum[0:NH, 0:1], dt_sp[0:NH, c0:c0 + 128], AX.X, ALU.add), inc=s_dve)
        b.wait("vector", s_dve, vq2)
        vds = b.run("vector", lambda e: e.tensor_tensor(dtsum[0:NH, 1:2], dtsum[0:NH, 0:1],
                                                        a_col[0:NH, :], ALU.mult), inc=s_dve)
        b.wait("tensor", s_dve, vds)
        b.wait("tensor", s_act, aex)
        b.wait("tensor", s_dve, vcs)
        if mp < 4:
            return dict(vcs=vcs, aat=0, aex=aex, vbt=0, awt=0, vg=0)
        tc_ = b.run("tensor", lambda e: e.transpose(pb[2][0:1, 0:NH], dtsum[0:NH, 1:2],
                                                    identf_s[0:NH, 0:NH]), inc=s_te)
        b.wait("vector", s_te, tc_)
        vrow = b.run("vector", lambda e: e.tensor_copy(rowbuf[0:1, 0:NH], pb[2][0:1, 0:NH]), inc=s_dve)
        b.wait("tensor", s_dve, vrow)
        if mp < 5:
            return dict(vcs=vcs, aat=0, aex=aex, vbt=0, awt=0, vg=0)
        td = b.run("tensor", lambda e: e.matmul(pb[2][:, 0:NH], onesf_s[0:1, :],
                                                rowbuf[0:1, 0:NH], start=True, stop=True), inc=s_te)
        b.wait("scalar", s_te, td)
        aat = b.run("scalar", lambda e: e.activation(atot_bc[:], pb[2][:, 0:NH], AF.Exp), inc=s_act)
        pr = dict(vcs=vcs, aat=aat, aex=aex)
        if not main:
            b.wait("vector", s_te, td)
            b.wait("vector", s_dve, vcs)
            b.wait("vector", s_act, aat)
            vwa = b.run("vector", lambda e: e.tensor_tensor(wtilT[:], colst[:], pb[2][:, 0:NH], ALU.add), inc=s_dve)
            b.wait("scalar", s_dve, vwa)
            pr["awt"] = b.run("scalar", lambda e: e.activation(wtilT[:], wtilT[:], AF.Exp), inc=s_act)
        b.wait("tensor", s_rel2[rel_tgt[0]], rel_tgt[1])
        te_ = b.run("tensor", lambda e, sl=slot: e.transpose(
            pbf[:, 0:128], bcc[sl][:, 0:128], ident_s[:]), inc=s_te)
        b.wait("vector", s_te, te_)
        pr["vbt"] = b.run("vector", lambda e: e.tensor_copy(btile[:], pbf[:, 0:128]), inc=s_dve)
        if main:
            b.wait("tensor", s_act, aat)
            tg = b.run("tensor", lambda e, sl=slot: e.matmul(
                pb[2][:, 0:128], bcc[sl][:, 0:128], bcc[sl][:, 128:256],
                start=True, stop=True), inc=s_te)
            b.wait("vector", s_te, tg)
            pr["vg"] = b.run("vector", lambda e: e.tensor_tensor(gsb[:], pb[2][:, 0:128],
                                                                 tri_s[:], ALU.mult), inc=s_dve)
        return pr

    def emit_state_update(g, slot, pr, main):
        if main:
            src = e8.rearrange("p (h t) -> p h t", h=8)[:, :, 127:128].broadcast_to([128, 8, 64])
            b.wait("vector", s_act, pr["e_done"])
        else:
            src = wtilT[:, g * 8:(g + 1) * 8].broadcast_to([128, 8, 64])
            b.wait("vector", s_act, pr["awt"])
        vxw = b.run("vector", lambda e, g=g, sl=slot, s=src: e.tensor_tensor(
            xw8.rearrange("p (h t) -> p h t", h=8),
            xs_tm[sl].rearrange("p (h t) -> p h t", h=NH)[:, g * 8:(g + 1) * 8, :],
            s, ALU.mult), inc=s_dve)
        b.wait("tensor", s_dve, max(vxw, pr["vbt"], pr.get("hfree", 0)))
        th = b.run("tensor", lambda e: e.matmul(pb[6][:, 0:512], btile[:], xw8,
                                                start=True, stop=True), inc=s_te)
        b.wait("vector", s_te, th)
        b.wait("vector", s_act, pr["aat"])
        b.run("vector", lambda e, g=g: e.tensor_tensor(
            H.rearrange("p (h t) -> p h t", h=NH)[:, g * 8:(g + 1) * 8, :],
            H.rearrange("p (h t) -> p h t", h=NH)[:, g * 8:(g + 1) * 8, :],
            atot_bc[:, g * 8:(g + 1) * 8].broadcast_to([128, 8, 64]), ALU.mult))
        b.run("vector", lambda e, g=g: e.tensor_tensor(
            H[:, g * 512:g * 512 + 512], H[:, g * 512:g * 512 + 512], pb[6][:, 0:512], ALU.add))
        vH = b.run("vector", lambda e, g=g: e.tensor_copy(
            H_bf[:, g * 512:g * 512 + 512], H[:, g * 512:g * 512 + 512]), inc=s_dve)
        return vH

    if c.phases < 2.2:
        return _finalize(b, nc, c, dram)
    relq = []
    rel0 = emit_reload(0, 0, main=False)
    if c.phases < 2.4:
        return _finalize(b, nc, c, dram)
    hf = 0
    for hc in range(c.HC):
        slot = hc % 2
        nrel = emit_reload(hc + 1, (hc + 1) % 2, main=False) if hc + 1 < c.HC else None
        pr = chunk_prep(hc, False, slot, rel0 if hc == 0 else relq.pop())
        if nrel is not None:
            relq.append(nrel)
        if c.phases < 2.6:
            return _finalize(b, nc, c, dram)
        ng_lim = c.NG if c.phases >= 3 else max(1, int(round((c.phases - 2.6) * 20)))
        for g in range(min(c.NG, ng_lim)):
            pr["hfree"] = hf
            hf = emit_state_update(g, slot, pr, main=False)
        if c.phases < 3:
            return _finalize(b, nc, c, dram)
    barrier()

    if c.phases < 4:
        return _finalize(b, nc, c, dram)
    b.wait("sync", s_te, s_te.n)
    b.wait("sync", s_dve, s_dve.n)
    for k in range(KT):
        ld_sync(lnT[:, k * T_:(k + 1) * T_],
                lnspill[:, k * 128:(k + 1) * 128], transpose=True, sem=s_lnt)
    lnT2_ready = s_lnt.n
    b.wait("gpsimd", s_te, s_te.n)
    b.wait("gpsimd", s_dve, s_dve.n)
    b.wait("gpsimd", s_act, s_act.n)
    KC = max(1, KT // 4)
    for kc in range(0, KT, KC):
        kc1 = min(kc + KC, KT)
        lg = b.run("gpsimd", lambda e, kc=kc, kc1=kc1: e.dma_start(
            out=lAT[:, kc * ERT * 128:kc1 * ERT * 128].rearrange(
                "p (k o) -> p k o", o=ERT * 128),
            in_=w_lAT[kc * 128:kc1 * 128, :].rearrange("(k p) o -> p k o", p=128)),
            inc=s_misc)
    b.wait("tensor", s_lnt, lnT2_ready)
    b.wait("tensor", s_misc, lg)
    h_ev = {0: 0, 1: 0}
    hq = 0
    for et in range(ERT):
        for hi, (t0, tw) in enumerate(tsp):
            bank = hq % 2
            hq += 1
            b.wait("tensor", s_dve, h_ev[bank])

            def hmm(eng, et=et, t0=t0, tw=tw, bank=bank):
                ins = None
                for k in range(KT):
                    ins = eng.matmul(pb[bank][:, 0:tw],
                                     lAT[:, (k * ERT + et) * 128:(k * ERT + et) * 128 + 128],
                                     lnT[:, k * T_ + t0:k * T_ + t0 + tw],
                                     start=(k == 0), stop=(k == KT - 1))
                return ins
            th = b.run("tensor", hmm, inc=s_te)
            b.wait("vector", s_te, th)
            h_ev[bank] = b.run("vector", lambda e, et=et, t0=t0, tw=tw, bank=bank: e.tensor_scalar(
                hprime[:, et * T_ + t0:et * T_ + t0 + tw], pb[bank][:, 0:tw],
                probsc[:, et:et + 1], None, ALU.mult), inc=s_dve)
    vhp = rb_dve(hprime[:, 0:1])
    b.wait("tensor", s_dve, vhp)

    NT4 = min(4, NT)
    ffn_ev = {i: 0 for i in range(NT4)}
    evstl = 0
    lb_t = {0: 0, 1: 0}
    for os_ in range(D_ // 512):
        lbw = os_ % 2
        b.wait("gpsimd", s_te, lb_t[lbw])
        lbv = None
        for et in range(ERT):
            lbv = b.run("gpsimd", lambda e, et=et, os_=os_, lbw=lbw: e.dma_start(
                out=lBtq[lbw][et],
                in_=w_lBt[et * 128:(et + 1) * 128, os_ * 512:(os_ + 1) * 512]), inc=s_lb[lbw])
        for tp in range(NT // NT4):
            tis = [tp * NT4 + i for i in range(NT4)]
            for i, ti in enumerate(tis):
                b.wait("tensor", s_dve, ffn_ev[i])
                b.run("tensor", lambda e, i=i, os_=os_: e.matmul(
                    pb[i][:, 0:512], ones65[64:65, :], rows3[64:65, os_ * 512:(os_ + 1) * 512],
                    start=True, stop=False))
            for kq in range(NKQ):
                wi = qi % 2
                qi += 1
                b.wait("gpsimd", s_te, wfree[wi])
                lw = ld_w(wq[wi].rearrange("p (k o) -> p k o", k=KQ),
                          w_ffnT[kq * KQ * 128:(kq + 1) * KQ * 128, os_ * 512:(os_ + 1) * 512]
                          .rearrange("(k p) o -> p k o", p=128), wi)
                b.wait("tensor", s_ldw[wi], lw)

                def fmm(eng, wi=wi, kq=kq, tis=tis):
                    ins = None
                    for kk in range(KQ):
                        k = kq * KQ + kk
                        for i, ti in enumerate(tis):
                            ins = eng.matmul(pb[i][:, 0:512],
                                             lnT[:, k * T_ + ti * 128:k * T_ + ti * 128 + 128],
                                             wq[wi][:, kk * 512:(kk + 1) * 512],
                                             start=False, stop=False)
                    return ins
                b.run("tensor", fmm, inc=s_te)
                wfree[wi] = s_te.n
            b.wait("tensor", s_lb[lbw], lbv)

            def lmm(eng, tis=tis, lbw=lbw):
                ins = None
                for et in range(ERT):
                    for i, ti in enumerate(tis):
                        ins = eng.matmul(pb[i][:, 0:512],
                                         hprime[:, et * T_ + ti * 128:et * T_ + ti * 128 + 128],
                                         lBtq[lbw][et],
                                         start=False, stop=(et == ERT - 1))
                return ins
            tl = b.run("tensor", lmm, inc=s_te)
            for i, ti in enumerate(tis):
                b.wait("vector", s_te, tl)
                b.wait("vector", s_st, evstl)
                ve = b.run("vector", lambda e, i=i: e.tensor_copy(ev_bf[:, 0:512],
                                                                  pb[i][:, 0:512]), inc=s_dve)
                ffn_ev[i] = ve
                gv = rb_dve(ev_bf[:, 0:1])
                evstl = store(ffnspill[ti * 128:(ti + 1) * 128, os_ * 512:(os_ + 1) * 512],
                              ev_bf[:, 0:512], s_dve, gv)
        lb_t[lbw] = s_te.n
    barrier()

    if c.phases < 5:
        return _finalize(b, nc, c, dram)
    for h in range(NH):
        b.run("vector", lambda e, h=h: e.tensor_scalar(
            diagD[:, h * 128:(h + 1) * 128], ident_s[:], dcol_all[:, h:h + 1], None, ALU.mult))
    rb_dve(diagD[:, 0:1])
    nwev = {0: 0, 1: 0}
    for j in range(D_ // 512):
        bk = 2 + (j % 2)
        b.wait("tensor", s_dve, nwev[j % 2])
        tt = b.run("tensor", lambda e, j=j, bk=bk: e.matmul(
            pb[bk][:, 0:512], ones65[32:33, :], rows3[32:33, j * 512:(j + 1) * 512],
            start=True, stop=True), inc=s_te)
        b.wait("vector", s_te, tt)
        nwev[j % 2] = b.run("vector", lambda e, j=j, bk=bk: e.tensor_copy(
            nw_bc[:, j * 512:(j + 1) * 512], pb[bk][:, 0:512]), inc=s_dve)
    vnw = rb_dve(nw_bc[:, 0:1])
    b.wait("tensor", s_dve, vnw)

    vH_last = hf
    vg1 = 0
    rel0 = emit_reload(0, 0, main=True)
    relq = [rel0]
    y_sp = []
    for ci in range(c.NC):
        slot = ci % 2
        my_rel = relq.pop(0)
        if ci + 1 < c.NC:
            relq.append(emit_reload(ci + 1, (ci + 1) % 2, main=True))
        b.wait("sync", s_dve, vg1)
        szv = ld_sync(sz_tm[:], zspill[0:D_, H_ + ci * 128:H_ + ci * 128 + 128],
                      transpose=True, sem=s_sz)
        pr = chunk_prep(ci, True, slot, my_rel)
        yev = {0: 0, 1: 0}
        inter_free = pr["vg"]
        m_done = {0: 0, 1: 0}
        bc_free = {0: 0, 1: 0}
        for g in range(c.NG):
            ypb = pb[g % 2]
            b.wait("tensor", s_dve, max(vH_last, inter_free))
            b.wait("tensor", s_rel2[my_rel[0]], my_rel[1])
            t1m = b.run("tensor", lambda e, sl=slot, g=g: e.matmul(
                pb[2][:, 0:512], bcc[sl][:, 128:256], H_bf[:, g * 512:(g + 1) * 512],
                start=True, stop=True), inc=s_te)
            b.wait("vector", s_te, t1m)
            vt1 = b.run("vector", lambda e, g=g: e.tensor_tensor(
                t1buf.rearrange("p (h t) -> p h t", h=8),
                pb[2].rearrange("p (h t) -> p h t", h=8),
                expcT[:, g * 8:(g + 1) * 8].broadcast_to([128, 8, 64]), ALU.mult), inc=s_dve)
            inter_free = vt1
            b.wait("tensor", s_dve, yev[g % 2])
            e_done = 0
            for hl in range(8):
                h = g * 8 + hl
                bcb = pb[3] if hl % 2 == 0 else pb[5]
                if bc_free[hl % 2]:
                    b.wait("tensor", s_act, bc_free[hl % 2])
                tb_ = b.run("tensor", lambda e, h=h, bcb=bcb: e.matmul(
                    bcb[:, 0:128], dtAT[:, h:h + 1].broadcast_to([128, 128]),
                    ut_s[:], start=True, stop=True), inc=s_te)
                b.wait("scalar", s_te, tb_)
                b.wait("scalar", s_dve, pr["vcs"] if hl > 0 else max(pr["vcs"], vH_last))
                ae = b.run("scalar", lambda e, h=h, hl=hl, bcb=bcb: e.activation(
                    e8[:, hl * 128:(hl + 1) * 128], bcb[:, 0:128], AF.Exp,
                    bias=colst[:, h:h + 1]), inc=s_act)
                bc_free[hl % 2] = ae
                e_done = ae
                b.wait("vector", s_act, ae)
                b.wait("vector", s_dve, pr["vg"])
                b.wait("vector", s_te, m_done[hl % 2])
                vm = b.run("vector", lambda e, hl=hl: e.scalar_tensor_tensor(
                    mstat[hl % 2][:], e8[:, hl * 128:(hl + 1) * 128], 3.0e37, gsb[:],
                    ALU.min, ALU.mult), inc=s_dve)
                b.wait("tensor", s_dve, vm)

                def ymm(eng, h=h, hl=hl, sl=slot, yp=ypb):
                    eng.matmul(yp[:, hl * 64:hl * 64 + 64], mstat[hl % 2][:],
                               xs_tm[sl][:, h * 64:h * 64 + 64],
                               start=(hl == 0), stop=False)
                    return eng.matmul(yp[:, hl * 64:hl * 64 + 64],
                                      diagD[:, h * 128:(h + 1) * 128],
                                      xs_tm[sl][:, h * 64:h * 64 + 64],
                                      start=False, stop=(hl == 7))
                tym = b.run("tensor", ymm, inc=s_te)
                m_done[hl % 2] = tym
            pr["e_done"] = e_done
            pr["hfree"] = vH_last
            vH_last = emit_state_update(g, slot, pr, main=True)
            b.wait("vector", s_te, max(m_done[0], m_done[1]))
            vy = b.run("vector", lambda e, g=g, yp=ypb: e.tensor_tensor(
                yfull[:, g * 512:(g + 1) * 512], t1buf[:], yp[:, 0:512], ALU.add), inc=s_dve)
            yev[g % 2] = vy
        b.wait("vector", s_dve, max(yev[0], yev[1]))
        if c.debug:
            ygate = rb_dve(yfull[:, 0:1])
            ystt = store(ydump[ci * 128:(ci + 1) * 128, :], yfull, s_dve, ygate)
            b.wait("vector", s_st, ystt)
        b.wait("vector", s_sz, szv)
        vg1a = b.run("vector", lambda e: e.tensor_tensor(yfull, yfull, sz_tm[:], ALU.mult), inc=s_dve)
        vg1 = b.run("vector", lambda e: e.memset(lncol[:, 0:1], 0.0), inc=s_dve)
        b.wait("scalar", s_dve, max(vg1, vg1a))
        asq = b.run("scalar", lambda e, sl=slot: e.activation(
            xs_tm[sl][:], yfull, AF.Square, accum_out=lncol[:, 0:1]), inc=s_act)
        b.wait("scalar", s_act, asq)
        b.run("scalar", lambda e: e.activation(lncol[:, 1:2], lncol[:, 0:1], AF.Ln,
                                               scale=inv_d, bias=1e-5))
        ar2 = b.run("scalar", lambda e: e.activation(lncol[:, 1:2], lncol[:, 1:2], AF.Exp,
                                                     scale=-0.5), inc=s_act)
        b.wait("vector", s_act, ar2)
        b.wait("vector", s_st, y_sp[-1] if y_sp else 0)
        b.run("vector", lambda e: e.tensor_scalar(yfull, yfull, lncol[:, 1:2], None, ALU.mult))
        vyh = b.run("vector", lambda e: e.tensor_tensor(yhat_bf, yfull, nw_bc, ALU.mult), inc=s_dve)
        gv = rb_dve(yhat_bf[:, 0:1])
        y_sp.append(store(yspill[ci * 128:(ci + 1) * 128, :], yhat_bf, s_dve, gv))
    b_spills = s_st.n
    barrier()

    if c.phases < 6:
        return _finalize(b, nc, c, dram)
    b.wait("sync", s_st, b_spills)
    b.wait("sync", s_te, s_te.n)
    b.wait("sync", s_dve, s_dve.n)
    for k in range(KT):
        ld_sync(lnT[:, k * T_:(k + 1) * T_],
                yspill[:, k * 128:(k + 1) * 128], transpose=True, sem=s_lnt)
    ycm_ready = s_lnt.n
    ycm = lnT
    b.wait("tensor", s_lnt, ycm_ready)
    out_free = {i: 0 for i in range(NT4)}
    ld_free = (0, 0)
    ost = 0
    last_vo = 0
    for os_ in range(D_ // 512):
        for tp in range(NT // NT4):
            tis = [tp * NT4 + i for i in range(NT4)]
            to = 0
            for kq in range(NKQ):
                wi = qi % 2
                qi += 1
                b.wait("gpsimd", s_te, wfree[wi])
                lw = ld_w(wq[wi].rearrange("p (k o) -> p k o", k=KQ),
                          w_outproj[kq * KQ * 128:(kq + 1) * KQ * 128, os_ * 512:(os_ + 1) * 512]
                          .rearrange("(k p) o -> p k o", p=128), wi)
                b.wait("tensor", s_ldw[wi], lw)
                if kq == 0:
                    for i in range(NT4):
                        b.wait("tensor", s_dve, out_free[i])

                def omm(eng, wi=wi, kq=kq, tis=tis):
                    ins = None
                    for kk in range(KQ):
                        k = kq * KQ + kk
                        for i, ti in enumerate(tis):
                            ins = eng.matmul(pb[i][:, 0:512],
                                             ycm[:, k * T_ + ti * 128:k * T_ + ti * 128 + 128],
                                             wq[wi][:, kk * 512:(kk + 1) * 512],
                                             start=(k == 0), stop=(k == KT - 1))
                    return ins
                to = b.run("tensor", omm, inc=s_te)
                wfree[wi] = s_te.n
            for i, ti in enumerate(tis):
                gb = gbufs[ti % 2]
                xb_ = xrb[ti % 2]
                b.wait("sync", s_act, ld_free[0])
                b.wait("sync", s_dve, ld_free[1])
                lgv = ld_sync(gb, ffnspill[ti * 128:(ti + 1) * 128, os_ * 512:(os_ + 1) * 512],
                              sem=s_gx[ti % 2])
                lxv = ld_sync(xb_, x_in[ti * 128:(ti + 1) * 128, os_ * 512:(os_ + 1) * 512],
                              sem=s_gx[ti % 2])
                b.wait("scalar", s_gx[ti % 2], lxv)
                b.wait("scalar", s_dve, last_vo)
                ag = b.run("scalar", lambda e, gb=gb: e.activation(gsil, gb, AF.Silu), inc=s_act)
                b.wait("vector", s_te, to)
                b.wait("vector", s_act, ag)
                b.wait("vector", s_gx[ti % 2], lxv)
                b.wait("vector", s_st, ost)
                b.run("vector", lambda e, i=i: e.tensor_tensor(gsil, pb[i][:, 0:512], gsil, ALU.mult))
                vo = b.run("vector", lambda e, i=i, xb_=xb_: e.tensor_tensor(
                    ob[i % 2], gsil, xb_, ALU.add), inc=s_dve)
                out_free[i] = vo
                last_vo = vo
                ld_free = (ag, vo)
                gvo = rb_dve(ob[i % 2][:, 0:1])
                ost = store(out[ti * 128:(ti + 1) * 128, os_ * 512:(os_ + 1) * 512],
                            ob[i % 2], s_dve, gvo)

    if c.debug:
        for nm, t, shp in (("dbg_yd", ydump, [T_, D_]),
                           ("dbg_z", zspill, [2 * D_ + 2 * c.S, TH]),
                           ("dbg_ln", lnspill, [T_, D_]),
                           ("dbg_sc", scspill, [TH, D_]),
                           ("dbg_ffn", ffnspill, [T_, D_]),
                           ("dbg_y", yspill, [T_, D_])):
            o = dout(nm, shp, F32 if nm == "dbg_yd" else BF16)
            b.wait("sync", s_st, s_st.n)
            b.run("sync", lambda e, o=o, t=t: e.dma_start(out=o, in_=t), inc=s_ldx)
        oh = dout("dbg_h", [128, NH * 64], F32)
        b.wait("sync", s_dve, s_dve.n)
        b.run("sync", lambda e, o=oh: e.dma_start(out=o, in_=H[:]), inc=s_ldx)

    for sem_ in (s_st, s_ldx, s_misc, s_xa, s_lnt, s_sz, s_ldw[0], s_ldw[1],
                 s_lb[0], s_lb[1], s_rel2[0], s_rel2[1], s_gx[0], s_gx[1]):
        b.wait("sync", sem_, sem_.n)

    with b._ctx:
        with nc.Block() as block:
            @block.tensor
            def _(eng):
                for f in b.ops["tensor"]:
                    f(eng)

            @block.vector
            def _(eng):
                for f in b.ops["vector"]:
                    f(eng)

            @block.scalar
            def _(eng):
                for f in b.ops["scalar"]:
                    f(eng)

            @block.gpsimd
            def _(eng):
                for f in b.ops["gpsimd"]:
                    f(eng)

            @block.sync
            def _(eng):
                for f in b.ops["sync"]:
                    f(eng)
    return nc


def _mk_rows3(c, weights):
    r3 = np.zeros((65, c.D), np.float32)
    r3[0] = np.asarray(weights["ln_w"])
    r3[32] = np.asarray(weights["norm_w"])
    r3[64] = np.asarray(weights["ffn_b"])
    return r3


def make_inputs(cfg: Cfg, weights: dict, x_slab: np.ndarray, prev_slab: np.ndarray):
    c = cfg
    f32 = np.float32

    def asf(a):
        return np.ascontiguousarray(a, dtype=f32)

    wconv = np.asarray(weights["conv_w"])
    convc = np.zeros((128, c.XOT * c.K), f32)
    for xo in range(c.XOT):
        for k in range(c.K):
            convc[:, xo * c.K + k] = wconv[xo * 128:(xo + 1) * 128, k]
    mupcv = np.zeros((128, c.NOT), f32)
    mv = np.asarray(weights["mup_vector"])
    for o in range(c.NOT):
        w = min(128, c.PROJ - o * 128)
        mupcv[0:w, o] = mv[o * 128:o * 128 + w]
    A2d = np.asarray(weights["ffn_A"]).reshape(c.ER, c.D)
    Bt2d = np.asarray(weights["ffn_B"]).transpose(0, 2, 1).reshape(c.ER, c.D)
    return {
        "x_in": asf(x_slab),
        "prev_in": asf(prev_slab),
        "w_inproj": asf(weights["in_proj_w"]),
        "w_convc": convc,
        "w_outproj": asf(weights["out_proj_w"]),
        "w_ffnT": asf(np.asarray(weights["ffn_W"]).T),
        "w_lAT": asf(A2d.T),
        "w_lBt": asf(Bt2d),
        "probs_col": asf(np.repeat(np.asarray(weights["probs"]), c.R).reshape(c.ER // 128, 128).T),
        "mup_cols": mupcv,
        "rows3_in": (lambda r3: r3)(_mk_rows3(c, weights)),
        "alog_row": asf(np.asarray(weights["A_log"])[None, :]),
        "dtb_col": asf(np.asarray(weights["dt_bias"])[:, None]),
        "d_cols": asf(np.tile(np.asarray(weights["D"])[None, :], (128, 1))),
        "ident_bf": np.eye(128).astype(bf16),
        "ident_f": np.eye(64).astype(f32),
        "ut_bf": np.triu(np.ones((128, 128))).astype(bf16),
        "tri_bf": np.triu(np.ones((128, 128))).astype(bf16),
        "ones_bf": np.ones((1, 128), bf16),
        "ones65_bf": np.ones((65, 128), bf16),
        "ones_f": np.ones((1, 128), f32),
        "fconsts": np.tile(np.array([[1e-5, 1.0 / c.D, -0.5, -1.0, 1.0, 0.0, 3.0e37, 0.5]], f32), (128, 1)),
    }


from concourse.bass_utils import run_bass_kernel_spmd

_CFG = Cfg()
_NC_CACHE = {}


def _get_program():
    if "nc" not in _NC_CACHE:
        _NC_CACHE["nc"] = build_program(_CFG)
    return _NC_CACHE["nc"]


def kernel(**inputs):
    c = _CFG
    x = np.ascontiguousarray(np.asarray(inputs["L0_attn_hidden_states"], dtype=np.float32))
    B, L, D = x.shape
    assert (B, L, D) == (4, 2048, 4096)
    weights = {
        "in_proj_w": inputs["L0_attn_in_proj_w"],
        "conv_w": inputs["L0_attn_conv_w"],
        "out_proj_w": inputs["L0_attn_out_proj_w"],
        "ln_w": inputs["L0_attn_ln_w"],
        "norm_w": inputs["L0_attn_norm_w"],
        "mup_vector": inputs["L0_attn_mup_vector"],
        "A_log": inputs["L0_attn_A_log"],
        "dt_bias": inputs["L0_attn_dt_bias"],
        "D": inputs["L0_attn_D"],
        "ffn_W": inputs["L0_ffn_W"],
        "ffn_b": inputs["L0_ffn_b"],
        "ffn_A": inputs["L0_ffn_A"],
        "ffn_B": inputs["L0_ffn_B"],
        "probs": inputs["L0_ffn_probs"],
    }
    zeros_prev = np.zeros((c.HALO, c.D), np.float32)
    base = make_inputs(c, weights, x[0, :c.T], zeros_prev)
    nc = _get_program()
    in_maps = []
    for core in range(8):
        bt, hf = core // 2, core % 2
        m = dict(base)
        m["x_in"] = np.ascontiguousarray(x[bt, hf * c.T:(hf + 1) * c.T])
        m["prev_in"] = zeros_prev if hf == 0 else \
            np.ascontiguousarray(x[bt, c.T - c.HALO:c.T])
        in_maps.append(m)
    res = run_bass_kernel_spmd(nc, in_maps, core_ids=list(range(8)))
    out = np.empty((4, 2048, 4096), np.float32)
    for core in range(8):
        bt, hf = core // 2, core % 2
        out[bt, hf * c.T:(hf + 1) * c.T] = np.asarray(res.results[core]["out"])
    return out


# version 03
# speedup vs baseline: 1.0028x; 1.0028x over previous
import contextlib
import numpy as np
import ml_dtypes
import concourse.bass as bass
import concourse.mybir as mybir

F32 = mybir.dt.float32
BF16 = mybir.dt.bfloat16
AF = mybir.ActivationFunctionType
ALU = mybir.AluOpType
AX = mybir.AxisListType
bf16 = ml_dtypes.bfloat16

ENGINES = ("tensor", "vector", "scalar", "gpsimd", "sync")


class Cfg:
    def __init__(self, D=4096, S=128, NH=64, HD=64, K=4, T=1024, HALO=256,
                 E=16, R=16, debug=False, phases=6):
        self.D, self.S, self.NH, self.HD, self.K = D, S, NH, HD, K
        self.T, self.HALO, self.E, self.R = T, HALO, E, R
        self.debug = debug
        self.phases = phases
        self.CONV = D + 2 * S
        self.PROJ = 2 * D + 2 * S + NH
        self.KT = D // 128
        self.ZOT = D // 128
        self.XOT = self.CONV // 128
        self.NC = T // 128
        self.HC = HALO // 128
        self.TH = T + HALO
        self.NG = NH // 8
        self.ER = E * R
        self.NOT = (self.PROJ + 127) // 128
        self.DTW = self.PROJ - 128 * (self.NOT - 1)
        self.KQ = max(1, self.KT // 4)
        self.NKQ = (self.KT + self.KQ - 1) // self.KQ
        self.ERT = self.ER // 128
        assert D % 128 == 0 and self.CONV % 128 == 0 and S == 128 and HD == 64
        assert T % 128 == 0 and HALO % 128 == 0 and NH % 8 == 0 and self.ER % 128 == 0


class Sem:
    def __init__(self, h, step=1):
        self.h, self.step, self.n = h, step, 0

    def inc(self):
        self.n += self.step
        return self.n


class Build:
    def __init__(self, cfg):
        self.cfg = cfg
        self.nc = bass.Bass("TRN2", target_bir_lowering=False,
                            detect_race_conditions=False)
        self.ops = {e: [] for e in ENGINES}
        self._ctx = contextlib.ExitStack()

    def sem(self, name, step=1):
        sm = Sem(self._ctx.enter_context(self.nc.semaphore(name)), step)
        if not hasattr(self, "_allsems"):
            self._allsems = {}
        self._allsems[name] = sm
        return sm

    def sbuf(self, name, shape, dtype):
        return self._ctx.enter_context(self.nc.sbuf_tensor(name, shape, dtype))

    def psum(self, name, shape, dtype=F32):
        return self._ctx.enter_context(self.nc.psum_tensor(name, shape, dtype))

    def wait(self, e, sem, val):
        if val > 0:
            self.ops[e].append(lambda eng, s=sem.h, v=val: eng.wait_ge(s, v))

    def run(self, e, f, inc=None):
        incs = inc if isinstance(inc, (list, tuple)) else ([inc] if inc else [])
        tgts = [s.inc() for s in incs]

        def g(eng, f=f, incs=incs):
            ins = f(eng)
            for s in incs:
                ins = ins.then_inc(s.h, s.step)
        self.ops[e].append(g)
        return tgts[0] if len(tgts) == 1 else (tgts if tgts else None)


def _finalize(b, nc, c, dram):
    for name, sem_ in list(b._allsems.items()):
        b.wait("sync", sem_, sem_.n)
    with b._ctx:
        with nc.Block() as block:
            @block.tensor
            def _(eng):
                for f in b.ops["tensor"]:
                    f(eng)

            @block.vector
            def _(eng):
                for f in b.ops["vector"]:
                    f(eng)

            @block.scalar
            def _(eng):
                for f in b.ops["scalar"]:
                    f(eng)

            @block.gpsimd
            def _(eng):
                for f in b.ops["gpsimd"]:
                    f(eng)

            @block.sync
            def _(eng):
                for f in b.ops["sync"]:
                    f(eng)
    return nc


def build_program(cfg: Cfg):
    b = Build(cfg)
    nc, c = b.nc, cfg
    D_, T_, H_, TH, KT, NH = c.D, c.T, c.HALO, c.TH, c.KT, c.NH
    NT, HT = T_ // 128, H_ // 128
    tsp = [(i * 512, min(512, T_ - i * 512)) for i in range((T_ + 511) // 512)]
    KQ, NKQ, ERT = c.KQ, c.NKQ, c.ERT
    dram = {}

    def din(name, shape, dtype):
        dram[name] = nc.dram_tensor(name, shape, dtype, kind="ExternalInput").ap()
        return dram[name]

    def dout(name, shape, dtype):
        dram[name] = nc.dram_tensor(name, shape, dtype, kind="ExternalOutput").ap()
        return dram[name]

    def dint(name, shape, dtype):
        dram[name] = nc.dram_tensor(name, shape, dtype, kind="Internal").ap()
        return dram[name]

    x_in = din("x_in", [T_, D_], F32)
    prev_in = din("prev_in", [H_, D_], F32)
    NBF = (c.NOT + 1) // 2 - (1 if c.NOT % 2 == 1 else 0)
    w_ip_t = din("w_ip_t", [c.NKQ, NBF, 128, c.KQ * 256], F32)
    w_ip_rag = din("w_ip_rag", [D_, c.PROJ - NBF * 256], F32)
    w_convc = din("w_convc", [128, c.XOT * c.K], F32)
    w_out_t = din("w_out_t", [c.NKQ, D_ // 512, 128, c.KQ * 512], F32)
    w_ffn_t = din("w_ffn_t", [c.NKQ, D_ // 512, 128, c.KQ * 512], F32)
    w_lat_t = din("w_lat_t", [128, KT * c.ER], F32)
    w_lBt = din("w_lBt", [c.ER, D_], F32)
    probs_col = din("probs_col", [128, c.ER // 128], F32)
    mup_cols = din("mup_cols", [128, c.NOT], F32)
    rows3_in = din("rows3_in", [65, D_], F32)
    alog_row = din("alog_row", [1, NH], F32)
    dtb_col = din("dtb_col", [NH, 1], F32)
    d_cols = din("d_cols", [128, NH], F32)
    ident_bf = din("ident_bf", [128, 128], BF16)
    ident_f = din("ident_f", [64, 64], F32)
    ut_bf = din("ut_bf", [128, 128], BF16)
    tri_bf = din("tri_bf", [128, 128], BF16)
    ones_bf = din("ones_bf", [1, 128], BF16)
    ones65_bf = din("ones65_bf", [65, 128], BF16)
    ones_f = din("ones_f", [1, 128], F32)
    fconsts = din("fconsts", [128, 8], F32)
    out = dout("out", [T_, D_], F32)

    ydump = dint("ydump", [T_, D_], F32)
    lnspill = dint("lnspill", [T_, D_], BF16)
    scspill = dint("scspill", [TH, D_], BF16)
    zspill = dint("zspill", [2 * D_ + 2 * c.S, TH], BF16)
    ffnspill = dint("ffnspill", [T_, D_], BF16)
    yspill = dint("yspill", [T_, D_], BF16)

    sb = b.sbuf
    poolA = sb("poolA", [128, KT * T_], BF16)
    PB = max(KT * H_ + 512 + 3 + TH,
             KT * ERT * 128 + 2 * ERT * 512 + ERT * T_,
             NH * 128 + 2 * D_ + 512,
             2 * 512)
    poolB = sb("poolB", [128, PB], BF16)
    poolF = sb("poolF", [128, max(D_, 2560, 2 * TH)], F32)
    wq = [sb(f"wq{i}", [128, KQ * 512], BF16) for i in range(2)]
    xs_tm = [sb(f"xs_tm{i}", [128, D_], BF16) for i in range(2)]
    sz_tm = sb("sz_tm", [128, D_], BF16)
    bcc = [sb(f"bcc{i}", [128, 2 * 128], BF16) for i in range(2)]

    ln_bf = poolA[:, 0:D_]
    sc_bf = poolA[:, D_:2 * D_]
    lnT = poolA
    lnTh = poolB
    diag_cv = poolB[:, KT * H_:KT * H_ + 512]
    xbc_tmp = poolB[:, KT * H_ + 512:KT * H_ + 512 + 3 + TH]
    lAT = poolB[:, 0:KT * ERT * 128]
    lB0 = KT * ERT * 128
    lBtq = [[poolB[:, lB0 + (p * ERT + et) * 512: lB0 + (p * ERT + et) * 512 + 512]
             for et in range(ERT)] for p in range(2)]
    hprime = poolB[:, lB0 + 2 * ERT * 512:lB0 + 2 * ERT * 512 + ERT * T_]
    diagD = poolB[:, 0:NH * 128]
    nw_bc = poolB[:, NH * 128:NH * 128 + D_]
    yhat_bf = poolB[:, NH * 128 + D_:NH * 128 + 2 * D_]
    xw8 = poolB[:, NH * 128 + 2 * D_:NH * 128 + 2 * D_ + 512]
    gbufs = [poolB[:, i * 512:(i + 1) * 512] for i in range(2)]
    xa = poolF[:, 0:D_]
    dtscr_f = poolF[0:NH, 0:TH]
    dt_t2 = poolF[0:64, TH:2 * TH]
    yfull = poolF[:, 0:D_]
    xrb = [poolF[:, i * 512:(i + 1) * 512] for i in range(2)]
    ob = [poolF[:, 1024 + i * 512:1024 + (i + 1) * 512] for i in range(2)]
    gsil = poolF[:, 2048:2560]
    sqjunkA = xs_tm[0]
    lnw_bc = xs_tm[1]

    ev_bf = sb("ev_bf", [128, max(TH, 512)], BF16)
    convw = sb("convw", [128, c.XOT * c.K], F32)
    mupc = sb("mupc", [128, c.NOT], F32)
    dt_sp = sb("dt_sp", [NH, TH], F32)
    H = sb("H", [128, NH * 64], F32)
    H_bf = sb("H_bf", [128, NH * 64], BF16)
    dtT = sb("dtT", [128, NH], F32)
    logdtT = sb("logdtT", [128, NH], F32)
    dtAT = sb("dtAT", [128, NH], BF16)
    colst = sb("colst", [128, NH], F32)
    expcT = sb("expcT", [128, NH], F32)
    a_bc = sb("a_bc", [128, NH], F32)
    a_col = sb("a_col", [NH, 1], F32)
    atot_bc = sb("atot_bc", [128, NH], F32)
    wtilT = sb("wtilT", [128, NH], F32)
    rowbuf = sb("rowbuf", [1, 128], F32)
    dtsum = sb("dtsum", [NH, 2], F32)
    gsb = sb("gsb", [128, 128], F32)
    btile = sb("btile", [128, 128], BF16)
    e8 = sb("e8", [128, 8 * 128], BF16)
    mstat = [sb(f"mstat{i}", [128, 128], BF16) for i in range(2)]
    t1buf = sb("t1buf", [128, 512], F32)
    dcol_all = sb("dcol_all", [128, NH], F32)
    ident_s = sb("ident_s", [128, 128], BF16)
    identf_s = sb("identf_s", [64, 64], F32)
    ut_s = sb("ut_s", [128, 128], BF16)
    tri_s = sb("tri_s", [128, 128], BF16)
    ones_s = sb("ones_s", [1, 128], BF16)
    ones65 = sb("ones65", [65, 128], BF16)
    onesf_s = sb("onesf_s", [1, 128], F32)
    dtb_s = sb("dtb_s", [NH, 1], F32)
    arow_s = sb("arow_s", [1, NH], F32)
    arow_bf = sb("arow_bf", [1, NH], BF16)
    rows3 = sb("rows3", [65, D_], BF16)
    probsc = sb("probsc", [128, max(ERT, 1)], F32)
    lncol = sb("lncol", [128, 8], F32)
    fconst = sb("fconst", [128, 8], F32)
    scr = sb("scr", [128, 8], F32)

    pb = [b.psum(f"pb{i}", [128, 512], F32) for i in range(7)]
    pbf = b.psum("pbf", [128, 1024], BF16)

    s_te = b.sem("s_te")
    s_dve = b.sem("s_dve")
    s_act = b.sem("s_act")
    s_ldx = b.sem("s_ldx", 16)
    s_ldw = [b.sem(f"s_ldw{i}", 16) for i in range(2)]
    s_st = b.sem("s_st", 16)
    s_misc = b.sem("s_misc", 16)
    s_lb = [b.sem(f"s_lb{i}", 16) for i in range(2)]
    s_xa = b.sem("s_xa", 16)
    s_lnt = b.sem("s_lnt", 16)
    s_rel2 = [b.sem(f"s_rel2{i}", 16) for i in range(2)]
    s_sz = b.sem("s_sz", 16)
    s_gx = [b.sem(f"s_gx{i}", 16) for i in range(2)]

    def ld_sync(dst, src, transpose=False, sem=None):
        return b.run("sync", lambda e, d=dst, s=src, t=transpose:
                     e.dma_start(out=d, in_=s, transpose=t), inc=(sem or s_ldx))

    def ld_w(dst, src, wi):
        return b.run("gpsimd", lambda e, d=dst, s=src: e.dma_start(out=d, in_=s),
                     inc=s_ldw[wi])

    def store(dst, src, gate_sem, gate_val):
        b.wait("scalar", gate_sem, gate_val)
        return b.run("scalar", lambda e, d=dst, s=src: e.dma_start(out=d, in_=s), inc=s_st)

    def rb_act(tile):
        p = tile.shape[0]
        b.wait("scalar", s_act, s_act.n)
        return b.run("scalar", lambda e, t=tile, p=p: e.activation(scr[0:p, 6:7], t, AF.Copy), inc=s_act)

    def rb_dve(tile):
        p = tile.shape[0]
        b.wait("vector", s_dve, s_dve.n)
        return b.run("vector", lambda e, t=tile, p=p: e.tensor_copy(scr[0:p, 7:8], t), inc=s_dve)

    def barrier():
        te0, dv0, ac0 = s_te.n, s_dve.n, s_act.n
        b.wait("tensor", s_dve, dv0)
        b.wait("tensor", s_act, ac0)
        b.wait("vector", s_te, te0)
        b.wait("vector", s_act, ac0)
        b.wait("scalar", s_te, te0)
        b.wait("scalar", s_dve, dv0)

    ld_sync(fconst[:], fconsts)
    for _i, _v in enumerate([1e-5, 1.0 / D_, -0.5, -1.0, 1.0, 0.0, 3.0e37, 0.5]):
        nc.const_aps.register(F32, float(_v), fconst[:, _i:_i + 1])
    ld_sync(ident_s[:], ident_bf)
    ld_sync(identf_s[:], ident_f)
    ld_sync(ut_s[:], ut_bf)
    ld_sync(tri_s[:], tri_bf)
    ld_sync(ones_s[:], ones_bf)
    ld_sync(ones65[:], ones65_bf)
    ld_sync(onesf_s[:], ones_f)
    ld_sync(convw[:], w_convc)
    ld_sync(mupc[:], mup_cols)
    ld_sync(dtb_s[:], dtb_col)
    ld_sync(arow_s[:], alog_row)
    ld_sync(dcol_all[:], d_cols)
    cst = ld_sync(probsc[:], probs_col)
    gw0 = b.run("gpsimd", lambda e: e.dma_start(out=rows3[:], in_=rows3_in), inc=s_misc)

    b.wait("vector", s_ldx, cst)
    b.run("vector", lambda e: e.memset(H[:, 0:NH * 64], 0.0))
    b.run("vector", lambda e: e.memset(H_bf[:, 0:NH * 64], 0.0), inc=s_dve)

    b.wait("tensor", s_ldx, cst)
    b.wait("tensor", s_misc, gw0)
    lnwev = {0: 0, 1: 0}
    for j in range(D_ // 512):
        b.wait("tensor", s_dve, lnwev[j % 2])
        tt = b.run("tensor", lambda e, j=j: e.matmul(
            pb[j % 2][:], ones_s[0:1, :], rows3[0:1, j * 512:(j + 1) * 512],
            start=True, stop=True), inc=s_te)
        b.wait("vector", s_te, tt)
        lnwev[j % 2] = b.run("vector", lambda e, j=j: e.tensor_copy(
            lnw_bc[:, j * 512:(j + 1) * 512], pb[j % 2][:]), inc=s_dve)
    b.wait("scalar", s_ldx, cst)
    b.run("scalar", lambda e: e.activation(arow_s[:], arow_s[:], AF.Exp))
    b.run("scalar", lambda e: e.mul(arow_s[:], arow_s[:], -1.0))
    aa = b.run("scalar", lambda e: e.copy(arow_bf[:], arow_s[:]), inc=s_act)
    b.wait("tensor", s_act, aa)
    tt = b.run("tensor", lambda e: e.matmul(pb[2][:, 0:NH], ones_s[0:1, :],
                                            arow_bf[0:1, :], start=True, stop=True), inc=s_te)
    t2 = b.run("tensor", lambda e: e.transpose(pb[3][0:NH, 0:1], arow_s[0:1, 0:NH],
                                               identf_s[0:1, 0:1]), inc=s_te)
    b.wait("vector", s_te, t2)
    b.run("vector", lambda e: e.tensor_copy(a_bc[:], pb[2][:, 0:NH]))
    b.run("vector", lambda e: e.tensor_copy(a_col[:], pb[3][0:NH, 0:1]), inc=s_dve)

    inv_d = 1.0 / D_
    a5_hist = 0
    st_hist = 0
    for i in range(NT + HT):
        halo = i >= NT
        src = prev_in[(i - NT) * 128:(i - NT + 1) * 128, :] if halo else \
            x_in[i * 128:(i + 1) * 128, :]
        b.wait("sync", s_act, a5_hist)
        lt = ld_sync(xa, src, sem=s_xa)
        b.wait("vector", s_xa, lt)
        vrd = b.run("vector", lambda e: e.tensor_reduce(lncol[:, 0:1], xa, AX.X, ALU.add), inc=s_dve)
        b.wait("vector", s_dve, vrd)
        v1 = b.run("vector", lambda e: e.tensor_scalar(lncol[:, 1:2], lncol[:, 0:1], inv_d, None, ALU.mult), inc=s_dve)
        b.wait("scalar", s_xa, lt)
        b.wait("scalar", s_dve, v1)
        vz0 = b.run("vector", lambda e: e.memset(lncol[:, 2:3], 0.0), inc=s_dve)
        b.wait("scalar", s_dve, vz0)
        a1 = b.run("scalar", lambda e: e.activation(
            sqjunkA[:], xa, AF.Square, accum_out=lncol[:, 2:3]), inc=s_act)
        b.wait("vector", s_act, a1)
        b.run("vector", lambda e: e.tensor_tensor(lncol[:, 3:4], lncol[:, 1:2], lncol[:, 1:2], ALU.mult))
        vq = b.run("vector", lambda e: e.tensor_scalar(lncol[:, 2:3], lncol[:, 2:3], inv_d, None, ALU.mult), inc=s_dve)
        b.wait("vector", s_dve, vq)
        v2 = b.run("vector", lambda e: e.tensor_tensor(lncol[:, 2:3], lncol[:, 2:3], lncol[:, 3:4], ALU.subtract), inc=s_dve)
        b.wait("scalar", s_dve, v2)
        b.run("scalar", lambda e: e.activation(lncol[:, 3:4], lncol[:, 2:3], AF.Ln, bias=1e-5))
        a2 = b.run("scalar", lambda e: e.activation(lncol[:, 3:4], lncol[:, 3:4], AF.Exp, scale=-0.5), inc=s_act)
        b.wait("vector", s_act, a2)
        vq = b.run("vector", lambda e: e.tensor_tensor(lncol[:, 4:5], lncol[:, 2:3], lncol[:, 3:4], ALU.mult), inc=s_dve)
        b.wait("vector", s_dve, vq)
        v3 = b.run("vector", lambda e: e.tensor_tensor(lncol[:, 4:5], lncol[:, 4:5], lncol[:, 3:4], ALU.mult), inc=s_dve)
        b.wait("scalar", s_dve, v3)
        b.run("scalar", lambda e: e.activation(lncol[:, 4:5], lncol[:, 4:5], AF.Ln, bias=1e-5))
        a3 = b.run("scalar", lambda e: e.activation(lncol[:, 4:5], lncol[:, 4:5], AF.Exp, scale=-0.5), inc=s_act)
        b.wait("vector", s_act, a3)
        vq = b.run("vector", lambda e: e.tensor_tensor(lncol[:, 5:6], lncol[:, 1:2], lncol[:, 3:4], ALU.mult), inc=s_dve)
        b.wait("vector", s_dve, vq)
        vq = b.run("vector", lambda e: e.tensor_scalar(lncol[:, 5:6], lncol[:, 5:6], -1.0, None, ALU.mult), inc=s_dve)
        b.run("vector", lambda e: e.tensor_tensor(lncol[:, 6:7], lncol[:, 3:4], lncol[:, 4:5], ALU.mult))
        b.wait("vector", s_dve, vq)
        v4 = b.run("vector", lambda e: e.tensor_tensor(lncol[:, 7:8], lncol[:, 5:6], lncol[:, 4:5], ALU.mult), inc=s_dve)
        b.wait("scalar", s_dve, v4)
        b.wait("scalar", s_st, st_hist)
        if not halo:
            b.run("scalar", lambda e: e.activation(
                ln_bf, xa, AF.Identity, bias=lncol[:, 5:6], scale=lncol[:, 3:4]))
        a5 = b.run("scalar", lambda e: e.activation(
            sc_bf, xa, AF.Identity, bias=lncol[:, 7:8], scale=lncol[:, 6:7]), inc=s_act)
        a5_hist = a5
        b.wait("vector", s_act, a5)
        v5 = b.run("vector", lambda e: e.tensor_tensor(sc_bf, sc_bf, lnw_bc[:], ALU.mult), inc=s_dve)
        g1 = rb_dve(sc_bf[:, 0:1])
        row0 = (T_ + (i - NT) * 128) if halo else i * 128
        if not halo:
            ga = rb_act(ln_bf[:, 0:1])
            store(lnspill[i * 128:(i + 1) * 128, :], ln_bf, s_act, ga)
        st_hist = store(scspill[row0:row0 + 128, :], sc_bf, s_dve, g1)
    a1_spills = s_st.n
    barrier()

    b.wait("sync", s_st, a1_spills)
    b.wait("sync", s_dve, s_dve.n)
    b.wait("sync", s_act, s_act.n)
    for k in range(KT):
        ld_sync(lnT[:, k * T_:(k + 1) * T_],
                scspill[0:T_, k * 128:(k + 1) * 128], transpose=True, sem=s_lnt)
        ld_sync(lnTh[:, k * H_:(k + 1) * H_],
                scspill[T_:TH, k * 128:(k + 1) * 128], transpose=True, sem=s_lnt)
    lnT_ready = s_lnt.n

    if c.phases < 2:
        return _finalize(b, nc, c, dram)
    n_blk = (c.NOT + 1) // 2
    pfree = {0: (None, 0), 1: (None, 0)}
    wfree = {0: 0, 1: 0}
    evst = 0
    conv_prev = 0
    qi = 0
    b.wait("tensor", s_lnt, lnT_ready)
    for blk in range(n_blk):
        o0 = blk * 2
        ntile = min(2, c.NOT - o0)
        cols0 = o0 * 128
        cols1 = min((o0 + 2) * 128, c.PROJ)
        bw = cols1 - cols0
        psets = [[pb[0], pb[1], pb[2]], [pb[3], pb[4], pb[5]]]
        for oj in range(ntile):
            sfree, vfree = pfree[oj]
            if sfree is not None:
                b.wait("tensor", sfree, vfree)
        tdone = [0] * ntile
        for kq in range(NKQ):
            wi = qi % 2
            qi += 1
            b.wait("gpsimd", s_te, wfree[wi])
            if bw == 256:
                lw = ld_w(wq[wi][:, 0:KQ * 256], w_ip_t[kq, blk], wi)
            else:
                lw = ld_w(wq[wi][:, 0:KQ * bw].rearrange("p (k o) -> p k o", k=KQ),
                          w_ip_rag[kq * KQ * 128:min((kq + 1) * KQ * 128, D_), :]
                          .rearrange("(k p) o -> p k o", p=128), wi)
            b.wait("tensor", s_ldw[wi], lw)
            for oj in range(ntile):
                o = o0 + oj
                is_dt = (o == c.NOT - 1)
                owid = c.DTW if is_dt else 128
                is_z = o < c.ZOT
                ps = psets[oj]

                def mms(eng, wi=wi, kq=kq, oj=oj, owid=owid, is_z=is_z, ps=ps, bw=bw):
                    ins = None
                    for kk in range(KQ):
                        k = kq * KQ + kk
                        if k >= KT:
                            break
                        st = wq[wi][:, kk * bw + oj * 128: kk * bw + oj * 128 + owid]
                        first = (k == 0)
                        last = (k == KT - 1)
                        for hi, (t0, tw) in enumerate(tsp):
                            ins = eng.matmul(ps[hi][0:owid, 0:tw], st,
                                             lnT[:, k * T_ + t0:k * T_ + t0 + tw],
                                             start=first, stop=last)
                        if not is_z:
                            ins = eng.matmul(ps[2][0:owid, 0:H_], st,
                                             lnTh[:, k * H_:(k + 1) * H_],
                                             start=first, stop=last)
                    return ins
                tdone[oj] = b.run("tensor", mms, inc=s_te)
            wfree[wi] = s_te.n
        for oj in range(ntile):
            o = o0 + oj
            is_dt = (o == c.NOT - 1)
            owid = c.DTW if is_dt else 128
            is_z = o < c.ZOT
            ps = psets[oj]
            mupcol = mupc[:, o:o + 1]
            if is_z:
                b.wait("scalar", s_te, tdone[oj])
                b.wait("scalar", s_st, evst)
                for hi, (t0, tw) in enumerate(tsp):
                    b.run("scalar", lambda e, hi=hi, t0=t0, tw=tw, ps=ps, mc=mupcol: e.activation(
                        ev_bf[:, t0:t0 + tw], ps[hi][:, 0:tw], AF.Silu, scale=mc))
                ga = rb_act(ev_bf[:, 0:1])
                pfree[oj] = (s_act, ga)
                evst = store(zspill[o * 128:(o + 1) * 128, H_:TH], ev_bf[:, 0:T_], s_act, ga)
            else:
                b.wait("vector", s_te, tdone[oj])
                b.wait("vector", s_te, conv_prev)
                for hi, (t0, tw) in enumerate(tsp):
                    b.run("vector", lambda e, hi=hi, t0=t0, tw=tw, ps=ps, mc=mupcol, w=owid: e.tensor_scalar(
                        xbc_tmp[0:w, 3 + H_ + t0:3 + H_ + t0 + tw], ps[hi][0:w, 0:tw],
                        mc[0:w, :], None, ALU.mult))
                vv1 = b.run("vector", lambda e, ps=ps, mc=mupcol, w=owid: e.tensor_scalar(
                    xbc_tmp[0:w, 3:3 + H_], ps[2][0:w, 0:H_], mc[0:w, :], None, ALU.mult), inc=s_dve)
                if is_dt:
                    b.run("vector", lambda e, w=owid: e.tensor_copy(
                        dtscr_f[0:w, 0:TH], xbc_tmp[0:w, 3:3 + TH]))
                    vd = rb_dve(dtscr_f[0:NH, 0:1])
                    pfree[oj] = (s_dve, vd)
                else:
                    xo = o - c.ZOT
                    b.run("vector", lambda e: e.memset(xbc_tmp[:, 0:3], 0.0))
                    for kk in range(c.K):
                        b.run("vector", lambda e, xo=xo, kk=kk: e.tensor_scalar(
                            diag_cv[:, kk * 128:(kk + 1) * 128], ident_s[:],
                            convw[:, (xo * c.K + kk):(xo * c.K + kk + 1)], None, ALU.mult))
                    vd = b.run("vector", lambda e: e.tensor_copy(scr[:, 1:2], diag_cv[:, 0:1]), inc=s_dve)
                    b.wait("tensor", s_dve, vd)

                    def conv_mms(eng, ps=ps):
                        ins = None
                        for kk in range(c.K):
                            dg = diag_cv[:, kk * 128:(kk + 1) * 128]
                            ins = eng.matmul(ps[2][:, 0:H_], dg, xbc_tmp[:, kk:kk + H_],
                                             start=(kk == 0), stop=(kk == c.K - 1))
                            for hi, (t0, tw) in enumerate(tsp):
                                ins = eng.matmul(ps[hi][:, 0:tw], dg,
                                                 xbc_tmp[:, H_ + t0 + kk:H_ + t0 + kk + tw],
                                                 start=(kk == 0), stop=(kk == c.K - 1))
                        return ins
                    tc = b.run("tensor", conv_mms, inc=s_te)
                    conv_prev = tc
                    b.wait("scalar", s_te, tc)
                    b.wait("scalar", s_st, evst)
                    b.run("scalar", lambda e, ps=ps: e.activation(
                        ev_bf[:, 0:H_], ps[2][:, 0:H_], AF.Silu))
                    for hi, (t0, tw) in enumerate(tsp):
                        b.run("scalar", lambda e, hi=hi, t0=t0, tw=tw, ps=ps: e.activation(
                            ev_bf[:, H_ + t0:H_ + t0 + tw], ps[hi][:, 0:tw], AF.Silu))
                    ga = rb_act(ev_bf[:, 0:1])
                    pfree[oj] = (s_act, ga)
                    evst = store(zspill[D_ + xo * 128:D_ + (xo + 1) * 128, :],
                                 ev_bf[:, 0:TH], s_act, ga)
    a2_spills = s_st.n
    barrier()

    b.run("vector", lambda e: e.tensor_scalar(dtscr_f[0:NH, 0:TH], dtscr_f[0:NH, 0:TH],
                                              dtb_s[0:NH, :], None, ALU.add))
    vr = b.run("vector", lambda e: e.tensor_scalar(dt_sp[0:NH, :], dtscr_f[0:NH, 0:TH],
                                                   0.0, None, ALU.max), inc=s_dve)
    b.wait("scalar", s_dve, vr)
    b.run("scalar", lambda e: e.activation(dt_t2[0:NH, 0:TH], dtscr_f[0:NH, 0:TH], AF.Abs))
    b.run("scalar", lambda e: e.activation(dt_t2[0:NH, 0:TH], dt_t2[0:NH, 0:TH], AF.Exp, scale=-1.0))
    asp = b.run("scalar", lambda e: e.activation(dt_t2[0:NH, 0:TH], dt_t2[0:NH, 0:TH], AF.Ln, bias=1.0), inc=s_act)
    b.wait("vector", s_act, asp)
    vsp = b.run("vector", lambda e: e.tensor_tensor(dt_sp[0:NH, :], dt_sp[0:NH, :],
                                                    dt_t2[0:NH, 0:TH], ALU.add), inc=s_dve)

    def emit_reload(ci, slot, main):
        col0 = (H_ + ci * 128) if main else ci * 128
        b.wait("sync", s_st, a2_spills)
        b.wait("sync", s_te, s_te.n)
        b.wait("sync", s_dve, s_dve.n)
        b.wait("sync", s_act, s_act.n)
        ld_sync(xs_tm[slot][:], zspill[D_:2 * D_, col0:col0 + 128], transpose=True,
                sem=s_rel2[slot])
        return (slot, ld_sync(bcc[slot].rearrange("p (a t) -> p a t", a=2),
                              zspill[2 * D_:2 * D_ + 2 * c.S, col0:col0 + 128]
                              .rearrange("(a p) t -> p a t", p=128), sem=s_rel2[slot]))

    def chunk_prep(ci, main, slot, rel_tgt):
        mp = getattr(c, "microphase", 9)
        barrier()
        col0 = (H_ + ci * 128) if main else ci * 128
        if mp < 0.5:
            return dict(vcs=0, aat=0, aex=0, vbt=0, awt=0, vg=0)
        b.wait("tensor", s_dve, vsp)
        ta = b.run("tensor", lambda e, c0=col0: e.transpose(
            pb[2][:, 0:NH], dt_sp[0:NH, c0:c0 + 128], identf_s[0:NH, 0:NH]), inc=s_te)
        if mp < 1:
            tq = b.run("tensor", lambda e: e.matmul(
                pb[3][:, 0:128], ident_s[:], ut_s[:], start=True, stop=True), inc=s_te)
            b.wait("vector", s_te, tq)
            b.run("vector", lambda e: e.tensor_copy(gsb[:], pb[3][:, 0:128]), inc=s_dve)
            return dict(vcs=0, aat=0, aex=0, vbt=0, awt=0, vg=0)
        b.wait("vector", s_te, ta)
        vda = b.run("vector", lambda e: e.tensor_copy(dtT[:], pb[2][:, 0:NH]), inc=s_dve)
        if mp < 2:
            return dict(vcs=0, aat=0, aex=0, vbt=0, awt=0, vg=0)
        b.wait("scalar", s_dve, vda)
        ala = b.run("scalar", lambda e: e.activation(logdtT[:], dtT[:], AF.Ln), inc=s_act)
        if mp < 2.2:
            return dict(vcs=0, aat=0, aex=0, vbt=0, awt=0, vg=0)
        b.wait("vector", s_dve, vda)
        vab = b.run("vector", lambda e: e.tensor_tensor(dtAT[:], dtT[:], a_bc[:], ALU.mult), inc=s_dve)
        if mp < 2.4:
            return dict(vcs=0, aat=0, aex=0, vbt=0, awt=0, vg=0)
        b.wait("tensor", s_dve, vab)
        tb_ = b.run("tensor", lambda e: e.matmul(pb[2][:, 0:NH], ut_s[:], dtAT[:],
                                                 start=True, stop=True), inc=s_te)
        if mp < 2.6:
            return dict(vcs=0, aat=0, aex=0, vbt=0, awt=0, vg=0)
        b.wait("vector", s_te, tb_)
        b.wait("vector", s_act, ala)
        vcs = b.run("vector", lambda e: e.tensor_tensor(colst[:], logdtT[:], pb[2][:, 0:NH], ALU.subtract), inc=s_dve)
        if mp < 2.8:
            return dict(vcs=0, aat=0, aex=0, vbt=0, awt=0, vg=0)
        b.wait("scalar", s_te, tb_)
        b.wait("scalar", s_dve, vcs)
        aex = b.run("scalar", lambda e: e.activation(expcT[:], pb[2][:, 0:NH], AF.Exp), inc=s_act)
        if mp < 3:
            return dict(vcs=vcs, aat=0, aex=aex, vbt=0, awt=0, vg=0)
        vq2 = b.run("vector", lambda e, c0=col0: e.tensor_reduce(
            dtsum[0:NH, 0:1], dt_sp[0:NH, c0:c0 + 128], AX.X, ALU.add), inc=s_dve)
        b.wait("vector", s_dve, vq2)
        vds = b.run("vector", lambda e: e.tensor_tensor(dtsum[0:NH, 1:2], dtsum[0:NH, 0:1],
                                                        a_col[0:NH, :], ALU.mult), inc=s_dve)
        b.wait("tensor", s_dve, vds)
        b.wait("tensor", s_act, aex)
        b.wait("tensor", s_dve, vcs)
        if mp < 4:
            return dict(vcs=vcs, aat=0, aex=aex, vbt=0, awt=0, vg=0)
        tc_ = b.run("tensor", lambda e: e.transpose(pb[2][0:1, 0:NH], dtsum[0:NH, 1:2],
                                                    identf_s[0:NH, 0:NH]), inc=s_te)
        b.wait("vector", s_te, tc_)
        vrow = b.run("vector", lambda e: e.tensor_copy(rowbuf[0:1, 0:NH], pb[2][0:1, 0:NH]), inc=s_dve)
        b.wait("tensor", s_dve, vrow)
        if mp < 5:
            return dict(vcs=vcs, aat=0, aex=aex, vbt=0, awt=0, vg=0)
        td = b.run("tensor", lambda e: e.matmul(pb[2][:, 0:NH], onesf_s[0:1, :],
                                                rowbuf[0:1, 0:NH], start=True, stop=True), inc=s_te)
        b.wait("scalar", s_te, td)
        aat = b.run("scalar", lambda e: e.activation(atot_bc[:], pb[2][:, 0:NH], AF.Exp), inc=s_act)
        pr = dict(vcs=vcs, aat=aat, aex=aex)
        if not main:
            b.wait("vector", s_te, td)
            b.wait("vector", s_dve, vcs)
            b.wait("vector", s_act, aat)
            vwa = b.run("vector", lambda e: e.tensor_tensor(wtilT[:], colst[:], pb[2][:, 0:NH], ALU.add), inc=s_dve)
            b.wait("scalar", s_dve, vwa)
            pr["awt"] = b.run("scalar", lambda e: e.activation(wtilT[:], wtilT[:], AF.Exp), inc=s_act)
        b.wait("tensor", s_rel2[rel_tgt[0]], rel_tgt[1])
        te_ = b.run("tensor", lambda e, sl=slot: e.transpose(
            pbf[:, 0:128], bcc[sl][:, 0:128], ident_s[:]), inc=s_te)
        b.wait("vector", s_te, te_)
        pr["vbt"] = b.run("vector", lambda e: e.tensor_copy(btile[:], pbf[:, 0:128]), inc=s_dve)
        if main:
            b.wait("tensor", s_act, aat)
            tg = b.run("tensor", lambda e, sl=slot: e.matmul(
                pb[2][:, 0:128], bcc[sl][:, 0:128], bcc[sl][:, 128:256],
                start=True, stop=True), inc=s_te)
            b.wait("vector", s_te, tg)
            pr["vg"] = b.run("vector", lambda e: e.tensor_tensor(gsb[:], pb[2][:, 0:128],
                                                                 tri_s[:], ALU.mult), inc=s_dve)
        return pr

    def emit_state_update(g, slot, pr, main):
        if main:
            src = e8.rearrange("p (h t) -> p h t", h=8)[:, :, 127:128].broadcast_to([128, 8, 64])
            b.wait("vector", s_act, pr["e_done"])
        else:
            src = wtilT[:, g * 8:(g + 1) * 8].broadcast_to([128, 8, 64])
            b.wait("vector", s_act, pr["awt"])
        vxw = b.run("vector", lambda e, g=g, sl=slot, s=src: e.tensor_tensor(
            xw8.rearrange("p (h t) -> p h t", h=8),
            xs_tm[sl].rearrange("p (h t) -> p h t", h=NH)[:, g * 8:(g + 1) * 8, :],
            s, ALU.mult), inc=s_dve)
        b.wait("tensor", s_dve, max(vxw, pr["vbt"], pr.get("hfree", 0)))
        th = b.run("tensor", lambda e: e.matmul(pb[6][:, 0:512], btile[:], xw8,
                                                start=True, stop=True), inc=s_te)
        b.wait("vector", s_te, th)
        b.wait("vector", s_act, pr["aat"])
        b.run("vector", lambda e, g=g: e.tensor_tensor(
            H.rearrange("p (h t) -> p h t", h=NH)[:, g * 8:(g + 1) * 8, :],
            H.rearrange("p (h t) -> p h t", h=NH)[:, g * 8:(g + 1) * 8, :],
            atot_bc[:, g * 8:(g + 1) * 8].broadcast_to([128, 8, 64]), ALU.mult))
        b.run("vector", lambda e, g=g: e.tensor_tensor(
            H[:, g * 512:g * 512 + 512], H[:, g * 512:g * 512 + 512], pb[6][:, 0:512], ALU.add))
        vH = b.run("vector", lambda e, g=g: e.tensor_copy(
            H_bf[:, g * 512:g * 512 + 512], H[:, g * 512:g * 512 + 512]), inc=s_dve)
        return vH

    if c.phases < 2.2:
        return _finalize(b, nc, c, dram)
    relq = []
    rel0 = emit_reload(0, 0, main=False)
    if c.phases < 2.4:
        return _finalize(b, nc, c, dram)
    hf = 0
    for hc in range(c.HC):
        slot = hc % 2
        nrel = emit_reload(hc + 1, (hc + 1) % 2, main=False) if hc + 1 < c.HC else None
        pr = chunk_prep(hc, False, slot, rel0 if hc == 0 else relq.pop())
        if nrel is not None:
            relq.append(nrel)
        if c.phases < 2.6:
            return _finalize(b, nc, c, dram)
        ng_lim = c.NG if c.phases >= 3 else max(1, int(round((c.phases - 2.6) * 20)))
        for g in range(min(c.NG, ng_lim)):
            pr["hfree"] = hf
            hf = emit_state_update(g, slot, pr, main=False)
        if c.phases < 3:
            return _finalize(b, nc, c, dram)
    barrier()

    if c.phases < 4:
        return _finalize(b, nc, c, dram)
    b.wait("sync", s_te, s_te.n)
    b.wait("sync", s_dve, s_dve.n)
    for k in range(KT):
        ld_sync(lnT[:, k * T_:(k + 1) * T_],
                lnspill[:, k * 128:(k + 1) * 128], transpose=True, sem=s_lnt)
    lnT2_ready = s_lnt.n
    b.wait("gpsimd", s_te, s_te.n)
    b.wait("gpsimd", s_dve, s_dve.n)
    b.wait("gpsimd", s_act, s_act.n)
    lg = b.run("gpsimd", lambda e: e.dma_start(out=lAT[:], in_=w_lat_t), inc=s_misc)
    b.wait("tensor", s_lnt, lnT2_ready)
    b.wait("tensor", s_misc, lg)
    h_ev = {0: 0, 1: 0}
    hq = 0
    for et in range(ERT):
        for hi, (t0, tw) in enumerate(tsp):
            bank = hq % 2
            hq += 1
            b.wait("tensor", s_dve, h_ev[bank])

            def hmm(eng, et=et, t0=t0, tw=tw, bank=bank):
                ins = None
                for k in range(KT):
                    ins = eng.matmul(pb[bank][:, 0:tw],
                                     lAT[:, (k * ERT + et) * 128:(k * ERT + et) * 128 + 128],
                                     lnT[:, k * T_ + t0:k * T_ + t0 + tw],
                                     start=(k == 0), stop=(k == KT - 1))
                return ins
            th = b.run("tensor", hmm, inc=s_te)
            b.wait("vector", s_te, th)
            h_ev[bank] = b.run("vector", lambda e, et=et, t0=t0, tw=tw, bank=bank: e.tensor_scalar(
                hprime[:, et * T_ + t0:et * T_ + t0 + tw], pb[bank][:, 0:tw],
                probsc[:, et:et + 1], None, ALU.mult), inc=s_dve)
    vhp = rb_dve(hprime[:, 0:1])
    b.wait("tensor", s_dve, vhp)

    NT4 = min(4, NT)
    ffn_ev = {i: 0 for i in range(NT4)}
    evstl = 0
    lb_t = {0: 0, 1: 0}
    for os_ in range(D_ // 512):
        lbw = os_ % 2
        b.wait("gpsimd", s_te, lb_t[lbw])
        lbv = None
        for et in range(ERT):
            lbv = b.run("gpsimd", lambda e, et=et, os_=os_, lbw=lbw: e.dma_start(
                out=lBtq[lbw][et],
                in_=w_lBt[et * 128:(et + 1) * 128, os_ * 512:(os_ + 1) * 512]), inc=s_lb[lbw])
        for tp in range(NT // NT4):
            tis = [tp * NT4 + i for i in range(NT4)]
            for i, ti in enumerate(tis):
                b.wait("tensor", s_dve, ffn_ev[i])
                b.run("tensor", lambda e, i=i, os_=os_: e.matmul(
                    pb[i][:, 0:512], ones65[64:65, :], rows3[64:65, os_ * 512:(os_ + 1) * 512],
                    start=True, stop=False))
            for kq in range(NKQ):
                wi = qi % 2
                qi += 1
                b.wait("gpsimd", s_te, wfree[wi])
                lw = ld_w(wq[wi][:], w_ffn_t[kq, os_], wi)
                b.wait("tensor", s_ldw[wi], lw)

                def fmm(eng, wi=wi, kq=kq, tis=tis):
                    ins = None
                    for kk in range(KQ):
                        k = kq * KQ + kk
                        for i, ti in enumerate(tis):
                            ins = eng.matmul(pb[i][:, 0:512],
                                             lnT[:, k * T_ + ti * 128:k * T_ + ti * 128 + 128],
                                             wq[wi][:, kk * 512:(kk + 1) * 512],
                                             start=False, stop=False)
                    return ins
                b.run("tensor", fmm, inc=s_te)
                wfree[wi] = s_te.n
            b.wait("tensor", s_lb[lbw], lbv)

            def lmm(eng, tis=tis, lbw=lbw):
                ins = None
                for et in range(ERT):
                    for i, ti in enumerate(tis):
                        ins = eng.matmul(pb[i][:, 0:512],
                                         hprime[:, et * T_ + ti * 128:et * T_ + ti * 128 + 128],
                                         lBtq[lbw][et],
                                         start=False, stop=(et == ERT - 1))
                return ins
            tl = b.run("tensor", lmm, inc=s_te)
            for i, ti in enumerate(tis):
                b.wait("vector", s_te, tl)
                b.wait("vector", s_st, evstl)
                ve = b.run("vector", lambda e, i=i: e.tensor_copy(ev_bf[:, 0:512],
                                                                  pb[i][:, 0:512]), inc=s_dve)
                ffn_ev[i] = ve
                gv = rb_dve(ev_bf[:, 0:1])
                evstl = store(ffnspill[ti * 128:(ti + 1) * 128, os_ * 512:(os_ + 1) * 512],
                              ev_bf[:, 0:512], s_dve, gv)
        lb_t[lbw] = s_te.n
    barrier()

    if c.phases < 5:
        return _finalize(b, nc, c, dram)
    for h in range(NH):
        b.run("vector", lambda e, h=h: e.tensor_scalar(
            diagD[:, h * 128:(h + 1) * 128], ident_s[:], dcol_all[:, h:h + 1], None, ALU.mult))
    rb_dve(diagD[:, 0:1])
    nwev = {0: 0, 1: 0}
    for j in range(D_ // 512):
        bk = 2 + (j % 2)
        b.wait("tensor", s_dve, nwev[j % 2])
        tt = b.run("tensor", lambda e, j=j, bk=bk: e.matmul(
            pb[bk][:, 0:512], ones65[32:33, :], rows3[32:33, j * 512:(j + 1) * 512],
            start=True, stop=True), inc=s_te)
        b.wait("vector", s_te, tt)
        nwev[j % 2] = b.run("vector", lambda e, j=j, bk=bk: e.tensor_copy(
            nw_bc[:, j * 512:(j + 1) * 512], pb[bk][:, 0:512]), inc=s_dve)
    vnw = rb_dve(nw_bc[:, 0:1])
    b.wait("tensor", s_dve, vnw)

    vH_last = hf
    vg1 = 0
    rel0 = emit_reload(0, 0, main=True)
    relq = [rel0]
    y_sp = []
    for ci in range(c.NC):
        slot = ci % 2
        my_rel = relq.pop(0)
        if ci + 1 < c.NC:
            relq.append(emit_reload(ci + 1, (ci + 1) % 2, main=True))
        b.wait("sync", s_dve, vg1)
        szv = ld_sync(sz_tm[:], zspill[0:D_, H_ + ci * 128:H_ + ci * 128 + 128],
                      transpose=True, sem=s_sz)
        pr = chunk_prep(ci, True, slot, my_rel)
        yev = {0: 0, 1: 0}
        inter_free = pr["vg"]
        m_done = {0: 0, 1: 0}
        bc_free = {0: 0, 1: 0}
        for g in range(c.NG):
            ypb = pb[g % 2]
            b.wait("tensor", s_dve, max(vH_last, inter_free))
            b.wait("tensor", s_rel2[my_rel[0]], my_rel[1])
            t1m = b.run("tensor", lambda e, sl=slot, g=g: e.matmul(
                pb[2][:, 0:512], bcc[sl][:, 128:256], H_bf[:, g * 512:(g + 1) * 512],
                start=True, stop=True), inc=s_te)
            b.wait("vector", s_te, t1m)
            vt1 = b.run("vector", lambda e, g=g: e.tensor_tensor(
                t1buf.rearrange("p (h t) -> p h t", h=8),
                pb[2].rearrange("p (h t) -> p h t", h=8),
                expcT[:, g * 8:(g + 1) * 8].broadcast_to([128, 8, 64]), ALU.mult), inc=s_dve)
            inter_free = vt1
            b.wait("tensor", s_dve, yev[g % 2])
            e_done = 0
            for hl in range(8):
                h = g * 8 + hl
                bcb = pb[3] if hl % 2 == 0 else pb[5]
                if bc_free[hl % 2]:
                    b.wait("tensor", s_act, bc_free[hl % 2])
                tb_ = b.run("tensor", lambda e, h=h, bcb=bcb: e.matmul(
                    bcb[:, 0:128], dtAT[:, h:h + 1].broadcast_to([128, 128]),
                    ut_s[:], start=True, stop=True), inc=s_te)
                b.wait("scalar", s_te, tb_)
                b.wait("scalar", s_dve, pr["vcs"] if hl > 0 else max(pr["vcs"], vH_last))
                ae = b.run("scalar", lambda e, h=h, hl=hl, bcb=bcb: e.activation(
                    e8[:, hl * 128:(hl + 1) * 128], bcb[:, 0:128], AF.Exp,
                    bias=colst[:, h:h + 1]), inc=s_act)
                bc_free[hl % 2] = ae
                e_done = ae
                b.wait("vector", s_act, ae)
                b.wait("vector", s_dve, pr["vg"])
                b.wait("vector", s_te, m_done[hl % 2])
                vm = b.run("vector", lambda e, hl=hl: e.scalar_tensor_tensor(
                    mstat[hl % 2][:], e8[:, hl * 128:(hl + 1) * 128], 3.0e37, gsb[:],
                    ALU.min, ALU.mult), inc=s_dve)
                b.wait("tensor", s_dve, vm)

                def ymm(eng, h=h, hl=hl, sl=slot, yp=ypb):
                    eng.matmul(yp[:, hl * 64:hl * 64 + 64], mstat[hl % 2][:],
                               xs_tm[sl][:, h * 64:h * 64 + 64],
                               start=(hl == 0), stop=False)
                    return eng.matmul(yp[:, hl * 64:hl * 64 + 64],
                                      diagD[:, h * 128:(h + 1) * 128],
                                      xs_tm[sl][:, h * 64:h * 64 + 64],
                                      start=False, stop=(hl == 7))
                tym = b.run("tensor", ymm, inc=s_te)
                m_done[hl % 2] = tym
            pr["e_done"] = e_done
            pr["hfree"] = vH_last
            vH_last = emit_state_update(g, slot, pr, main=True)
            b.wait("vector", s_te, max(m_done[0], m_done[1]))
            vy = b.run("vector", lambda e, g=g, yp=ypb: e.tensor_tensor(
                yfull[:, g * 512:(g + 1) * 512], t1buf[:], yp[:, 0:512], ALU.add), inc=s_dve)
            yev[g % 2] = vy
        b.wait("vector", s_dve, max(yev[0], yev[1]))
        if c.debug:
            ygate = rb_dve(yfull[:, 0:1])
            ystt = store(ydump[ci * 128:(ci + 1) * 128, :], yfull, s_dve, ygate)
            b.wait("vector", s_st, ystt)
        b.wait("vector", s_sz, szv)
        vg1a = b.run("vector", lambda e: e.tensor_tensor(yfull, yfull, sz_tm[:], ALU.mult), inc=s_dve)
        vg1 = b.run("vector", lambda e: e.memset(lncol[:, 0:1], 0.0), inc=s_dve)
        b.wait("scalar", s_dve, max(vg1, vg1a))
        asq = b.run("scalar", lambda e, sl=slot: e.activation(
            xs_tm[sl][:], yfull, AF.Square, accum_out=lncol[:, 0:1]), inc=s_act)
        b.wait("scalar", s_act, asq)
        b.run("scalar", lambda e: e.activation(lncol[:, 1:2], lncol[:, 0:1], AF.Ln,
                                               scale=inv_d, bias=1e-5))
        ar2 = b.run("scalar", lambda e: e.activation(lncol[:, 1:2], lncol[:, 1:2], AF.Exp,
                                                     scale=-0.5), inc=s_act)
        b.wait("vector", s_act, ar2)
        b.wait("vector", s_st, y_sp[-1] if y_sp else 0)
        b.run("vector", lambda e: e.tensor_scalar(yfull, yfull, lncol[:, 1:2], None, ALU.mult))
        vyh = b.run("vector", lambda e: e.tensor_tensor(yhat_bf, yfull, nw_bc, ALU.mult), inc=s_dve)
        gv = rb_dve(yhat_bf[:, 0:1])
        y_sp.append(store(yspill[ci * 128:(ci + 1) * 128, :], yhat_bf, s_dve, gv))
    b_spills = s_st.n
    barrier()

    if c.phases < 6:
        return _finalize(b, nc, c, dram)
    b.wait("sync", s_st, b_spills)
    b.wait("sync", s_te, s_te.n)
    b.wait("sync", s_dve, s_dve.n)
    for k in range(KT):
        ld_sync(lnT[:, k * T_:(k + 1) * T_],
                yspill[:, k * 128:(k + 1) * 128], transpose=True, sem=s_lnt)
    ycm_ready = s_lnt.n
    ycm = lnT
    b.wait("tensor", s_lnt, ycm_ready)
    out_free = {i: 0 for i in range(NT4)}
    ld_free = (0, 0)
    ost = 0
    last_vo = 0
    for os_ in range(D_ // 512):
        for tp in range(NT // NT4):
            tis = [tp * NT4 + i for i in range(NT4)]
            to = 0
            for kq in range(NKQ):
                wi = qi % 2
                qi += 1
                b.wait("gpsimd", s_te, wfree[wi])
                lw = ld_w(wq[wi][:], w_out_t[kq, os_], wi)
                b.wait("tensor", s_ldw[wi], lw)
                if kq == 0:
                    for i in range(NT4):
                        b.wait("tensor", s_dve, out_free[i])

                def omm(eng, wi=wi, kq=kq, tis=tis):
                    ins = None
                    for kk in range(KQ):
                        k = kq * KQ + kk
                        for i, ti in enumerate(tis):
                            ins = eng.matmul(pb[i][:, 0:512],
                                             ycm[:, k * T_ + ti * 128:k * T_ + ti * 128 + 128],
                                             wq[wi][:, kk * 512:(kk + 1) * 512],
                                             start=(k == 0), stop=(k == KT - 1))
                    return ins
                to = b.run("tensor", omm, inc=s_te)
                wfree[wi] = s_te.n
            for i, ti in enumerate(tis):
                gb = gbufs[ti % 2]
                xb_ = xrb[ti % 2]
                b.wait("sync", s_act, ld_free[0])
                b.wait("sync", s_dve, ld_free[1])
                lgv = ld_sync(gb, ffnspill[ti * 128:(ti + 1) * 128, os_ * 512:(os_ + 1) * 512],
                              sem=s_gx[ti % 2])
                lxv = ld_sync(xb_, x_in[ti * 128:(ti + 1) * 128, os_ * 512:(os_ + 1) * 512],
                              sem=s_gx[ti % 2])
                b.wait("scalar", s_gx[ti % 2], lxv)
                b.wait("scalar", s_dve, last_vo)
                ag = b.run("scalar", lambda e, gb=gb: e.activation(gsil, gb, AF.Silu), inc=s_act)
                b.wait("vector", s_te, to)
                b.wait("vector", s_act, ag)
                b.wait("vector", s_gx[ti % 2], lxv)
                b.wait("vector", s_st, ost)
                b.run("vector", lambda e, i=i: e.tensor_tensor(gsil, pb[i][:, 0:512], gsil, ALU.mult))
                vo = b.run("vector", lambda e, i=i, xb_=xb_: e.tensor_tensor(
                    ob[i % 2], gsil, xb_, ALU.add), inc=s_dve)
                out_free[i] = vo
                last_vo = vo
                ld_free = (ag, vo)
                gvo = rb_dve(ob[i % 2][:, 0:1])
                ost = store(out[ti * 128:(ti + 1) * 128, os_ * 512:(os_ + 1) * 512],
                            ob[i % 2], s_dve, gvo)

    if c.debug:
        for nm, t, shp in (("dbg_yd", ydump, [T_, D_]),
                           ("dbg_z", zspill, [2 * D_ + 2 * c.S, TH]),
                           ("dbg_ln", lnspill, [T_, D_]),
                           ("dbg_sc", scspill, [TH, D_]),
                           ("dbg_ffn", ffnspill, [T_, D_]),
                           ("dbg_y", yspill, [T_, D_])):
            o = dout(nm, shp, F32 if nm == "dbg_yd" else BF16)
            b.wait("sync", s_st, s_st.n)
            b.run("sync", lambda e, o=o, t=t: e.dma_start(out=o, in_=t), inc=s_ldx)
        oh = dout("dbg_h", [128, NH * 64], F32)
        b.wait("sync", s_dve, s_dve.n)
        b.run("sync", lambda e, o=oh: e.dma_start(out=o, in_=H[:]), inc=s_ldx)

    for sem_ in (s_st, s_ldx, s_misc, s_xa, s_lnt, s_sz, s_ldw[0], s_ldw[1],
                 s_lb[0], s_lb[1], s_rel2[0], s_rel2[1], s_gx[0], s_gx[1]):
        b.wait("sync", sem_, sem_.n)

    with b._ctx:
        with nc.Block() as block:
            @block.tensor
            def _(eng):
                for f in b.ops["tensor"]:
                    f(eng)

            @block.vector
            def _(eng):
                for f in b.ops["vector"]:
                    f(eng)

            @block.scalar
            def _(eng):
                for f in b.ops["scalar"]:
                    f(eng)

            @block.gpsimd
            def _(eng):
                for f in b.ops["gpsimd"]:
                    f(eng)

            @block.sync
            def _(eng):
                for f in b.ops["sync"]:
                    f(eng)
    return nc


def _mk_rows3(c, weights):
    r3 = np.zeros((65, c.D), np.float32)
    r3[0] = np.asarray(weights["ln_w"])
    r3[32] = np.asarray(weights["norm_w"])
    r3[64] = np.asarray(weights["ffn_b"])
    return r3


def make_inputs(cfg: Cfg, weights: dict, x_slab: np.ndarray, prev_slab: np.ndarray):
    c = cfg
    f32 = np.float32

    def asf(a):
        return np.ascontiguousarray(a, dtype=f32)

    wconv = np.asarray(weights["conv_w"])
    convc = np.zeros((128, c.XOT * c.K), f32)
    for xo in range(c.XOT):
        for k in range(c.K):
            convc[:, xo * c.K + k] = wconv[xo * 128:(xo + 1) * 128, k]
    mupcv = np.zeros((128, c.NOT), f32)
    mv = np.asarray(weights["mup_vector"])
    for o in range(c.NOT):
        w = min(128, c.PROJ - o * 128)
        mupcv[0:w, o] = mv[o * 128:o * 128 + w]
    A2d = np.asarray(weights["ffn_A"]).reshape(c.ER, c.D)
    Bt2d = np.asarray(weights["ffn_B"]).transpose(0, 2, 1).reshape(c.ER, c.D)
    wip = np.asarray(weights["in_proj_w"], f32)
    NBF = (c.NOT + 1) // 2 - (1 if c.NOT % 2 == 1 else 0)
    wipf = wip[:, :NBF * 256].reshape(c.NKQ, c.KQ, 128, NBF, 256)
    wipf = np.ascontiguousarray(wipf.transpose(0, 3, 2, 1, 4).reshape(
        c.NKQ, NBF, 128, c.KQ * 256))
    wop = np.asarray(weights["out_proj_w"], f32)
    wopt = np.ascontiguousarray(
        wop.reshape(c.NKQ, c.KQ, 128, c.D // 512, 512).transpose(0, 3, 2, 1, 4)
        .reshape(c.NKQ, c.D // 512, 128, c.KQ * 512))
    wff = np.ascontiguousarray(np.asarray(weights["ffn_W"], f32).T)
    wfft = np.ascontiguousarray(
        wff.reshape(c.NKQ, c.KQ, 128, c.D // 512, 512).transpose(0, 3, 2, 1, 4)
        .reshape(c.NKQ, c.D // 512, 128, c.KQ * 512))
    latt = np.ascontiguousarray(
        A2d.T.reshape(c.KT, 128, c.ER).transpose(1, 0, 2).reshape(128, c.KT * c.ER)).astype(f32)
    return {
        "x_in": asf(x_slab),
        "prev_in": asf(prev_slab),
        "w_ip_t": wipf,
        "w_ip_rag": asf(wip[:, NBF * 256:]),
        "w_convc": convc,
        "w_out_t": wopt,
        "w_ffn_t": wfft,
        "w_lat_t": latt,
        "w_lBt": asf(Bt2d),
        "probs_col": asf(np.repeat(np.asarray(weights["probs"]), c.R).reshape(c.ER // 128, 128).T),
        "mup_cols": mupcv,
        "rows3_in": (lambda r3: r3)(_mk_rows3(c, weights)),
        "alog_row": asf(np.asarray(weights["A_log"])[None, :]),
        "dtb_col": asf(np.asarray(weights["dt_bias"])[:, None]),
        "d_cols": asf(np.tile(np.asarray(weights["D"])[None, :], (128, 1))),
        "ident_bf": np.eye(128).astype(bf16),
        "ident_f": np.eye(64).astype(f32),
        "ut_bf": np.triu(np.ones((128, 128))).astype(bf16),
        "tri_bf": np.triu(np.ones((128, 128))).astype(bf16),
        "ones_bf": np.ones((1, 128), bf16),
        "ones65_bf": np.ones((65, 128), bf16),
        "ones_f": np.ones((1, 128), f32),
        "fconsts": np.tile(np.array([[1e-5, 1.0 / c.D, -0.5, -1.0, 1.0, 0.0, 3.0e37, 0.5]], f32), (128, 1)),
    }


from concourse.bass_utils import run_bass_kernel_spmd

_CFG = Cfg()
_NC_CACHE = {}


def _get_program():
    if "nc" not in _NC_CACHE:
        _NC_CACHE["nc"] = build_program(_CFG)
    return _NC_CACHE["nc"]


def kernel(**inputs):
    c = _CFG
    x = np.ascontiguousarray(np.asarray(inputs["L0_attn_hidden_states"], dtype=np.float32))
    B, L, D = x.shape
    assert (B, L, D) == (4, 2048, 4096)
    weights = {
        "in_proj_w": inputs["L0_attn_in_proj_w"],
        "conv_w": inputs["L0_attn_conv_w"],
        "out_proj_w": inputs["L0_attn_out_proj_w"],
        "ln_w": inputs["L0_attn_ln_w"],
        "norm_w": inputs["L0_attn_norm_w"],
        "mup_vector": inputs["L0_attn_mup_vector"],
        "A_log": inputs["L0_attn_A_log"],
        "dt_bias": inputs["L0_attn_dt_bias"],
        "D": inputs["L0_attn_D"],
        "ffn_W": inputs["L0_ffn_W"],
        "ffn_b": inputs["L0_ffn_b"],
        "ffn_A": inputs["L0_ffn_A"],
        "ffn_B": inputs["L0_ffn_B"],
        "probs": inputs["L0_ffn_probs"],
    }
    zeros_prev = np.zeros((c.HALO, c.D), np.float32)
    base = make_inputs(c, weights, x[0, :c.T], zeros_prev)
    nc = _get_program()
    in_maps = []
    for core in range(8):
        bt, hf = core // 2, core % 2
        m = dict(base)
        m["x_in"] = np.ascontiguousarray(x[bt, hf * c.T:(hf + 1) * c.T])
        m["prev_in"] = zeros_prev if hf == 0 else \
            np.ascontiguousarray(x[bt, c.T - c.HALO:c.T])
        in_maps.append(m)
    res = run_bass_kernel_spmd(nc, in_maps, core_ids=list(range(8)))
    out = np.empty((4, 2048, 4096), np.float32)
    for core in range(8):
        bt, hf = core // 2, core % 2
        out[bt, hf * c.T:(hf + 1) * c.T] = np.asarray(res.results[core]["out"])
    return out


# version 04
# speedup vs baseline: 1.0298x; 1.0269x over previous
import contextlib
import numpy as np
import ml_dtypes
import concourse.bass as bass
import concourse.mybir as mybir

F32 = mybir.dt.float32
BF16 = mybir.dt.bfloat16
AF = mybir.ActivationFunctionType
ALU = mybir.AluOpType
AX = mybir.AxisListType
bf16 = ml_dtypes.bfloat16

ENGINES = ("tensor", "vector", "scalar", "gpsimd", "sync")


class Cfg:
    def __init__(self, D=4096, S=128, NH=64, HD=64, K=4, T=1024, HALO=256,
                 E=16, R=16, debug=False, phases=6):
        self.D, self.S, self.NH, self.HD, self.K = D, S, NH, HD, K
        self.T, self.HALO, self.E, self.R = T, HALO, E, R
        self.debug = debug
        self.phases = phases
        self.CONV = D + 2 * S
        self.PROJ = 2 * D + 2 * S + NH
        self.KT = D // 128
        self.ZOT = D // 128
        self.XOT = self.CONV // 128
        self.NC = T // 128
        self.HC = HALO // 128
        self.TH = T + HALO
        self.NG = NH // 8
        self.ER = E * R
        self.NOT = (self.PROJ + 127) // 128
        self.DTW = self.PROJ - 128 * (self.NOT - 1)
        self.KQ = max(1, self.KT // 4)
        self.NKQ = (self.KT + self.KQ - 1) // self.KQ
        self.ERT = self.ER // 128
        assert D % 128 == 0 and self.CONV % 128 == 0 and S == 128 and HD == 64
        assert T % 128 == 0 and HALO % 128 == 0 and NH % 8 == 0 and self.ER % 128 == 0


class Sem:
    def __init__(self, h, step=1):
        self.h, self.step, self.n = h, step, 0

    def inc(self):
        self.n += self.step
        return self.n


class Build:
    def __init__(self, cfg):
        self.cfg = cfg
        self.nc = bass.Bass("TRN2", target_bir_lowering=False,
                            detect_race_conditions=False)
        self.ops = {e: [] for e in ENGINES}
        self._ctx = contextlib.ExitStack()

    def sem(self, name, step=1):
        sm = Sem(self._ctx.enter_context(self.nc.semaphore(name)), step)
        if not hasattr(self, "_allsems"):
            self._allsems = {}
        self._allsems[name] = sm
        return sm

    def sbuf(self, name, shape, dtype):
        return self._ctx.enter_context(self.nc.sbuf_tensor(name, shape, dtype))

    def psum(self, name, shape, dtype=F32):
        return self._ctx.enter_context(self.nc.psum_tensor(name, shape, dtype))

    def wait(self, e, sem, val):
        if val > 0:
            self.ops[e].append(lambda eng, s=sem.h, v=val: eng.wait_ge(s, v))

    def run(self, e, f, inc=None):
        incs = inc if isinstance(inc, (list, tuple)) else ([inc] if inc else [])
        tgts = [s.inc() for s in incs]

        def g(eng, f=f, incs=incs):
            ins = f(eng)
            for s in incs:
                ins = ins.then_inc(s.h, s.step)
        self.ops[e].append(g)
        return tgts[0] if len(tgts) == 1 else (tgts if tgts else None)


def _finalize(b, nc, c, dram):
    for name, sem_ in list(b._allsems.items()):
        b.wait("sync", sem_, sem_.n)
    with b._ctx:
        with nc.Block() as block:
            @block.tensor
            def _(eng):
                for f in b.ops["tensor"]:
                    f(eng)

            @block.vector
            def _(eng):
                for f in b.ops["vector"]:
                    f(eng)

            @block.scalar
            def _(eng):
                for f in b.ops["scalar"]:
                    f(eng)

            @block.gpsimd
            def _(eng):
                for f in b.ops["gpsimd"]:
                    f(eng)

            @block.sync
            def _(eng):
                for f in b.ops["sync"]:
                    f(eng)
    return nc


def build_program(cfg: Cfg):
    b = Build(cfg)
    nc, c = b.nc, cfg
    D_, T_, H_, TH, KT, NH = c.D, c.T, c.HALO, c.TH, c.KT, c.NH
    NT, HT = T_ // 128, H_ // 128
    tsp = [(i * 512, min(512, T_ - i * 512)) for i in range((T_ + 511) // 512)]
    KQ, NKQ, ERT = c.KQ, c.NKQ, c.ERT
    dram = {}

    def din(name, shape, dtype):
        dram[name] = nc.dram_tensor(name, shape, dtype, kind="ExternalInput").ap()
        return dram[name]

    def dout(name, shape, dtype):
        dram[name] = nc.dram_tensor(name, shape, dtype, kind="ExternalOutput").ap()
        return dram[name]

    def dint(name, shape, dtype):
        dram[name] = nc.dram_tensor(name, shape, dtype, kind="Internal").ap()
        return dram[name]

    x_in = din("x_in", [T_, D_], F32)
    prev_in = din("prev_in", [H_, D_], F32)
    NBF = (c.NOT + 1) // 2 - (1 if c.NOT % 2 == 1 else 0)
    w_ip_t = din("w_ip_t", [c.NKQ, NBF, 128, c.KQ * 256], F32)
    w_ip_rag = din("w_ip_rag", [D_, c.PROJ - NBF * 256], F32)
    w_convc = din("w_convc", [128, c.XOT * c.K], F32)
    w_out_t = din("w_out_t", [c.NKQ, D_ // 512, 128, c.KQ * 512], F32)
    w_ffn_t = din("w_ffn_t", [c.NKQ, D_ // 512, 128, c.KQ * 512], F32)
    w_lat_t = din("w_lat_t", [128, KT * c.ER], F32)
    w_lBt = din("w_lBt", [c.ER, D_], F32)
    probs_col = din("probs_col", [128, c.ER // 128], F32)
    mup_cols = din("mup_cols", [128, c.NOT], F32)
    rows3_in = din("rows3_in", [65, D_], F32)
    alog_row = din("alog_row", [1, NH], F32)
    dtb_col = din("dtb_col", [NH, 1], F32)
    d_cols = din("d_cols", [128, NH], F32)
    ident_bf = din("ident_bf", [128, 128], BF16)
    ident_f = din("ident_f", [64, 64], F32)
    ut_bf = din("ut_bf", [128, 128], BF16)
    tri_bf = din("tri_bf", [128, 128], BF16)
    ones_bf = din("ones_bf", [1, 128], BF16)
    ones65_bf = din("ones65_bf", [65, 128], BF16)
    ones_f = din("ones_f", [1, 128], F32)
    fconsts = din("fconsts", [128, 8], F32)
    out = dout("out", [T_, D_], F32)

    ydump = dint("ydump", [T_, D_], F32)
    lnspill = dint("lnspill", [T_, D_], BF16)
    scspill = dint("scspill", [TH, D_], BF16)
    zspill = dint("zspill", [2 * D_ + 2 * c.S, TH], BF16)
    ffnspill = dint("ffnspill", [T_, D_], BF16)
    yspill = dint("yspill", [T_, D_], BF16)

    sb = b.sbuf
    poolA = sb("poolA", [128, KT * T_], BF16)
    PB = max(KT * H_ + 512 + 3 + TH,
             KT * ERT * 128 + 2 * ERT * 512 + ERT * T_,
             NH * 128 + 2 * D_ + 512,
             2 * 512)
    poolB = sb("poolB", [128, PB], BF16)
    poolF = sb("poolF", [128, max(D_, 2560 + TH, 2 * TH)], F32)
    wq = [sb(f"wq{i}", [128, KQ * 512], BF16) for i in range(2)]
    xs_tm = [sb(f"xs_tm{i}", [128, D_], BF16) for i in range(2)]
    sz_tm = sb("sz_tm", [128, D_], BF16)
    bcc = [sb(f"bcc{i}", [128, 2 * 128], BF16) for i in range(2)]

    ln_bf = poolA[:, 0:D_]
    sc_bf = poolA[:, D_:2 * D_]
    lnT = poolA
    lnTh = poolB
    diag_cv = poolB[:, KT * H_:KT * H_ + 512]
    xbc_tmp = poolB[:, KT * H_ + 512:KT * H_ + 512 + 3 + TH]
    lAT = poolB[:, 0:KT * ERT * 128]
    lB0 = KT * ERT * 128
    lBtq = [[poolB[:, lB0 + (p * ERT + et) * 512: lB0 + (p * ERT + et) * 512 + 512]
             for et in range(ERT)] for p in range(2)]
    hprime = poolB[:, lB0 + 2 * ERT * 512:lB0 + 2 * ERT * 512 + ERT * T_]
    diagD = poolB[:, 0:NH * 128]
    nw_bc = poolB[:, NH * 128:NH * 128 + D_]
    yhat_bf = poolB[:, NH * 128 + D_:NH * 128 + 2 * D_]
    xw8 = poolB[:, NH * 128 + 2 * D_:NH * 128 + 2 * D_ + 512]
    gbufs = [poolB[:, i * 512:(i + 1) * 512] for i in range(2)]
    xa = poolF[:, 0:D_]
    dtscr_f = poolF[0:NH, 0:TH]
    dt_t2 = poolF[0:64, TH:2 * TH]
    yfull = poolF[:, 0:D_]
    xrb = [poolF[:, i * 512:(i + 1) * 512] for i in range(2)]
    ob = [poolF[:, 1024 + i * 512:1024 + (i + 1) * 512] for i in range(2)]
    gsil = poolF[:, 2048:2560]
    acc_cv = poolF[:, poolF.shape[1] - TH:]
    sqjunkA = xs_tm[0]
    lnw_bc = xs_tm[1]

    ev_bf = sb("ev_bf", [128, max(TH, 512)], BF16)
    convw = sb("convw", [128, c.XOT * c.K], F32)
    mupc = sb("mupc", [128, c.NOT], F32)
    dt_sp = sb("dt_sp", [NH, TH], F32)
    H = sb("H", [128, NH * 64], F32)
    H_bf = sb("H_bf", [128, NH * 64], BF16)
    dtT = sb("dtT", [128, NH], F32)
    logdtT = sb("logdtT", [128, NH], F32)
    dtAT = sb("dtAT", [128, NH], BF16)
    colst = sb("colst", [128, NH], F32)
    expcT = sb("expcT", [128, NH], F32)
    a_bc = sb("a_bc", [128, NH], F32)
    a_col = sb("a_col", [NH, 1], F32)
    atot_bc = sb("atot_bc", [128, NH], F32)
    wtilT = sb("wtilT", [128, NH], F32)
    rowbuf = sb("rowbuf", [1, 128], F32)
    dtsum = sb("dtsum", [NH, 2], F32)
    gsb = sb("gsb", [128, 128], F32)
    btile = sb("btile", [128, 128], BF16)
    e8 = sb("e8", [128, 8 * 128], BF16)
    mstat = [sb(f"mstat{i}", [128, 128], BF16) for i in range(2)]
    t1buf = sb("t1buf", [128, 512], F32)
    dcol_all = sb("dcol_all", [128, NH], F32)
    ident_s = sb("ident_s", [128, 128], BF16)
    identf_s = sb("identf_s", [64, 64], F32)
    ut_s = sb("ut_s", [128, 128], BF16)
    tri_s = sb("tri_s", [128, 128], BF16)
    ones_s = sb("ones_s", [1, 128], BF16)
    ones65 = sb("ones65", [65, 128], BF16)
    onesf_s = sb("onesf_s", [1, 128], F32)
    dtb_s = sb("dtb_s", [NH, 1], F32)
    arow_s = sb("arow_s", [1, NH], F32)
    arow_bf = sb("arow_bf", [1, NH], BF16)
    rows3 = sb("rows3", [65, D_], BF16)
    probsc = sb("probsc", [128, max(ERT, 1)], F32)
    lncol = sb("lncol", [128, 8], F32)
    fconst = sb("fconst", [128, 8], F32)
    scr = sb("scr", [128, 8], F32)

    pb = [b.psum(f"pb{i}", [128, 512], F32) for i in range(7)]
    pbf = b.psum("pbf", [128, 1024], BF16)

    s_te = b.sem("s_te")
    s_dve = b.sem("s_dve")
    s_act = b.sem("s_act")
    s_ldx = b.sem("s_ldx", 16)
    s_ldw = [b.sem(f"s_ldw{i}", 16) for i in range(2)]
    s_st = b.sem("s_st", 16)
    s_misc = b.sem("s_misc", 16)
    s_lb = [b.sem(f"s_lb{i}", 16) for i in range(2)]
    s_xa = b.sem("s_xa", 16)
    s_lnt = b.sem("s_lnt", 16)
    s_rel2 = [b.sem(f"s_rel2{i}", 16) for i in range(2)]
    s_sz = b.sem("s_sz", 16)
    s_gx = [b.sem(f"s_gx{i}", 16) for i in range(2)]

    def ld_sync(dst, src, transpose=False, sem=None):
        return b.run("sync", lambda e, d=dst, s=src, t=transpose:
                     e.dma_start(out=d, in_=s, transpose=t), inc=(sem or s_ldx))

    def ld_w(dst, src, wi):
        return b.run("gpsimd", lambda e, d=dst, s=src: e.dma_start(out=d, in_=s),
                     inc=s_ldw[wi])

    def store(dst, src, gate_sem, gate_val):
        b.wait("scalar", gate_sem, gate_val)
        return b.run("scalar", lambda e, d=dst, s=src: e.dma_start(out=d, in_=s), inc=s_st)

    def rb_act(tile):
        p = tile.shape[0]
        b.wait("scalar", s_act, s_act.n)
        return b.run("scalar", lambda e, t=tile, p=p: e.activation(scr[0:p, 6:7], t, AF.Copy), inc=s_act)

    def rb_dve(tile):
        p = tile.shape[0]
        b.wait("vector", s_dve, s_dve.n)
        return b.run("vector", lambda e, t=tile, p=p: e.tensor_copy(scr[0:p, 7:8], t), inc=s_dve)

    def barrier():
        te0, dv0, ac0 = s_te.n, s_dve.n, s_act.n
        b.wait("tensor", s_dve, dv0)
        b.wait("tensor", s_act, ac0)
        b.wait("vector", s_te, te0)
        b.wait("vector", s_act, ac0)
        b.wait("scalar", s_te, te0)
        b.wait("scalar", s_dve, dv0)

    ld_sync(fconst[:], fconsts)
    for _i, _v in enumerate([1e-5, 1.0 / D_, -0.5, -1.0, 1.0, 0.0, 3.0e37, 0.5]):
        nc.const_aps.register(F32, float(_v), fconst[:, _i:_i + 1])
    ld_sync(ident_s[:], ident_bf)
    ld_sync(identf_s[:], ident_f)
    ld_sync(ut_s[:], ut_bf)
    ld_sync(tri_s[:], tri_bf)
    ld_sync(ones_s[:], ones_bf)
    ld_sync(ones65[:], ones65_bf)
    ld_sync(onesf_s[:], ones_f)
    ld_sync(convw[:], w_convc)
    ld_sync(mupc[:], mup_cols)
    ld_sync(dtb_s[:], dtb_col)
    ld_sync(arow_s[:], alog_row)
    ld_sync(dcol_all[:], d_cols)
    cst = ld_sync(probsc[:], probs_col)
    gw0 = b.run("gpsimd", lambda e: e.dma_start(out=rows3[:], in_=rows3_in), inc=s_misc)

    b.wait("vector", s_ldx, cst)
    b.run("vector", lambda e: e.memset(H[:, 0:NH * 64], 0.0))
    b.run("vector", lambda e: e.memset(H_bf[:, 0:NH * 64], 0.0), inc=s_dve)

    b.wait("tensor", s_ldx, cst)
    b.wait("tensor", s_misc, gw0)
    lnwev = {0: 0, 1: 0}
    for j in range(D_ // 512):
        b.wait("tensor", s_dve, lnwev[j % 2])
        tt = b.run("tensor", lambda e, j=j: e.matmul(
            pb[j % 2][:], ones_s[0:1, :], rows3[0:1, j * 512:(j + 1) * 512],
            start=True, stop=True), inc=s_te)
        b.wait("vector", s_te, tt)
        lnwev[j % 2] = b.run("vector", lambda e, j=j: e.tensor_copy(
            lnw_bc[:, j * 512:(j + 1) * 512], pb[j % 2][:]), inc=s_dve)
    b.wait("scalar", s_ldx, cst)
    b.run("scalar", lambda e: e.activation(arow_s[:], arow_s[:], AF.Exp))
    b.run("scalar", lambda e: e.mul(arow_s[:], arow_s[:], -1.0))
    aa = b.run("scalar", lambda e: e.copy(arow_bf[:], arow_s[:]), inc=s_act)
    b.wait("tensor", s_act, aa)
    tt = b.run("tensor", lambda e: e.matmul(pb[2][:, 0:NH], ones_s[0:1, :],
                                            arow_bf[0:1, :], start=True, stop=True), inc=s_te)
    t2 = b.run("tensor", lambda e: e.transpose(pb[3][0:NH, 0:1], arow_s[0:1, 0:NH],
                                               identf_s[0:1, 0:1]), inc=s_te)
    b.wait("vector", s_te, t2)
    b.run("vector", lambda e: e.tensor_copy(a_bc[:], pb[2][:, 0:NH]))
    b.run("vector", lambda e: e.tensor_copy(a_col[:], pb[3][0:NH, 0:1]), inc=s_dve)

    inv_d = 1.0 / D_
    a5_hist = 0
    st_hist = 0
    for i in range(NT + HT):
        halo = i >= NT
        src = prev_in[(i - NT) * 128:(i - NT + 1) * 128, :] if halo else \
            x_in[i * 128:(i + 1) * 128, :]
        b.wait("sync", s_act, a5_hist)
        lt = ld_sync(xa, src, sem=s_xa)
        b.wait("vector", s_xa, lt)
        vrd = b.run("vector", lambda e: e.tensor_reduce(lncol[:, 0:1], xa, AX.X, ALU.add), inc=s_dve)
        b.wait("vector", s_dve, vrd)
        v1 = b.run("vector", lambda e: e.tensor_scalar(lncol[:, 1:2], lncol[:, 0:1], inv_d, None, ALU.mult), inc=s_dve)
        b.wait("scalar", s_xa, lt)
        b.wait("scalar", s_dve, v1)
        vz0 = b.run("vector", lambda e: e.memset(lncol[:, 2:3], 0.0), inc=s_dve)
        b.wait("scalar", s_dve, vz0)
        a1 = b.run("scalar", lambda e: e.activation(
            sqjunkA[:], xa, AF.Square, accum_out=lncol[:, 2:3]), inc=s_act)
        b.wait("vector", s_act, a1)
        b.run("vector", lambda e: e.tensor_tensor(lncol[:, 3:4], lncol[:, 1:2], lncol[:, 1:2], ALU.mult))
        vq = b.run("vector", lambda e: e.tensor_scalar(lncol[:, 2:3], lncol[:, 2:3], inv_d, None, ALU.mult), inc=s_dve)
        b.wait("vector", s_dve, vq)
        v2 = b.run("vector", lambda e: e.tensor_tensor(lncol[:, 2:3], lncol[:, 2:3], lncol[:, 3:4], ALU.subtract), inc=s_dve)
        b.wait("scalar", s_dve, v2)
        b.run("scalar", lambda e: e.activation(lncol[:, 3:4], lncol[:, 2:3], AF.Ln, bias=1e-5))
        a2 = b.run("scalar", lambda e: e.activation(lncol[:, 3:4], lncol[:, 3:4], AF.Exp, scale=-0.5), inc=s_act)
        b.wait("vector", s_act, a2)
        vq = b.run("vector", lambda e: e.tensor_tensor(lncol[:, 4:5], lncol[:, 2:3], lncol[:, 3:4], ALU.mult), inc=s_dve)
        b.wait("vector", s_dve, vq)
        v3 = b.run("vector", lambda e: e.tensor_tensor(lncol[:, 4:5], lncol[:, 4:5], lncol[:, 3:4], ALU.mult), inc=s_dve)
        b.wait("scalar", s_dve, v3)
        b.run("scalar", lambda e: e.activation(lncol[:, 4:5], lncol[:, 4:5], AF.Ln, bias=1e-5))
        a3 = b.run("scalar", lambda e: e.activation(lncol[:, 4:5], lncol[:, 4:5], AF.Exp, scale=-0.5), inc=s_act)
        b.wait("vector", s_act, a3)
        vq = b.run("vector", lambda e: e.tensor_tensor(lncol[:, 5:6], lncol[:, 1:2], lncol[:, 3:4], ALU.mult), inc=s_dve)
        b.wait("vector", s_dve, vq)
        vq = b.run("vector", lambda e: e.tensor_scalar(lncol[:, 5:6], lncol[:, 5:6], -1.0, None, ALU.mult), inc=s_dve)
        b.run("vector", lambda e: e.tensor_tensor(lncol[:, 6:7], lncol[:, 3:4], lncol[:, 4:5], ALU.mult))
        b.wait("vector", s_dve, vq)
        v4 = b.run("vector", lambda e: e.tensor_tensor(lncol[:, 7:8], lncol[:, 5:6], lncol[:, 4:5], ALU.mult), inc=s_dve)
        b.wait("scalar", s_dve, v4)
        b.wait("scalar", s_st, st_hist)
        if not halo:
            b.run("scalar", lambda e: e.activation(
                ln_bf, xa, AF.Identity, bias=lncol[:, 5:6], scale=lncol[:, 3:4]))
        a5 = b.run("scalar", lambda e: e.activation(
            sc_bf, xa, AF.Identity, bias=lncol[:, 7:8], scale=lncol[:, 6:7]), inc=s_act)
        a5_hist = a5
        b.wait("vector", s_act, a5)
        v5 = b.run("vector", lambda e: e.tensor_tensor(sc_bf, sc_bf, lnw_bc[:], ALU.mult), inc=s_dve)
        g1 = rb_dve(sc_bf[:, 0:1])
        row0 = (T_ + (i - NT) * 128) if halo else i * 128
        if not halo:
            ga = rb_act(ln_bf[:, 0:1])
            store(lnspill[i * 128:(i + 1) * 128, :], ln_bf, s_act, ga)
        st_hist = store(scspill[row0:row0 + 128, :], sc_bf, s_dve, g1)
    a1_spills = s_st.n
    barrier()

    b.wait("sync", s_st, a1_spills)
    b.wait("sync", s_dve, s_dve.n)
    b.wait("sync", s_act, s_act.n)
    for k in range(KT):
        ld_sync(lnT[:, k * T_:(k + 1) * T_],
                scspill[0:T_, k * 128:(k + 1) * 128], transpose=True, sem=s_lnt)
        ld_sync(lnTh[:, k * H_:(k + 1) * H_],
                scspill[T_:TH, k * 128:(k + 1) * 128], transpose=True, sem=s_lnt)
    lnT_ready = s_lnt.n

    if c.phases < 2:
        return _finalize(b, nc, c, dram)
    n_blk = (c.NOT + 1) // 2
    pfree = {0: (None, 0), 1: (None, 0)}
    wfree = {0: 0, 1: 0}
    evst = 0
    conv_prev = 0
    sil_prev = 0
    qi = 0
    b.wait("tensor", s_lnt, lnT_ready)
    for blk in range(n_blk):
        o0 = blk * 2
        ntile = min(2, c.NOT - o0)
        cols0 = o0 * 128
        cols1 = min((o0 + 2) * 128, c.PROJ)
        bw = cols1 - cols0
        psets = [[pb[0], pb[1], pb[2]], [pb[3], pb[4], pb[5]]]
        for oj in range(ntile):
            sfree, vfree = pfree[oj]
            if sfree is not None:
                b.wait("tensor", sfree, vfree)
        tdone = [0] * ntile
        for kq in range(NKQ):
            wi = qi % 2
            qi += 1
            b.wait("gpsimd", s_te, wfree[wi])
            if bw == 256:
                lw = ld_w(wq[wi][:, 0:KQ * 256], w_ip_t[kq, blk], wi)
            else:
                lw = ld_w(wq[wi][:, 0:KQ * bw].rearrange("p (k o) -> p k o", k=KQ),
                          w_ip_rag[kq * KQ * 128:min((kq + 1) * KQ * 128, D_), :]
                          .rearrange("(k p) o -> p k o", p=128), wi)
            b.wait("tensor", s_ldw[wi], lw)
            for oj in range(ntile):
                o = o0 + oj
                is_dt = (o == c.NOT - 1)
                owid = c.DTW if is_dt else 128
                is_z = o < c.ZOT
                ps = psets[oj]

                def mms(eng, wi=wi, kq=kq, oj=oj, owid=owid, is_z=is_z, ps=ps, bw=bw):
                    ins = None
                    for kk in range(KQ):
                        k = kq * KQ + kk
                        if k >= KT:
                            break
                        st = wq[wi][:, kk * bw + oj * 128: kk * bw + oj * 128 + owid]
                        first = (k == 0)
                        last = (k == KT - 1)
                        for hi, (t0, tw) in enumerate(tsp):
                            ins = eng.matmul(ps[hi][0:owid, 0:tw], st,
                                             lnT[:, k * T_ + t0:k * T_ + t0 + tw],
                                             start=first, stop=last)
                        if not is_z:
                            ins = eng.matmul(ps[2][0:owid, 0:H_], st,
                                             lnTh[:, k * H_:(k + 1) * H_],
                                             start=first, stop=last)
                    return ins
                tdone[oj] = b.run("tensor", mms, inc=s_te)
            wfree[wi] = s_te.n
        for oj in range(ntile):
            o = o0 + oj
            is_dt = (o == c.NOT - 1)
            owid = c.DTW if is_dt else 128
            is_z = o < c.ZOT
            ps = psets[oj]
            mupcol = mupc[:, o:o + 1]
            if is_z:
                b.wait("scalar", s_te, tdone[oj])
                b.wait("scalar", s_st, evst)
                for hi, (t0, tw) in enumerate(tsp):
                    b.run("scalar", lambda e, hi=hi, t0=t0, tw=tw, ps=ps, mc=mupcol: e.activation(
                        ev_bf[:, t0:t0 + tw], ps[hi][:, 0:tw], AF.Silu, scale=mc))
                ga = rb_act(ev_bf[:, 0:1])
                pfree[oj] = (s_act, ga)
                evst = store(zspill[o * 128:(o + 1) * 128, H_:TH], ev_bf[:, 0:T_], s_act, ga)
            else:
                b.wait("vector", s_te, tdone[oj])
                b.wait("vector", s_act, sil_prev)
                for hi, (t0, tw) in enumerate(tsp):
                    b.run("vector", lambda e, hi=hi, t0=t0, tw=tw, ps=ps, mc=mupcol, w=owid: e.tensor_scalar(
                        xbc_tmp[0:w, 3 + H_ + t0:3 + H_ + t0 + tw], ps[hi][0:w, 0:tw],
                        mc[0:w, :], None, ALU.mult))
                vv1 = b.run("vector", lambda e, ps=ps, mc=mupcol, w=owid: e.tensor_scalar(
                    xbc_tmp[0:w, 3:3 + H_], ps[2][0:w, 0:H_], mc[0:w, :], None, ALU.mult), inc=s_dve)
                pfree[oj] = (s_dve, vv1)
                if is_dt:
                    b.run("vector", lambda e, w=owid: e.tensor_copy(
                        dtscr_f[0:w, 0:TH], xbc_tmp[0:w, 3:3 + TH]))
                    vd = rb_dve(dtscr_f[0:NH, 0:1])
                else:
                    xo = o - c.ZOT
                    b.run("vector", lambda e: e.memset(xbc_tmp[:, 0:3], 0.0))
                    b.run("vector", lambda e, xo=xo: e.tensor_scalar(
                        acc_cv[:, 0:TH], xbc_tmp[:, 0:TH],
                        convw[:, (xo * c.K):(xo * c.K) + 1], None, ALU.mult))
                    for kk in range(1, c.K):
                        b.run("vector", lambda e, xo=xo, kk=kk: e.scalar_tensor_tensor(
                            acc_cv[:, 0:TH], xbc_tmp[:, kk:kk + TH],
                            convw[:, (xo * c.K + kk):(xo * c.K + kk + 1)],
                            acc_cv[:, 0:TH], ALU.mult, ALU.add))
                    vd = b.run("vector", lambda e: e.tensor_copy(scr[:, 1:2], acc_cv[:, 0:1]), inc=s_dve)
                    b.wait("scalar", s_dve, vd)
                    b.wait("scalar", s_st, evst)
                    sil = b.run("scalar", lambda e: e.activation(
                        ev_bf[:, 0:TH], acc_cv[:, 0:TH], AF.Silu), inc=s_act)
                    sil_prev = sil
                    ga = rb_act(ev_bf[:, 0:1])
                    evst = store(zspill[D_ + xo * 128:D_ + (xo + 1) * 128, :],
                                 ev_bf[:, 0:TH], s_act, ga)
    a2_spills = s_st.n
    barrier()

    b.run("vector", lambda e: e.tensor_scalar(dtscr_f[0:NH, 0:TH], dtscr_f[0:NH, 0:TH],
                                              dtb_s[0:NH, :], None, ALU.add))
    vr = b.run("vector", lambda e: e.tensor_scalar(dt_sp[0:NH, :], dtscr_f[0:NH, 0:TH],
                                                   0.0, None, ALU.max), inc=s_dve)
    b.wait("scalar", s_dve, vr)
    b.run("scalar", lambda e: e.activation(dt_t2[0:NH, 0:TH], dtscr_f[0:NH, 0:TH], AF.Abs))
    b.run("scalar", lambda e: e.activation(dt_t2[0:NH, 0:TH], dt_t2[0:NH, 0:TH], AF.Exp, scale=-1.0))
    asp = b.run("scalar", lambda e: e.activation(dt_t2[0:NH, 0:TH], dt_t2[0:NH, 0:TH], AF.Ln, bias=1.0), inc=s_act)
    b.wait("vector", s_act, asp)
    vsp = b.run("vector", lambda e: e.tensor_tensor(dt_sp[0:NH, :], dt_sp[0:NH, :],
                                                    dt_t2[0:NH, 0:TH], ALU.add), inc=s_dve)

    def emit_reload(ci, slot, main):
        col0 = (H_ + ci * 128) if main else ci * 128
        b.wait("sync", s_st, a2_spills)
        b.wait("sync", s_te, s_te.n)
        b.wait("sync", s_dve, s_dve.n)
        b.wait("sync", s_act, s_act.n)
        ld_sync(xs_tm[slot][:], zspill[D_:2 * D_, col0:col0 + 128], transpose=True,
                sem=s_rel2[slot])
        return (slot, ld_sync(bcc[slot].rearrange("p (a t) -> p a t", a=2),
                              zspill[2 * D_:2 * D_ + 2 * c.S, col0:col0 + 128]
                              .rearrange("(a p) t -> p a t", p=128), sem=s_rel2[slot]))

    def chunk_prep(ci, main, slot, rel_tgt):
        mp = getattr(c, "microphase", 9)
        barrier()
        col0 = (H_ + ci * 128) if main else ci * 128
        if mp < 0.5:
            return dict(vcs=0, aat=0, aex=0, vbt=0, awt=0, vg=0)
        b.wait("tensor", s_dve, vsp)
        ta = b.run("tensor", lambda e, c0=col0: e.transpose(
            pb[2][:, 0:NH], dt_sp[0:NH, c0:c0 + 128], identf_s[0:NH, 0:NH]), inc=s_te)
        if mp < 1:
            tq = b.run("tensor", lambda e: e.matmul(
                pb[3][:, 0:128], ident_s[:], ut_s[:], start=True, stop=True), inc=s_te)
            b.wait("vector", s_te, tq)
            b.run("vector", lambda e: e.tensor_copy(gsb[:], pb[3][:, 0:128]), inc=s_dve)
            return dict(vcs=0, aat=0, aex=0, vbt=0, awt=0, vg=0)
        b.wait("vector", s_te, ta)
        vda = b.run("vector", lambda e: e.tensor_copy(dtT[:], pb[2][:, 0:NH]), inc=s_dve)
        if mp < 2:
            return dict(vcs=0, aat=0, aex=0, vbt=0, awt=0, vg=0)
        b.wait("scalar", s_dve, vda)
        ala = b.run("scalar", lambda e: e.activation(logdtT[:], dtT[:], AF.Ln), inc=s_act)
        if mp < 2.2:
            return dict(vcs=0, aat=0, aex=0, vbt=0, awt=0, vg=0)
        b.wait("vector", s_dve, vda)
        vab = b.run("vector", lambda e: e.tensor_tensor(dtAT[:], dtT[:], a_bc[:], ALU.mult), inc=s_dve)
        if mp < 2.4:
            return dict(vcs=0, aat=0, aex=0, vbt=0, awt=0, vg=0)
        b.wait("tensor", s_dve, vab)
        tb_ = b.run("tensor", lambda e: e.matmul(pb[2][:, 0:NH], ut_s[:], dtAT[:],
                                                 start=True, stop=True), inc=s_te)
        if mp < 2.6:
            return dict(vcs=0, aat=0, aex=0, vbt=0, awt=0, vg=0)
        b.wait("vector", s_te, tb_)
        b.wait("vector", s_act, ala)
        vcs = b.run("vector", lambda e: e.tensor_tensor(colst[:], logdtT[:], pb[2][:, 0:NH], ALU.subtract), inc=s_dve)
        if mp < 2.8:
            return dict(vcs=0, aat=0, aex=0, vbt=0, awt=0, vg=0)
        b.wait("scalar", s_te, tb_)
        b.wait("scalar", s_dve, vcs)
        aex = b.run("scalar", lambda e: e.activation(expcT[:], pb[2][:, 0:NH], AF.Exp), inc=s_act)
        if mp < 3:
            return dict(vcs=vcs, aat=0, aex=aex, vbt=0, awt=0, vg=0)
        vq2 = b.run("vector", lambda e, c0=col0: e.tensor_reduce(
            dtsum[0:NH, 0:1], dt_sp[0:NH, c0:c0 + 128], AX.X, ALU.add), inc=s_dve)
        b.wait("vector", s_dve, vq2)
        vds = b.run("vector", lambda e: e.tensor_tensor(dtsum[0:NH, 1:2], dtsum[0:NH, 0:1],
                                                        a_col[0:NH, :], ALU.mult), inc=s_dve)
        b.wait("tensor", s_dve, vds)
        b.wait("tensor", s_act, aex)
        b.wait("tensor", s_dve, vcs)
        if mp < 4:
            return dict(vcs=vcs, aat=0, aex=aex, vbt=0, awt=0, vg=0)
        tc_ = b.run("tensor", lambda e: e.transpose(pb[2][0:1, 0:NH], dtsum[0:NH, 1:2],
                                                    identf_s[0:NH, 0:NH]), inc=s_te)
        b.wait("vector", s_te, tc_)
        vrow = b.run("vector", lambda e: e.tensor_copy(rowbuf[0:1, 0:NH], pb[2][0:1, 0:NH]), inc=s_dve)
        b.wait("tensor", s_dve, vrow)
        if mp < 5:
            return dict(vcs=vcs, aat=0, aex=aex, vbt=0, awt=0, vg=0)
        td = b.run("tensor", lambda e: e.matmul(pb[2][:, 0:NH], onesf_s[0:1, :],
                                                rowbuf[0:1, 0:NH], start=True, stop=True), inc=s_te)
        b.wait("scalar", s_te, td)
        aat = b.run("scalar", lambda e: e.activation(atot_bc[:], pb[2][:, 0:NH], AF.Exp), inc=s_act)
        pr = dict(vcs=vcs, aat=aat, aex=aex)
        if not main:
            b.wait("vector", s_te, td)
            b.wait("vector", s_dve, vcs)
            b.wait("vector", s_act, aat)
            vwa = b.run("vector", lambda e: e.tensor_tensor(wtilT[:], colst[:], pb[2][:, 0:NH], ALU.add), inc=s_dve)
            b.wait("scalar", s_dve, vwa)
            pr["awt"] = b.run("scalar", lambda e: e.activation(wtilT[:], wtilT[:], AF.Exp), inc=s_act)
        b.wait("tensor", s_rel2[rel_tgt[0]], rel_tgt[1])
        te_ = b.run("tensor", lambda e, sl=slot: e.transpose(
            pbf[:, 0:128], bcc[sl][:, 0:128], ident_s[:]), inc=s_te)
        b.wait("vector", s_te, te_)
        pr["vbt"] = b.run("vector", lambda e: e.tensor_copy(btile[:], pbf[:, 0:128]), inc=s_dve)
        if main:
            b.wait("tensor", s_act, aat)
            tg = b.run("tensor", lambda e, sl=slot: e.matmul(
                pb[2][:, 0:128], bcc[sl][:, 0:128], bcc[sl][:, 128:256],
                start=True, stop=True), inc=s_te)
            b.wait("vector", s_te, tg)
            pr["vg"] = b.run("vector", lambda e: e.tensor_tensor(gsb[:], pb[2][:, 0:128],
                                                                 tri_s[:], ALU.mult), inc=s_dve)
        return pr

    def emit_state_update(g, slot, pr, main):
        if main:
            src = e8.rearrange("p (h t) -> p h t", h=8)[:, :, 127:128].broadcast_to([128, 8, 64])
            b.wait("vector", s_act, pr["e_done"])
        else:
            src = wtilT[:, g * 8:(g + 1) * 8].broadcast_to([128, 8, 64])
            b.wait("vector", s_act, pr["awt"])
        vxw = b.run("vector", lambda e, g=g, sl=slot, s=src: e.tensor_tensor(
            xw8.rearrange("p (h t) -> p h t", h=8),
            xs_tm[sl].rearrange("p (h t) -> p h t", h=NH)[:, g * 8:(g + 1) * 8, :],
            s, ALU.mult), inc=s_dve)
        b.wait("tensor", s_dve, max(vxw, pr["vbt"], pr.get("hfree", 0)))
        th = b.run("tensor", lambda e: e.matmul(pb[6][:, 0:512], btile[:], xw8,
                                                start=True, stop=True), inc=s_te)
        b.wait("vector", s_te, th)
        b.wait("vector", s_act, pr["aat"])
        b.run("vector", lambda e, g=g: e.tensor_tensor(
            H.rearrange("p (h t) -> p h t", h=NH)[:, g * 8:(g + 1) * 8, :],
            H.rearrange("p (h t) -> p h t", h=NH)[:, g * 8:(g + 1) * 8, :],
            atot_bc[:, g * 8:(g + 1) * 8].broadcast_to([128, 8, 64]), ALU.mult))
        b.run("vector", lambda e, g=g: e.tensor_tensor(
            H[:, g * 512:g * 512 + 512], H[:, g * 512:g * 512 + 512], pb[6][:, 0:512], ALU.add))
        vH = b.run("vector", lambda e, g=g: e.tensor_copy(
            H_bf[:, g * 512:g * 512 + 512], H[:, g * 512:g * 512 + 512]), inc=s_dve)
        return vH

    if c.phases < 2.2:
        return _finalize(b, nc, c, dram)
    relq = []
    rel0 = emit_reload(0, 0, main=False)
    if c.phases < 2.4:
        return _finalize(b, nc, c, dram)
    hf = 0
    for hc in range(c.HC):
        slot = hc % 2
        nrel = emit_reload(hc + 1, (hc + 1) % 2, main=False) if hc + 1 < c.HC else None
        pr = chunk_prep(hc, False, slot, rel0 if hc == 0 else relq.pop())
        if nrel is not None:
            relq.append(nrel)
        if c.phases < 2.6:
            return _finalize(b, nc, c, dram)
        ng_lim = c.NG if c.phases >= 3 else max(1, int(round((c.phases - 2.6) * 20)))
        for g in range(min(c.NG, ng_lim)):
            pr["hfree"] = hf
            hf = emit_state_update(g, slot, pr, main=False)
        if c.phases < 3:
            return _finalize(b, nc, c, dram)
    barrier()

    if c.phases < 4:
        return _finalize(b, nc, c, dram)
    b.wait("sync", s_te, s_te.n)
    b.wait("sync", s_dve, s_dve.n)
    for k in range(KT):
        ld_sync(lnT[:, k * T_:(k + 1) * T_],
                lnspill[:, k * 128:(k + 1) * 128], transpose=True, sem=s_lnt)
    lnT2_ready = s_lnt.n
    b.wait("gpsimd", s_te, s_te.n)
    b.wait("gpsimd", s_dve, s_dve.n)
    b.wait("gpsimd", s_act, s_act.n)
    lg = b.run("gpsimd", lambda e: e.dma_start(out=lAT[:], in_=w_lat_t), inc=s_misc)
    b.wait("tensor", s_lnt, lnT2_ready)
    b.wait("tensor", s_misc, lg)
    h_ev = {0: 0, 1: 0}
    hq = 0
    for et in range(ERT):
        for hi, (t0, tw) in enumerate(tsp):
            bank = hq % 2
            hq += 1
            b.wait("tensor", s_dve, h_ev[bank])

            def hmm(eng, et=et, t0=t0, tw=tw, bank=bank):
                ins = None
                for k in range(KT):
                    ins = eng.matmul(pb[bank][:, 0:tw],
                                     lAT[:, (k * ERT + et) * 128:(k * ERT + et) * 128 + 128],
                                     lnT[:, k * T_ + t0:k * T_ + t0 + tw],
                                     start=(k == 0), stop=(k == KT - 1))
                return ins
            th = b.run("tensor", hmm, inc=s_te)
            b.wait("vector", s_te, th)
            h_ev[bank] = b.run("vector", lambda e, et=et, t0=t0, tw=tw, bank=bank: e.tensor_scalar(
                hprime[:, et * T_ + t0:et * T_ + t0 + tw], pb[bank][:, 0:tw],
                probsc[:, et:et + 1], None, ALU.mult), inc=s_dve)
    vhp = rb_dve(hprime[:, 0:1])
    b.wait("tensor", s_dve, vhp)

    NT4 = min(4, NT)
    ffn_ev = {i: 0 for i in range(NT4)}
    evstl = 0
    lb_t = {0: 0, 1: 0}
    for os_ in range(D_ // 512):
        lbw = os_ % 2
        b.wait("gpsimd", s_te, lb_t[lbw])
        lbv = None
        for et in range(ERT):
            lbv = b.run("gpsimd", lambda e, et=et, os_=os_, lbw=lbw: e.dma_start(
                out=lBtq[lbw][et],
                in_=w_lBt[et * 128:(et + 1) * 128, os_ * 512:(os_ + 1) * 512]), inc=s_lb[lbw])
        for tp in range(NT // NT4):
            tis = [tp * NT4 + i for i in range(NT4)]
            for i, ti in enumerate(tis):
                b.wait("tensor", s_dve, ffn_ev[i])
                b.run("tensor", lambda e, i=i, os_=os_: e.matmul(
                    pb[i][:, 0:512], ones65[64:65, :], rows3[64:65, os_ * 512:(os_ + 1) * 512],
                    start=True, stop=False))
            for kq in range(NKQ):
                wi = qi % 2
                qi += 1
                b.wait("gpsimd", s_te, wfree[wi])
                lw = ld_w(wq[wi][:], w_ffn_t[kq, os_], wi)
                b.wait("tensor", s_ldw[wi], lw)

                def fmm(eng, wi=wi, kq=kq, tis=tis):
                    ins = None
                    for kk in range(KQ):
                        k = kq * KQ + kk
                        for i, ti in enumerate(tis):
                            ins = eng.matmul(pb[i][:, 0:512],
                                             lnT[:, k * T_ + ti * 128:k * T_ + ti * 128 + 128],
                                             wq[wi][:, kk * 512:(kk + 1) * 512],
                                             start=False, stop=False)
                    return ins
                b.run("tensor", fmm, inc=s_te)
                wfree[wi] = s_te.n
            b.wait("tensor", s_lb[lbw], lbv)

            def lmm(eng, tis=tis, lbw=lbw):
                ins = None
                for et in range(ERT):
                    for i, ti in enumerate(tis):
                        ins = eng.matmul(pb[i][:, 0:512],
                                         hprime[:, et * T_ + ti * 128:et * T_ + ti * 128 + 128],
                                         lBtq[lbw][et],
                                         start=False, stop=(et == ERT - 1))
                return ins
            tl = b.run("tensor", lmm, inc=s_te)
            for i, ti in enumerate(tis):
                b.wait("vector", s_te, tl)
                b.wait("vector", s_st, evstl)
                ve = b.run("vector", lambda e, i=i: e.tensor_copy(ev_bf[:, 0:512],
                                                                  pb[i][:, 0:512]), inc=s_dve)
                ffn_ev[i] = ve
                gv = rb_dve(ev_bf[:, 0:1])
                evstl = store(ffnspill[ti * 128:(ti + 1) * 128, os_ * 512:(os_ + 1) * 512],
                              ev_bf[:, 0:512], s_dve, gv)
        lb_t[lbw] = s_te.n
    barrier()

    if c.phases < 5:
        return _finalize(b, nc, c, dram)
    for h in range(NH):
        b.run("vector", lambda e, h=h: e.tensor_scalar(
            diagD[:, h * 128:(h + 1) * 128], ident_s[:], dcol_all[:, h:h + 1], None, ALU.mult))
    rb_dve(diagD[:, 0:1])
    nwev = {0: 0, 1: 0}
    for j in range(D_ // 512):
        bk = 2 + (j % 2)
        b.wait("tensor", s_dve, nwev[j % 2])
        tt = b.run("tensor", lambda e, j=j, bk=bk: e.matmul(
            pb[bk][:, 0:512], ones65[32:33, :], rows3[32:33, j * 512:(j + 1) * 512],
            start=True, stop=True), inc=s_te)
        b.wait("vector", s_te, tt)
        nwev[j % 2] = b.run("vector", lambda e, j=j, bk=bk: e.tensor_copy(
            nw_bc[:, j * 512:(j + 1) * 512], pb[bk][:, 0:512]), inc=s_dve)
    vnw = rb_dve(nw_bc[:, 0:1])
    b.wait("tensor", s_dve, vnw)

    vH_last = hf
    vg1 = 0
    rel0 = emit_reload(0, 0, main=True)
    relq = [rel0]
    y_sp = []
    for ci in range(c.NC):
        slot = ci % 2
        my_rel = relq.pop(0)
        if ci + 1 < c.NC:
            relq.append(emit_reload(ci + 1, (ci + 1) % 2, main=True))
        b.wait("sync", s_dve, vg1)
        szv = ld_sync(sz_tm[:], zspill[0:D_, H_ + ci * 128:H_ + ci * 128 + 128],
                      transpose=True, sem=s_sz)
        pr = chunk_prep(ci, True, slot, my_rel)
        yev = {0: 0, 1: 0}
        inter_free = pr["vg"]
        m_done = {0: 0, 1: 0}
        bc_free = {0: 0, 1: 0}
        for g in range(c.NG):
            ypb = pb[g % 2]
            b.wait("tensor", s_dve, max(vH_last, inter_free))
            b.wait("tensor", s_rel2[my_rel[0]], my_rel[1])
            t1m = b.run("tensor", lambda e, sl=slot, g=g: e.matmul(
                pb[2][:, 0:512], bcc[sl][:, 128:256], H_bf[:, g * 512:(g + 1) * 512],
                start=True, stop=True), inc=s_te)
            b.wait("vector", s_te, t1m)
            vt1 = b.run("vector", lambda e, g=g: e.tensor_tensor(
                t1buf.rearrange("p (h t) -> p h t", h=8),
                pb[2].rearrange("p (h t) -> p h t", h=8),
                expcT[:, g * 8:(g + 1) * 8].broadcast_to([128, 8, 64]), ALU.mult), inc=s_dve)
            inter_free = vt1
            b.wait("tensor", s_dve, yev[g % 2])
            e_done = 0
            for hl in range(8):
                h = g * 8 + hl
                bcb = pb[3] if hl % 2 == 0 else pb[5]
                if bc_free[hl % 2]:
                    b.wait("tensor", s_act, bc_free[hl % 2])
                tb_ = b.run("tensor", lambda e, h=h, bcb=bcb: e.matmul(
                    bcb[:, 0:128], dtAT[:, h:h + 1].broadcast_to([128, 128]),
                    ut_s[:], start=True, stop=True), inc=s_te)
                b.wait("scalar", s_te, tb_)
                b.wait("scalar", s_dve, pr["vcs"] if hl > 0 else max(pr["vcs"], vH_last))
                ae = b.run("scalar", lambda e, h=h, hl=hl, bcb=bcb: e.activation(
                    e8[:, hl * 128:(hl + 1) * 128], bcb[:, 0:128], AF.Exp,
                    bias=colst[:, h:h + 1]), inc=s_act)
                bc_free[hl % 2] = ae
                e_done = ae
                b.wait("vector", s_act, ae)
                b.wait("vector", s_dve, pr["vg"])
                b.wait("vector", s_te, m_done[hl % 2])
                vm = b.run("vector", lambda e, hl=hl: e.scalar_tensor_tensor(
                    mstat[hl % 2][:], e8[:, hl * 128:(hl + 1) * 128], 3.0e37, gsb[:],
                    ALU.min, ALU.mult), inc=s_dve)
                b.wait("tensor", s_dve, vm)

                def ymm(eng, h=h, hl=hl, sl=slot, yp=ypb):
                    eng.matmul(yp[:, hl * 64:hl * 64 + 64], mstat[hl % 2][:],
                               xs_tm[sl][:, h * 64:h * 64 + 64],
                               start=(hl == 0), stop=False)
                    return eng.matmul(yp[:, hl * 64:hl * 64 + 64],
                                      diagD[:, h * 128:(h + 1) * 128],
                                      xs_tm[sl][:, h * 64:h * 64 + 64],
                                      start=False, stop=(hl == 7))
                tym = b.run("tensor", ymm, inc=s_te)
                m_done[hl % 2] = tym
            pr["e_done"] = e_done
            pr["hfree"] = vH_last
            vH_last = emit_state_update(g, slot, pr, main=True)
            b.wait("vector", s_te, max(m_done[0], m_done[1]))
            vy = b.run("vector", lambda e, g=g, yp=ypb: e.tensor_tensor(
                yfull[:, g * 512:(g + 1) * 512], t1buf[:], yp[:, 0:512], ALU.add), inc=s_dve)
            yev[g % 2] = vy
        b.wait("vector", s_dve, max(yev[0], yev[1]))
        if c.debug:
            ygate = rb_dve(yfull[:, 0:1])
            ystt = store(ydump[ci * 128:(ci + 1) * 128, :], yfull, s_dve, ygate)
            b.wait("vector", s_st, ystt)
        b.wait("vector", s_sz, szv)
        vg1a = b.run("vector", lambda e: e.tensor_tensor(yfull, yfull, sz_tm[:], ALU.mult), inc=s_dve)
        vg1 = b.run("vector", lambda e: e.memset(lncol[:, 0:1], 0.0), inc=s_dve)
        b.wait("scalar", s_dve, max(vg1, vg1a))
        asq = b.run("scalar", lambda e, sl=slot: e.activation(
            xs_tm[sl][:], yfull, AF.Square, accum_out=lncol[:, 0:1]), inc=s_act)
        b.wait("scalar", s_act, asq)
        b.run("scalar", lambda e: e.activation(lncol[:, 1:2], lncol[:, 0:1], AF.Ln,
                                               scale=inv_d, bias=1e-5))
        ar2 = b.run("scalar", lambda e: e.activation(lncol[:, 1:2], lncol[:, 1:2], AF.Exp,
                                                     scale=-0.5), inc=s_act)
        b.wait("vector", s_act, ar2)
        b.wait("vector", s_st, y_sp[-1] if y_sp else 0)
        b.run("vector", lambda e: e.tensor_scalar(yfull, yfull, lncol[:, 1:2], None, ALU.mult))
        vyh = b.run("vector", lambda e: e.tensor_tensor(yhat_bf, yfull, nw_bc, ALU.mult), inc=s_dve)
        gv = rb_dve(yhat_bf[:, 0:1])
        y_sp.append(store(yspill[ci * 128:(ci + 1) * 128, :], yhat_bf, s_dve, gv))
    b_spills = s_st.n
    barrier()

    if c.phases < 6:
        return _finalize(b, nc, c, dram)
    b.wait("sync", s_st, b_spills)
    b.wait("sync", s_te, s_te.n)
    b.wait("sync", s_dve, s_dve.n)
    for k in range(KT):
        ld_sync(lnT[:, k * T_:(k + 1) * T_],
                yspill[:, k * 128:(k + 1) * 128], transpose=True, sem=s_lnt)
    ycm_ready = s_lnt.n
    ycm = lnT
    b.wait("tensor", s_lnt, ycm_ready)
    out_free = {i: 0 for i in range(NT4)}
    ld_free = (0, 0)
    ost = 0
    last_vo = 0
    for os_ in range(D_ // 512):
        for tp in range(NT // NT4):
            tis = [tp * NT4 + i for i in range(NT4)]
            to = 0
            for kq in range(NKQ):
                wi = qi % 2
                qi += 1
                b.wait("gpsimd", s_te, wfree[wi])
                lw = ld_w(wq[wi][:], w_out_t[kq, os_], wi)
                b.wait("tensor", s_ldw[wi], lw)
                if kq == 0:
                    for i in range(NT4):
                        b.wait("tensor", s_dve, out_free[i])

                def omm(eng, wi=wi, kq=kq, tis=tis):
                    ins = None
                    for kk in range(KQ):
                        k = kq * KQ + kk
                        for i, ti in enumerate(tis):
                            ins = eng.matmul(pb[i][:, 0:512],
                                             ycm[:, k * T_ + ti * 128:k * T_ + ti * 128 + 128],
                                             wq[wi][:, kk * 512:(kk + 1) * 512],
                                             start=(k == 0), stop=(k == KT - 1))
                    return ins
                to = b.run("tensor", omm, inc=s_te)
                wfree[wi] = s_te.n
            for i, ti in enumerate(tis):
                gb = gbufs[ti % 2]
                xb_ = xrb[ti % 2]
                b.wait("sync", s_act, ld_free[0])
                b.wait("sync", s_dve, ld_free[1])
                lgv = ld_sync(gb, ffnspill[ti * 128:(ti + 1) * 128, os_ * 512:(os_ + 1) * 512],
                              sem=s_gx[ti % 2])
                lxv = ld_sync(xb_, x_in[ti * 128:(ti + 1) * 128, os_ * 512:(os_ + 1) * 512],
                              sem=s_gx[ti % 2])
                b.wait("scalar", s_gx[ti % 2], lxv)
                b.wait("scalar", s_dve, last_vo)
                ag = b.run("scalar", lambda e, gb=gb: e.activation(gsil, gb, AF.Silu), inc=s_act)
                b.wait("vector", s_te, to)
                b.wait("vector", s_act, ag)
                b.wait("vector", s_gx[ti % 2], lxv)
                b.wait("vector", s_st, ost)
                b.run("vector", lambda e, i=i: e.tensor_tensor(gsil, pb[i][:, 0:512], gsil, ALU.mult))
                vo = b.run("vector", lambda e, i=i, xb_=xb_: e.tensor_tensor(
                    ob[i % 2], gsil, xb_, ALU.add), inc=s_dve)
                out_free[i] = vo
                last_vo = vo
                ld_free = (ag, vo)
                gvo = rb_dve(ob[i % 2][:, 0:1])
                ost = store(out[ti * 128:(ti + 1) * 128, os_ * 512:(os_ + 1) * 512],
                            ob[i % 2], s_dve, gvo)

    if c.debug:
        for nm, t, shp in (("dbg_yd", ydump, [T_, D_]),
                           ("dbg_z", zspill, [2 * D_ + 2 * c.S, TH]),
                           ("dbg_ln", lnspill, [T_, D_]),
                           ("dbg_sc", scspill, [TH, D_]),
                           ("dbg_ffn", ffnspill, [T_, D_]),
                           ("dbg_y", yspill, [T_, D_])):
            o = dout(nm, shp, F32 if nm == "dbg_yd" else BF16)
            b.wait("sync", s_st, s_st.n)
            b.run("sync", lambda e, o=o, t=t: e.dma_start(out=o, in_=t), inc=s_ldx)
        oh = dout("dbg_h", [128, NH * 64], F32)
        b.wait("sync", s_dve, s_dve.n)
        b.run("sync", lambda e, o=oh: e.dma_start(out=o, in_=H[:]), inc=s_ldx)

    for sem_ in (s_st, s_ldx, s_misc, s_xa, s_lnt, s_sz, s_ldw[0], s_ldw[1],
                 s_lb[0], s_lb[1], s_rel2[0], s_rel2[1], s_gx[0], s_gx[1]):
        b.wait("sync", sem_, sem_.n)

    with b._ctx:
        with nc.Block() as block:
            @block.tensor
            def _(eng):
                for f in b.ops["tensor"]:
                    f(eng)

            @block.vector
            def _(eng):
                for f in b.ops["vector"]:
                    f(eng)

            @block.scalar
            def _(eng):
                for f in b.ops["scalar"]:
                    f(eng)

            @block.gpsimd
            def _(eng):
                for f in b.ops["gpsimd"]:
                    f(eng)

            @block.sync
            def _(eng):
                for f in b.ops["sync"]:
                    f(eng)
    return nc


def _mk_rows3(c, weights):
    r3 = np.zeros((65, c.D), np.float32)
    r3[0] = np.asarray(weights["ln_w"])
    r3[32] = np.asarray(weights["norm_w"])
    r3[64] = np.asarray(weights["ffn_b"])
    return r3


def make_inputs(cfg: Cfg, weights: dict, x_slab: np.ndarray, prev_slab: np.ndarray):
    c = cfg
    f32 = np.float32

    def asf(a):
        return np.ascontiguousarray(a, dtype=f32)

    wconv = np.asarray(weights["conv_w"])
    convc = np.zeros((128, c.XOT * c.K), f32)
    for xo in range(c.XOT):
        for k in range(c.K):
            convc[:, xo * c.K + k] = wconv[xo * 128:(xo + 1) * 128, k]
    mupcv = np.zeros((128, c.NOT), f32)
    mv = np.asarray(weights["mup_vector"])
    for o in range(c.NOT):
        w = min(128, c.PROJ - o * 128)
        mupcv[0:w, o] = mv[o * 128:o * 128 + w]
    A2d = np.asarray(weights["ffn_A"]).reshape(c.ER, c.D)
    Bt2d = np.asarray(weights["ffn_B"]).transpose(0, 2, 1).reshape(c.ER, c.D)
    wip = np.asarray(weights["in_proj_w"], f32)
    NBF = (c.NOT + 1) // 2 - (1 if c.NOT % 2 == 1 else 0)
    wipf = wip[:, :NBF * 256].reshape(c.NKQ, c.KQ, 128, NBF, 256)
    wipf = np.ascontiguousarray(wipf.transpose(0, 3, 2, 1, 4).reshape(
        c.NKQ, NBF, 128, c.KQ * 256))
    wop = np.asarray(weights["out_proj_w"], f32)
    wopt = np.ascontiguousarray(
        wop.reshape(c.NKQ, c.KQ, 128, c.D // 512, 512).transpose(0, 3, 2, 1, 4)
        .reshape(c.NKQ, c.D // 512, 128, c.KQ * 512))
    wff = np.ascontiguousarray(np.asarray(weights["ffn_W"], f32).T)
    wfft = np.ascontiguousarray(
        wff.reshape(c.NKQ, c.KQ, 128, c.D // 512, 512).transpose(0, 3, 2, 1, 4)
        .reshape(c.NKQ, c.D // 512, 128, c.KQ * 512))
    latt = np.ascontiguousarray(
        A2d.T.reshape(c.KT, 128, c.ER).transpose(1, 0, 2).reshape(128, c.KT * c.ER)).astype(f32)
    return {
        "x_in": asf(x_slab),
        "prev_in": asf(prev_slab),
        "w_ip_t": wipf,
        "w_ip_rag": asf(wip[:, NBF * 256:]),
        "w_convc": convc,
        "w_out_t": wopt,
        "w_ffn_t": wfft,
        "w_lat_t": latt,
        "w_lBt": asf(Bt2d),
        "probs_col": asf(np.repeat(np.asarray(weights["probs"]), c.R).reshape(c.ER // 128, 128).T),
        "mup_cols": mupcv,
        "rows3_in": (lambda r3: r3)(_mk_rows3(c, weights)),
        "alog_row": asf(np.asarray(weights["A_log"])[None, :]),
        "dtb_col": asf(np.asarray(weights["dt_bias"])[:, None]),
        "d_cols": asf(np.tile(np.asarray(weights["D"])[None, :], (128, 1))),
        "ident_bf": np.eye(128).astype(bf16),
        "ident_f": np.eye(64).astype(f32),
        "ut_bf": np.triu(np.ones((128, 128))).astype(bf16),
        "tri_bf": np.triu(np.ones((128, 128))).astype(bf16),
        "ones_bf": np.ones((1, 128), bf16),
        "ones65_bf": np.ones((65, 128), bf16),
        "ones_f": np.ones((1, 128), f32),
        "fconsts": np.tile(np.array([[1e-5, 1.0 / c.D, -0.5, -1.0, 1.0, 0.0, 3.0e37, 0.5]], f32), (128, 1)),
    }


from concourse.bass_utils import run_bass_kernel_spmd

_CFG = Cfg()
_NC_CACHE = {}


def _get_program():
    if "nc" not in _NC_CACHE:
        _NC_CACHE["nc"] = build_program(_CFG)
    return _NC_CACHE["nc"]


def kernel(**inputs):
    c = _CFG
    x = np.ascontiguousarray(np.asarray(inputs["L0_attn_hidden_states"], dtype=np.float32))
    B, L, D = x.shape
    assert (B, L, D) == (4, 2048, 4096)
    weights = {
        "in_proj_w": inputs["L0_attn_in_proj_w"],
        "conv_w": inputs["L0_attn_conv_w"],
        "out_proj_w": inputs["L0_attn_out_proj_w"],
        "ln_w": inputs["L0_attn_ln_w"],
        "norm_w": inputs["L0_attn_norm_w"],
        "mup_vector": inputs["L0_attn_mup_vector"],
        "A_log": inputs["L0_attn_A_log"],
        "dt_bias": inputs["L0_attn_dt_bias"],
        "D": inputs["L0_attn_D"],
        "ffn_W": inputs["L0_ffn_W"],
        "ffn_b": inputs["L0_ffn_b"],
        "ffn_A": inputs["L0_ffn_A"],
        "ffn_B": inputs["L0_ffn_B"],
        "probs": inputs["L0_ffn_probs"],
    }
    zeros_prev = np.zeros((c.HALO, c.D), np.float32)
    base = make_inputs(c, weights, x[0, :c.T], zeros_prev)
    nc = _get_program()
    in_maps = []
    for core in range(8):
        bt, hf = core // 2, core % 2
        m = dict(base)
        m["x_in"] = np.ascontiguousarray(x[bt, hf * c.T:(hf + 1) * c.T])
        m["prev_in"] = zeros_prev if hf == 0 else \
            np.ascontiguousarray(x[bt, c.T - c.HALO:c.T])
        in_maps.append(m)
    res = run_bass_kernel_spmd(nc, in_maps, core_ids=list(range(8)))
    out = np.empty((4, 2048, 4096), np.float32)
    for core in range(8):
        bt, hf = core // 2, core % 2
        out[bt, hf * c.T:(hf + 1) * c.T] = np.asarray(res.results[core]["out"])
    return out


# version 08
# speedup vs baseline: 1.1188x; 1.0864x over previous
import contextlib
import numpy as np
import ml_dtypes
import concourse.bass as bass
import concourse.mybir as mybir

F32 = mybir.dt.float32
BF16 = mybir.dt.bfloat16
AF = mybir.ActivationFunctionType
ALU = mybir.AluOpType
AX = mybir.AxisListType
bf16 = ml_dtypes.bfloat16

ENGINES = ("tensor", "vector", "scalar", "gpsimd", "sync")


class Cfg:
    def __init__(self, D=4096, S=128, NH=64, HD=64, K=4, T=1024, HALO=256,
                 E=16, R=16, debug=False, phases=6):
        self.D, self.S, self.NH, self.HD, self.K = D, S, NH, HD, K
        self.T, self.HALO, self.E, self.R = T, HALO, E, R
        self.debug = debug
        self.phases = phases
        self.CONV = D + 2 * S
        self.PROJ = 2 * D + 2 * S + NH
        self.KT = D // 128
        self.ZOT = D // 128
        self.XOT = self.CONV // 128
        self.NC = T // 128
        self.HC = HALO // 128
        self.TH = T + HALO
        self.NG = NH // 8
        self.ER = E * R
        self.NOT = (self.PROJ + 127) // 128
        self.DTW = self.PROJ - 128 * (self.NOT - 1)
        self.KQ = max(1, self.KT // 4)
        self.NKQ = (self.KT + self.KQ - 1) // self.KQ
        self.ERT = self.ER // 128
        assert D % 128 == 0 and self.CONV % 128 == 0 and S == 128 and HD == 64
        assert T % 128 == 0 and HALO % 128 == 0 and NH % 8 == 0 and self.ER % 128 == 0


_LAST_MARKS = []
_LAST_B = None


class Sem:
    def __init__(self, h, step=1):
        self.h, self.step, self.n = h, step, 0

    def inc(self):
        self.n += self.step
        return self.n


class Build:
    def __init__(self, cfg):
        self.cfg = cfg
        self.nc = bass.Bass("TRN2", target_bir_lowering=False,
                            detect_race_conditions=False)
        self.ops = {e: [] for e in ENGINES}
        self._ctx = contextlib.ExitStack()

    def sem(self, name, step=1):
        sm = Sem(self._ctx.enter_context(self.nc.semaphore(name)), step)
        if not hasattr(self, "_allsems"):
            self._allsems = {}
        self._allsems[name] = sm
        return sm

    def sbuf(self, name, shape, dtype):
        return self._ctx.enter_context(self.nc.sbuf_tensor(name, shape, dtype))

    def psum(self, name, shape, dtype=F32):
        return self._ctx.enter_context(self.nc.psum_tensor(name, shape, dtype))

    def wait(self, e, sem, val):
        if val > 0:
            self.ops[e].append(lambda eng, s=sem.h, v=val: eng.wait_ge(s, v))

    def mark(self, name):
        if not hasattr(self, "marks"):
            self.marks = []
        self.marks.append((name, len(self.ops["tensor"])))

    def run(self, e, f, inc=None):
        if not hasattr(self, "nrun"):
            self.nrun = {}
        self.nrun[e] = self.nrun.get(e, 0) + 1
        incs = inc if isinstance(inc, (list, tuple)) else ([inc] if inc else [])
        tgts = [s.inc() for s in incs]

        def g(eng, f=f, incs=incs):
            ins = f(eng)
            for s in incs:
                ins = ins.then_inc(s.h, s.step)
        self.ops[e].append(g)
        return tgts[0] if len(tgts) == 1 else (tgts if tgts else None)


def _finalize(b, nc, c, dram):
    for name, sem_ in list(b._allsems.items()):
        b.wait("sync", sem_, sem_.n)
    with b._ctx:
        with nc.Block() as block:
            @block.tensor
            def _(eng):
                class _CE:
                    n = 0

                    def __getattr__(self, a):
                        v = getattr(eng, a)
                        if a in ("matmul", "transpose"):
                            def w(*ar, **kw):
                                _CE.n += 1
                                return v(*ar, **kw)
                            return w
                        return v
                ce = _CE()
                runmarks = {r: nm for nm, r in getattr(b, "marks", [])}
                mmarks = []
                nwait = 0
                for idx, f in enumerate(b.ops["tensor"]):
                    if idx in runmarks:
                        mmarks.append((runmarks[idx], _CE.n))
                    f(ce)
                b.mm_marks = mmarks

            @block.vector
            def _(eng):
                for f in b.ops["vector"]:
                    f(eng)

            @block.scalar
            def _(eng):
                for f in b.ops["scalar"]:
                    f(eng)

            @block.gpsimd
            def _(eng):
                for f in b.ops["gpsimd"]:
                    f(eng)

            @block.sync
            def _(eng):
                for f in b.ops["sync"]:
                    f(eng)
    return nc


def build_program(cfg: Cfg):
    b = Build(cfg)
    nc, c = b.nc, cfg
    D_, T_, H_, TH, KT, NH = c.D, c.T, c.HALO, c.TH, c.KT, c.NH
    NT, HT = T_ // 128, H_ // 128
    tsp = [(i * 512, min(512, T_ - i * 512)) for i in range((T_ + 511) // 512)]
    KQ, NKQ, ERT = c.KQ, c.NKQ, c.ERT
    dram = {}

    def din(name, shape, dtype):
        dram[name] = nc.dram_tensor(name, shape, dtype, kind="ExternalInput").ap()
        return dram[name]

    def dout(name, shape, dtype):
        dram[name] = nc.dram_tensor(name, shape, dtype, kind="ExternalOutput").ap()
        return dram[name]

    def dint(name, shape, dtype):
        dram[name] = nc.dram_tensor(name, shape, dtype, kind="Internal").ap()
        return dram[name]

    x_in = din("x_in", [T_, D_], F32)
    prev_in = din("prev_in", [H_, D_], F32)
    NBF = (c.NOT + 1) // 2 - (1 if c.NOT % 2 == 1 else 0)
    w_ip_t = din("w_ip_t", [c.NKQ, NBF, 128, c.KQ * 256], F32)
    w_ip_rag = din("w_ip_rag", [D_, c.PROJ - NBF * 256], F32)
    w_convc = din("w_convc", [128, c.XOT * c.K], F32)
    w_out_t = din("w_out_t", [c.NKQ, D_ // 512, 128, c.KQ * 512], F32)
    w_ffn_t = din("w_ffn_t", [c.NKQ, D_ // 512, 128, c.KQ * 512], F32)
    w_lat_t = din("w_lat_t", [128, KT * c.ER], F32)
    w_lBt = din("w_lBt", [c.ER, D_], F32)
    probs_col = din("probs_col", [128, c.ER // 128], F32)
    mup_cols = din("mup_cols", [128, c.NOT], F32)
    rows3_in = din("rows3_in", [65, D_], F32)
    alog_row = din("alog_row", [1, NH], F32)
    dtb_col = din("dtb_col", [NH, 1], F32)
    d_cols = din("d_cols", [128, NH], F32)
    ident_bf = din("ident_bf", [128, 128], BF16)
    ident_f = din("ident_f", [64, 64], F32)
    ut_bf = din("ut_bf", [128, 128], BF16)
    tri_bf = din("tri_bf", [128, 128], BF16)
    ones_bf = din("ones_bf", [1, 128], BF16)
    ones65_bf = din("ones65_bf", [65, 128], BF16)
    ones_f = din("ones_f", [1, 128], F32)
    fconsts = din("fconsts", [128, 8], F32)
    out = dout("out", [T_, D_], F32)

    ydump = dint("ydump", [T_, D_], F32)
    lnspill = dint("lnspill", [T_, D_], BF16)
    scspill = dint("scspill", [TH, D_], BF16)
    zspill = dint("zspill", [2 * D_ + 2 * c.S, TH], BF16)
    ffnspill = dint("ffnspill", [T_, D_], BF16)
    yspill = dint("yspill", [T_, D_], BF16)

    sb = b.sbuf
    poolA = sb("poolA", [128, KT * T_], BF16)
    PB = max(KT * H_ + 2 * (3 + TH),
             KT * ERT * 128 + 2 * ERT * 512 + ERT * T_,
             NH * 128 + 2 * D_ + 512,
             2 * 512)
    poolB = sb("poolB", [128, PB], BF16)
    poolF = sb("poolF", [128, max(D_, 3072, 2560 + TH, 2 * TH)], F32)
    wq = [sb(f"wq{i}", [128, KQ * 512], BF16) for i in range(2)]
    xs_tm = [sb(f"xs_tm{i}", [128, D_], BF16) for i in range(2)]
    sz_tm = sb("sz_tm", [128, D_], BF16)
    bcc = [sb(f"bcc{i}", [128, 2 * 128], BF16) for i in range(2)]

    ln_bf = sz_tm
    lnT = poolA
    lnTh = poolB
    xbc2 = [poolB[:, KT * H_ + i * (3 + TH):KT * H_ + (i + 1) * (3 + TH)]
            for i in range(2)]
    lAT = poolB[:, 0:KT * ERT * 128]
    lB0 = KT * ERT * 128
    lBtq = [[poolB[:, lB0 + (p * ERT + et) * 512: lB0 + (p * ERT + et) * 512 + 512]
             for et in range(ERT)] for p in range(2)]
    hprime = poolB[:, lB0 + 2 * ERT * 512:lB0 + 2 * ERT * 512 + ERT * T_]
    diagD = poolB[:, 0:NH * 128]
    nw_bc = poolB[:, NH * 128:NH * 128 + D_]
    yhat_bf = poolB[:, NH * 128 + D_:NH * 128 + 2 * D_]
    xw8 = poolB[:, NH * 128 + 2 * D_:NH * 128 + 2 * D_ + 512]
    gbufs = [poolB[:, i * 512:(i + 1) * 512] for i in range(2)]
    xa = poolF[:, 0:D_]
    dtscr_f = poolF[0:NH, 0:TH]
    dt_t2 = poolF[0:64, TH:2 * TH]
    yfull = poolF[:, 0:D_]
    xrb = [poolF[:, i * 512:(i + 1) * 512] for i in range(2)]
    ob = [poolF[:, 1024 + i * 512:1024 + (i + 1) * 512] for i in range(2)]
    gsil = poolF[:, 2048:2560]
    acc_cv = poolF[:, poolF.shape[1] - TH:]
    sqjunkA = xs_tm[0]
    sc_bf = xs_tm[1]

    ev_bf = sb("ev_bf", [128, max(TH, 1024)], BF16)
    convw = sb("convw", [128, c.XOT * c.K], F32)
    mupc = sb("mupc", [128, c.NOT], F32)
    dt_sp = sb("dt_sp", [NH, TH], F32)
    H = sb("H", [128, NH * 64], F32)
    H_bf = sb("H_bf", [128, NH * 64], BF16)
    dtT = sb("dtT", [128, NH], F32)
    logdtT = sb("logdtT", [128, NH], F32)
    dtAT = sb("dtAT", [128, NH], BF16)
    colst = sb("colst", [128, NH], F32)
    expcT = sb("expcT", [128, NH], F32)
    a_bc = sb("a_bc", [128, NH], F32)
    a_col = sb("a_col", [NH, 1], F32)
    atot_bc = sb("atot_bc", [128, NH], F32)
    wtilT = sb("wtilT", [128, NH], F32)
    rowbuf = sb("rowbuf", [1, 128], F32)
    dtsum = sb("dtsum", [NH, 2], F32)
    gsb = sb("gsb", [128, 128], F32)
    btile = sb("btile", [128, 128], BF16)
    e8 = sb("e8", [128, 8 * 128], BF16)
    mstat = [sb(f"mstat{i}", [128, 128], BF16) for i in range(2)]
    t1buf = sb("t1buf", [128, 512], F32)
    dcol_all = sb("dcol_all", [128, NH], F32)
    ident_s = sb("ident_s", [128, 128], BF16)
    identf_s = sb("identf_s", [64, 64], F32)
    ut_s = sb("ut_s", [128, 128], BF16)
    tri_s = sb("tri_s", [128, 128], BF16)
    ones_s = sb("ones_s", [1, 128], BF16)
    ones65 = sb("ones65", [65, 128], BF16)
    onesf_s = sb("onesf_s", [1, 128], F32)
    dtb_s = sb("dtb_s", [NH, 1], F32)
    arow_s = sb("arow_s", [1, NH], F32)
    arow_bf = sb("arow_bf", [1, NH], BF16)
    rows3 = sb("rows3", [65, D_], BF16)
    probsc = sb("probsc", [128, max(ERT, 1)], F32)
    lncol = sb("lncol", [128, 8], F32)
    lncol2 = sb("lncol2", [128, 12 * (T_ // 128 + c.HALO // 128)], F32)
    lnwT = sb("lnwT", [128, KT], F32)
    fconst = sb("fconst", [128, 8], F32)
    scr = sb("scr", [128, 8], F32)

    pb = [b.psum(f"pb{i}", [128, 512], F32) for i in range(7)]
    pbf = b.psum("pbf", [128, 1024], BF16)

    s_te = b.sem("s_te")
    s_dve = b.sem("s_dve")
    s_act = b.sem("s_act")
    s_ldx = b.sem("s_ldx", 16)
    s_ldw = [b.sem(f"s_ldw{i}", 16) for i in range(2)]
    s_st = b.sem("s_st", 16)
    s_misc = b.sem("s_misc", 16)
    s_gp = b.sem("s_gp_eng")
    s_lb = [b.sem(f"s_lb{i}", 16) for i in range(2)]
    s_xa = b.sem("s_xa", 16)
    s_xb = b.sem("s_xb", 16)
    s_lnt = b.sem("s_lnt", 16)
    s_rel2 = [b.sem(f"s_rel2{i}", 16) for i in range(2)]
    s_sz = b.sem("s_sz", 16)
    s_gx = [b.sem(f"s_gx{i}", 16) for i in range(2)]

    def ld_sync(dst, src, transpose=False, sem=None):
        return b.run("sync", lambda e, d=dst, s=src, t=transpose:
                     e.dma_start(out=d, in_=s, transpose=t), inc=(sem or s_ldx))

    def ld_w(dst, src, wi):
        return b.run("gpsimd", lambda e, d=dst, s=src: e.dma_start(out=d, in_=s),
                     inc=s_ldw[wi])

    def store(dst, src, gate_sem, gate_val):
        b.wait("scalar", gate_sem, gate_val)
        return b.run("scalar", lambda e, d=dst, s=src: e.dma_start(out=d, in_=s), inc=s_st)

    def rb_act(tile):
        p = tile.shape[0]
        b.wait("scalar", s_act, s_act.n)
        return b.run("scalar", lambda e, t=tile, p=p: e.activation(scr[0:p, 6:7], t, AF.Copy), inc=s_act)

    def rb_dve(tile):
        p = tile.shape[0]
        b.wait("vector", s_dve, s_dve.n)
        return b.run("vector", lambda e, t=tile, p=p: e.tensor_copy(scr[0:p, 7:8], t), inc=s_dve)

    def barrier():
        te0, dv0, ac0 = s_te.n, s_dve.n, s_act.n
        b.wait("tensor", s_dve, dv0)
        b.wait("tensor", s_act, ac0)
        b.wait("vector", s_te, te0)
        b.wait("vector", s_act, ac0)
        b.wait("scalar", s_te, te0)
        b.wait("scalar", s_dve, dv0)

    ld_sync(fconst[:], fconsts)
    for _i, _v in enumerate([1e-5, 1.0 / D_, -0.5, -1.0, 1.0, 0.0, 3.0e37, 0.5]):
        nc.const_aps.register(F32, float(_v), fconst[:, _i:_i + 1])
    ld_sync(ident_s[:], ident_bf)
    ld_sync(identf_s[:], ident_f)
    ld_sync(ut_s[:], ut_bf)
    ld_sync(tri_s[:], tri_bf)
    ld_sync(ones_s[:], ones_bf)
    ld_sync(ones65[:], ones65_bf)
    ld_sync(onesf_s[:], ones_f)
    ld_sync(convw[:], w_convc)
    ld_sync(mupc[:], mup_cols)
    ld_sync(dtb_s[:], dtb_col)
    ld_sync(arow_s[:], alog_row)
    ld_sync(dcol_all[:], d_cols)
    cst = ld_sync(probsc[:], probs_col)
    gw0 = b.run("gpsimd", lambda e: e.dma_start(out=rows3[:], in_=rows3_in), inc=s_misc)

    b.wait("vector", s_ldx, cst)
    b.run("vector", lambda e: e.memset(H[:, 0:NH * 64], 0.0))
    b.run("vector", lambda e: e.memset(H_bf[:, 0:NH * 64], 0.0), inc=s_dve)
    b.run("vector", lambda e: e.memset(lncol2[:], 0.0))
    b.run("vector", lambda e: e.memset(xbc2[0][:, 0:3], 0.0))
    zmv = b.run("vector", lambda e: e.memset(xbc2[1][:, 0:3], 0.0), inc=s_dve)

    b.wait("tensor", s_ldx, cst)
    b.wait("tensor", s_misc, gw0)
    for kk in range(KT):
        tw_ = b.run("tensor", lambda e, kk=kk: e.matmul(
            pb[6][:, kk:kk + 1], rows3[0:1, kk * 128:(kk + 1) * 128],
            ones_s[0:1, 0:1], start=True, stop=True),
            inc=(s_te if kk == KT - 1 else None))
    b.wait("vector", s_te, tw_)
    lwv = b.run("vector", lambda e: e.tensor_copy(lnwT[:], pb[6][:, 0:KT]), inc=s_dve)
    b.wait("scalar", s_ldx, cst)
    b.run("scalar", lambda e: e.activation(arow_s[:], arow_s[:], AF.Exp))
    b.run("scalar", lambda e: e.mul(arow_s[:], arow_s[:], -1.0))
    aa = b.run("scalar", lambda e: e.copy(arow_bf[:], arow_s[:]), inc=s_act)
    b.wait("tensor", s_act, aa)
    tt = b.run("tensor", lambda e: e.matmul(pb[2][:, 0:NH], ones_s[0:1, :],
                                            arow_bf[0:1, :], start=True, stop=True), inc=s_te)
    t2 = b.run("tensor", lambda e: e.transpose(pb[3][0:NH, 0:1], arow_s[0:1, 0:NH],
                                               identf_s[0:1, 0:1]), inc=s_te)
    b.wait("vector", s_te, t2)
    b.run("vector", lambda e: e.tensor_copy(a_bc[:], pb[2][:, 0:NH]))
    acopy = b.run("vector", lambda e: e.tensor_copy(a_col[:], pb[3][0:NH, 0:1]), inc=s_dve)

    pre_lw = [ld_w(wq[q][:, 0:KQ * 256], w_ip_t[q, 0], q) for q in range(2)]

    b.mark("A1")
    inv_d = 1.0 / D_
    HD2 = D_ // 2
    CH2 = KT // 2
    gp_a1_base = s_gp.n
    ap_prev = [0, 0]
    tr_prev = [0, 0]
    nst = [0, 0]
    ev_bank = [0, 0, 0, 0]
    rot = 0
    b.wait("tensor", s_dve, acopy)
    for i in range(NT + HT):
        halo = i >= NT
        q = 12 * i
        cl = [lncol2[:, q + k:q + k + 1] for k in range(12)]
        src = prev_in[(i - NT) * 128:(i - NT + 1) * 128, :] if halo else \
            x_in[i * 128:(i + 1) * 128, :]
        b.wait("sync", s_act, ap_prev[0])
        ldA = ld_sync(xa[:, 0:HD2], src[:, 0:HD2], sem=s_xa)
        b.wait("sync", s_act, ap_prev[1])
        ldB = ld_sync(xa[:, HD2:D_], src[:, HD2:D_], sem=s_xb)
        b.wait("vector", s_xa, ldA)
        b.run("vector", lambda e, d=cl[0]: e.tensor_reduce(d, xa[:, 0:HD2], AX.X, ALU.add))
        b.wait("vector", s_xb, ldB)
        vsum = b.run("vector", lambda e, d=cl[1]: e.tensor_reduce(d, xa[:, HD2:D_], AX.X, ALU.add), inc=s_dve)
        if i == 0:
            b.wait("scalar", s_dve, zmv)
        b.wait("scalar", s_xa, ldA)
        b.run("scalar", lambda e, d=cl[2]: e.activation(
            sqjunkA[:, 0:HD2], xa[:, 0:HD2], AF.Square, accum_out=d))
        b.wait("scalar", s_xb, ldB)
        aq = b.run("scalar", lambda e, d=cl[3]: e.activation(
            sqjunkA[:, HD2:D_], xa[:, HD2:D_], AF.Square, accum_out=d), inc=s_act)
        b.wait("vector", s_act, aq)
        b.wait("vector", s_dve, vsum)
        b.run("vector", lambda e, d=cl[4], a=cl[0], bb=cl[1]: e.tensor_tensor(d, a, bb, ALU.add))
        v = b.run("vector", lambda e, d=cl[5], a=cl[2], bb=cl[3]: e.tensor_tensor(d, a, bb, ALU.add), inc=s_dve)
        b.wait("vector", s_dve, v)
        b.run("vector", lambda e, d=cl[4]: e.tensor_scalar(d, d, inv_d, None, ALU.mult))
        v = b.run("vector", lambda e, d=cl[5]: e.tensor_scalar(d, d, inv_d, None, ALU.mult), inc=s_dve)
        b.wait("vector", s_dve, v)
        v = b.run("vector", lambda e, d=cl[11], a=cl[4]: e.tensor_tensor(d, a, a, ALU.mult), inc=s_dve)
        b.wait("vector", s_dve, v)
        vvar = b.run("vector", lambda e, d=cl[5], a=cl[11]: e.tensor_tensor(d, d, a, ALU.subtract), inc=s_dve)
        b.wait("scalar", s_dve, vvar)
        b.run("scalar", lambda e, d=cl[6], a=cl[5]: e.activation(d, a, AF.Ln, bias=1e-5))
        arstd = b.run("scalar", lambda e, d=cl[6]: e.activation(d, d, AF.Exp, scale=-0.5), inc=s_act)
        b.wait("vector", s_act, arstd)
        v = b.run("vector", lambda e, d=cl[11], a=cl[5], bb=cl[6]: e.tensor_tensor(d, a, bb, ALU.mult), inc=s_dve)
        b.wait("vector", s_dve, v)
        vr2 = b.run("vector", lambda e, d=cl[11], bb=cl[6]: e.tensor_tensor(d, d, bb, ALU.mult), inc=s_dve)
        b.wait("scalar", s_dve, vr2)
        b.run("scalar", lambda e, d=cl[7], a=cl[11]: e.activation(d, a, AF.Ln, bias=1e-5))
        ar = b.run("scalar", lambda e, d=cl[7]: e.activation(d, d, AF.Exp, scale=-0.5), inc=s_act)
        b.wait("vector", s_act, ar)
        b.run("vector", lambda e, d=cl[8], a=cl[4], bb=cl[6]: e.tensor_tensor(d, a, bb, ALU.mult))
        v = b.run("vector", lambda e, d=cl[9], a=cl[6], bb=cl[7]: e.tensor_tensor(d, a, bb, ALU.mult), inc=s_dve)
        b.wait("vector", s_dve, v)
        v = b.run("vector", lambda e, d=cl[8]: e.tensor_scalar(d, d, -1.0, None, ALU.mult), inc=s_dve)
        b.wait("vector", s_dve, v)
        b.run("vector", lambda e, d=cl[11], a=cl[6]: e.tensor_copy(d, a))
        vbias = b.run("vector", lambda e, d=cl[10], a=cl[8], bb=cl[7]: e.tensor_tensor(d, a, bb, ALU.mult), inc=s_dve)
        b.wait("scalar", s_dve, vbias)
        ap_new = [0, 0]
        for hf in range(2):
            c0, c1 = (0, HD2) if hf == 0 else (HD2, D_)
            b.wait("scalar", s_te, tr_prev[hf])
            if not halo:
                b.wait("scalar", s_st, nst[hf])
                b.run("scalar", lambda e, c0=c0, c1=c1, s=cl[11], bi=cl[8]: e.activation(
                    ln_bf[:, c0:c1], xa[:, c0:c1], AF.Identity, bias=bi, scale=s))
                ga_ = rb_act(ln_bf[:, c0:c0 + 1])
                nst[hf] = store(lnspill[i * 128:(i + 1) * 128, c0:c1],
                                ln_bf[:, c0:c1], s_act, ga_)
            ap_new[hf] = b.run("scalar", lambda e, c0=c0, c1=c1, s=cl[9], bi=cl[10]: e.activation(
                sc_bf[:, c0:c1], xa[:, c0:c1], AF.Identity, bias=bi, scale=s), inc=s_act)
            b.wait("tensor", s_act, ap_new[hf])
            for cb in range(hf * CH2, (hf + 1) * CH2):
                bank = rot % 4
                rot += 1
                b.wait("tensor", s_gp, ev_bank[bank])
                tr = b.run("tensor", lambda e, cb=cb, bank=bank: e.matmul(
                    pb[bank][:, 0:128], sc_bf[:, cb * 128:(cb + 1) * 128],
                    ident_s[:], start=True, stop=True), inc=s_te)
                b.wait("vector", s_te, tr)
                if halo:
                    dst = lnTh[:, cb * H_ + (i - NT) * 128:cb * H_ + (i - NT) * 128 + 128]
                else:
                    dst = lnT[:, cb * T_ + i * 128:cb * T_ + i * 128 + 128]
                ev_bank[bank] = b.run("vector", lambda e, dst=dst, bank=bank, cb=cb: e.tensor_scalar(
                    dst, pb[bank][:, 0:128], lnwT[:, cb:cb + 1], None, ALU.mult), inc=s_gp)
            tr_prev[hf] = s_te.n
        ap_prev = ap_new
    gp_a1_end = s_gp.n

    b.mark("A2")
    if c.phases < 2:
        return _finalize(b, nc, c, dram)
    n_blk = (c.NOT + 1) // 2
    pfree = {0: (None, 0), 1: (None, 0)}
    wfree = {0: 0, 1: 0}
    conv_free = {0: 0, 1: 0}
    evst = 0
    sil_prev = 0
    qi = 0
    b.wait("tensor", s_gp, gp_a1_end)
    for blk in range(n_blk):
        o0 = blk * 2
        ntile = min(2, c.NOT - o0)
        cols0 = o0 * 128
        cols1 = min((o0 + 2) * 128, c.PROJ)
        bw = cols1 - cols0
        psets = [[pb[0], pb[1], pb[2]], [pb[3], pb[4], pb[5]]]
        for oj in range(ntile):
            sfree, vfree = pfree[oj]
            if sfree is not None:
                b.wait("tensor", sfree, vfree)
        tdone = [0] * ntile
        for kq in range(NKQ):
            wi = qi % 2
            if qi < 2:
                lw = pre_lw[qi]
            else:
                b.wait("gpsimd", s_te, wfree[wi])
                if bw == 256:
                    lw = ld_w(wq[wi][:, 0:KQ * 256], w_ip_t[kq, blk], wi)
                else:
                    lw = ld_w(wq[wi][:, 0:KQ * bw].rearrange("p (k o) -> p k o", k=KQ),
                              w_ip_rag[kq * KQ * 128:min((kq + 1) * KQ * 128, D_), :]
                              .rearrange("(k p) o -> p k o", p=128), wi)
            qi += 1
            b.wait("tensor", s_ldw[wi], lw)
            for oj in range(ntile):
                o = o0 + oj
                is_dt = (o == c.NOT - 1)
                owid = c.DTW if is_dt else 128
                is_z = o < c.ZOT
                ps = psets[oj]

                def mms(eng, wi=wi, kq=kq, oj=oj, owid=owid, is_z=is_z, ps=ps, bw=bw):
                    ins = None
                    for kk in range(KQ):
                        k = kq * KQ + kk
                        if k >= KT:
                            break
                        st = wq[wi][:, kk * bw + oj * 128: kk * bw + oj * 128 + owid]
                        first = (k == 0)
                        last = (k == KT - 1)
                        for hi, (t0, tw) in enumerate(tsp):
                            ins = eng.matmul(ps[hi][0:owid, 0:tw], st,
                                             lnT[:, k * T_ + t0:k * T_ + t0 + tw],
                                             start=first, stop=last)
                        if not is_z:
                            ins = eng.matmul(ps[2][0:owid, 0:H_], st,
                                             lnTh[:, k * H_:(k + 1) * H_],
                                             start=first, stop=last)
                    return ins
                tdone[oj] = b.run("tensor", mms, inc=s_te)
            wfree[wi] = s_te.n
        conv_jobs = []
        for oj in range(ntile):
            o = o0 + oj
            is_dt = (o == c.NOT - 1)
            owid = c.DTW if is_dt else 128
            is_z = o < c.ZOT
            ps = psets[oj]
            mupcol = mupc[:, o:o + 1]
            if is_z:
                b.wait("scalar", s_te, tdone[oj])
                b.wait("scalar", s_st, evst)
                for hi, (t0, tw) in enumerate(tsp):
                    b.run("scalar", lambda e, hi=hi, t0=t0, tw=tw, ps=ps, mc=mupcol: e.activation(
                        ev_bf[:, t0:t0 + tw], ps[hi][:, 0:tw], AF.Silu, scale=mc))
                ga = rb_act(ev_bf[:, 0:1])
                pfree[oj] = (s_act, ga)
                evst = store(zspill[o * 128:(o + 1) * 128, H_:TH], ev_bf[:, 0:T_], s_act, ga)
            else:
                b.wait("vector", s_te, tdone[oj])
                b.wait("vector", s_dve, conv_free[oj])
                xb = xbc2[oj]
                for hi, (t0, tw) in enumerate(tsp):
                    b.run("vector", lambda e, hi=hi, t0=t0, tw=tw, ps=ps, mc=mupcol, w=owid, xb=xb: e.tensor_scalar(
                        xb[0:w, 3 + H_ + t0:3 + H_ + t0 + tw], ps[hi][0:w, 0:tw],
                        mc[0:w, :], None, ALU.mult))
                vv1 = b.run("vector", lambda e, ps=ps, mc=mupcol, w=owid, xb=xb: e.tensor_scalar(
                    xb[0:w, 3:3 + H_], ps[2][0:w, 0:H_], mc[0:w, :], None, ALU.mult), inc=s_dve)
                pfree[oj] = (s_dve, vv1)
                if is_dt:
                    b.run("vector", lambda e, w=owid, xb=xb: e.tensor_copy(
                        dtscr_f[0:w, 0:TH], xb[0:w, 3:3 + TH]))
                    rb_dve(dtscr_f[0:NH, 0:1])
                else:
                    conv_jobs.append((oj, o - c.ZOT, vv1))
        for (oj, xo, vv1) in conv_jobs:
            xb = xbc2[oj]
            b.wait("vector", s_act, sil_prev)
            b.wait("vector", s_dve, vv1)
            b.run("vector", lambda e, xo=xo, xb=xb: e.tensor_scalar(
                acc_cv[:, 0:TH], xb[:, 0:TH],
                convw[:, (xo * c.K):(xo * c.K) + 1], None, ALU.mult))
            for kk in range(1, c.K):
                gcv = b.run("vector", lambda e, xo=xo, kk=kk, xb=xb: e.scalar_tensor_tensor(
                    acc_cv[:, 0:TH], xb[:, kk:kk + TH],
                    convw[:, (xo * c.K + kk):(xo * c.K + kk + 1)],
                    acc_cv[:, 0:TH], ALU.mult, ALU.add),
                    inc=(s_dve if kk == c.K - 1 else None))
            conv_free[oj] = gcv
            b.wait("scalar", s_dve, gcv)
            b.wait("scalar", s_st, evst)
            sil = b.run("scalar", lambda e: e.activation(
                ev_bf[:, 0:TH], acc_cv[:, 0:TH], AF.Silu), inc=s_act)
            sil_prev = sil
            ga = rb_act(ev_bf[:, 0:1])
            evst = store(zspill[D_ + xo * 128:D_ + (xo + 1) * 128, :],
                         ev_bf[:, 0:TH], s_act, ga)
    a2_spills = s_st.n
    barrier()

    b.run("vector", lambda e: e.tensor_scalar(dtscr_f[0:NH, 0:TH], dtscr_f[0:NH, 0:TH],
                                              dtb_s[0:NH, :], None, ALU.add))
    vr = b.run("vector", lambda e: e.tensor_scalar(dt_sp[0:NH, :], dtscr_f[0:NH, 0:TH],
                                                   0.0, None, ALU.max), inc=s_dve)
    b.wait("scalar", s_dve, vr)
    b.run("scalar", lambda e: e.activation(dt_t2[0:NH, 0:TH], dtscr_f[0:NH, 0:TH], AF.Abs))
    b.run("scalar", lambda e: e.activation(dt_t2[0:NH, 0:TH], dt_t2[0:NH, 0:TH], AF.Exp, scale=-1.0))
    asp = b.run("scalar", lambda e: e.activation(dt_t2[0:NH, 0:TH], dt_t2[0:NH, 0:TH], AF.Ln, bias=1.0), inc=s_act)
    b.wait("vector", s_act, asp)
    vsp = b.run("vector", lambda e: e.tensor_tensor(dt_sp[0:NH, :], dt_sp[0:NH, :],
                                                    dt_t2[0:NH, 0:TH], ALU.add), inc=s_dve)

    def emit_reload(ci, slot, main):
        col0 = (H_ + ci * 128) if main else ci * 128
        b.wait("sync", s_st, a2_spills)
        b.wait("sync", s_te, s_te.n)
        b.wait("sync", s_dve, s_dve.n)
        b.wait("sync", s_act, s_act.n)
        ld_sync(xs_tm[slot][:], zspill[D_:2 * D_, col0:col0 + 128], transpose=True,
                sem=s_rel2[slot])
        return (slot, ld_sync(bcc[slot].rearrange("p (a t) -> p a t", a=2),
                              zspill[2 * D_:2 * D_ + 2 * c.S, col0:col0 + 128]
                              .rearrange("(a p) t -> p a t", p=128), sem=s_rel2[slot]))

    def chunk_prep(ci, main, slot, rel_tgt):
        mp = getattr(c, "microphase", 9)
        barrier()
        col0 = (H_ + ci * 128) if main else ci * 128
        if mp < 0.5:
            return dict(vcs=0, aat=0, aex=0, vbt=0, awt=0, vg=0)
        b.wait("tensor", s_dve, vsp)
        ta = b.run("tensor", lambda e, c0=col0: e.transpose(
            pb[2][:, 0:NH], dt_sp[0:NH, c0:c0 + 128], identf_s[0:NH, 0:NH]), inc=s_te)
        if mp < 1:
            tq = b.run("tensor", lambda e: e.matmul(
                pb[3][:, 0:128], ident_s[:], ut_s[:], start=True, stop=True), inc=s_te)
            b.wait("vector", s_te, tq)
            b.run("vector", lambda e: e.tensor_copy(gsb[:], pb[3][:, 0:128]), inc=s_dve)
            return dict(vcs=0, aat=0, aex=0, vbt=0, awt=0, vg=0)
        b.wait("vector", s_te, ta)
        vda = b.run("vector", lambda e: e.tensor_copy(dtT[:], pb[2][:, 0:NH]), inc=s_dve)
        if mp < 2:
            return dict(vcs=0, aat=0, aex=0, vbt=0, awt=0, vg=0)
        b.wait("scalar", s_dve, vda)
        ala = b.run("scalar", lambda e: e.activation(logdtT[:], dtT[:], AF.Ln), inc=s_act)
        if mp < 2.2:
            return dict(vcs=0, aat=0, aex=0, vbt=0, awt=0, vg=0)
        b.wait("vector", s_dve, vda)
        vab = b.run("vector", lambda e: e.tensor_tensor(dtAT[:], dtT[:], a_bc[:], ALU.mult), inc=s_dve)
        if mp < 2.4:
            return dict(vcs=0, aat=0, aex=0, vbt=0, awt=0, vg=0)
        b.wait("tensor", s_dve, vab)
        tb_ = b.run("tensor", lambda e: e.matmul(pb[2][:, 0:NH], ut_s[:], dtAT[:],
                                                 start=True, stop=True), inc=s_te)
        if mp < 2.6:
            return dict(vcs=0, aat=0, aex=0, vbt=0, awt=0, vg=0)
        b.wait("vector", s_te, tb_)
        b.wait("vector", s_act, ala)
        vcs = b.run("vector", lambda e: e.tensor_tensor(colst[:], logdtT[:], pb[2][:, 0:NH], ALU.subtract), inc=s_dve)
        if mp < 2.8:
            return dict(vcs=0, aat=0, aex=0, vbt=0, awt=0, vg=0)
        b.wait("scalar", s_te, tb_)
        b.wait("scalar", s_dve, vcs)
        aex = b.run("scalar", lambda e: e.activation(expcT[:], pb[2][:, 0:NH], AF.Exp), inc=s_act)
        if mp < 3:
            return dict(vcs=vcs, aat=0, aex=aex, vbt=0, awt=0, vg=0)
        vq2 = b.run("vector", lambda e, c0=col0: e.tensor_reduce(
            dtsum[0:NH, 0:1], dt_sp[0:NH, c0:c0 + 128], AX.X, ALU.add), inc=s_dve)
        b.wait("vector", s_dve, vq2)
        vds = b.run("vector", lambda e: e.tensor_tensor(dtsum[0:NH, 1:2], dtsum[0:NH, 0:1],
                                                        a_col[0:NH, :], ALU.mult), inc=s_dve)
        b.wait("tensor", s_dve, vds)
        b.wait("tensor", s_act, aex)
        b.wait("tensor", s_dve, vcs)
        if mp < 4:
            return dict(vcs=vcs, aat=0, aex=aex, vbt=0, awt=0, vg=0)
        tc_ = b.run("tensor", lambda e: e.transpose(pb[2][0:1, 0:NH], dtsum[0:NH, 1:2],
                                                    identf_s[0:NH, 0:NH]), inc=s_te)
        b.wait("vector", s_te, tc_)
        vrow = b.run("vector", lambda e: e.tensor_copy(rowbuf[0:1, 0:NH], pb[2][0:1, 0:NH]), inc=s_dve)
        b.wait("tensor", s_dve, vrow)
        if mp < 5:
            return dict(vcs=vcs, aat=0, aex=aex, vbt=0, awt=0, vg=0)
        td = b.run("tensor", lambda e: e.matmul(pb[2][:, 0:NH], onesf_s[0:1, :],
                                                rowbuf[0:1, 0:NH], start=True, stop=True), inc=s_te)
        b.wait("scalar", s_te, td)
        aat = b.run("scalar", lambda e: e.activation(atot_bc[:], pb[2][:, 0:NH], AF.Exp), inc=s_act)
        pr = dict(vcs=vcs, aat=aat, aex=aex)
        if not main:
            b.wait("vector", s_te, td)
            b.wait("vector", s_dve, vcs)
            b.wait("vector", s_act, aat)
            vwa = b.run("vector", lambda e: e.tensor_tensor(wtilT[:], colst[:], pb[2][:, 0:NH], ALU.add), inc=s_dve)
            b.wait("scalar", s_dve, vwa)
            pr["awt"] = b.run("scalar", lambda e: e.activation(wtilT[:], wtilT[:], AF.Exp), inc=s_act)
        b.wait("tensor", s_rel2[rel_tgt[0]], rel_tgt[1])
        te_ = b.run("tensor", lambda e, sl=slot: e.transpose(
            pbf[:, 0:128], bcc[sl][:, 0:128], ident_s[:]), inc=s_te)
        b.wait("vector", s_te, te_)
        pr["vbt"] = b.run("vector", lambda e: e.tensor_copy(btile[:], pbf[:, 0:128]), inc=s_dve)
        if main:
            b.wait("tensor", s_act, aat)
            tg = b.run("tensor", lambda e, sl=slot: e.matmul(
                pb[2][:, 0:128], bcc[sl][:, 0:128], bcc[sl][:, 128:256],
                start=True, stop=True), inc=s_te)
            b.wait("vector", s_te, tg)
            pr["vg"] = b.run("vector", lambda e: e.tensor_tensor(gsb[:], pb[2][:, 0:128],
                                                                 tri_s[:], ALU.mult), inc=s_dve)
        return pr

    def emit_state_update(g, slot, pr, main):
        if main:
            src = e8.rearrange("p (h t) -> p h t", h=8)[:, :, 127:128].broadcast_to([128, 8, 64])
            b.wait("vector", s_act, pr["e_done"])
        else:
            src = wtilT[:, g * 8:(g + 1) * 8].broadcast_to([128, 8, 64])
            b.wait("vector", s_act, pr["awt"])
        vxw = b.run("vector", lambda e, g=g, sl=slot, s=src: e.tensor_tensor(
            xw8.rearrange("p (h t) -> p h t", h=8),
            xs_tm[sl].rearrange("p (h t) -> p h t", h=NH)[:, g * 8:(g + 1) * 8, :],
            s, ALU.mult), inc=s_dve)
        b.wait("tensor", s_dve, max(vxw, pr["vbt"], pr.get("hfree", 0)))
        th = b.run("tensor", lambda e: e.matmul(pb[6][:, 0:512], btile[:], xw8,
                                                start=True, stop=True), inc=s_te)
        b.wait("vector", s_te, th)
        b.wait("vector", s_act, pr["aat"])
        b.run("vector", lambda e, g=g: e.tensor_tensor(
            H.rearrange("p (h t) -> p h t", h=NH)[:, g * 8:(g + 1) * 8, :],
            H.rearrange("p (h t) -> p h t", h=NH)[:, g * 8:(g + 1) * 8, :],
            atot_bc[:, g * 8:(g + 1) * 8].broadcast_to([128, 8, 64]), ALU.mult))
        b.run("vector", lambda e, g=g: e.tensor_tensor(
            H[:, g * 512:g * 512 + 512], H[:, g * 512:g * 512 + 512], pb[6][:, 0:512], ALU.add))
        vH = b.run("vector", lambda e, g=g: e.tensor_copy(
            H_bf[:, g * 512:g * 512 + 512], H[:, g * 512:g * 512 + 512]), inc=s_dve)
        return vH

    b.mark("A2.5")
    if c.phases < 2.2:
        return _finalize(b, nc, c, dram)
    relq = []
    rel0 = emit_reload(0, 0, main=False)
    if c.phases < 2.4:
        return _finalize(b, nc, c, dram)
    hf = 0
    for hc in range(c.HC):
        slot = hc % 2
        nrel = emit_reload(hc + 1, (hc + 1) % 2, main=False) if hc + 1 < c.HC else None
        pr = chunk_prep(hc, False, slot, rel0 if hc == 0 else relq.pop())
        if nrel is not None:
            relq.append(nrel)
        if c.phases < 2.6:
            return _finalize(b, nc, c, dram)
        ng_lim = c.NG if c.phases >= 3 else max(1, int(round((c.phases - 2.6) * 20)))
        for g in range(min(c.NG, ng_lim)):
            pr["hfree"] = hf
            hf = emit_state_update(g, slot, pr, main=False)
        if c.phases < 3:
            return _finalize(b, nc, c, dram)
    barrier()

    b.mark("A3")
    if c.phases < 4:
        return _finalize(b, nc, c, dram)
    b.wait("sync", s_te, s_te.n)
    b.wait("sync", s_dve, s_dve.n)
    for k in range(KT):
        ld_sync(lnT[:, k * T_:(k + 1) * T_],
                lnspill[:, k * 128:(k + 1) * 128], transpose=True, sem=s_lnt)
    lnT2_ready = s_lnt.n
    b.wait("gpsimd", s_te, s_te.n)
    b.wait("gpsimd", s_dve, s_dve.n)
    b.wait("gpsimd", s_act, s_act.n)
    lg = b.run("gpsimd", lambda e: e.dma_start(out=lAT[:], in_=w_lat_t), inc=s_misc)
    b.wait("tensor", s_lnt, lnT2_ready)
    b.wait("tensor", s_misc, lg)
    h_ev = {0: 0, 1: 0}
    hq = 0
    for et in range(ERT):
        for hi, (t0, tw) in enumerate(tsp):
            bank = hq % 2
            hq += 1
            b.wait("tensor", s_dve, h_ev[bank])

            def hmm(eng, et=et, t0=t0, tw=tw, bank=bank):
                ins = None
                for k in range(KT):
                    ins = eng.matmul(pb[bank][:, 0:tw],
                                     lAT[:, (k * ERT + et) * 128:(k * ERT + et) * 128 + 128],
                                     lnT[:, k * T_ + t0:k * T_ + t0 + tw],
                                     start=(k == 0), stop=(k == KT - 1))
                return ins
            th = b.run("tensor", hmm, inc=s_te)
            b.wait("vector", s_te, th)
            h_ev[bank] = b.run("vector", lambda e, et=et, t0=t0, tw=tw, bank=bank: e.tensor_scalar(
                hprime[:, et * T_ + t0:et * T_ + t0 + tw], pb[bank][:, 0:tw],
                probsc[:, et:et + 1], None, ALU.mult), inc=s_dve)
    vhp = rb_dve(hprime[:, 0:1])
    b.wait("tensor", s_dve, vhp)

    NT4 = min(4, NT)
    ffn_ev = {i: 0 for i in range(NT4)}
    evstl = {0: 0, 1: 0}
    lb_t = {0: 0, 1: 0}
    for os_ in range(D_ // 512):
        lbw = os_ % 2
        b.wait("gpsimd", s_te, lb_t[lbw])
        lbv = None
        for et in range(ERT):
            lbv = b.run("gpsimd", lambda e, et=et, os_=os_, lbw=lbw: e.dma_start(
                out=lBtq[lbw][et],
                in_=w_lBt[et * 128:(et + 1) * 128, os_ * 512:(os_ + 1) * 512]), inc=s_lb[lbw])
        for tp in range(NT // NT4):
            tis = [tp * NT4 + i for i in range(NT4)]
            for i, ti in enumerate(tis):
                b.wait("tensor", s_dve, ffn_ev[i])
                b.run("tensor", lambda e, i=i, os_=os_: e.matmul(
                    pb[i][:, 0:512], ones65[64:65, :], rows3[64:65, os_ * 512:(os_ + 1) * 512],
                    start=True, stop=False))
            for kq in range(NKQ):
                wi = qi % 2
                qi += 1
                b.wait("gpsimd", s_te, wfree[wi])
                lw = ld_w(wq[wi][:], w_ffn_t[kq, os_], wi)
                b.wait("tensor", s_ldw[wi], lw)

                def fmm(eng, wi=wi, kq=kq, tis=tis):
                    ins = None
                    for kk in range(KQ):
                        k = kq * KQ + kk
                        for i, ti in enumerate(tis):
                            ins = eng.matmul(pb[i][:, 0:512],
                                             lnT[:, k * T_ + ti * 128:k * T_ + ti * 128 + 128],
                                             wq[wi][:, kk * 512:(kk + 1) * 512],
                                             start=False, stop=False)
                    return ins
                b.run("tensor", fmm, inc=s_te)
                wfree[wi] = s_te.n
            b.wait("tensor", s_lb[lbw], lbv)

            def lmm(eng, tis=tis, lbw=lbw):
                ins = None
                for et in range(ERT):
                    for i, ti in enumerate(tis):
                        ins = eng.matmul(pb[i][:, 0:512],
                                         hprime[:, et * T_ + ti * 128:et * T_ + ti * 128 + 128],
                                         lBtq[lbw][et],
                                         start=False, stop=(et == ERT - 1))
                return ins
            tl = b.run("tensor", lmm, inc=s_te)
            for i, ti in enumerate(tis):
                p = ti % 2
                b.wait("vector", s_te, tl)
                b.wait("vector", s_st, evstl[p])
                ve = b.run("vector", lambda e, i=i, p=p: e.tensor_copy(
                    ev_bf[:, p * 512:p * 512 + 512], pb[i][:, 0:512]), inc=s_dve)
                ffn_ev[i] = ve
                gv = rb_dve(ev_bf[:, p * 512:p * 512 + 1])
                evstl[p] = store(ffnspill[ti * 128:(ti + 1) * 128, os_ * 512:(os_ + 1) * 512],
                                 ev_bf[:, p * 512:p * 512 + 512], s_dve, gv)
        lb_t[lbw] = s_te.n
    barrier()

    b.mark("B")
    if c.phases < 5:
        return _finalize(b, nc, c, dram)
    for h in range(NH):
        b.run("vector", lambda e, h=h: e.tensor_scalar(
            diagD[:, h * 128:(h + 1) * 128], ident_s[:], dcol_all[:, h:h + 1], None, ALU.mult))
    rb_dve(diagD[:, 0:1])
    nwev = {0: 0, 1: 0}
    for j in range(D_ // 512):
        bk = 2 + (j % 2)
        b.wait("tensor", s_dve, nwev[j % 2])
        tt = b.run("tensor", lambda e, j=j, bk=bk: e.matmul(
            pb[bk][:, 0:512], ones65[32:33, :], rows3[32:33, j * 512:(j + 1) * 512],
            start=True, stop=True), inc=s_te)
        b.wait("vector", s_te, tt)
        nwev[j % 2] = b.run("vector", lambda e, j=j, bk=bk: e.tensor_copy(
            nw_bc[:, j * 512:(j + 1) * 512], pb[bk][:, 0:512]), inc=s_dve)
    vnw = rb_dve(nw_bc[:, 0:1])
    b.wait("tensor", s_dve, vnw)

    vH_last = hf
    vg1 = 0
    rel0 = emit_reload(0, 0, main=True)
    relq = [rel0]
    y_sp = []
    for ci in range(c.NC):
        slot = ci % 2
        my_rel = relq.pop(0)
        if ci + 1 < c.NC:
            relq.append(emit_reload(ci + 1, (ci + 1) % 2, main=True))
        b.wait("sync", s_dve, vg1)
        szv = ld_sync(sz_tm[:], zspill[0:D_, H_ + ci * 128:H_ + ci * 128 + 128],
                      transpose=True, sem=s_sz)
        pr = chunk_prep(ci, True, slot, my_rel)
        yev = {0: 0, 1: 0}
        inter_free = pr["vg"]
        m_done = {0: 0, 1: 0}
        bc_free = {0: 0, 1: 0}
        for g in range(c.NG):
            ypb = pb[g % 2]
            b.wait("tensor", s_dve, max(vH_last, inter_free))
            b.wait("tensor", s_rel2[my_rel[0]], my_rel[1])
            t1m = b.run("tensor", lambda e, sl=slot, g=g: e.matmul(
                pb[2][:, 0:512], bcc[sl][:, 128:256], H_bf[:, g * 512:(g + 1) * 512],
                start=True, stop=True), inc=s_te)
            b.wait("vector", s_te, t1m)
            vt1 = b.run("vector", lambda e, g=g: e.tensor_tensor(
                t1buf.rearrange("p (h t) -> p h t", h=8),
                pb[2].rearrange("p (h t) -> p h t", h=8),
                expcT[:, g * 8:(g + 1) * 8].broadcast_to([128, 8, 64]), ALU.mult), inc=s_dve)
            inter_free = vt1
            b.wait("tensor", s_dve, yev[g % 2])
            e_done = 0
            for hl in range(8):
                h = g * 8 + hl
                bcb = pb[3] if hl % 2 == 0 else pb[5]
                if bc_free[hl % 2]:
                    b.wait("tensor", s_act, bc_free[hl % 2])
                tb_ = b.run("tensor", lambda e, h=h, bcb=bcb: e.matmul(
                    bcb[:, 0:128], dtAT[:, h:h + 1].broadcast_to([128, 128]),
                    ut_s[:], start=True, stop=True), inc=s_te)
                b.wait("scalar", s_te, tb_)
                b.wait("scalar", s_dve, pr["vcs"] if hl > 0 else max(pr["vcs"], vH_last))
                ae = b.run("scalar", lambda e, h=h, hl=hl, bcb=bcb: e.activation(
                    e8[:, hl * 128:(hl + 1) * 128], bcb[:, 0:128], AF.Exp,
                    bias=colst[:, h:h + 1]), inc=s_act)
                bc_free[hl % 2] = ae
                e_done = ae
                b.wait("vector", s_act, ae)
                b.wait("vector", s_dve, pr["vg"])
                b.wait("vector", s_te, m_done[hl % 2])
                vm = b.run("vector", lambda e, hl=hl: e.scalar_tensor_tensor(
                    mstat[hl % 2][:], e8[:, hl * 128:(hl + 1) * 128], 3.0e37, gsb[:],
                    ALU.min, ALU.mult), inc=s_dve)
                b.wait("tensor", s_dve, vm)

                def ymm(eng, h=h, hl=hl, sl=slot, yp=ypb):
                    eng.matmul(yp[:, hl * 64:hl * 64 + 64], mstat[hl % 2][:],
                               xs_tm[sl][:, h * 64:h * 64 + 64],
                               start=(hl == 0), stop=False)
                    return eng.matmul(yp[:, hl * 64:hl * 64 + 64],
                                      diagD[:, h * 128:(h + 1) * 128],
                                      xs_tm[sl][:, h * 64:h * 64 + 64],
                                      start=False, stop=(hl == 7))
                tym = b.run("tensor", ymm, inc=s_te)
                m_done[hl % 2] = tym
            pr["e_done"] = e_done
            pr["hfree"] = vH_last
            vH_last = emit_state_update(g, slot, pr, main=True)
            b.wait("vector", s_te, max(m_done[0], m_done[1]))
            vy = b.run("vector", lambda e, g=g, yp=ypb: e.tensor_tensor(
                yfull[:, g * 512:(g + 1) * 512], t1buf[:], yp[:, 0:512], ALU.add), inc=s_dve)
            yev[g % 2] = vy
        b.wait("vector", s_dve, max(yev[0], yev[1]))
        if c.debug:
            ygate = rb_dve(yfull[:, 0:1])
            ystt = store(ydump[ci * 128:(ci + 1) * 128, :], yfull, s_dve, ygate)
            b.wait("vector", s_st, ystt)
        b.wait("vector", s_sz, szv)
        vg1a = b.run("vector", lambda e: e.tensor_tensor(yfull, yfull, sz_tm[:], ALU.mult), inc=s_dve)
        vg1 = b.run("vector", lambda e: e.memset(lncol[:, 0:1], 0.0), inc=s_dve)
        b.wait("scalar", s_dve, max(vg1, vg1a))
        asq = b.run("scalar", lambda e, sl=slot: e.activation(
            xs_tm[sl][:], yfull, AF.Square, accum_out=lncol[:, 0:1]), inc=s_act)
        b.wait("scalar", s_act, asq)
        b.run("scalar", lambda e: e.activation(lncol[:, 1:2], lncol[:, 0:1], AF.Ln,
                                               scale=inv_d, bias=1e-5))
        ar2 = b.run("scalar", lambda e: e.activation(lncol[:, 1:2], lncol[:, 1:2], AF.Exp,
                                                     scale=-0.5), inc=s_act)
        b.wait("vector", s_act, ar2)
        b.wait("vector", s_st, y_sp[-1] if y_sp else 0)
        b.run("vector", lambda e: e.tensor_scalar(yfull, yfull, lncol[:, 1:2], None, ALU.mult))
        vyh = b.run("vector", lambda e: e.tensor_tensor(yhat_bf, yfull, nw_bc, ALU.mult), inc=s_dve)
        gv = rb_dve(yhat_bf[:, 0:1])
        y_sp.append(store(yspill[ci * 128:(ci + 1) * 128, :], yhat_bf, s_dve, gv))
    b_spills = s_st.n
    barrier()

    b.mark("C")
    if c.phases < 6:
        return _finalize(b, nc, c, dram)
    b.wait("sync", s_st, b_spills)
    b.wait("sync", s_te, s_te.n)
    b.wait("sync", s_dve, s_dve.n)
    for k in range(KT):
        ld_sync(lnT[:, k * T_:(k + 1) * T_],
                yspill[:, k * 128:(k + 1) * 128], transpose=True, sem=s_lnt)
    ycm_ready = s_lnt.n
    ycm = lnT
    b.wait("tensor", s_lnt, ycm_ready)
    out_free = {i: 0 for i in range(NT4)}
    ld_free = {0: (0, 0), 1: (0, 0)}
    ost = {0: 0, 1: 0}
    gsil2 = [poolF[:, 2048 + j * 512:2048 + (j + 1) * 512] for j in range(2)]
    for os_ in range(D_ // 512):
        for tp in range(NT // NT4):
            tis = [tp * NT4 + i for i in range(NT4)]
            to = 0
            for kq in range(NKQ):
                wi = qi % 2
                qi += 1
                b.wait("gpsimd", s_te, wfree[wi])
                lw = ld_w(wq[wi][:], w_out_t[kq, os_], wi)
                b.wait("tensor", s_ldw[wi], lw)
                if kq == 0:
                    for i in range(NT4):
                        b.wait("tensor", s_dve, out_free[i])

                def omm(eng, wi=wi, kq=kq, tis=tis):
                    ins = None
                    for kk in range(KQ):
                        k = kq * KQ + kk
                        for i, ti in enumerate(tis):
                            ins = eng.matmul(pb[i][:, 0:512],
                                             ycm[:, k * T_ + ti * 128:k * T_ + ti * 128 + 128],
                                             wq[wi][:, kk * 512:(kk + 1) * 512],
                                             start=(k == 0), stop=(k == KT - 1))
                    return ins
                to = b.run("tensor", omm, inc=s_te)
                wfree[wi] = s_te.n
            for i, ti in enumerate(tis):
                p = ti % 2
                gb = gbufs[p]
                xb_ = xrb[p]
                gs = gsil2[p]
                b.wait("sync", s_act, ld_free[p][0])
                b.wait("sync", s_dve, ld_free[p][1])
                lgv = ld_sync(gb, ffnspill[ti * 128:(ti + 1) * 128, os_ * 512:(os_ + 1) * 512],
                              sem=s_gx[p])
                lxv = ld_sync(xb_, x_in[ti * 128:(ti + 1) * 128, os_ * 512:(os_ + 1) * 512],
                              sem=s_gx[p])
                b.wait("scalar", s_gx[p], lxv)
                b.wait("scalar", s_dve, ld_free[p][1])
                ag = b.run("scalar", lambda e, gb=gb, gs=gs: e.activation(gs, gb, AF.Silu), inc=s_act)
                b.wait("vector", s_te, to)
                b.wait("vector", s_act, ag)
                b.wait("vector", s_gx[p], lxv)
                b.wait("vector", s_st, ost[p])
                vm = b.run("vector", lambda e, i=i, gs=gs: e.tensor_tensor(
                    gs, pb[i][:, 0:512], gs, ALU.mult), inc=s_dve)
                out_free[i] = vm
                vo = b.run("vector", lambda e, i=i, xb_=xb_, gs=gs: e.tensor_tensor(
                    ob[i % 2], gs, xb_, ALU.add), inc=s_dve)
                ld_free[p] = (ag, vo)
                gvo = rb_dve(ob[i % 2][:, 0:1])
                ost[p] = store(out[ti * 128:(ti + 1) * 128, os_ * 512:(os_ + 1) * 512],
                               ob[i % 2], s_dve, gvo)

    if c.debug:
        for nm, t, shp in (("dbg_yd", ydump, [T_, D_]),
                           ("dbg_z", zspill, [2 * D_ + 2 * c.S, TH]),
                           ("dbg_ln", lnspill, [T_, D_]),
                           ("dbg_sc", scspill, [TH, D_]),
                           ("dbg_ffn", ffnspill, [T_, D_]),
                           ("dbg_y", yspill, [T_, D_])):
            o = dout(nm, shp, F32 if nm == "dbg_yd" else BF16)
            b.wait("sync", s_st, s_st.n)
            b.run("sync", lambda e, o=o, t=t: e.dma_start(out=o, in_=t), inc=s_ldx)
        oh = dout("dbg_h", [128, NH * 64], F32)
        b.wait("sync", s_dve, s_dve.n)
        b.run("sync", lambda e, o=oh: e.dma_start(out=o, in_=H[:]), inc=s_ldx)

    global _LAST_MARKS, _LAST_B
    _LAST_MARKS = getattr(b, "marks", [])
    _LAST_B = b
    for sem_ in (s_st, s_ldx, s_misc, s_xa, s_xb, s_lnt, s_sz, s_ldw[0], s_ldw[1],
                 s_lb[0], s_lb[1], s_rel2[0], s_rel2[1], s_gx[0], s_gx[1]):
        b.wait("sync", sem_, sem_.n)

    with b._ctx:
        with nc.Block() as block:
            @block.tensor
            def _(eng):
                class _CE:
                    n = 0

                    def __getattr__(self, a):
                        v = getattr(eng, a)
                        if a in ("matmul", "transpose"):
                            def w(*ar, **kw):
                                _CE.n += 1
                                return v(*ar, **kw)
                            return w
                        return v
                ce = _CE()
                runmarks = {r: nm for nm, r in getattr(b, "marks", [])}
                mmarks = []
                nwait = 0
                for idx, f in enumerate(b.ops["tensor"]):
                    if idx in runmarks:
                        mmarks.append((runmarks[idx], _CE.n))
                    f(ce)
                b.mm_marks = mmarks

            @block.vector
            def _(eng):
                for f in b.ops["vector"]:
                    f(eng)

            @block.scalar
            def _(eng):
                for f in b.ops["scalar"]:
                    f(eng)

            @block.gpsimd
            def _(eng):
                for f in b.ops["gpsimd"]:
                    f(eng)

            @block.sync
            def _(eng):
                for f in b.ops["sync"]:
                    f(eng)
    return nc


def _mk_rows3(c, weights):
    r3 = np.zeros((65, c.D), np.float32)
    r3[0] = np.asarray(weights["ln_w"])
    r3[32] = np.asarray(weights["norm_w"])
    r3[64] = np.asarray(weights["ffn_b"])
    return r3


def make_inputs(cfg: Cfg, weights: dict, x_slab: np.ndarray, prev_slab: np.ndarray):
    c = cfg
    f32 = np.float32

    def asf(a):
        return np.ascontiguousarray(a, dtype=f32)

    wconv = np.asarray(weights["conv_w"])
    convc = np.zeros((128, c.XOT * c.K), f32)
    for xo in range(c.XOT):
        for k in range(c.K):
            convc[:, xo * c.K + k] = wconv[xo * 128:(xo + 1) * 128, k]
    mupcv = np.zeros((128, c.NOT), f32)
    mv = np.asarray(weights["mup_vector"])
    for o in range(c.NOT):
        w = min(128, c.PROJ - o * 128)
        mupcv[0:w, o] = mv[o * 128:o * 128 + w]
    A2d = np.asarray(weights["ffn_A"]).reshape(c.ER, c.D)
    Bt2d = np.asarray(weights["ffn_B"]).transpose(0, 2, 1).reshape(c.ER, c.D)
    wip = np.asarray(weights["in_proj_w"], f32)
    NBF = (c.NOT + 1) // 2 - (1 if c.NOT % 2 == 1 else 0)
    wipf = wip[:, :NBF * 256].reshape(c.NKQ, c.KQ, 128, NBF, 256)
    wipf = np.ascontiguousarray(wipf.transpose(0, 3, 2, 1, 4).reshape(
        c.NKQ, NBF, 128, c.KQ * 256))
    wop = np.asarray(weights["out_proj_w"], f32)
    wopt = np.ascontiguousarray(
        wop.reshape(c.NKQ, c.KQ, 128, c.D // 512, 512).transpose(0, 3, 2, 1, 4)
        .reshape(c.NKQ, c.D // 512, 128, c.KQ * 512))
    wff = np.ascontiguousarray(np.asarray(weights["ffn_W"], f32).T)
    wfft = np.ascontiguousarray(
        wff.reshape(c.NKQ, c.KQ, 128, c.D // 512, 512).transpose(0, 3, 2, 1, 4)
        .reshape(c.NKQ, c.D // 512, 128, c.KQ * 512))
    latt = np.ascontiguousarray(
        A2d.T.reshape(c.KT, 128, c.ER).transpose(1, 0, 2).reshape(128, c.KT * c.ER)).astype(f32)
    return {
        "x_in": asf(x_slab),
        "prev_in": asf(prev_slab),
        "w_ip_t": wipf,
        "w_ip_rag": asf(wip[:, NBF * 256:]),
        "w_convc": convc,
        "w_out_t": wopt,
        "w_ffn_t": wfft,
        "w_lat_t": latt,
        "w_lBt": asf(Bt2d),
        "probs_col": asf(np.repeat(np.asarray(weights["probs"]), c.R).reshape(c.ER // 128, 128).T),
        "mup_cols": mupcv,
        "rows3_in": (lambda r3: r3)(_mk_rows3(c, weights)),
        "alog_row": asf(np.asarray(weights["A_log"])[None, :]),
        "dtb_col": asf(np.asarray(weights["dt_bias"])[:, None]),
        "d_cols": asf(np.tile(np.asarray(weights["D"])[None, :], (128, 1))),
        "ident_bf": np.eye(128).astype(bf16),
        "ident_f": np.eye(64).astype(f32),
        "ut_bf": np.triu(np.ones((128, 128))).astype(bf16),
        "tri_bf": np.triu(np.ones((128, 128))).astype(bf16),
        "ones_bf": np.ones((1, 128), bf16),
        "ones65_bf": np.ones((65, 128), bf16),
        "ones_f": np.ones((1, 128), f32),
        "fconsts": np.tile(np.array([[1e-5, 1.0 / c.D, -0.5, -1.0, 1.0, 0.0, 3.0e37, 0.5]], f32), (128, 1)),
    }


from concourse.bass_utils import run_bass_kernel_spmd

_CFG = Cfg()
_NC_CACHE = {}


def _get_program():
    if "nc" not in _NC_CACHE:
        _NC_CACHE["nc"] = build_program(_CFG)
    return _NC_CACHE["nc"]


def kernel(**inputs):
    c = _CFG
    x = np.ascontiguousarray(np.asarray(inputs["L0_attn_hidden_states"], dtype=np.float32))
    B, L, D = x.shape
    assert (B, L, D) == (4, 2048, 4096)
    weights = {
        "in_proj_w": inputs["L0_attn_in_proj_w"],
        "conv_w": inputs["L0_attn_conv_w"],
        "out_proj_w": inputs["L0_attn_out_proj_w"],
        "ln_w": inputs["L0_attn_ln_w"],
        "norm_w": inputs["L0_attn_norm_w"],
        "mup_vector": inputs["L0_attn_mup_vector"],
        "A_log": inputs["L0_attn_A_log"],
        "dt_bias": inputs["L0_attn_dt_bias"],
        "D": inputs["L0_attn_D"],
        "ffn_W": inputs["L0_ffn_W"],
        "ffn_b": inputs["L0_ffn_b"],
        "ffn_A": inputs["L0_ffn_A"],
        "ffn_B": inputs["L0_ffn_B"],
        "probs": inputs["L0_ffn_probs"],
    }
    zeros_prev = np.zeros((c.HALO, c.D), np.float32)
    base = make_inputs(c, weights, x[0, :c.T], zeros_prev)
    nc = _get_program()
    in_maps = []
    for core in range(8):
        bt, hf = core // 2, core % 2
        m = dict(base)
        m["x_in"] = np.ascontiguousarray(x[bt, hf * c.T:(hf + 1) * c.T])
        m["prev_in"] = zeros_prev if hf == 0 else \
            np.ascontiguousarray(x[bt, c.T - c.HALO:c.T])
        in_maps.append(m)
    res = run_bass_kernel_spmd(nc, in_maps, core_ids=list(range(8)))
    out = np.empty((4, 2048, 4096), np.float32)
    for core in range(8):
        bt, hf = core // 2, core % 2
        out[bt, hf * c.T:(hf + 1) * c.T] = np.asarray(res.results[core]["out"])
    return out


# version 09
# speedup vs baseline: 1.2751x; 1.1398x over previous
import contextlib
import numpy as np
import ml_dtypes
import concourse.bass as bass
import concourse.mybir as mybir

F32 = mybir.dt.float32
BF16 = mybir.dt.bfloat16
AF = mybir.ActivationFunctionType
ALU = mybir.AluOpType
AX = mybir.AxisListType
bf16 = ml_dtypes.bfloat16

ENGINES = ("tensor", "vector", "scalar", "gpsimd", "sync")


class Cfg:
    def __init__(self, D=4096, S=128, NH=64, HD=64, K=4, T=1024, HALO=256,
                 E=16, R=16, debug=False, phases=6, bphase=9):
        self.D, self.S, self.NH, self.HD, self.K = D, S, NH, HD, K
        self.T, self.HALO, self.E, self.R = T, HALO, E, R
        self.debug = debug
        self.phases = phases
        self.bphase = bphase
        self.CONV = D + 2 * S
        self.PROJ = 2 * D + 2 * S + NH
        self.KT = D // 128
        self.ZOT = D // 128
        self.XOT = self.CONV // 128
        self.NC = T // 128
        self.HC = HALO // 128
        self.TH = T + HALO
        self.NG = NH // 8
        self.ER = E * R
        self.NOT = (self.PROJ + 127) // 128
        self.DTW = self.PROJ - 128 * (self.NOT - 1)
        self.KQ = max(1, self.KT // 4)
        self.NKQ = (self.KT + self.KQ - 1) // self.KQ
        self.ERT = self.ER // 128
        assert D % 128 == 0 and self.CONV % 128 == 0 and S == 128 and HD == 64
        assert T % 128 == 0 and HALO % 128 == 0 and NH % 8 == 0 and self.ER % 128 == 0


_LAST_MARKS = []
_LAST_B = None


class Sem:
    def __init__(self, h, step=1):
        self.h, self.step, self.n = h, step, 0

    def inc(self):
        self.n += self.step
        return self.n


class Build:
    def __init__(self, cfg):
        self.cfg = cfg
        self.nc = bass.Bass("TRN2", target_bir_lowering=False,
                            detect_race_conditions=False)
        self.ops = {e: [] for e in ENGINES}
        self._ctx = contextlib.ExitStack()

    def sem(self, name, step=1):
        sm = Sem(self._ctx.enter_context(self.nc.semaphore(name)), step)
        if not hasattr(self, "_allsems"):
            self._allsems = {}
        self._allsems[name] = sm
        return sm

    def sbuf(self, name, shape, dtype):
        return self._ctx.enter_context(self.nc.sbuf_tensor(name, shape, dtype))

    def psum(self, name, shape, dtype=F32):
        return self._ctx.enter_context(self.nc.psum_tensor(name, shape, dtype))

    def wait(self, e, sem, val):
        if val > 0:
            self.ops[e].append(lambda eng, s=sem.h, v=val: eng.wait_ge(s, v))

    def mark(self, name):
        if not hasattr(self, "marks"):
            self.marks = []
        self.marks.append((name, len(self.ops["tensor"])))

    def run(self, e, f, inc=None):
        if not hasattr(self, "nrun"):
            self.nrun = {}
        self.nrun[e] = self.nrun.get(e, 0) + 1
        incs = inc if isinstance(inc, (list, tuple)) else ([inc] if inc else [])
        tgts = [s.inc() for s in incs]

        def g(eng, f=f, incs=incs):
            ins = f(eng)
            for s in incs:
                ins = ins.then_inc(s.h, s.step)
        self.ops[e].append(g)
        return tgts[0] if len(tgts) == 1 else (tgts if tgts else None)


def _finalize(b, nc, c, dram):
    for name, sem_ in list(b._allsems.items()):
        b.wait("sync", sem_, sem_.n)
    with b._ctx:
        with nc.Block() as block:
            @block.tensor
            def _(eng):
                class _CE:
                    n = 0

                    def __getattr__(self, a):
                        v = getattr(eng, a)
                        if a in ("matmul", "transpose"):
                            def w(*ar, **kw):
                                _CE.n += 1
                                return v(*ar, **kw)
                            return w
                        return v
                ce = _CE()
                runmarks = {r: nm for nm, r in getattr(b, "marks", [])}
                mmarks = []
                nwait = 0
                for idx, f in enumerate(b.ops["tensor"]):
                    if idx in runmarks:
                        mmarks.append((runmarks[idx], _CE.n))
                    f(ce)
                b.mm_marks = mmarks

            @block.vector
            def _(eng):
                for f in b.ops["vector"]:
                    f(eng)

            @block.scalar
            def _(eng):
                for f in b.ops["scalar"]:
                    f(eng)

            @block.gpsimd
            def _(eng):
                for f in b.ops["gpsimd"]:
                    f(eng)

            @block.sync
            def _(eng):
                for f in b.ops["sync"]:
                    f(eng)
    return nc


def build_program(cfg: Cfg):
    b = Build(cfg)
    nc, c = b.nc, cfg
    D_, T_, H_, TH, KT, NH = c.D, c.T, c.HALO, c.TH, c.KT, c.NH
    NT, HT = T_ // 128, H_ // 128
    tsp = [(i * 512, min(512, T_ - i * 512)) for i in range((T_ + 511) // 512)]
    KQ, NKQ, ERT = c.KQ, c.NKQ, c.ERT
    dram = {}

    def din(name, shape, dtype):
        dram[name] = nc.dram_tensor(name, shape, dtype, kind="ExternalInput").ap()
        return dram[name]

    def dout(name, shape, dtype):
        dram[name] = nc.dram_tensor(name, shape, dtype, kind="ExternalOutput").ap()
        return dram[name]

    def dint(name, shape, dtype):
        dram[name] = nc.dram_tensor(name, shape, dtype, kind="Internal").ap()
        return dram[name]

    x_in = din("x_in", [T_, D_], F32)
    prev_in = din("prev_in", [H_, D_], F32)
    NBF = (c.NOT + 1) // 2 - (1 if c.NOT % 2 == 1 else 0)
    w_ip_t = din("w_ip_t", [c.NKQ, NBF, 128, c.KQ * 256], F32)
    w_ip_rag = din("w_ip_rag", [D_, c.PROJ - NBF * 256], F32)
    w_convc = din("w_convc", [128, c.XOT * c.K], F32)
    w_out_t = din("w_out_t", [c.NKQ, D_ // 512, 128, c.KQ * 512], F32)
    w_ffn_t = din("w_ffn_t", [c.NKQ, D_ // 512, 128, c.KQ * 512], F32)
    w_lat_t = din("w_lat_t", [128, KT * c.ER], F32)
    w_lBt = din("w_lBt", [c.ER, D_], F32)
    probs_col = din("probs_col", [128, c.ER // 128], F32)
    mup_cols = din("mup_cols", [128, c.NOT], F32)
    rows3_in = din("rows3_in", [65, D_], F32)
    alog_row = din("alog_row", [1, NH], F32)
    dtb_col = din("dtb_col", [NH, 1], F32)
    d_cols = din("d_cols", [128, NH], F32)
    ident_bf = din("ident_bf", [128, 128], BF16)
    ident_f = din("ident_f", [64, 64], F32)
    ut_bf = din("ut_bf", [128, 128], BF16)
    tri_bf = din("tri_bf", [128, 128], BF16)
    ones_bf = din("ones_bf", [1, 128], BF16)
    ones65_bf = din("ones65_bf", [65, 128], BF16)
    ones_f = din("ones_f", [1, 128], F32)
    fconsts = din("fconsts", [128, 8], F32)
    out = dout("out", [T_, D_], F32)

    ydump = dint("ydump", [T_, D_], F32)
    lnspill = dint("lnspill", [T_, D_], BF16)
    scspill = dint("scspill", [TH, D_], BF16)
    zspill = dint("zspill", [2 * D_ + 2 * c.S, TH], BF16)
    ffnspill = dint("ffnspill", [T_, D_], BF16)
    yspill = dint("yspill", [T_, D_], BF16)

    sb = b.sbuf
    poolA = sb("poolA", [128, KT * T_], BF16)
    PB = max(KT * H_ + 2 * (3 + TH),
             KT * ERT * 128 + 2 * ERT * 512 + ERT * T_,
             NH * 128 + 2 * D_ + 512,
             2 * 512)
    poolB = sb("poolB", [128, PB], BF16)
    poolF = sb("poolF", [128, max(D_, 3072, 2560 + TH, 2 * TH)], F32)
    wq = [sb(f"wq{i}", [128, KQ * 512], BF16) for i in range(2)]
    xs_tm = [sb(f"xs_tm{i}", [128, D_], BF16) for i in range(2)]
    sz_tm = sb("sz_tm", [128, D_], BF16)
    bcc = [sb(f"bcc{i}", [128, 2 * 128], BF16) for i in range(2)]

    ln_bf = sz_tm
    lnT = poolA
    lnTh = poolB
    xbc2 = [poolB[:, KT * H_ + i * (3 + TH):KT * H_ + (i + 1) * (3 + TH)]
            for i in range(2)]
    lAT = poolB[:, 0:KT * ERT * 128]
    lB0 = KT * ERT * 128
    lBtq = [[poolB[:, lB0 + (p * ERT + et) * 512: lB0 + (p * ERT + et) * 512 + 512]
             for et in range(ERT)] for p in range(2)]
    hprime = poolB[:, lB0 + 2 * ERT * 512:lB0 + 2 * ERT * 512 + ERT * T_]
    diagD = poolB[:, 0:NH * 128]
    nw_bc = poolB[:, NH * 128:NH * 128 + D_]
    yhat_bf = poolB[:, NH * 128 + D_:NH * 128 + 2 * D_]
    xw8 = poolB[:, NH * 128 + 2 * D_:NH * 128 + 2 * D_ + 512]
    gbufs = [poolB[:, i * 512:(i + 1) * 512] for i in range(2)]
    xa = poolF[:, 0:D_]
    dtscr_f = poolF[0:NH, 0:TH]
    dt_t2 = poolF[0:64, TH:2 * TH]
    yfull = poolF[:, 0:D_]
    xrb = [poolF[:, i * 512:(i + 1) * 512] for i in range(2)]
    ob = [poolF[:, 1024 + i * 512:1024 + (i + 1) * 512] for i in range(2)]
    gsil = poolF[:, 2048:2560]
    acc_cv = poolF[:, poolF.shape[1] - TH:]
    sqjunkA = xs_tm[0]
    sc_bf = xs_tm[1]

    ev_bf = sb("ev_bf", [128, max(TH, 1024)], BF16)
    convw = sb("convw", [128, c.XOT * c.K], F32)
    mupc = sb("mupc", [128, c.NOT], F32)
    dt_sp = sb("dt_sp", [NH, TH], F32)
    H = sb("H", [128, NH * 64], F32)
    H_bf = sb("H_bf", [128, NH * 64], BF16)
    dtT = sb("dtT", [128, NH], F32)
    logdtT = sb("logdtT", [128, NH], F32)
    dtAT = sb("dtAT", [128, NH], BF16)
    colst = sb("colst", [128, NH], F32)
    expcT = sb("expcT", [128, NH], F32)
    a_bc = sb("a_bc", [128, NH], F32)
    a_col = sb("a_col", [NH, 1], F32)
    atot_bc = sb("atot_bc", [128, NH], F32)
    wtilT = sb("wtilT", [128, NH], F32)
    rowbuf = sb("rowbuf", [1, 128], F32)
    dtsum = sb("dtsum", [NH, 2], F32)
    gsb = sb("gsb", [128, 128], F32)
    btile = sb("btile", [128, 128], BF16)
    e8 = sb("e8", [128, 8 * 128], BF16)
    mstat8 = sb("mstat8", [128, 8 * 128], BF16)
    t1buf = sb("t1buf", [128, 512], F32)
    dcol_all = sb("dcol_all", [128, NH], F32)
    ident_s = sb("ident_s", [128, 128], BF16)
    identf_s = sb("identf_s", [64, 64], F32)
    ut_s = sb("ut_s", [128, 128], BF16)
    tri_s = sb("tri_s", [128, 128], BF16)
    ones_s = sb("ones_s", [1, 128], BF16)
    ones65 = sb("ones65", [65, 128], BF16)
    onesf_s = sb("onesf_s", [1, 128], F32)
    dtb_s = sb("dtb_s", [NH, 1], F32)
    arow_s = sb("arow_s", [1, NH], F32)
    arow_bf = sb("arow_bf", [1, NH], BF16)
    rows3 = sb("rows3", [65, D_], BF16)
    probsc = sb("probsc", [128, max(ERT, 1)], F32)
    lncol = sb("lncol", [128, 8], F32)
    lncol2 = sb("lncol2", [128, 12 * (T_ // 128 + c.HALO // 128)], F32)
    lnwT = sb("lnwT", [128, KT], F32)
    fconst = sb("fconst", [128, 8], F32)
    scr = sb("scr", [128, 8], F32)

    pb = [b.psum(f"pb{i}", [128, 512], F32) for i in range(7)]
    pbf = b.psum("pbf", [128, 1024], BF16)

    s_te = b.sem("s_te")
    s_dve = b.sem("s_dve")
    s_act = b.sem("s_act")
    s_ldx = b.sem("s_ldx", 16)
    s_ldw = [b.sem(f"s_ldw{i}", 16) for i in range(2)]
    s_st = b.sem("s_st", 16)
    s_misc = b.sem("s_misc", 16)
    s_gp = b.sem("s_gp_eng")
    s_lb = [b.sem(f"s_lb{i}", 16) for i in range(2)]
    s_xa = b.sem("s_xa", 16)
    s_xb = b.sem("s_xb", 16)
    s_lnt = b.sem("s_lnt", 16)
    s_rel2 = [b.sem(f"s_rel2{i}", 16) for i in range(2)]
    s_sz = b.sem("s_sz", 16)
    s_gx = [b.sem(f"s_gx{i}", 16) for i in range(2)]

    def ld_sync(dst, src, transpose=False, sem=None):
        return b.run("sync", lambda e, d=dst, s=src, t=transpose:
                     e.dma_start(out=d, in_=s, transpose=t), inc=(sem or s_ldx))

    def ld_w(dst, src, wi):
        return b.run("gpsimd", lambda e, d=dst, s=src: e.dma_start(out=d, in_=s),
                     inc=s_ldw[wi])

    def store(dst, src, gate_sem, gate_val):
        b.wait("scalar", gate_sem, gate_val)
        return b.run("scalar", lambda e, d=dst, s=src: e.dma_start(out=d, in_=s), inc=s_st)

    def rb_act(tile):
        p = tile.shape[0]
        b.wait("scalar", s_act, s_act.n)
        return b.run("scalar", lambda e, t=tile, p=p: e.activation(scr[0:p, 6:7], t, AF.Copy), inc=s_act)

    def rb_dve(tile):
        p = tile.shape[0]
        b.wait("vector", s_dve, s_dve.n)
        return b.run("vector", lambda e, t=tile, p=p: e.tensor_copy(scr[0:p, 7:8], t), inc=s_dve)

    def barrier():
        te0, dv0, ac0 = s_te.n, s_dve.n, s_act.n
        b.wait("tensor", s_dve, dv0)
        b.wait("tensor", s_act, ac0)
        b.wait("vector", s_te, te0)
        b.wait("vector", s_act, ac0)
        b.wait("scalar", s_te, te0)
        b.wait("scalar", s_dve, dv0)

    ld_sync(fconst[:], fconsts)
    for _i, _v in enumerate([1e-5, 1.0 / D_, -0.5, -1.0, 1.0, 0.0, 3.0e37, 0.5]):
        nc.const_aps.register(F32, float(_v), fconst[:, _i:_i + 1])
    ld_sync(ident_s[:], ident_bf)
    ld_sync(identf_s[:], ident_f)
    ld_sync(ut_s[:], ut_bf)
    ld_sync(tri_s[:], tri_bf)
    ld_sync(ones_s[:], ones_bf)
    ld_sync(ones65[:], ones65_bf)
    ld_sync(onesf_s[:], ones_f)
    ld_sync(convw[:], w_convc)
    ld_sync(mupc[:], mup_cols)
    ld_sync(dtb_s[:], dtb_col)
    ld_sync(arow_s[:], alog_row)
    ld_sync(dcol_all[:], d_cols)
    cst = ld_sync(probsc[:], probs_col)
    gw0 = b.run("gpsimd", lambda e: e.dma_start(out=rows3[:], in_=rows3_in), inc=s_misc)

    b.wait("vector", s_ldx, cst)
    b.run("vector", lambda e: e.memset(H[:, 0:NH * 64], 0.0))
    b.run("vector", lambda e: e.memset(H_bf[:, 0:NH * 64], 0.0), inc=s_dve)
    b.run("vector", lambda e: e.memset(lncol2[:], 0.0))
    b.run("vector", lambda e: e.memset(xbc2[0][:, 0:3], 0.0))
    zmv = b.run("vector", lambda e: e.memset(xbc2[1][:, 0:3], 0.0), inc=s_dve)

    b.wait("tensor", s_ldx, cst)
    b.wait("tensor", s_misc, gw0)
    for kk in range(KT):
        tw_ = b.run("tensor", lambda e, kk=kk: e.matmul(
            pb[6][:, kk:kk + 1], rows3[0:1, kk * 128:(kk + 1) * 128],
            ones_s[0:1, 0:1], start=True, stop=True),
            inc=(s_te if kk == KT - 1 else None))
    b.wait("vector", s_te, tw_)
    lwv = b.run("vector", lambda e: e.tensor_copy(lnwT[:], pb[6][:, 0:KT]), inc=s_dve)
    b.wait("scalar", s_ldx, cst)
    b.run("scalar", lambda e: e.activation(arow_s[:], arow_s[:], AF.Exp))
    b.run("scalar", lambda e: e.mul(arow_s[:], arow_s[:], -1.0))
    aa = b.run("scalar", lambda e: e.copy(arow_bf[:], arow_s[:]), inc=s_act)
    b.wait("tensor", s_act, aa)
    tt = b.run("tensor", lambda e: e.matmul(pb[2][:, 0:NH], ones_s[0:1, :],
                                            arow_bf[0:1, :], start=True, stop=True), inc=s_te)
    t2 = b.run("tensor", lambda e: e.transpose(pb[3][0:NH, 0:1], arow_s[0:1, 0:NH],
                                               identf_s[0:1, 0:1]), inc=s_te)
    b.wait("vector", s_te, t2)
    b.run("vector", lambda e: e.tensor_copy(a_bc[:], pb[2][:, 0:NH]))
    acopy = b.run("vector", lambda e: e.tensor_copy(a_col[:], pb[3][0:NH, 0:1]), inc=s_dve)

    pre_lw = [ld_w(wq[q][:, 0:KQ * 256], w_ip_t[q, 0], q) for q in range(2)]

    b.mark("A1")
    inv_d = 1.0 / D_
    HD2 = D_ // 2
    CH2 = KT // 2
    gp_a1_base = s_gp.n
    ap_prev = [0, 0]
    tr_prev = [0, 0]
    nst = [0, 0]
    ev_bank = [0, 0, 0, 0]
    rot = 0
    b.wait("tensor", s_dve, acopy)
    for i in range(NT + HT):
        halo = i >= NT
        q = 12 * i
        cl = [lncol2[:, q + k:q + k + 1] for k in range(12)]
        src = prev_in[(i - NT) * 128:(i - NT + 1) * 128, :] if halo else \
            x_in[i * 128:(i + 1) * 128, :]
        b.wait("sync", s_act, ap_prev[0])
        ldA = ld_sync(xa[:, 0:HD2], src[:, 0:HD2], sem=s_xa)
        b.wait("sync", s_act, ap_prev[1])
        ldB = ld_sync(xa[:, HD2:D_], src[:, HD2:D_], sem=s_xb)
        b.wait("vector", s_xa, ldA)
        b.run("vector", lambda e, d=cl[0]: e.tensor_reduce(d, xa[:, 0:HD2], AX.X, ALU.add))
        b.wait("vector", s_xb, ldB)
        vsum = b.run("vector", lambda e, d=cl[1]: e.tensor_reduce(d, xa[:, HD2:D_], AX.X, ALU.add), inc=s_dve)
        if i == 0:
            b.wait("scalar", s_dve, zmv)
        b.wait("scalar", s_xa, ldA)
        b.run("scalar", lambda e, d=cl[2]: e.activation(
            sqjunkA[:, 0:HD2], xa[:, 0:HD2], AF.Square, accum_out=d))
        b.wait("scalar", s_xb, ldB)
        aq = b.run("scalar", lambda e, d=cl[3]: e.activation(
            sqjunkA[:, HD2:D_], xa[:, HD2:D_], AF.Square, accum_out=d), inc=s_act)
        b.wait("vector", s_act, aq)
        b.wait("vector", s_dve, vsum)
        b.run("vector", lambda e, d=cl[4], a=cl[0], bb=cl[1]: e.tensor_tensor(d, a, bb, ALU.add))
        v = b.run("vector", lambda e, d=cl[5], a=cl[2], bb=cl[3]: e.tensor_tensor(d, a, bb, ALU.add), inc=s_dve)
        b.wait("vector", s_dve, v)
        b.run("vector", lambda e, d=cl[4]: e.tensor_scalar(d, d, inv_d, None, ALU.mult))
        v = b.run("vector", lambda e, d=cl[5]: e.tensor_scalar(d, d, inv_d, None, ALU.mult), inc=s_dve)
        b.wait("vector", s_dve, v)
        v = b.run("vector", lambda e, d=cl[11], a=cl[4]: e.tensor_tensor(d, a, a, ALU.mult), inc=s_dve)
        b.wait("vector", s_dve, v)
        vvar = b.run("vector", lambda e, d=cl[5], a=cl[11]: e.tensor_tensor(d, d, a, ALU.subtract), inc=s_dve)
        b.wait("scalar", s_dve, vvar)
        b.run("scalar", lambda e, d=cl[6], a=cl[5]: e.activation(d, a, AF.Ln, bias=1e-5))
        arstd = b.run("scalar", lambda e, d=cl[6]: e.activation(d, d, AF.Exp, scale=-0.5), inc=s_act)
        b.wait("vector", s_act, arstd)
        v = b.run("vector", lambda e, d=cl[11], a=cl[5], bb=cl[6]: e.tensor_tensor(d, a, bb, ALU.mult), inc=s_dve)
        b.wait("vector", s_dve, v)
        vr2 = b.run("vector", lambda e, d=cl[11], bb=cl[6]: e.tensor_tensor(d, d, bb, ALU.mult), inc=s_dve)
        b.wait("scalar", s_dve, vr2)
        b.run("scalar", lambda e, d=cl[7], a=cl[11]: e.activation(d, a, AF.Ln, bias=1e-5))
        ar = b.run("scalar", lambda e, d=cl[7]: e.activation(d, d, AF.Exp, scale=-0.5), inc=s_act)
        b.wait("vector", s_act, ar)
        b.run("vector", lambda e, d=cl[8], a=cl[4], bb=cl[6]: e.tensor_tensor(d, a, bb, ALU.mult))
        v = b.run("vector", lambda e, d=cl[9], a=cl[6], bb=cl[7]: e.tensor_tensor(d, a, bb, ALU.mult), inc=s_dve)
        b.wait("vector", s_dve, v)
        v = b.run("vector", lambda e, d=cl[8]: e.tensor_scalar(d, d, -1.0, None, ALU.mult), inc=s_dve)
        b.wait("vector", s_dve, v)
        b.run("vector", lambda e, d=cl[11], a=cl[6]: e.tensor_copy(d, a))
        vbias = b.run("vector", lambda e, d=cl[10], a=cl[8], bb=cl[7]: e.tensor_tensor(d, a, bb, ALU.mult), inc=s_dve)
        b.wait("scalar", s_dve, vbias)
        ap_new = [0, 0]
        for hf in range(2):
            c0, c1 = (0, HD2) if hf == 0 else (HD2, D_)
            b.wait("scalar", s_te, tr_prev[hf])
            if not halo:
                b.wait("scalar", s_st, nst[hf])
                b.run("scalar", lambda e, c0=c0, c1=c1, s=cl[11], bi=cl[8]: e.activation(
                    ln_bf[:, c0:c1], xa[:, c0:c1], AF.Identity, bias=bi, scale=s))
                ga_ = rb_act(ln_bf[:, c0:c0 + 1])
                nst[hf] = store(lnspill[i * 128:(i + 1) * 128, c0:c1],
                                ln_bf[:, c0:c1], s_act, ga_)
            ap_new[hf] = b.run("scalar", lambda e, c0=c0, c1=c1, s=cl[9], bi=cl[10]: e.activation(
                sc_bf[:, c0:c1], xa[:, c0:c1], AF.Identity, bias=bi, scale=s), inc=s_act)
            b.wait("tensor", s_act, ap_new[hf])
            for cb in range(hf * CH2, (hf + 1) * CH2):
                bank = rot % 4
                rot += 1
                b.wait("tensor", s_gp, ev_bank[bank])
                tr = b.run("tensor", lambda e, cb=cb, bank=bank: e.matmul(
                    pb[bank][:, 0:128], sc_bf[:, cb * 128:(cb + 1) * 128],
                    ident_s[:], start=True, stop=True), inc=s_te)
                b.wait("vector", s_te, tr)
                if halo:
                    dst = lnTh[:, cb * H_ + (i - NT) * 128:cb * H_ + (i - NT) * 128 + 128]
                else:
                    dst = lnT[:, cb * T_ + i * 128:cb * T_ + i * 128 + 128]
                ev_bank[bank] = b.run("vector", lambda e, dst=dst, bank=bank, cb=cb: e.tensor_scalar(
                    dst, pb[bank][:, 0:128], lnwT[:, cb:cb + 1], None, ALU.mult), inc=s_gp)
            tr_prev[hf] = s_te.n
        ap_prev = ap_new
    gp_a1_end = s_gp.n

    b.mark("A2")
    if c.phases < 2:
        return _finalize(b, nc, c, dram)
    n_blk = (c.NOT + 1) // 2
    pfree = {0: (None, 0), 1: (None, 0)}
    wfree = {0: 0, 1: 0}
    conv_free = {0: 0, 1: 0}
    evst = 0
    sil_prev = 0
    qi = 0
    b.wait("tensor", s_gp, gp_a1_end)
    for blk in range(n_blk):
        o0 = blk * 2
        ntile = min(2, c.NOT - o0)
        cols0 = o0 * 128
        cols1 = min((o0 + 2) * 128, c.PROJ)
        bw = cols1 - cols0
        psets = [[pb[0], pb[1], pb[2]], [pb[3], pb[4], pb[5]]]
        for oj in range(ntile):
            sfree, vfree = pfree[oj]
            if sfree is not None:
                b.wait("tensor", sfree, vfree)
        tdone = [0] * ntile
        for kq in range(NKQ):
            wi = qi % 2
            if qi < 2:
                lw = pre_lw[qi]
            else:
                b.wait("gpsimd", s_te, wfree[wi])
                if bw == 256:
                    lw = ld_w(wq[wi][:, 0:KQ * 256], w_ip_t[kq, blk], wi)
                else:
                    lw = ld_w(wq[wi][:, 0:KQ * bw].rearrange("p (k o) -> p k o", k=KQ),
                              w_ip_rag[kq * KQ * 128:min((kq + 1) * KQ * 128, D_), :]
                              .rearrange("(k p) o -> p k o", p=128), wi)
            qi += 1
            b.wait("tensor", s_ldw[wi], lw)
            for oj in range(ntile):
                o = o0 + oj
                is_dt = (o == c.NOT - 1)
                owid = c.DTW if is_dt else 128
                is_z = o < c.ZOT
                ps = psets[oj]

                def mms(eng, wi=wi, kq=kq, oj=oj, owid=owid, is_z=is_z, ps=ps, bw=bw):
                    ins = None
                    for kk in range(KQ):
                        k = kq * KQ + kk
                        if k >= KT:
                            break
                        st = wq[wi][:, kk * bw + oj * 128: kk * bw + oj * 128 + owid]
                        first = (k == 0)
                        last = (k == KT - 1)
                        for hi, (t0, tw) in enumerate(tsp):
                            ins = eng.matmul(ps[hi][0:owid, 0:tw], st,
                                             lnT[:, k * T_ + t0:k * T_ + t0 + tw],
                                             start=first, stop=last)
                        if not is_z:
                            ins = eng.matmul(ps[2][0:owid, 0:H_], st,
                                             lnTh[:, k * H_:(k + 1) * H_],
                                             start=first, stop=last)
                    return ins
                tdone[oj] = b.run("tensor", mms, inc=s_te)
            wfree[wi] = s_te.n
        conv_jobs = []
        for oj in range(ntile):
            o = o0 + oj
            is_dt = (o == c.NOT - 1)
            owid = c.DTW if is_dt else 128
            is_z = o < c.ZOT
            ps = psets[oj]
            mupcol = mupc[:, o:o + 1]
            if is_z:
                b.wait("scalar", s_te, tdone[oj])
                b.wait("scalar", s_st, evst)
                for hi, (t0, tw) in enumerate(tsp):
                    b.run("scalar", lambda e, hi=hi, t0=t0, tw=tw, ps=ps, mc=mupcol: e.activation(
                        ev_bf[:, t0:t0 + tw], ps[hi][:, 0:tw], AF.Silu, scale=mc))
                ga = rb_act(ev_bf[:, 0:1])
                pfree[oj] = (s_act, ga)
                evst = store(zspill[o * 128:(o + 1) * 128, H_:TH], ev_bf[:, 0:T_], s_act, ga)
            else:
                b.wait("vector", s_te, tdone[oj])
                b.wait("vector", s_dve, conv_free[oj])
                xb = xbc2[oj]
                for hi, (t0, tw) in enumerate(tsp):
                    b.run("vector", lambda e, hi=hi, t0=t0, tw=tw, ps=ps, mc=mupcol, w=owid, xb=xb: e.tensor_scalar(
                        xb[0:w, 3 + H_ + t0:3 + H_ + t0 + tw], ps[hi][0:w, 0:tw],
                        mc[0:w, :], None, ALU.mult))
                vv1 = b.run("vector", lambda e, ps=ps, mc=mupcol, w=owid, xb=xb: e.tensor_scalar(
                    xb[0:w, 3:3 + H_], ps[2][0:w, 0:H_], mc[0:w, :], None, ALU.mult), inc=s_dve)
                pfree[oj] = (s_dve, vv1)
                if is_dt:
                    b.run("vector", lambda e, w=owid, xb=xb: e.tensor_copy(
                        dtscr_f[0:w, 0:TH], xb[0:w, 3:3 + TH]))
                    rb_dve(dtscr_f[0:NH, 0:1])
                else:
                    conv_jobs.append((oj, o - c.ZOT, vv1))
        for (oj, xo, vv1) in conv_jobs:
            xb = xbc2[oj]
            b.wait("vector", s_act, sil_prev)
            b.wait("vector", s_dve, vv1)
            b.run("vector", lambda e, xo=xo, xb=xb: e.tensor_scalar(
                acc_cv[:, 0:TH], xb[:, 0:TH],
                convw[:, (xo * c.K):(xo * c.K) + 1], None, ALU.mult))
            for kk in range(1, c.K):
                gcv = b.run("vector", lambda e, xo=xo, kk=kk, xb=xb: e.scalar_tensor_tensor(
                    acc_cv[:, 0:TH], xb[:, kk:kk + TH],
                    convw[:, (xo * c.K + kk):(xo * c.K + kk + 1)],
                    acc_cv[:, 0:TH], ALU.mult, ALU.add),
                    inc=(s_dve if kk == c.K - 1 else None))
            conv_free[oj] = gcv
            b.wait("scalar", s_dve, gcv)
            b.wait("scalar", s_st, evst)
            sil = b.run("scalar", lambda e: e.activation(
                ev_bf[:, 0:TH], acc_cv[:, 0:TH], AF.Silu), inc=s_act)
            sil_prev = sil
            ga = rb_act(ev_bf[:, 0:1])
            evst = store(zspill[D_ + xo * 128:D_ + (xo + 1) * 128, :],
                         ev_bf[:, 0:TH], s_act, ga)
    a2_spills = s_st.n
    barrier()

    b.run("vector", lambda e: e.tensor_scalar(dtscr_f[0:NH, 0:TH], dtscr_f[0:NH, 0:TH],
                                              dtb_s[0:NH, :], None, ALU.add))
    vr = b.run("vector", lambda e: e.tensor_scalar(dt_sp[0:NH, :], dtscr_f[0:NH, 0:TH],
                                                   0.0, None, ALU.max), inc=s_dve)
    b.wait("scalar", s_dve, vr)
    b.run("scalar", lambda e: e.activation(dt_t2[0:NH, 0:TH], dtscr_f[0:NH, 0:TH], AF.Abs))
    b.run("scalar", lambda e: e.activation(dt_t2[0:NH, 0:TH], dt_t2[0:NH, 0:TH], AF.Exp, scale=-1.0))
    asp = b.run("scalar", lambda e: e.activation(dt_t2[0:NH, 0:TH], dt_t2[0:NH, 0:TH], AF.Ln, bias=1.0), inc=s_act)
    b.wait("vector", s_act, asp)
    vsp = b.run("vector", lambda e: e.tensor_tensor(dt_sp[0:NH, :], dt_sp[0:NH, :],
                                                    dt_t2[0:NH, 0:TH], ALU.add), inc=s_dve)

    def emit_reload(ci, slot, main):
        col0 = (H_ + ci * 128) if main else ci * 128
        b.wait("sync", s_st, a2_spills)
        b.wait("sync", s_te, s_te.n)
        b.wait("sync", s_dve, s_dve.n)
        b.wait("sync", s_act, s_act.n)
        ld_sync(xs_tm[slot][:], zspill[D_:2 * D_, col0:col0 + 128], transpose=True,
                sem=s_rel2[slot])
        return (slot, ld_sync(bcc[slot].rearrange("p (a t) -> p a t", a=2),
                              zspill[2 * D_:2 * D_ + 2 * c.S, col0:col0 + 128]
                              .rearrange("(a p) t -> p a t", p=128), sem=s_rel2[slot]))

    def chunk_prep(ci, main, slot, rel_tgt):
        mp = getattr(c, "microphase", 9)
        barrier()
        col0 = (H_ + ci * 128) if main else ci * 128
        if mp < 0.5:
            return dict(vcs=0, aat=0, aex=0, vbt=0, awt=0, vg=0)
        b.wait("tensor", s_dve, vsp)
        ta = b.run("tensor", lambda e, c0=col0: e.transpose(
            pb[2][:, 0:NH], dt_sp[0:NH, c0:c0 + 128], identf_s[0:NH, 0:NH]), inc=s_te)
        if mp < 1:
            tq = b.run("tensor", lambda e: e.matmul(
                pb[3][:, 0:128], ident_s[:], ut_s[:], start=True, stop=True), inc=s_te)
            b.wait("vector", s_te, tq)
            b.run("vector", lambda e: e.tensor_copy(gsb[:], pb[3][:, 0:128]), inc=s_dve)
            return dict(vcs=0, aat=0, aex=0, vbt=0, awt=0, vg=0)
        b.wait("vector", s_te, ta)
        vda = b.run("vector", lambda e: e.tensor_copy(dtT[:], pb[2][:, 0:NH]), inc=s_dve)
        if mp < 2:
            return dict(vcs=0, aat=0, aex=0, vbt=0, awt=0, vg=0)
        b.wait("scalar", s_dve, vda)
        ala = b.run("scalar", lambda e: e.activation(logdtT[:], dtT[:], AF.Ln), inc=s_act)
        if mp < 2.2:
            return dict(vcs=0, aat=0, aex=0, vbt=0, awt=0, vg=0)
        b.wait("vector", s_dve, vda)
        vab = b.run("vector", lambda e: e.tensor_tensor(dtAT[:], dtT[:], a_bc[:], ALU.mult), inc=s_dve)
        if mp < 2.4:
            return dict(vcs=0, aat=0, aex=0, vbt=0, awt=0, vg=0)
        b.wait("tensor", s_dve, vab)
        tb_ = b.run("tensor", lambda e: e.matmul(pb[2][:, 0:NH], ut_s[:], dtAT[:],
                                                 start=True, stop=True), inc=s_te)
        if mp < 2.6:
            return dict(vcs=0, aat=0, aex=0, vbt=0, awt=0, vg=0)
        b.wait("vector", s_te, tb_)
        b.wait("vector", s_act, ala)
        vcs = b.run("vector", lambda e: e.tensor_tensor(colst[:], logdtT[:], pb[2][:, 0:NH], ALU.subtract), inc=s_dve)
        if mp < 2.8:
            return dict(vcs=0, aat=0, aex=0, vbt=0, awt=0, vg=0)
        b.wait("scalar", s_te, tb_)
        b.wait("scalar", s_dve, vcs)
        aex = b.run("scalar", lambda e: e.activation(expcT[:], pb[2][:, 0:NH], AF.Exp), inc=s_act)
        if mp < 3:
            return dict(vcs=vcs, aat=0, aex=aex, vbt=0, awt=0, vg=0)
        vq2 = b.run("vector", lambda e, c0=col0: e.tensor_reduce(
            dtsum[0:NH, 0:1], dt_sp[0:NH, c0:c0 + 128], AX.X, ALU.add), inc=s_dve)
        b.wait("vector", s_dve, vq2)
        vds = b.run("vector", lambda e: e.tensor_tensor(dtsum[0:NH, 1:2], dtsum[0:NH, 0:1],
                                                        a_col[0:NH, :], ALU.mult), inc=s_dve)
        b.wait("tensor", s_dve, vds)
        b.wait("tensor", s_act, aex)
        b.wait("tensor", s_dve, vcs)
        if mp < 4:
            return dict(vcs=vcs, aat=0, aex=aex, vbt=0, awt=0, vg=0)
        tc_ = b.run("tensor", lambda e: e.transpose(pb[2][0:1, 0:NH], dtsum[0:NH, 1:2],
                                                    identf_s[0:NH, 0:NH]), inc=s_te)
        b.wait("vector", s_te, tc_)
        vrow = b.run("vector", lambda e: e.tensor_copy(rowbuf[0:1, 0:NH], pb[2][0:1, 0:NH]), inc=s_dve)
        b.wait("tensor", s_dve, vrow)
        if mp < 5:
            return dict(vcs=vcs, aat=0, aex=aex, vbt=0, awt=0, vg=0)
        td = b.run("tensor", lambda e: e.matmul(pb[2][:, 0:NH], onesf_s[0:1, :],
                                                rowbuf[0:1, 0:NH], start=True, stop=True), inc=s_te)
        b.wait("scalar", s_te, td)
        aat = b.run("scalar", lambda e: e.activation(atot_bc[:], pb[2][:, 0:NH], AF.Exp), inc=s_act)
        pr = dict(vcs=vcs, aat=aat, aex=aex)
        if not main:
            b.wait("vector", s_te, td)
            b.wait("vector", s_dve, vcs)
            b.wait("vector", s_act, aat)
            vwa = b.run("vector", lambda e: e.tensor_tensor(wtilT[:], colst[:], pb[2][:, 0:NH], ALU.add), inc=s_dve)
            b.wait("scalar", s_dve, vwa)
            pr["awt"] = b.run("scalar", lambda e: e.activation(wtilT[:], wtilT[:], AF.Exp), inc=s_act)
        b.wait("tensor", s_rel2[rel_tgt[0]], rel_tgt[1])
        te_ = b.run("tensor", lambda e, sl=slot: e.transpose(
            pbf[:, 0:128], bcc[sl][:, 0:128], ident_s[:]), inc=s_te)
        b.wait("vector", s_te, te_)
        pr["vbt"] = b.run("vector", lambda e: e.tensor_copy(btile[:], pbf[:, 0:128]), inc=s_dve)
        if main:
            b.wait("tensor", s_act, aat)
            tg = b.run("tensor", lambda e, sl=slot: e.matmul(
                pb[2][:, 0:128], bcc[sl][:, 0:128], bcc[sl][:, 128:256],
                start=True, stop=True), inc=s_te)
            b.wait("vector", s_te, tg)
            pr["vg"] = b.run("vector", lambda e: e.tensor_tensor(gsb[:], pb[2][:, 0:128],
                                                                 tri_s[:], ALU.mult), inc=s_dve)
        return pr

    def emit_state_update(g, slot, pr, main):
        if main:
            src = e8.rearrange("p (h t) -> p h t", h=8)[:, :, 127:128].broadcast_to([128, 8, 64])
            b.wait("vector", s_act, pr["e_done"])
        else:
            src = wtilT[:, g * 8:(g + 1) * 8].broadcast_to([128, 8, 64])
            b.wait("vector", s_act, pr["awt"])
        vxw = b.run("vector", lambda e, g=g, sl=slot, s=src: e.tensor_tensor(
            xw8.rearrange("p (h t) -> p h t", h=8),
            xs_tm[sl].rearrange("p (h t) -> p h t", h=NH)[:, g * 8:(g + 1) * 8, :],
            s, ALU.mult), inc=s_dve)
        b.wait("tensor", s_dve, max(vxw, pr["vbt"], pr.get("hfree", 0)))
        th = b.run("tensor", lambda e: e.matmul(pb[6][:, 0:512], btile[:], xw8,
                                                start=True, stop=True), inc=s_te)
        b.wait("vector", s_te, th)
        b.wait("vector", s_act, pr["aat"])
        b.run("vector", lambda e, g=g: e.tensor_tensor(
            H.rearrange("p (h t) -> p h t", h=NH)[:, g * 8:(g + 1) * 8, :],
            H.rearrange("p (h t) -> p h t", h=NH)[:, g * 8:(g + 1) * 8, :],
            atot_bc[:, g * 8:(g + 1) * 8].broadcast_to([128, 8, 64]), ALU.mult))
        b.run("vector", lambda e, g=g: e.tensor_tensor(
            H[:, g * 512:g * 512 + 512], H[:, g * 512:g * 512 + 512], pb[6][:, 0:512], ALU.add))
        vH = b.run("vector", lambda e, g=g: e.tensor_copy(
            H_bf[:, g * 512:g * 512 + 512], H[:, g * 512:g * 512 + 512]), inc=s_dve)
        return vH, vxw

    b.mark("A2.5")
    if c.phases < 2.2:
        return _finalize(b, nc, c, dram)
    relq = []
    rel0 = emit_reload(0, 0, main=False)
    if c.phases < 2.4:
        return _finalize(b, nc, c, dram)
    hf = 0
    for hc in range(c.HC):
        slot = hc % 2
        nrel = emit_reload(hc + 1, (hc + 1) % 2, main=False) if hc + 1 < c.HC else None
        pr = chunk_prep(hc, False, slot, rel0 if hc == 0 else relq.pop())
        if nrel is not None:
            relq.append(nrel)
        if c.phases < 2.6:
            return _finalize(b, nc, c, dram)
        ng_lim = c.NG if c.phases >= 3 else max(1, int(round((c.phases - 2.6) * 20)))
        for g in range(min(c.NG, ng_lim)):
            pr["hfree"] = hf
            hf = emit_state_update(g, slot, pr, main=False)[0]
        if c.phases < 3:
            return _finalize(b, nc, c, dram)
    barrier()

    b.mark("A3")
    if c.phases < 4:
        return _finalize(b, nc, c, dram)
    b.wait("sync", s_te, s_te.n)
    b.wait("sync", s_dve, s_dve.n)
    for k in range(KT):
        ld_sync(lnT[:, k * T_:(k + 1) * T_],
                lnspill[:, k * 128:(k + 1) * 128], transpose=True, sem=s_lnt)
    lnT2_ready = s_lnt.n
    b.wait("gpsimd", s_te, s_te.n)
    b.wait("gpsimd", s_dve, s_dve.n)
    b.wait("gpsimd", s_act, s_act.n)
    lg = b.run("gpsimd", lambda e: e.dma_start(out=lAT[:], in_=w_lat_t), inc=s_misc)
    b.wait("tensor", s_lnt, lnT2_ready)
    b.wait("tensor", s_misc, lg)
    h_ev = {0: 0, 1: 0}
    hq = 0
    for et in range(ERT):
        for hi, (t0, tw) in enumerate(tsp):
            bank = hq % 2
            hq += 1
            b.wait("tensor", s_dve, h_ev[bank])

            def hmm(eng, et=et, t0=t0, tw=tw, bank=bank):
                ins = None
                for k in range(KT):
                    ins = eng.matmul(pb[bank][:, 0:tw],
                                     lAT[:, (k * ERT + et) * 128:(k * ERT + et) * 128 + 128],
                                     lnT[:, k * T_ + t0:k * T_ + t0 + tw],
                                     start=(k == 0), stop=(k == KT - 1))
                return ins
            th = b.run("tensor", hmm, inc=s_te)
            b.wait("vector", s_te, th)
            h_ev[bank] = b.run("vector", lambda e, et=et, t0=t0, tw=tw, bank=bank: e.tensor_scalar(
                hprime[:, et * T_ + t0:et * T_ + t0 + tw], pb[bank][:, 0:tw],
                probsc[:, et:et + 1], None, ALU.mult), inc=s_dve)
    vhp = rb_dve(hprime[:, 0:1])
    b.wait("tensor", s_dve, vhp)

    NT4 = min(4, NT)
    ffn_ev = {i: 0 for i in range(NT4)}
    evstl = {0: 0, 1: 0}
    lb_t = {0: 0, 1: 0}
    for os_ in range(D_ // 512):
        lbw = os_ % 2
        b.wait("gpsimd", s_te, lb_t[lbw])
        lbv = None
        for et in range(ERT):
            lbv = b.run("gpsimd", lambda e, et=et, os_=os_, lbw=lbw: e.dma_start(
                out=lBtq[lbw][et],
                in_=w_lBt[et * 128:(et + 1) * 128, os_ * 512:(os_ + 1) * 512]), inc=s_lb[lbw])
        for tp in range(NT // NT4):
            tis = [tp * NT4 + i for i in range(NT4)]
            for i, ti in enumerate(tis):
                b.wait("tensor", s_dve, ffn_ev[i])
                b.run("tensor", lambda e, i=i, os_=os_: e.matmul(
                    pb[i][:, 0:512], ones65[64:65, :], rows3[64:65, os_ * 512:(os_ + 1) * 512],
                    start=True, stop=False))
            for kq in range(NKQ):
                wi = qi % 2
                qi += 1
                b.wait("gpsimd", s_te, wfree[wi])
                lw = ld_w(wq[wi][:], w_ffn_t[kq, os_], wi)
                b.wait("tensor", s_ldw[wi], lw)

                def fmm(eng, wi=wi, kq=kq, tis=tis):
                    ins = None
                    for kk in range(KQ):
                        k = kq * KQ + kk
                        for i, ti in enumerate(tis):
                            ins = eng.matmul(pb[i][:, 0:512],
                                             lnT[:, k * T_ + ti * 128:k * T_ + ti * 128 + 128],
                                             wq[wi][:, kk * 512:(kk + 1) * 512],
                                             start=False, stop=False)
                    return ins
                b.run("tensor", fmm, inc=s_te)
                wfree[wi] = s_te.n
            b.wait("tensor", s_lb[lbw], lbv)

            def lmm(eng, tis=tis, lbw=lbw):
                ins = None
                for et in range(ERT):
                    for i, ti in enumerate(tis):
                        ins = eng.matmul(pb[i][:, 0:512],
                                         hprime[:, et * T_ + ti * 128:et * T_ + ti * 128 + 128],
                                         lBtq[lbw][et],
                                         start=False, stop=(et == ERT - 1))
                return ins
            tl = b.run("tensor", lmm, inc=s_te)
            for i, ti in enumerate(tis):
                p = ti % 2
                b.wait("vector", s_te, tl)
                b.wait("vector", s_st, evstl[p])
                ve = b.run("vector", lambda e, i=i, p=p: e.tensor_copy(
                    ev_bf[:, p * 512:p * 512 + 512], pb[i][:, 0:512]), inc=s_dve)
                ffn_ev[i] = ve
                gv = rb_dve(ev_bf[:, p * 512:p * 512 + 1])
                evstl[p] = store(ffnspill[ti * 128:(ti + 1) * 128, os_ * 512:(os_ + 1) * 512],
                                 ev_bf[:, p * 512:p * 512 + 512], s_dve, gv)
        lb_t[lbw] = s_te.n
    barrier()

    b.mark("B")
    if c.phases < 5:
        return _finalize(b, nc, c, dram)
    for h in range(NH):
        b.run("vector", lambda e, h=h: e.tensor_scalar(
            diagD[:, h * 128:(h + 1) * 128], ident_s[:], dcol_all[:, h:h + 1], None, ALU.mult))
    rb_dve(diagD[:, 0:1])
    nwev = {0: 0, 1: 0}
    for j in range(D_ // 512):
        bk = 2 + (j % 2)
        b.wait("tensor", s_dve, nwev[j % 2])
        tt = b.run("tensor", lambda e, j=j, bk=bk: e.matmul(
            pb[bk][:, 0:512], ones65[32:33, :], rows3[32:33, j * 512:(j + 1) * 512],
            start=True, stop=True), inc=s_te)
        b.wait("vector", s_te, tt)
        nwev[j % 2] = b.run("vector", lambda e, j=j, bk=bk: e.tensor_copy(
            nw_bc[:, j * 512:(j + 1) * 512], pb[bk][:, 0:512]), inc=s_dve)
    vnw = rb_dve(nw_bc[:, 0:1])
    b.wait("tensor", s_dve, vnw)
    bp = getattr(c, "bphase", 9)
    if bp < 1:
        return _finalize(b, nc, c, dram)

    vH_last = hf
    vxw_last = 0
    vHg = {g: hf for g in range(c.NG)}
    vg1 = 0
    rel0 = emit_reload(0, 0, main=True)
    relq = [rel0]
    y_sp = []
    t1_free = {0: 0, 1: 0}
    for ci in range(c.NC):
        slot = ci % 2
        my_rel = relq.pop(0)
        if ci + 1 < c.NC:
            relq.append(emit_reload(ci + 1, (ci + 1) % 2, main=True))
        b.wait("sync", s_dve, vg1)
        szv = ld_sync(sz_tm[:], zspill[0:D_, H_ + ci * 128:H_ + ci * 128 + 128],
                      transpose=True, sem=s_sz)
        pr = chunk_prep(ci, True, slot, my_rel)
        if bp < 1.5:
            return _finalize(b, nc, c, dram)
        t1_free[0] = max(t1_free[0], pr["vg"])
        yev = {0: 0, 1: 0}
        m_done = {0: 0, 1: 0}
        bc_free = {0: 0, 1: 0}
        for g in range(c.NG):
            ypb = pb[g % 2]
            t1b = pb[2] if g % 2 == 0 else pb[4]
            b.wait("tensor", s_dve, max(vHg[g], t1_free[g % 2]))
            b.wait("tensor", s_rel2[my_rel[0]], my_rel[1])
            if g == 0:
                b.wait("tensor", s_act, pr["aat"])
            t1m = b.run("tensor", lambda e, sl=slot, g=g, t1b=t1b: e.matmul(
                t1b[:, 0:512], bcc[sl][:, 128:256], H_bf[:, g * 512:(g + 1) * 512],
                start=True, stop=True), inc=s_te)
            b.wait("vector", s_te, t1m)
            if g == 0:
                b.wait("vector", s_act, pr["aat"])
            vt1 = b.run("vector", lambda e, g=g, t1b=t1b: e.tensor_tensor(
                t1buf.rearrange("p (h t) -> p h t", h=8),
                t1b.rearrange("p (h t) -> p h t", h=8),
                expcT[:, g * 8:(g + 1) * 8].broadcast_to([128, 8, 64]), ALU.mult), inc=s_dve)
            t1_free[g % 2] = vt1
            if bp < 2:
                return _finalize(b, nc, c, dram)
            b.wait("tensor", s_dve, yev[g % 2])
            e_done = 0
            for hb in range(2):
                hls = [hb * 4 + j for j in range(4)]
                bcb = pb[3] if hb == 0 else pb[5]
                if bc_free[hb]:
                    b.wait("tensor", s_act, bc_free[hb])
                elif hb == 0:
                    b.wait("tensor", s_dve, pr["vg"])

                def bmm(eng, g=g, hls=hls, bcb=bcb):
                    ins = None
                    for j, hl in enumerate(hls):
                        h = g * 8 + hl
                        ins = eng.matmul(bcb[:, j * 128:(j + 1) * 128],
                                         dtAT[:, h:h + 1].broadcast_to([128, 128]),
                                         ut_s[:], start=True, stop=True)
                    return ins
                tb_ = b.run("tensor", bmm, inc=s_te)
                if bp < 2.2:
                    return _finalize(b, nc, c, dram)
                b.wait("scalar", s_te, tb_)
                b.wait("scalar", s_dve, pr["vcs"] if hb > 0
                       else max(pr["vcs"], vxw_last))
                for j, hl in enumerate(hls):
                    h = g * 8 + hl
                    ae = b.run("scalar", lambda e, h=h, hl=hl, j=j, bcb=bcb: e.activation(
                        e8[:, hl * 128:(hl + 1) * 128], bcb[:, j * 128:(j + 1) * 128],
                        AF.Exp, bias=colst[:, h:h + 1]),
                        inc=(s_act if j == 3 else None))
                bc_free[hb] = ae
                e_done = ae
                if bp < 2.4:
                    return _finalize(b, nc, c, dram)
                b.wait("vector", s_act, ae)
                b.wait("vector", s_dve, pr["vg"])
                b.wait("vector", s_te, m_done[hb])
                for j, hl in enumerate(hls):
                    vm = b.run("vector", lambda e, hl=hl: e.scalar_tensor_tensor(
                        mstat8[:, hl * 128:(hl + 1) * 128],
                        e8[:, hl * 128:(hl + 1) * 128], 3.0e37, gsb[:],
                        ALU.min, ALU.mult), inc=(s_dve if j == 3 else None))
                b.wait("tensor", s_dve, vm)
                if bp < 2.6:
                    return _finalize(b, nc, c, dram)

                def ymm(eng, g=g, hls=hls, sl=slot, yp=ypb):
                    ins = None
                    for hl in hls:
                        h = g * 8 + hl
                        eng.matmul(yp[:, hl * 64:hl * 64 + 64],
                                   mstat8[:, hl * 128:(hl + 1) * 128],
                                   xs_tm[sl][:, h * 64:h * 64 + 64],
                                   start=(hl == 0), stop=False)
                        ins = eng.matmul(yp[:, hl * 64:hl * 64 + 64],
                                         diagD[:, h * 128:(h + 1) * 128],
                                         xs_tm[sl][:, h * 64:h * 64 + 64],
                                         start=False, stop=(hl == 7))
                    return ins
                tym = b.run("tensor", ymm, inc=s_te)
                m_done[hb] = tym
                if bp < 2.8:
                    return _finalize(b, nc, c, dram)
            pr["e_done"] = e_done
            pr["hfree"] = vH_last
            vH_last, vxw_last = emit_state_update(g, slot, pr, main=True)
            vHg[g] = vH_last
            b.wait("vector", s_te, max(m_done[0], m_done[1]))
            vy = b.run("vector", lambda e, g=g, yp=ypb: e.tensor_tensor(
                yfull[:, g * 512:(g + 1) * 512], t1buf[:], yp[:, 0:512], ALU.add), inc=s_dve)
            yev[g % 2] = vy
            if bp < 3.5:
                return _finalize(b, nc, c, dram)
        b.wait("vector", s_dve, max(yev[0], yev[1]))
        if c.debug:
            ygate = rb_dve(yfull[:, 0:1])
            ystt = store(ydump[ci * 128:(ci + 1) * 128, :], yfull, s_dve, ygate)
            b.wait("vector", s_st, ystt)
        b.wait("vector", s_sz, szv)
        vg1a = b.run("vector", lambda e: e.tensor_tensor(yfull, yfull, sz_tm[:], ALU.mult), inc=s_dve)
        vg1 = b.run("vector", lambda e: e.memset(lncol[:, 0:1], 0.0), inc=s_dve)
        b.wait("scalar", s_dve, max(vg1, vg1a))
        asq = b.run("scalar", lambda e, sl=slot: e.activation(
            xs_tm[sl][:], yfull, AF.Square, accum_out=lncol[:, 0:1]), inc=s_act)
        b.wait("scalar", s_act, asq)
        b.run("scalar", lambda e: e.activation(lncol[:, 1:2], lncol[:, 0:1], AF.Ln,
                                               scale=inv_d, bias=1e-5))
        ar2 = b.run("scalar", lambda e: e.activation(lncol[:, 1:2], lncol[:, 1:2], AF.Exp,
                                                     scale=-0.5), inc=s_act)
        b.wait("vector", s_act, ar2)
        b.wait("vector", s_st, y_sp[-1] if y_sp else 0)
        b.run("vector", lambda e: e.tensor_scalar(yfull, yfull, lncol[:, 1:2], None, ALU.mult))
        vyh = b.run("vector", lambda e: e.tensor_tensor(yhat_bf, yfull, nw_bc, ALU.mult), inc=s_dve)
        gv = rb_dve(yhat_bf[:, 0:1])
        y_sp.append(store(yspill[ci * 128:(ci + 1) * 128, :], yhat_bf, s_dve, gv))
    b_spills = s_st.n
    barrier()

    b.mark("C")
    if c.phases < 6:
        return _finalize(b, nc, c, dram)
    b.wait("sync", s_st, b_spills)
    b.wait("sync", s_te, s_te.n)
    b.wait("sync", s_dve, s_dve.n)
    for k in range(KT):
        ld_sync(lnT[:, k * T_:(k + 1) * T_],
                yspill[:, k * 128:(k + 1) * 128], transpose=True, sem=s_lnt)
    ycm_ready = s_lnt.n
    ycm = lnT
    b.wait("tensor", s_lnt, ycm_ready)
    out_free = {i: 0 for i in range(NT4)}
    ld_free = {0: (0, 0), 1: (0, 0)}
    ost = {0: 0, 1: 0}
    gsil2 = [poolF[:, 2048 + j * 512:2048 + (j + 1) * 512] for j in range(2)]
    for os_ in range(D_ // 512):
        for tp in range(NT // NT4):
            tis = [tp * NT4 + i for i in range(NT4)]
            to = 0
            for kq in range(NKQ):
                wi = qi % 2
                qi += 1
                b.wait("gpsimd", s_te, wfree[wi])
                lw = ld_w(wq[wi][:], w_out_t[kq, os_], wi)
                b.wait("tensor", s_ldw[wi], lw)
                if kq == 0:
                    for i in range(NT4):
                        b.wait("tensor", s_dve, out_free[i])

                def omm(eng, wi=wi, kq=kq, tis=tis):
                    ins = None
                    for kk in range(KQ):
                        k = kq * KQ + kk
                        for i, ti in enumerate(tis):
                            ins = eng.matmul(pb[i][:, 0:512],
                                             ycm[:, k * T_ + ti * 128:k * T_ + ti * 128 + 128],
                                             wq[wi][:, kk * 512:(kk + 1) * 512],
                                             start=(k == 0), stop=(k == KT - 1))
                    return ins
                to = b.run("tensor", omm, inc=s_te)
                wfree[wi] = s_te.n
            for i, ti in enumerate(tis):
                p = ti % 2
                gb = gbufs[p]
                xb_ = xrb[p]
                gs = gsil2[p]
                b.wait("sync", s_act, ld_free[p][0])
                b.wait("sync", s_dve, ld_free[p][1])
                lgv = ld_sync(gb, ffnspill[ti * 128:(ti + 1) * 128, os_ * 512:(os_ + 1) * 512],
                              sem=s_gx[p])
                lxv = ld_sync(xb_, x_in[ti * 128:(ti + 1) * 128, os_ * 512:(os_ + 1) * 512],
                              sem=s_gx[p])
                b.wait("scalar", s_gx[p], lxv)
                b.wait("scalar", s_dve, ld_free[p][1])
                ag = b.run("scalar", lambda e, gb=gb, gs=gs: e.activation(gs, gb, AF.Silu), inc=s_act)
                b.wait("vector", s_te, to)
                b.wait("vector", s_act, ag)
                b.wait("vector", s_gx[p], lxv)
                b.wait("vector", s_st, ost[p])
                vm = b.run("vector", lambda e, i=i, gs=gs: e.tensor_tensor(
                    gs, pb[i][:, 0:512], gs, ALU.mult), inc=s_dve)
                out_free[i] = vm
                vo = b.run("vector", lambda e, i=i, xb_=xb_, gs=gs: e.tensor_tensor(
                    ob[i % 2], gs, xb_, ALU.add), inc=s_dve)
                ld_free[p] = (ag, vo)
                gvo = rb_dve(ob[i % 2][:, 0:1])
                ost[p] = store(out[ti * 128:(ti + 1) * 128, os_ * 512:(os_ + 1) * 512],
                               ob[i % 2], s_dve, gvo)

    if c.debug:
        for nm, t, shp in (("dbg_yd", ydump, [T_, D_]),
                           ("dbg_z", zspill, [2 * D_ + 2 * c.S, TH]),
                           ("dbg_ln", lnspill, [T_, D_]),
                           ("dbg_sc", scspill, [TH, D_]),
                           ("dbg_ffn", ffnspill, [T_, D_]),
                           ("dbg_y", yspill, [T_, D_])):
            o = dout(nm, shp, F32 if nm == "dbg_yd" else BF16)
            b.wait("sync", s_st, s_st.n)
            b.run("sync", lambda e, o=o, t=t: e.dma_start(out=o, in_=t), inc=s_ldx)
        oh = dout("dbg_h", [128, NH * 64], F32)
        b.wait("sync", s_dve, s_dve.n)
        b.run("sync", lambda e, o=oh: e.dma_start(out=o, in_=H[:]), inc=s_ldx)

    global _LAST_MARKS, _LAST_B
    _LAST_MARKS = getattr(b, "marks", [])
    _LAST_B = b
    for sem_ in (s_st, s_ldx, s_misc, s_xa, s_xb, s_lnt, s_sz, s_ldw[0], s_ldw[1],
                 s_lb[0], s_lb[1], s_rel2[0], s_rel2[1], s_gx[0], s_gx[1]):
        b.wait("sync", sem_, sem_.n)

    with b._ctx:
        with nc.Block() as block:
            @block.tensor
            def _(eng):
                class _CE:
                    n = 0

                    def __getattr__(self, a):
                        v = getattr(eng, a)
                        if a in ("matmul", "transpose"):
                            def w(*ar, **kw):
                                _CE.n += 1
                                return v(*ar, **kw)
                            return w
                        return v
                ce = _CE()
                runmarks = {r: nm for nm, r in getattr(b, "marks", [])}
                mmarks = []
                nwait = 0
                for idx, f in enumerate(b.ops["tensor"]):
                    if idx in runmarks:
                        mmarks.append((runmarks[idx], _CE.n))
                    f(ce)
                b.mm_marks = mmarks

            @block.vector
            def _(eng):
                for f in b.ops["vector"]:
                    f(eng)

            @block.scalar
            def _(eng):
                for f in b.ops["scalar"]:
                    f(eng)

            @block.gpsimd
            def _(eng):
                for f in b.ops["gpsimd"]:
                    f(eng)

            @block.sync
            def _(eng):
                for f in b.ops["sync"]:
                    f(eng)
    return nc


def _mk_rows3(c, weights):
    r3 = np.zeros((65, c.D), np.float32)
    r3[0] = np.asarray(weights["ln_w"])
    r3[32] = np.asarray(weights["norm_w"])
    r3[64] = np.asarray(weights["ffn_b"])
    return r3


def make_inputs(cfg: Cfg, weights: dict, x_slab: np.ndarray, prev_slab: np.ndarray):
    c = cfg
    f32 = np.float32

    def asf(a):
        return np.ascontiguousarray(a, dtype=f32)

    wconv = np.asarray(weights["conv_w"])
    convc = np.zeros((128, c.XOT * c.K), f32)
    for xo in range(c.XOT):
        for k in range(c.K):
            convc[:, xo * c.K + k] = wconv[xo * 128:(xo + 1) * 128, k]
    mupcv = np.zeros((128, c.NOT), f32)
    mv = np.asarray(weights["mup_vector"])
    for o in range(c.NOT):
        w = min(128, c.PROJ - o * 128)
        mupcv[0:w, o] = mv[o * 128:o * 128 + w]
    A2d = np.asarray(weights["ffn_A"]).reshape(c.ER, c.D)
    Bt2d = np.asarray(weights["ffn_B"]).transpose(0, 2, 1).reshape(c.ER, c.D)
    wip = np.asarray(weights["in_proj_w"], f32)
    NBF = (c.NOT + 1) // 2 - (1 if c.NOT % 2 == 1 else 0)
    wipf = wip[:, :NBF * 256].reshape(c.NKQ, c.KQ, 128, NBF, 256)
    wipf = np.ascontiguousarray(wipf.transpose(0, 3, 2, 1, 4).reshape(
        c.NKQ, NBF, 128, c.KQ * 256))
    wop = np.asarray(weights["out_proj_w"], f32)
    wopt = np.ascontiguousarray(
        wop.reshape(c.NKQ, c.KQ, 128, c.D // 512, 512).transpose(0, 3, 2, 1, 4)
        .reshape(c.NKQ, c.D // 512, 128, c.KQ * 512))
    wff = np.ascontiguousarray(np.asarray(weights["ffn_W"], f32).T)
    wfft = np.ascontiguousarray(
        wff.reshape(c.NKQ, c.KQ, 128, c.D // 512, 512).transpose(0, 3, 2, 1, 4)
        .reshape(c.NKQ, c.D // 512, 128, c.KQ * 512))
    latt = np.ascontiguousarray(
        A2d.T.reshape(c.KT, 128, c.ER).transpose(1, 0, 2).reshape(128, c.KT * c.ER)).astype(f32)
    return {
        "x_in": asf(x_slab),
        "prev_in": asf(prev_slab),
        "w_ip_t": wipf,
        "w_ip_rag": asf(wip[:, NBF * 256:]),
        "w_convc": convc,
        "w_out_t": wopt,
        "w_ffn_t": wfft,
        "w_lat_t": latt,
        "w_lBt": asf(Bt2d),
        "probs_col": asf(np.repeat(np.asarray(weights["probs"]), c.R).reshape(c.ER // 128, 128).T),
        "mup_cols": mupcv,
        "rows3_in": (lambda r3: r3)(_mk_rows3(c, weights)),
        "alog_row": asf(np.asarray(weights["A_log"])[None, :]),
        "dtb_col": asf(np.asarray(weights["dt_bias"])[:, None]),
        "d_cols": asf(np.tile(np.asarray(weights["D"])[None, :], (128, 1))),
        "ident_bf": np.eye(128).astype(bf16),
        "ident_f": np.eye(64).astype(f32),
        "ut_bf": np.triu(np.ones((128, 128))).astype(bf16),
        "tri_bf": np.triu(np.ones((128, 128))).astype(bf16),
        "ones_bf": np.ones((1, 128), bf16),
        "ones65_bf": np.ones((65, 128), bf16),
        "ones_f": np.ones((1, 128), f32),
        "fconsts": np.tile(np.array([[1e-5, 1.0 / c.D, -0.5, -1.0, 1.0, 0.0, 3.0e37, 0.5]], f32), (128, 1)),
    }


from concourse.bass_utils import run_bass_kernel_spmd

_CFG = Cfg()
_NC_CACHE = {}


def _get_program():
    if "nc" not in _NC_CACHE:
        _NC_CACHE["nc"] = build_program(_CFG)
    return _NC_CACHE["nc"]


def kernel(**inputs):
    c = _CFG
    x = np.ascontiguousarray(np.asarray(inputs["L0_attn_hidden_states"], dtype=np.float32))
    B, L, D = x.shape
    assert (B, L, D) == (4, 2048, 4096)
    weights = {
        "in_proj_w": inputs["L0_attn_in_proj_w"],
        "conv_w": inputs["L0_attn_conv_w"],
        "out_proj_w": inputs["L0_attn_out_proj_w"],
        "ln_w": inputs["L0_attn_ln_w"],
        "norm_w": inputs["L0_attn_norm_w"],
        "mup_vector": inputs["L0_attn_mup_vector"],
        "A_log": inputs["L0_attn_A_log"],
        "dt_bias": inputs["L0_attn_dt_bias"],
        "D": inputs["L0_attn_D"],
        "ffn_W": inputs["L0_ffn_W"],
        "ffn_b": inputs["L0_ffn_b"],
        "ffn_A": inputs["L0_ffn_A"],
        "ffn_B": inputs["L0_ffn_B"],
        "probs": inputs["L0_ffn_probs"],
    }
    zeros_prev = np.zeros((c.HALO, c.D), np.float32)
    base = make_inputs(c, weights, x[0, :c.T], zeros_prev)
    nc = _get_program()
    in_maps = []
    for core in range(8):
        bt, hf = core // 2, core % 2
        m = dict(base)
        m["x_in"] = np.ascontiguousarray(x[bt, hf * c.T:(hf + 1) * c.T])
        m["prev_in"] = zeros_prev if hf == 0 else \
            np.ascontiguousarray(x[bt, c.T - c.HALO:c.T])
        in_maps.append(m)
    res = run_bass_kernel_spmd(nc, in_maps, core_ids=list(range(8)))
    out = np.empty((4, 2048, 4096), np.float32)
    for core in range(8):
        bt, hf = core // 2, core % 2
        out[bt, hf * c.T:(hf + 1) * c.T] = np.asarray(res.results[core]["out"])
    return out


# version 10
# speedup vs baseline: 1.3286x; 1.0419x over previous
import contextlib
import numpy as np
import ml_dtypes
import concourse.bass as bass
import concourse.mybir as mybir

F32 = mybir.dt.float32
BF16 = mybir.dt.bfloat16
AF = mybir.ActivationFunctionType
ALU = mybir.AluOpType
AX = mybir.AxisListType
bf16 = ml_dtypes.bfloat16

ENGINES = ("tensor", "vector", "scalar", "gpsimd", "sync")


class Cfg:
    def __init__(self, D=4096, S=128, NH=64, HD=64, K=4, T=1024, HALO=256,
                 E=16, R=16, debug=False, phases=6, bphase=9):
        self.D, self.S, self.NH, self.HD, self.K = D, S, NH, HD, K
        self.T, self.HALO, self.E, self.R = T, HALO, E, R
        self.debug = debug
        self.phases = phases
        self.bphase = bphase
        self.CONV = D + 2 * S
        self.PROJ = 2 * D + 2 * S + NH
        self.KT = D // 128
        self.ZOT = D // 128
        self.XOT = self.CONV // 128
        self.NC = T // 128
        self.HC = HALO // 128
        self.TH = T + HALO
        self.NG = NH // 8
        self.ER = E * R
        self.NOT = (self.PROJ + 127) // 128
        self.DTW = self.PROJ - 128 * (self.NOT - 1)
        self.KQ = max(1, self.KT // 4)
        self.NKQ = (self.KT + self.KQ - 1) // self.KQ
        self.ERT = self.ER // 128
        assert D % 128 == 0 and self.CONV % 128 == 0 and S == 128 and HD == 64
        assert T % 128 == 0 and HALO % 128 == 0 and NH % 8 == 0 and self.ER % 128 == 0


_LAST_MARKS = []
_LAST_B = None


class Sem:
    def __init__(self, h, step=1):
        self.h, self.step, self.n = h, step, 0

    def inc(self):
        self.n += self.step
        return self.n


class Build:
    def __init__(self, cfg):
        self.cfg = cfg
        self.nc = bass.Bass("TRN2", target_bir_lowering=False,
                            detect_race_conditions=False)
        self.ops = {e: [] for e in ENGINES}
        self._ctx = contextlib.ExitStack()

    def sem(self, name, step=1):
        sm = Sem(self._ctx.enter_context(self.nc.semaphore(name)), step)
        if not hasattr(self, "_allsems"):
            self._allsems = {}
        self._allsems[name] = sm
        return sm

    def sbuf(self, name, shape, dtype):
        return self._ctx.enter_context(self.nc.sbuf_tensor(name, shape, dtype))

    def psum(self, name, shape, dtype=F32):
        return self._ctx.enter_context(self.nc.psum_tensor(name, shape, dtype))

    def wait(self, e, sem, val):
        if val > 0:
            self.ops[e].append(lambda eng, s=sem.h, v=val: eng.wait_ge(s, v))

    def mark(self, name):
        if not hasattr(self, "marks"):
            self.marks = []
        self.marks.append((name, len(self.ops["tensor"])))

    def run(self, e, f, inc=None):
        if not hasattr(self, "nrun"):
            self.nrun = {}
        self.nrun[e] = self.nrun.get(e, 0) + 1
        incs = inc if isinstance(inc, (list, tuple)) else ([inc] if inc else [])
        tgts = [s.inc() for s in incs]

        def g(eng, f=f, incs=incs):
            ins = f(eng)
            for s in incs:
                ins = ins.then_inc(s.h, s.step)
        self.ops[e].append(g)
        return tgts[0] if len(tgts) == 1 else (tgts if tgts else None)


def _finalize(b, nc, c, dram):
    for name, sem_ in list(b._allsems.items()):
        b.wait("sync", sem_, sem_.n)
    with b._ctx:
        with nc.Block() as block:
            @block.tensor
            def _(eng):
                class _CE:
                    n = 0

                    def __getattr__(self, a):
                        v = getattr(eng, a)
                        if a in ("matmul", "transpose"):
                            def w(*ar, **kw):
                                _CE.n += 1
                                return v(*ar, **kw)
                            return w
                        return v
                ce = _CE()
                runmarks = {r: nm for nm, r in getattr(b, "marks", [])}
                mmarks = []
                nwait = 0
                for idx, f in enumerate(b.ops["tensor"]):
                    if idx in runmarks:
                        mmarks.append((runmarks[idx], _CE.n))
                    f(ce)
                b.mm_marks = mmarks

            @block.vector
            def _(eng):
                for f in b.ops["vector"]:
                    f(eng)

            @block.scalar
            def _(eng):
                for f in b.ops["scalar"]:
                    f(eng)

            @block.gpsimd
            def _(eng):
                for f in b.ops["gpsimd"]:
                    f(eng)

            @block.sync
            def _(eng):
                for f in b.ops["sync"]:
                    f(eng)
    return nc


def build_program(cfg: Cfg):
    b = Build(cfg)
    nc, c = b.nc, cfg
    D_, T_, H_, TH, KT, NH = c.D, c.T, c.HALO, c.TH, c.KT, c.NH
    NT, HT = T_ // 128, H_ // 128
    tsp = [(i * 512, min(512, T_ - i * 512)) for i in range((T_ + 511) // 512)]
    KQ, NKQ, ERT = c.KQ, c.NKQ, c.ERT
    dram = {}

    def din(name, shape, dtype):
        dram[name] = nc.dram_tensor(name, shape, dtype, kind="ExternalInput").ap()
        return dram[name]

    def dout(name, shape, dtype):
        dram[name] = nc.dram_tensor(name, shape, dtype, kind="ExternalOutput").ap()
        return dram[name]

    def dint(name, shape, dtype):
        dram[name] = nc.dram_tensor(name, shape, dtype, kind="Internal").ap()
        return dram[name]

    x_in = din("x_in", [T_, D_], F32)
    prev_in = din("prev_in", [H_, D_], F32)
    NBF = (c.NOT + 1) // 2 - (1 if c.NOT % 2 == 1 else 0)
    w_ip_t = din("w_ip_t", [c.NKQ, NBF, 128, c.KQ * 256], F32)
    w_ip_rag = din("w_ip_rag", [D_, c.PROJ - NBF * 256], F32)
    w_convc = din("w_convc", [128, c.XOT * c.K], F32)
    w_out_t = din("w_out_t", [c.NKQ, D_ // 512, 128, c.KQ * 512], F32)
    w_ffn_t = din("w_ffn_t", [c.NKQ, D_ // 512, 128, c.KQ * 512], F32)
    w_lat_t = din("w_lat_t", [128, KT * c.ER], F32)
    w_lBt = din("w_lBt", [c.ER, D_], F32)
    probs_col = din("probs_col", [128, c.ER // 128], F32)
    mup_cols = din("mup_cols", [128, c.NOT], F32)
    rows3_in = din("rows3_in", [65, D_], F32)
    alog_row = din("alog_row", [1, NH], F32)
    dtb_col = din("dtb_col", [NH, 1], F32)
    d_cols = din("d_cols", [128, NH], F32)
    ident_bf = din("ident_bf", [128, 128], BF16)
    ident_f = din("ident_f", [64, 64], F32)
    ut_bf = din("ut_bf", [128, 128], BF16)
    tri_bf = din("tri_bf", [128, 128], BF16)
    ones_bf = din("ones_bf", [1, 128], BF16)
    ones65_bf = din("ones65_bf", [65, 128], BF16)
    ones_f = din("ones_f", [1, 128], F32)
    fconsts = din("fconsts", [128, 8], F32)
    out = dout("out", [T_, D_], F32)

    ydump = dint("ydump", [T_, D_], F32)
    lnspill = dint("lnspill", [T_, D_], BF16)
    scspill = dint("scspill", [TH, D_], BF16)
    zspill = dint("zspill", [2 * D_ + 2 * c.S, TH], BF16)
    ffnspill = dint("ffnspill", [T_, D_], BF16)
    yspill = dint("yspill", [T_, D_], BF16)

    sb = b.sbuf
    poolA = sb("poolA", [128, KT * T_], BF16)
    PB = max(KT * H_ + 2 * (3 + TH),
             KT * ERT * 128 + 2 * ERT * 512 + ERT * T_,
             NH * 128 + 2 * D_ + 512,
             2 * 512)
    poolB = sb("poolB", [128, PB], BF16)
    poolF = sb("poolF", [128, max(D_, 4096, 2560 + TH, 2 * TH)], F32)
    wq = [sb(f"wq{i}", [128, KQ * 512], BF16) for i in range(2)]
    xs_tm = [sb(f"xs_tm{i}", [128, D_], BF16) for i in range(2)]
    sz_tm = sb("sz_tm", [128, D_], BF16)
    bcc = [sb(f"bcc{i}", [128, 2 * 128], BF16) for i in range(2)]

    ln_bf = sz_tm
    lnT = poolA
    lnTh = poolB
    xbc2 = [poolB[:, KT * H_ + i * (3 + TH):KT * H_ + (i + 1) * (3 + TH)]
            for i in range(2)]
    lAT = poolB[:, 0:KT * ERT * 128]
    lB0 = KT * ERT * 128
    lBtq = [[poolB[:, lB0 + (p * ERT + et) * 512: lB0 + (p * ERT + et) * 512 + 512]
             for et in range(ERT)] for p in range(2)]
    hprime = poolB[:, lB0 + 2 * ERT * 512:lB0 + 2 * ERT * 512 + ERT * T_]
    diagD = poolB[:, 0:NH * 128]
    nw_bc = poolB[:, NH * 128:NH * 128 + D_]
    yhat_bf = poolB[:, NH * 128 + D_:NH * 128 + 2 * D_]
    xw8 = poolB[:, NH * 128 + 2 * D_:NH * 128 + 2 * D_ + 512]
    gbufs = [poolB[:, i * 512:(i + 1) * 512] for i in range(2)]
    xa = poolF[:, 0:D_]
    dtscr_f = poolF[0:NH, 0:TH]
    dt_t2 = poolF[0:64, TH:2 * TH]
    yfull = poolF[:, 0:D_]
    xrb = [poolF[:, i * 512:(i + 1) * 512] for i in range(2)]
    ob = [poolF[:, 1024 + i * 512:1024 + (i + 1) * 512] for i in range(2)]
    gsil = poolF[:, 2048:2560]
    acc_cv = poolF[:, poolF.shape[1] - TH:]
    sqjunkA = xs_tm[0]
    sc_bf = xs_tm[1]

    ev_bf = sb("ev_bf", [128, max(TH, 1024)], BF16)
    convw = sb("convw", [128, c.XOT * c.K], F32)
    mupc = sb("mupc", [128, c.NOT], F32)
    dt_sp = sb("dt_sp", [NH, TH], F32)
    H = sb("H", [128, NH * 64], F32)
    H_bf = sb("H_bf", [128, NH * 64], BF16)
    dtT = sb("dtT", [128, NH], F32)
    logdtT = sb("logdtT", [128, NH], F32)
    dtAT = sb("dtAT", [128, NH], BF16)
    colst = sb("colst", [128, NH], F32)
    expcT = sb("expcT", [128, NH], F32)
    a_bc = sb("a_bc", [128, NH], F32)
    a_col = sb("a_col", [NH, 1], F32)
    atot_bc = sb("atot_bc", [128, NH], F32)
    wtilT = sb("wtilT", [128, NH], F32)
    rowbuf = sb("rowbuf", [1, 128], F32)
    dtsum = sb("dtsum", [NH, 2], F32)
    gsb = sb("gsb", [128, 128], F32)
    btile = sb("btile", [128, 128], BF16)
    e8 = sb("e8", [128, 8 * 128], BF16)
    mstat8 = sb("mstat8", [128, 8 * 128], BF16)
    t1buf = sb("t1buf", [128, 512], F32)
    dcol_all = sb("dcol_all", [128, NH], F32)
    ident_s = sb("ident_s", [128, 128], BF16)
    identf_s = sb("identf_s", [64, 64], F32)
    ut_s = sb("ut_s", [128, 128], BF16)
    tri_s = sb("tri_s", [128, 128], BF16)
    ones_s = sb("ones_s", [1, 128], BF16)
    ones65 = sb("ones65", [65, 128], BF16)
    onesf_s = sb("onesf_s", [1, 128], F32)
    dtb_s = sb("dtb_s", [NH, 1], F32)
    arow_s = sb("arow_s", [1, NH], F32)
    arow_bf = sb("arow_bf", [1, NH], BF16)
    rows3 = sb("rows3", [65, D_], BF16)
    probsc = sb("probsc", [128, max(ERT, 1)], F32)
    lncol = sb("lncol", [128, 8], F32)
    lncol2 = sb("lncol2", [128, 12 * (T_ // 128 + c.HALO // 128)], F32)
    lnwT = sb("lnwT", [128, KT], F32)
    fconst = sb("fconst", [128, 8], F32)
    scr = sb("scr", [128, 8], F32)

    pb = [b.psum(f"pb{i}", [128, 512], F32) for i in range(7)]
    pbf = b.psum("pbf", [128, 1024], BF16)

    s_te = b.sem("s_te")
    s_dve = b.sem("s_dve")
    s_act = b.sem("s_act")
    s_ldx = b.sem("s_ldx", 16)
    s_ldw = [b.sem(f"s_ldw{i}", 16) for i in range(2)]
    s_st = b.sem("s_st", 16)
    s_misc = b.sem("s_misc", 16)
    s_gp = b.sem("s_gp_eng")
    s_lb = [b.sem(f"s_lb{i}", 16) for i in range(2)]
    s_xa = b.sem("s_xa", 16)
    s_xb = b.sem("s_xb", 16)
    s_lnt = b.sem("s_lnt", 16)
    s_rel2 = [b.sem(f"s_rel2{i}", 16) for i in range(2)]
    s_sz = b.sem("s_sz", 16)
    s_gx = [b.sem(f"s_gx{i}", 16) for i in range(2)]

    def ld_sync(dst, src, transpose=False, sem=None):
        return b.run("sync", lambda e, d=dst, s=src, t=transpose:
                     e.dma_start(out=d, in_=s, transpose=t), inc=(sem or s_ldx))

    def ld_w(dst, src, wi):
        return b.run("gpsimd", lambda e, d=dst, s=src: e.dma_start(out=d, in_=s),
                     inc=s_ldw[wi])

    def store(dst, src, gate_sem, gate_val):
        b.wait("scalar", gate_sem, gate_val)
        return b.run("scalar", lambda e, d=dst, s=src: e.dma_start(out=d, in_=s), inc=s_st)

    def rb_act(tile):
        p = tile.shape[0]
        b.wait("scalar", s_act, s_act.n)
        return b.run("scalar", lambda e, t=tile, p=p: e.activation(scr[0:p, 6:7], t, AF.Copy), inc=s_act)

    def rb_dve(tile):
        p = tile.shape[0]
        b.wait("vector", s_dve, s_dve.n)
        return b.run("vector", lambda e, t=tile, p=p: e.tensor_copy(scr[0:p, 7:8], t), inc=s_dve)

    def barrier():
        te0, dv0, ac0 = s_te.n, s_dve.n, s_act.n
        b.wait("tensor", s_dve, dv0)
        b.wait("tensor", s_act, ac0)
        b.wait("vector", s_te, te0)
        b.wait("vector", s_act, ac0)
        b.wait("scalar", s_te, te0)
        b.wait("scalar", s_dve, dv0)

    ld_sync(fconst[:], fconsts)
    for _i, _v in enumerate([1e-5, 1.0 / D_, -0.5, -1.0, 1.0, 0.0, 3.0e37, 0.5]):
        nc.const_aps.register(F32, float(_v), fconst[:, _i:_i + 1])
    ld_sync(ident_s[:], ident_bf)
    ld_sync(identf_s[:], ident_f)
    ld_sync(ut_s[:], ut_bf)
    ld_sync(tri_s[:], tri_bf)
    ld_sync(ones_s[:], ones_bf)
    ld_sync(ones65[:], ones65_bf)
    ld_sync(onesf_s[:], ones_f)
    ld_sync(convw[:], w_convc)
    ld_sync(mupc[:], mup_cols)
    ld_sync(dtb_s[:], dtb_col)
    ld_sync(arow_s[:], alog_row)
    ld_sync(dcol_all[:], d_cols)
    cst = ld_sync(probsc[:], probs_col)
    gw0 = b.run("gpsimd", lambda e: e.dma_start(out=rows3[:], in_=rows3_in), inc=s_misc)

    b.wait("vector", s_ldx, cst)
    b.run("vector", lambda e: e.memset(H[:, 0:NH * 64], 0.0))
    b.run("vector", lambda e: e.memset(H_bf[:, 0:NH * 64], 0.0), inc=s_dve)
    b.run("vector", lambda e: e.memset(lncol2[:], 0.0))
    b.run("vector", lambda e: e.memset(xbc2[0][:, 0:3], 0.0))
    zmv = b.run("vector", lambda e: e.memset(xbc2[1][:, 0:3], 0.0), inc=s_dve)

    b.wait("tensor", s_ldx, cst)
    b.wait("tensor", s_misc, gw0)
    for kk in range(KT):
        tw_ = b.run("tensor", lambda e, kk=kk: e.matmul(
            pb[6][:, kk:kk + 1], rows3[0:1, kk * 128:(kk + 1) * 128],
            ones_s[0:1, 0:1], start=True, stop=True),
            inc=(s_te if kk == KT - 1 else None))
    b.wait("vector", s_te, tw_)
    lwv = b.run("vector", lambda e: e.tensor_copy(lnwT[:], pb[6][:, 0:KT]), inc=s_dve)
    b.wait("scalar", s_ldx, cst)
    b.run("scalar", lambda e: e.activation(arow_s[:], arow_s[:], AF.Exp))
    b.run("scalar", lambda e: e.mul(arow_s[:], arow_s[:], -1.0))
    aa = b.run("scalar", lambda e: e.copy(arow_bf[:], arow_s[:]), inc=s_act)
    b.wait("tensor", s_act, aa)
    tt = b.run("tensor", lambda e: e.matmul(pb[2][:, 0:NH], ones_s[0:1, :],
                                            arow_bf[0:1, :], start=True, stop=True), inc=s_te)
    t2 = b.run("tensor", lambda e: e.transpose(pb[3][0:NH, 0:1], arow_s[0:1, 0:NH],
                                               identf_s[0:1, 0:1]), inc=s_te)
    b.wait("vector", s_te, t2)
    b.run("vector", lambda e: e.tensor_copy(a_bc[:], pb[2][:, 0:NH]))
    acopy = b.run("vector", lambda e: e.tensor_copy(a_col[:], pb[3][0:NH, 0:1]), inc=s_dve)

    pre_lw = [ld_w(wq[q][:, 0:KQ * 256], w_ip_t[q, 0], q) for q in range(2)]

    b.mark("A1")
    inv_d = 1.0 / D_
    HD2 = D_ // 2
    CH2 = KT // 2
    gp_a1_base = s_gp.n
    ap_prev = [0, 0]
    tr_prev = [0, 0]
    nst = [0, 0]
    ev_bank = [0, 0, 0, 0]
    rot = 0
    b.wait("tensor", s_dve, acopy)
    for i in range(NT + HT):
        halo = i >= NT
        q = 12 * i
        cl = [lncol2[:, q + k:q + k + 1] for k in range(12)]
        src = prev_in[(i - NT) * 128:(i - NT + 1) * 128, :] if halo else \
            x_in[i * 128:(i + 1) * 128, :]
        b.wait("sync", s_act, ap_prev[0])
        ldA = ld_sync(xa[:, 0:HD2], src[:, 0:HD2], sem=s_xa)
        b.wait("sync", s_act, ap_prev[1])
        ldB = ld_sync(xa[:, HD2:D_], src[:, HD2:D_], sem=s_xb)
        b.wait("vector", s_xa, ldA)
        b.run("vector", lambda e, d=cl[0]: e.tensor_reduce(d, xa[:, 0:HD2], AX.X, ALU.add))
        b.wait("vector", s_xb, ldB)
        vsum = b.run("vector", lambda e, d=cl[1]: e.tensor_reduce(d, xa[:, HD2:D_], AX.X, ALU.add), inc=s_dve)
        if i == 0:
            b.wait("scalar", s_dve, zmv)
        b.wait("scalar", s_xa, ldA)
        b.run("scalar", lambda e, d=cl[2]: e.activation(
            sqjunkA[:, 0:HD2], xa[:, 0:HD2], AF.Square, accum_out=d))
        b.wait("scalar", s_xb, ldB)
        aq = b.run("scalar", lambda e, d=cl[3]: e.activation(
            sqjunkA[:, HD2:D_], xa[:, HD2:D_], AF.Square, accum_out=d), inc=s_act)
        b.wait("vector", s_act, aq)
        b.wait("vector", s_dve, vsum)
        b.run("vector", lambda e, d=cl[4], a=cl[0], bb=cl[1]: e.tensor_tensor(d, a, bb, ALU.add))
        v = b.run("vector", lambda e, d=cl[5], a=cl[2], bb=cl[3]: e.tensor_tensor(d, a, bb, ALU.add), inc=s_dve)
        b.wait("vector", s_dve, v)
        b.run("vector", lambda e, d=cl[4]: e.tensor_scalar(d, d, inv_d, None, ALU.mult))
        v = b.run("vector", lambda e, d=cl[5]: e.tensor_scalar(d, d, inv_d, None, ALU.mult), inc=s_dve)
        b.wait("vector", s_dve, v)
        v = b.run("vector", lambda e, d=cl[11], a=cl[4]: e.tensor_tensor(d, a, a, ALU.mult), inc=s_dve)
        b.wait("vector", s_dve, v)
        vvar = b.run("vector", lambda e, d=cl[5], a=cl[11]: e.tensor_tensor(d, d, a, ALU.subtract), inc=s_dve)
        b.wait("scalar", s_dve, vvar)
        b.run("scalar", lambda e, d=cl[6], a=cl[5]: e.activation(d, a, AF.Ln, bias=1e-5))
        arstd = b.run("scalar", lambda e, d=cl[6]: e.activation(d, d, AF.Exp, scale=-0.5), inc=s_act)
        b.wait("vector", s_act, arstd)
        v = b.run("vector", lambda e, d=cl[11], a=cl[5], bb=cl[6]: e.tensor_tensor(d, a, bb, ALU.mult), inc=s_dve)
        b.wait("vector", s_dve, v)
        vr2 = b.run("vector", lambda e, d=cl[11], bb=cl[6]: e.tensor_tensor(d, d, bb, ALU.mult), inc=s_dve)
        b.wait("scalar", s_dve, vr2)
        b.run("scalar", lambda e, d=cl[7], a=cl[11]: e.activation(d, a, AF.Ln, bias=1e-5))
        ar = b.run("scalar", lambda e, d=cl[7]: e.activation(d, d, AF.Exp, scale=-0.5), inc=s_act)
        b.wait("vector", s_act, ar)
        b.run("vector", lambda e, d=cl[8], a=cl[4], bb=cl[6]: e.tensor_tensor(d, a, bb, ALU.mult))
        v = b.run("vector", lambda e, d=cl[9], a=cl[6], bb=cl[7]: e.tensor_tensor(d, a, bb, ALU.mult), inc=s_dve)
        b.wait("vector", s_dve, v)
        v = b.run("vector", lambda e, d=cl[8]: e.tensor_scalar(d, d, -1.0, None, ALU.mult), inc=s_dve)
        b.wait("vector", s_dve, v)
        b.run("vector", lambda e, d=cl[11], a=cl[6]: e.tensor_copy(d, a))
        vbias = b.run("vector", lambda e, d=cl[10], a=cl[8], bb=cl[7]: e.tensor_tensor(d, a, bb, ALU.mult), inc=s_dve)
        b.wait("scalar", s_dve, vbias)
        ap_new = [0, 0]
        for hf in range(2):
            c0, c1 = (0, HD2) if hf == 0 else (HD2, D_)
            b.wait("scalar", s_te, tr_prev[hf])
            if not halo:
                b.wait("scalar", s_st, nst[hf])
                b.run("scalar", lambda e, c0=c0, c1=c1, s=cl[11], bi=cl[8]: e.activation(
                    ln_bf[:, c0:c1], xa[:, c0:c1], AF.Identity, bias=bi, scale=s))
                ga_ = rb_act(ln_bf[:, c0:c0 + 1])
                nst[hf] = store(lnspill[i * 128:(i + 1) * 128, c0:c1],
                                ln_bf[:, c0:c1], s_act, ga_)
            ap_new[hf] = b.run("scalar", lambda e, c0=c0, c1=c1, s=cl[9], bi=cl[10]: e.activation(
                sc_bf[:, c0:c1], xa[:, c0:c1], AF.Identity, bias=bi, scale=s), inc=s_act)
            b.wait("tensor", s_act, ap_new[hf])
            for cb in range(hf * CH2, (hf + 1) * CH2):
                bank = rot % 4
                rot += 1
                b.wait("tensor", s_gp, ev_bank[bank])
                tr = b.run("tensor", lambda e, cb=cb, bank=bank: e.matmul(
                    pb[bank][:, 0:128], sc_bf[:, cb * 128:(cb + 1) * 128],
                    ident_s[:], start=True, stop=True), inc=s_te)
                b.wait("vector", s_te, tr)
                if halo:
                    dst = lnTh[:, cb * H_ + (i - NT) * 128:cb * H_ + (i - NT) * 128 + 128]
                else:
                    dst = lnT[:, cb * T_ + i * 128:cb * T_ + i * 128 + 128]
                ev_bank[bank] = b.run("vector", lambda e, dst=dst, bank=bank, cb=cb: e.tensor_scalar(
                    dst, pb[bank][:, 0:128], lnwT[:, cb:cb + 1], None, ALU.mult), inc=s_gp)
            tr_prev[hf] = s_te.n
        ap_prev = ap_new
    gp_a1_end = s_gp.n

    b.mark("A2")
    if c.phases < 2:
        return _finalize(b, nc, c, dram)
    n_blk = (c.NOT + 1) // 2
    pfree = {0: (None, 0), 1: (None, 0)}
    wfree = {0: 0, 1: 0}
    conv_free = {0: 0, 1: 0}
    evst = 0
    sil_prev = 0
    qi = 0
    b.wait("tensor", s_gp, gp_a1_end)
    for blk in range(n_blk):
        o0 = blk * 2
        ntile = min(2, c.NOT - o0)
        cols0 = o0 * 128
        cols1 = min((o0 + 2) * 128, c.PROJ)
        bw = cols1 - cols0
        psets = [[pb[0], pb[1], pb[2]], [pb[3], pb[4], pb[5]]]
        for oj in range(ntile):
            sfree, vfree = pfree[oj]
            if sfree is not None:
                b.wait("tensor", sfree, vfree)
        tdone = [0] * ntile
        for kq in range(NKQ):
            wi = qi % 2
            if qi < 2:
                lw = pre_lw[qi]
            else:
                b.wait("gpsimd", s_te, wfree[wi])
                if bw == 256:
                    lw = ld_w(wq[wi][:, 0:KQ * 256], w_ip_t[kq, blk], wi)
                else:
                    lw = ld_w(wq[wi][:, 0:KQ * bw].rearrange("p (k o) -> p k o", k=KQ),
                              w_ip_rag[kq * KQ * 128:min((kq + 1) * KQ * 128, D_), :]
                              .rearrange("(k p) o -> p k o", p=128), wi)
            qi += 1
            b.wait("tensor", s_ldw[wi], lw)
            for oj in range(ntile):
                o = o0 + oj
                is_dt = (o == c.NOT - 1)
                owid = c.DTW if is_dt else 128
                is_z = o < c.ZOT
                ps = psets[oj]

                def mms(eng, wi=wi, kq=kq, oj=oj, owid=owid, is_z=is_z, ps=ps, bw=bw):
                    ins = None
                    for kk in range(KQ):
                        k = kq * KQ + kk
                        if k >= KT:
                            break
                        st = wq[wi][:, kk * bw + oj * 128: kk * bw + oj * 128 + owid]
                        first = (k == 0)
                        last = (k == KT - 1)
                        for hi, (t0, tw) in enumerate(tsp):
                            ins = eng.matmul(ps[hi][0:owid, 0:tw], st,
                                             lnT[:, k * T_ + t0:k * T_ + t0 + tw],
                                             start=first, stop=last)
                        if not is_z:
                            ins = eng.matmul(ps[2][0:owid, 0:H_], st,
                                             lnTh[:, k * H_:(k + 1) * H_],
                                             start=first, stop=last)
                    return ins
                tdone[oj] = b.run("tensor", mms, inc=s_te)
            wfree[wi] = s_te.n
        conv_jobs = []
        for oj in range(ntile):
            o = o0 + oj
            is_dt = (o == c.NOT - 1)
            owid = c.DTW if is_dt else 128
            is_z = o < c.ZOT
            ps = psets[oj]
            mupcol = mupc[:, o:o + 1]
            if is_z:
                b.wait("scalar", s_te, tdone[oj])
                b.wait("scalar", s_st, evst)
                for hi, (t0, tw) in enumerate(tsp):
                    b.run("scalar", lambda e, hi=hi, t0=t0, tw=tw, ps=ps, mc=mupcol: e.activation(
                        ev_bf[:, t0:t0 + tw], ps[hi][:, 0:tw], AF.Silu, scale=mc))
                ga = rb_act(ev_bf[:, 0:1])
                pfree[oj] = (s_act, ga)
                evst = store(zspill[o * 128:(o + 1) * 128, H_:TH], ev_bf[:, 0:T_], s_act, ga)
            else:
                b.wait("vector", s_te, tdone[oj])
                b.wait("vector", s_dve, conv_free[oj])
                xb = xbc2[oj]
                for hi, (t0, tw) in enumerate(tsp):
                    b.run("vector", lambda e, hi=hi, t0=t0, tw=tw, ps=ps, mc=mupcol, w=owid, xb=xb: e.tensor_scalar(
                        xb[0:w, 3 + H_ + t0:3 + H_ + t0 + tw], ps[hi][0:w, 0:tw],
                        mc[0:w, :], None, ALU.mult))
                vv1 = b.run("vector", lambda e, ps=ps, mc=mupcol, w=owid, xb=xb: e.tensor_scalar(
                    xb[0:w, 3:3 + H_], ps[2][0:w, 0:H_], mc[0:w, :], None, ALU.mult), inc=s_dve)
                pfree[oj] = (s_dve, vv1)
                if is_dt:
                    b.run("vector", lambda e, w=owid, xb=xb: e.tensor_copy(
                        dtscr_f[0:w, 0:TH], xb[0:w, 3:3 + TH]))
                    rb_dve(dtscr_f[0:NH, 0:1])
                else:
                    conv_jobs.append((oj, o - c.ZOT, vv1))
        for (oj, xo, vv1) in conv_jobs:
            xb = xbc2[oj]
            b.wait("vector", s_act, sil_prev)
            b.wait("vector", s_dve, vv1)
            b.run("vector", lambda e, xo=xo, xb=xb: e.tensor_scalar(
                acc_cv[:, 0:TH], xb[:, 0:TH],
                convw[:, (xo * c.K):(xo * c.K) + 1], None, ALU.mult))
            for kk in range(1, c.K):
                gcv = b.run("vector", lambda e, xo=xo, kk=kk, xb=xb: e.scalar_tensor_tensor(
                    acc_cv[:, 0:TH], xb[:, kk:kk + TH],
                    convw[:, (xo * c.K + kk):(xo * c.K + kk + 1)],
                    acc_cv[:, 0:TH], ALU.mult, ALU.add),
                    inc=(s_dve if kk == c.K - 1 else None))
            conv_free[oj] = gcv
            b.wait("scalar", s_dve, gcv)
            b.wait("scalar", s_st, evst)
            sil = b.run("scalar", lambda e: e.activation(
                ev_bf[:, 0:TH], acc_cv[:, 0:TH], AF.Silu), inc=s_act)
            sil_prev = sil
            ga = rb_act(ev_bf[:, 0:1])
            evst = store(zspill[D_ + xo * 128:D_ + (xo + 1) * 128, :],
                         ev_bf[:, 0:TH], s_act, ga)
    a2_spills = s_st.n
    barrier()

    b.run("vector", lambda e: e.tensor_scalar(dtscr_f[0:NH, 0:TH], dtscr_f[0:NH, 0:TH],
                                              dtb_s[0:NH, :], None, ALU.add))
    vr = b.run("vector", lambda e: e.tensor_scalar(dt_sp[0:NH, :], dtscr_f[0:NH, 0:TH],
                                                   0.0, None, ALU.max), inc=s_dve)
    b.wait("scalar", s_dve, vr)
    b.run("scalar", lambda e: e.activation(dt_t2[0:NH, 0:TH], dtscr_f[0:NH, 0:TH], AF.Abs))
    b.run("scalar", lambda e: e.activation(dt_t2[0:NH, 0:TH], dt_t2[0:NH, 0:TH], AF.Exp, scale=-1.0))
    asp = b.run("scalar", lambda e: e.activation(dt_t2[0:NH, 0:TH], dt_t2[0:NH, 0:TH], AF.Ln, bias=1.0), inc=s_act)
    b.wait("vector", s_act, asp)
    vsp = b.run("vector", lambda e: e.tensor_tensor(dt_sp[0:NH, :], dt_sp[0:NH, :],
                                                    dt_t2[0:NH, 0:TH], ALU.add), inc=s_dve)

    def emit_reload(ci, slot, main):
        col0 = (H_ + ci * 128) if main else ci * 128
        b.wait("sync", s_st, a2_spills)
        b.wait("sync", s_te, s_te.n)
        b.wait("sync", s_dve, s_dve.n)
        b.wait("sync", s_act, s_act.n)
        ld_sync(xs_tm[slot][:], zspill[D_:2 * D_, col0:col0 + 128], transpose=True,
                sem=s_rel2[slot])
        return (slot, ld_sync(bcc[slot].rearrange("p (a t) -> p a t", a=2),
                              zspill[2 * D_:2 * D_ + 2 * c.S, col0:col0 + 128]
                              .rearrange("(a p) t -> p a t", p=128), sem=s_rel2[slot]))

    def chunk_prep(ci, main, slot, rel_tgt):
        mp = getattr(c, "microphase", 9)
        barrier()
        col0 = (H_ + ci * 128) if main else ci * 128
        if mp < 0.5:
            return dict(vcs=0, aat=0, aex=0, vbt=0, awt=0, vg=0)
        b.wait("tensor", s_dve, vsp)
        ta = b.run("tensor", lambda e, c0=col0: e.transpose(
            pb[2][:, 0:NH], dt_sp[0:NH, c0:c0 + 128], identf_s[0:NH, 0:NH]), inc=s_te)
        if mp < 1:
            tq = b.run("tensor", lambda e: e.matmul(
                pb[3][:, 0:128], ident_s[:], ut_s[:], start=True, stop=True), inc=s_te)
            b.wait("vector", s_te, tq)
            b.run("vector", lambda e: e.tensor_copy(gsb[:], pb[3][:, 0:128]), inc=s_dve)
            return dict(vcs=0, aat=0, aex=0, vbt=0, awt=0, vg=0)
        b.wait("vector", s_te, ta)
        vda = b.run("vector", lambda e: e.tensor_copy(dtT[:], pb[2][:, 0:NH]), inc=s_dve)
        if mp < 2:
            return dict(vcs=0, aat=0, aex=0, vbt=0, awt=0, vg=0)
        b.wait("scalar", s_dve, vda)
        ala = b.run("scalar", lambda e: e.activation(logdtT[:], dtT[:], AF.Ln), inc=s_act)
        if mp < 2.2:
            return dict(vcs=0, aat=0, aex=0, vbt=0, awt=0, vg=0)
        b.wait("vector", s_dve, vda)
        vab = b.run("vector", lambda e: e.tensor_tensor(dtAT[:], dtT[:], a_bc[:], ALU.mult), inc=s_dve)
        if mp < 2.4:
            return dict(vcs=0, aat=0, aex=0, vbt=0, awt=0, vg=0)
        b.wait("tensor", s_dve, vab)
        tb_ = b.run("tensor", lambda e: e.matmul(pb[2][:, 0:NH], ut_s[:], dtAT[:],
                                                 start=True, stop=True), inc=s_te)
        if mp < 2.6:
            return dict(vcs=0, aat=0, aex=0, vbt=0, awt=0, vg=0)
        b.wait("vector", s_te, tb_)
        b.wait("vector", s_act, ala)
        vcs = b.run("vector", lambda e: e.tensor_tensor(colst[:], logdtT[:], pb[2][:, 0:NH], ALU.subtract), inc=s_dve)
        if mp < 2.8:
            return dict(vcs=0, aat=0, aex=0, vbt=0, awt=0, vg=0)
        b.wait("scalar", s_te, tb_)
        b.wait("scalar", s_dve, vcs)
        aex = b.run("scalar", lambda e: e.activation(expcT[:], pb[2][:, 0:NH], AF.Exp), inc=s_act)
        if mp < 3:
            return dict(vcs=vcs, aat=0, aex=aex, vbt=0, awt=0, vg=0)
        vq2 = b.run("vector", lambda e, c0=col0: e.tensor_reduce(
            dtsum[0:NH, 0:1], dt_sp[0:NH, c0:c0 + 128], AX.X, ALU.add), inc=s_dve)
        b.wait("vector", s_dve, vq2)
        vds = b.run("vector", lambda e: e.tensor_tensor(dtsum[0:NH, 1:2], dtsum[0:NH, 0:1],
                                                        a_col[0:NH, :], ALU.mult), inc=s_dve)
        b.wait("tensor", s_dve, vds)
        b.wait("tensor", s_act, aex)
        b.wait("tensor", s_dve, vcs)
        if mp < 4:
            return dict(vcs=vcs, aat=0, aex=aex, vbt=0, awt=0, vg=0)
        tc_ = b.run("tensor", lambda e: e.transpose(pb[2][0:1, 0:NH], dtsum[0:NH, 1:2],
                                                    identf_s[0:NH, 0:NH]), inc=s_te)
        b.wait("vector", s_te, tc_)
        vrow = b.run("vector", lambda e: e.tensor_copy(rowbuf[0:1, 0:NH], pb[2][0:1, 0:NH]), inc=s_dve)
        b.wait("tensor", s_dve, vrow)
        if mp < 5:
            return dict(vcs=vcs, aat=0, aex=aex, vbt=0, awt=0, vg=0)
        td = b.run("tensor", lambda e: e.matmul(pb[2][:, 0:NH], onesf_s[0:1, :],
                                                rowbuf[0:1, 0:NH], start=True, stop=True), inc=s_te)
        b.wait("scalar", s_te, td)
        aat = b.run("scalar", lambda e: e.activation(atot_bc[:], pb[2][:, 0:NH], AF.Exp), inc=s_act)
        pr = dict(vcs=vcs, aat=aat, aex=aex)
        if not main:
            b.wait("vector", s_te, td)
            b.wait("vector", s_dve, vcs)
            b.wait("vector", s_act, aat)
            vwa = b.run("vector", lambda e: e.tensor_tensor(wtilT[:], colst[:], pb[2][:, 0:NH], ALU.add), inc=s_dve)
            b.wait("scalar", s_dve, vwa)
            pr["awt"] = b.run("scalar", lambda e: e.activation(wtilT[:], wtilT[:], AF.Exp), inc=s_act)
        b.wait("tensor", s_rel2[rel_tgt[0]], rel_tgt[1])
        te_ = b.run("tensor", lambda e, sl=slot: e.transpose(
            pbf[:, 0:128], bcc[sl][:, 0:128], ident_s[:]), inc=s_te)
        b.wait("vector", s_te, te_)
        pr["vbt"] = b.run("vector", lambda e: e.tensor_copy(btile[:], pbf[:, 0:128]), inc=s_dve)
        if main:
            b.wait("tensor", s_act, aat)
            tg = b.run("tensor", lambda e, sl=slot: e.matmul(
                pb[2][:, 0:128], bcc[sl][:, 0:128], bcc[sl][:, 128:256],
                start=True, stop=True), inc=s_te)
            b.wait("vector", s_te, tg)
            pr["vg"] = b.run("vector", lambda e: e.tensor_tensor(gsb[:], pb[2][:, 0:128],
                                                                 tri_s[:], ALU.mult), inc=s_dve)
        return pr

    def emit_state_update(g, slot, pr, main):
        if main:
            src = e8.rearrange("p (h t) -> p h t", h=8)[:, :, 127:128].broadcast_to([128, 8, 64])
            b.wait("vector", s_act, pr["e_done"])
        else:
            src = wtilT[:, g * 8:(g + 1) * 8].broadcast_to([128, 8, 64])
            b.wait("vector", s_act, pr["awt"])
        vxw = b.run("vector", lambda e, g=g, sl=slot, s=src: e.tensor_tensor(
            xw8.rearrange("p (h t) -> p h t", h=8),
            xs_tm[sl].rearrange("p (h t) -> p h t", h=NH)[:, g * 8:(g + 1) * 8, :],
            s, ALU.mult), inc=s_dve)
        b.wait("tensor", s_dve, max(vxw, pr["vbt"], pr.get("hfree", 0)))
        th = b.run("tensor", lambda e: e.matmul(pb[6][:, 0:512], btile[:], xw8,
                                                start=True, stop=True), inc=s_te)
        b.wait("vector", s_te, th)
        b.wait("vector", s_act, pr["aat"])
        b.run("vector", lambda e, g=g: e.tensor_tensor(
            H.rearrange("p (h t) -> p h t", h=NH)[:, g * 8:(g + 1) * 8, :],
            H.rearrange("p (h t) -> p h t", h=NH)[:, g * 8:(g + 1) * 8, :],
            atot_bc[:, g * 8:(g + 1) * 8].broadcast_to([128, 8, 64]), ALU.mult))
        b.run("vector", lambda e, g=g: e.tensor_tensor(
            H[:, g * 512:g * 512 + 512], H[:, g * 512:g * 512 + 512], pb[6][:, 0:512], ALU.add))
        vH = b.run("vector", lambda e, g=g: e.tensor_copy(
            H_bf[:, g * 512:g * 512 + 512], H[:, g * 512:g * 512 + 512]), inc=s_dve)
        return vH, vxw

    b.mark("A2.5")
    if c.phases < 2.2:
        return _finalize(b, nc, c, dram)
    relq = []
    rel0 = emit_reload(0, 0, main=False)
    if c.phases < 2.4:
        return _finalize(b, nc, c, dram)
    hf = 0
    for hc in range(c.HC):
        slot = hc % 2
        nrel = emit_reload(hc + 1, (hc + 1) % 2, main=False) if hc + 1 < c.HC else None
        pr = chunk_prep(hc, False, slot, rel0 if hc == 0 else relq.pop())
        if nrel is not None:
            relq.append(nrel)
        if c.phases < 2.6:
            return _finalize(b, nc, c, dram)
        ng_lim = c.NG if c.phases >= 3 else max(1, int(round((c.phases - 2.6) * 20)))
        for g in range(min(c.NG, ng_lim)):
            pr["hfree"] = hf
            hf = emit_state_update(g, slot, pr, main=False)[0]
        if c.phases < 3:
            return _finalize(b, nc, c, dram)
    barrier()

    b.mark("A3")
    if c.phases < 4:
        return _finalize(b, nc, c, dram)
    b.wait("sync", s_te, s_te.n)
    b.wait("sync", s_dve, s_dve.n)
    for k in range(KT):
        ld_sync(lnT[:, k * T_:(k + 1) * T_],
                lnspill[:, k * 128:(k + 1) * 128], transpose=True, sem=s_lnt)
    lnT2_ready = s_lnt.n
    b.wait("gpsimd", s_te, s_te.n)
    b.wait("gpsimd", s_dve, s_dve.n)
    b.wait("gpsimd", s_act, s_act.n)
    lg = b.run("gpsimd", lambda e: e.dma_start(out=lAT[:], in_=w_lat_t), inc=s_misc)
    b.wait("tensor", s_lnt, lnT2_ready)
    b.wait("tensor", s_misc, lg)
    h_ev = {0: 0, 1: 0}
    hq = 0
    for et in range(ERT):
        for hi, (t0, tw) in enumerate(tsp):
            bank = hq % 2
            hq += 1
            b.wait("tensor", s_dve, h_ev[bank])

            def hmm(eng, et=et, t0=t0, tw=tw, bank=bank):
                ins = None
                for k in range(KT):
                    ins = eng.matmul(pb[bank][:, 0:tw],
                                     lAT[:, (k * ERT + et) * 128:(k * ERT + et) * 128 + 128],
                                     lnT[:, k * T_ + t0:k * T_ + t0 + tw],
                                     start=(k == 0), stop=(k == KT - 1))
                return ins
            th = b.run("tensor", hmm, inc=s_te)
            b.wait("vector", s_te, th)
            h_ev[bank] = b.run("vector", lambda e, et=et, t0=t0, tw=tw, bank=bank: e.tensor_scalar(
                hprime[:, et * T_ + t0:et * T_ + t0 + tw], pb[bank][:, 0:tw],
                probsc[:, et:et + 1], None, ALU.mult), inc=s_dve)
    vhp = rb_dve(hprime[:, 0:1])
    b.wait("tensor", s_dve, vhp)

    NT4 = min(4, NT)
    ffn_ev = {i: 0 for i in range(NT4)}
    evstl = {0: 0, 1: 0}
    lb_t = {0: 0, 1: 0}
    for os_ in range(D_ // 512):
        lbw = os_ % 2
        b.wait("gpsimd", s_te, lb_t[lbw])
        lbv = None
        for et in range(ERT):
            lbv = b.run("gpsimd", lambda e, et=et, os_=os_, lbw=lbw: e.dma_start(
                out=lBtq[lbw][et],
                in_=w_lBt[et * 128:(et + 1) * 128, os_ * 512:(os_ + 1) * 512]), inc=s_lb[lbw])
        for tp in range(NT // NT4):
            tis = [tp * NT4 + i for i in range(NT4)]
            for i, ti in enumerate(tis):
                b.wait("tensor", s_dve, ffn_ev[i])
                b.run("tensor", lambda e, i=i, os_=os_: e.matmul(
                    pb[i][:, 0:512], ones65[64:65, :], rows3[64:65, os_ * 512:(os_ + 1) * 512],
                    start=True, stop=False))
            for kq in range(NKQ):
                wi = qi % 2
                qi += 1
                b.wait("gpsimd", s_te, wfree[wi])
                lw = ld_w(wq[wi][:], w_ffn_t[kq, os_], wi)
                b.wait("tensor", s_ldw[wi], lw)

                def fmm(eng, wi=wi, kq=kq, tis=tis):
                    ins = None
                    for kk in range(KQ):
                        k = kq * KQ + kk
                        for i, ti in enumerate(tis):
                            ins = eng.matmul(pb[i][:, 0:512],
                                             lnT[:, k * T_ + ti * 128:k * T_ + ti * 128 + 128],
                                             wq[wi][:, kk * 512:(kk + 1) * 512],
                                             start=False, stop=False)
                    return ins
                b.run("tensor", fmm, inc=s_te)
                wfree[wi] = s_te.n
            b.wait("tensor", s_lb[lbw], lbv)

            def lmm(eng, tis=tis, lbw=lbw):
                ins = None
                for et in range(ERT):
                    for i, ti in enumerate(tis):
                        ins = eng.matmul(pb[i][:, 0:512],
                                         hprime[:, et * T_ + ti * 128:et * T_ + ti * 128 + 128],
                                         lBtq[lbw][et],
                                         start=False, stop=(et == ERT - 1))
                return ins
            tl = b.run("tensor", lmm, inc=s_te)
            for i, ti in enumerate(tis):
                p = ti % 2
                b.wait("vector", s_te, tl)
                b.wait("vector", s_st, evstl[p])
                ve = b.run("vector", lambda e, i=i, p=p: e.tensor_copy(
                    ev_bf[:, p * 512:p * 512 + 512], pb[i][:, 0:512]), inc=s_dve)
                ffn_ev[i] = ve
                gv = rb_dve(ev_bf[:, p * 512:p * 512 + 1])
                evstl[p] = store(ffnspill[ti * 128:(ti + 1) * 128, os_ * 512:(os_ + 1) * 512],
                                 ev_bf[:, p * 512:p * 512 + 512], s_dve, gv)
        lb_t[lbw] = s_te.n
    barrier()

    b.mark("B")
    if c.phases < 5:
        return _finalize(b, nc, c, dram)
    for h in range(NH):
        b.run("vector", lambda e, h=h: e.tensor_scalar(
            diagD[:, h * 128:(h + 1) * 128], ident_s[:], dcol_all[:, h:h + 1], None, ALU.mult))
    rb_dve(diagD[:, 0:1])
    nwev = {0: 0, 1: 0}
    for j in range(D_ // 512):
        bk = 2 + (j % 2)
        b.wait("tensor", s_dve, nwev[j % 2])
        tt = b.run("tensor", lambda e, j=j, bk=bk: e.matmul(
            pb[bk][:, 0:512], ones65[32:33, :], rows3[32:33, j * 512:(j + 1) * 512],
            start=True, stop=True), inc=s_te)
        b.wait("vector", s_te, tt)
        nwev[j % 2] = b.run("vector", lambda e, j=j, bk=bk: e.tensor_copy(
            nw_bc[:, j * 512:(j + 1) * 512], pb[bk][:, 0:512]), inc=s_dve)
    vnw = rb_dve(nw_bc[:, 0:1])
    b.wait("tensor", s_dve, vnw)
    bp = getattr(c, "bphase", 9)
    if bp < 1:
        return _finalize(b, nc, c, dram)

    vH_last = hf
    vxw_last = 0
    vHg = {g: hf for g in range(c.NG)}
    vg1 = 0
    rel0 = emit_reload(0, 0, main=True)
    relq = [rel0]
    y_sp = []
    t1_free = {0: 0, 1: 0}
    for ci in range(c.NC):
        slot = ci % 2
        my_rel = relq.pop(0)
        if ci + 1 < c.NC:
            relq.append(emit_reload(ci + 1, (ci + 1) % 2, main=True))
        b.wait("sync", s_dve, vg1)
        szv = ld_sync(sz_tm[:], zspill[0:D_, H_ + ci * 128:H_ + ci * 128 + 128],
                      transpose=True, sem=s_sz)
        pr = chunk_prep(ci, True, slot, my_rel)
        if bp < 1.5:
            return _finalize(b, nc, c, dram)
        t1_free[0] = max(t1_free[0], pr["vg"])
        yev = {0: 0, 1: 0}
        m_done = {0: 0, 1: 0}
        bc_free = {0: 0, 1: 0}
        for g in range(c.NG):
            ypb = pb[g % 2]
            t1b = pb[2] if g % 2 == 0 else pb[4]
            b.wait("tensor", s_dve, max(vHg[g], t1_free[g % 2]))
            b.wait("tensor", s_rel2[my_rel[0]], my_rel[1])
            if g == 0:
                b.wait("tensor", s_act, pr["aat"])
            t1m = b.run("tensor", lambda e, sl=slot, g=g, t1b=t1b: e.matmul(
                t1b[:, 0:512], bcc[sl][:, 128:256], H_bf[:, g * 512:(g + 1) * 512],
                start=True, stop=True), inc=s_te)
            b.wait("vector", s_te, t1m)
            if g == 0:
                b.wait("vector", s_act, pr["aat"])
            vt1 = b.run("vector", lambda e, g=g, t1b=t1b: e.tensor_tensor(
                t1buf.rearrange("p (h t) -> p h t", h=8),
                t1b.rearrange("p (h t) -> p h t", h=8),
                expcT[:, g * 8:(g + 1) * 8].broadcast_to([128, 8, 64]), ALU.mult), inc=s_dve)
            t1_free[g % 2] = vt1
            if bp < 2:
                return _finalize(b, nc, c, dram)
            b.wait("tensor", s_dve, yev[g % 2])
            e_done = 0
            for hb in range(2):
                hls = [hb * 4 + j for j in range(4)]
                bcb = pb[3] if hb == 0 else pb[5]
                if bc_free[hb]:
                    b.wait("tensor", s_act, bc_free[hb])
                elif hb == 0:
                    b.wait("tensor", s_dve, pr["vg"])

                def bmm(eng, g=g, hls=hls, bcb=bcb):
                    ins = None
                    for j, hl in enumerate(hls):
                        h = g * 8 + hl
                        ins = eng.matmul(bcb[:, j * 128:(j + 1) * 128],
                                         dtAT[:, h:h + 1].broadcast_to([128, 128]),
                                         ut_s[:], start=True, stop=True)
                    return ins
                tb_ = b.run("tensor", bmm, inc=s_te)
                if bp < 2.2:
                    return _finalize(b, nc, c, dram)
                b.wait("scalar", s_te, tb_)
                b.wait("scalar", s_dve, pr["vcs"] if hb > 0
                       else max(pr["vcs"], vxw_last))
                for j, hl in enumerate(hls):
                    h = g * 8 + hl
                    ae = b.run("scalar", lambda e, h=h, hl=hl, j=j, bcb=bcb: e.activation(
                        e8[:, hl * 128:(hl + 1) * 128], bcb[:, j * 128:(j + 1) * 128],
                        AF.Exp, bias=colst[:, h:h + 1]),
                        inc=(s_act if j == 3 else None))
                bc_free[hb] = ae
                e_done = ae
                if bp < 2.4:
                    return _finalize(b, nc, c, dram)
                b.wait("vector", s_act, ae)
                b.wait("vector", s_dve, pr["vg"])
                b.wait("vector", s_te, m_done[hb])
                for j, hl in enumerate(hls):
                    vm = b.run("vector", lambda e, hl=hl: e.scalar_tensor_tensor(
                        mstat8[:, hl * 128:(hl + 1) * 128],
                        e8[:, hl * 128:(hl + 1) * 128], 3.0e37, gsb[:],
                        ALU.min, ALU.mult), inc=(s_dve if j == 3 else None))
                b.wait("tensor", s_dve, vm)
                if bp < 2.6:
                    return _finalize(b, nc, c, dram)

                def ymm(eng, g=g, hls=hls, sl=slot, yp=ypb):
                    ins = None
                    for hl in hls:
                        h = g * 8 + hl
                        eng.matmul(yp[:, hl * 64:hl * 64 + 64],
                                   mstat8[:, hl * 128:(hl + 1) * 128],
                                   xs_tm[sl][:, h * 64:h * 64 + 64],
                                   start=(hl == 0), stop=False)
                        ins = eng.matmul(yp[:, hl * 64:hl * 64 + 64],
                                         diagD[:, h * 128:(h + 1) * 128],
                                         xs_tm[sl][:, h * 64:h * 64 + 64],
                                         start=False, stop=(hl == 7))
                    return ins
                tym = b.run("tensor", ymm, inc=s_te)
                m_done[hb] = tym
                if bp < 2.8:
                    return _finalize(b, nc, c, dram)
            pr["e_done"] = e_done
            pr["hfree"] = vH_last
            vH_last, vxw_last = emit_state_update(g, slot, pr, main=True)
            vHg[g] = vH_last
            b.wait("vector", s_te, max(m_done[0], m_done[1]))
            vy = b.run("vector", lambda e, g=g, yp=ypb: e.tensor_tensor(
                yfull[:, g * 512:(g + 1) * 512], t1buf[:], yp[:, 0:512], ALU.add), inc=s_dve)
            yev[g % 2] = vy
            if bp < 3.5:
                return _finalize(b, nc, c, dram)
        b.wait("vector", s_dve, max(yev[0], yev[1]))
        if c.debug:
            ygate = rb_dve(yfull[:, 0:1])
            ystt = store(ydump[ci * 128:(ci + 1) * 128, :], yfull, s_dve, ygate)
            b.wait("vector", s_st, ystt)
        b.wait("vector", s_sz, szv)
        vg1a = b.run("vector", lambda e: e.tensor_tensor(yfull, yfull, sz_tm[:], ALU.mult), inc=s_dve)
        vg1 = b.run("vector", lambda e: e.memset(lncol[:, 0:1], 0.0), inc=s_dve)
        b.wait("scalar", s_dve, max(vg1, vg1a))
        asq = b.run("scalar", lambda e, sl=slot: e.activation(
            xs_tm[sl][:], yfull, AF.Square, accum_out=lncol[:, 0:1]), inc=s_act)
        b.wait("scalar", s_act, asq)
        b.run("scalar", lambda e: e.activation(lncol[:, 1:2], lncol[:, 0:1], AF.Ln,
                                               scale=inv_d, bias=1e-5))
        ar2 = b.run("scalar", lambda e: e.activation(lncol[:, 1:2], lncol[:, 1:2], AF.Exp,
                                                     scale=-0.5), inc=s_act)
        b.wait("vector", s_act, ar2)
        b.wait("vector", s_st, y_sp[-1] if y_sp else 0)
        b.run("vector", lambda e: e.tensor_scalar(yfull, yfull, lncol[:, 1:2], None, ALU.mult))
        vyh = b.run("vector", lambda e: e.tensor_tensor(yhat_bf, yfull, nw_bc, ALU.mult), inc=s_dve)
        gv = rb_dve(yhat_bf[:, 0:1])
        y_sp.append(store(yspill[ci * 128:(ci + 1) * 128, :], yhat_bf, s_dve, gv))
    b_spills = s_st.n
    barrier()

    b.mark("C")
    if c.phases < 6:
        return _finalize(b, nc, c, dram)
    b.wait("sync", s_st, b_spills)
    b.wait("sync", s_te, s_te.n)
    b.wait("sync", s_dve, s_dve.n)
    for k in range(KT):
        ld_sync(lnT[:, k * T_:(k + 1) * T_],
                yspill[:, k * 128:(k + 1) * 128], transpose=True, sem=s_lnt)
    ycm_ready = s_lnt.n
    ycm = lnT
    b.wait("tensor", s_lnt, ycm_ready)
    out_free = {i: 0 for i in range(NT4)}
    ld_free = {0: (0, 0), 1: (0, 0)}
    ost = {0: 0, 1: 0}
    gsil4 = [poolF[:, 2048 + j * 512:2048 + (j + 1) * 512] for j in range(4)]
    xrb4 = [poolF[:, j * 512:(j + 1) * 512] for j in range(4)]
    gbufs4 = [poolB[:, j * 512:(j + 1) * 512] for j in range(4)]
    gs_free = {i: 0 for i in range(NT4)}
    ag_free = {i: 0 for i in range(NT4)}
    xb_free = {i: 0 for i in range(NT4)}
    for os_ in range(D_ // 512):
        for tp in range(NT // NT4):
            tis = [tp * NT4 + i for i in range(NT4)]
            to = 0
            for kq in range(NKQ):
                wi = qi % 2
                qi += 1
                b.wait("gpsimd", s_te, wfree[wi])
                lw = ld_w(wq[wi][:], w_out_t[kq, os_], wi)
                b.wait("tensor", s_ldw[wi], lw)
                if kq == 0:
                    for i in range(NT4):
                        b.wait("tensor", s_dve, out_free[i])

                def omm(eng, wi=wi, kq=kq, tis=tis):
                    ins = None
                    for kk in range(KQ):
                        k = kq * KQ + kk
                        for i, ti in enumerate(tis):
                            ins = eng.matmul(pb[i][:, 0:512],
                                             ycm[:, k * T_ + ti * 128:k * T_ + ti * 128 + 128],
                                             wq[wi][:, kk * 512:(kk + 1) * 512],
                                             start=(k == 0), stop=(k == KT - 1))
                    return ins
                to = b.run("tensor", omm, inc=s_te)
                wfree[wi] = s_te.n
            consume = []
            for i, ti in enumerate(tis):
                p = ti % 2
                gb = gbufs4[i]
                xb_ = xrb4[i]
                gs = gsil4[i]
                b.wait("sync", s_act, ag_free[i])
                b.wait("sync", s_dve, xb_free[i])
                lgv = ld_sync(gb, ffnspill[ti * 128:(ti + 1) * 128, os_ * 512:(os_ + 1) * 512],
                              sem=s_gx[p])
                lxv = ld_sync(xb_, x_in[ti * 128:(ti + 1) * 128, os_ * 512:(os_ + 1) * 512],
                              sem=s_gx[p])
                b.wait("scalar", s_gx[p], lxv)
                b.wait("scalar", s_st, gs_free[i])
                ag = b.run("scalar", lambda e, gb=gb, gs=gs: e.activation(gs, gb, AF.Silu), inc=s_act)
                ag_free[i] = ag
                consume.append((i, ti, p, gs, xb_, ag, lxv))
            b.wait("vector", s_te, to)
            for (i, ti, p, gs, xb_, ag, lxv) in consume:
                b.wait("vector", s_act, ag)
                b.wait("vector", s_gx[p], lxv)
                vm = b.run("vector", lambda e, i=i, gs=gs: e.tensor_tensor(
                    gs, pb[i][:, 0:512], gs, ALU.mult), inc=s_dve)
                out_free[i] = vm
                vo = b.run("vector", lambda e, xb_=xb_, gs=gs: e.tensor_tensor(
                    gs, gs, xb_, ALU.add), inc=s_dve)
                xb_free[i] = vo
                gvo = rb_dve(gs[:, 0:1])
                gs_free[i] = store(out[ti * 128:(ti + 1) * 128, os_ * 512:(os_ + 1) * 512],
                                   gs, s_dve, gvo)

    if c.debug:
        for nm, t, shp in (("dbg_yd", ydump, [T_, D_]),
                           ("dbg_z", zspill, [2 * D_ + 2 * c.S, TH]),
                           ("dbg_ln", lnspill, [T_, D_]),
                           ("dbg_sc", scspill, [TH, D_]),
                           ("dbg_ffn", ffnspill, [T_, D_]),
                           ("dbg_y", yspill, [T_, D_])):
            o = dout(nm, shp, F32 if nm == "dbg_yd" else BF16)
            b.wait("sync", s_st, s_st.n)
            b.run("sync", lambda e, o=o, t=t: e.dma_start(out=o, in_=t), inc=s_ldx)
        oh = dout("dbg_h", [128, NH * 64], F32)
        b.wait("sync", s_dve, s_dve.n)
        b.run("sync", lambda e, o=oh: e.dma_start(out=o, in_=H[:]), inc=s_ldx)

    global _LAST_MARKS, _LAST_B
    _LAST_MARKS = getattr(b, "marks", [])
    _LAST_B = b
    for sem_ in (s_st, s_ldx, s_misc, s_xa, s_xb, s_lnt, s_sz, s_ldw[0], s_ldw[1],
                 s_lb[0], s_lb[1], s_rel2[0], s_rel2[1], s_gx[0], s_gx[1]):
        b.wait("sync", sem_, sem_.n)

    with b._ctx:
        with nc.Block() as block:
            @block.tensor
            def _(eng):
                class _CE:
                    n = 0

                    def __getattr__(self, a):
                        v = getattr(eng, a)
                        if a in ("matmul", "transpose"):
                            def w(*ar, **kw):
                                _CE.n += 1
                                return v(*ar, **kw)
                            return w
                        return v
                ce = _CE()
                runmarks = {r: nm for nm, r in getattr(b, "marks", [])}
                mmarks = []
                nwait = 0
                for idx, f in enumerate(b.ops["tensor"]):
                    if idx in runmarks:
                        mmarks.append((runmarks[idx], _CE.n))
                    f(ce)
                b.mm_marks = mmarks

            @block.vector
            def _(eng):
                for f in b.ops["vector"]:
                    f(eng)

            @block.scalar
            def _(eng):
                for f in b.ops["scalar"]:
                    f(eng)

            @block.gpsimd
            def _(eng):
                for f in b.ops["gpsimd"]:
                    f(eng)

            @block.sync
            def _(eng):
                for f in b.ops["sync"]:
                    f(eng)
    return nc


def _mk_rows3(c, weights):
    r3 = np.zeros((65, c.D), np.float32)
    r3[0] = np.asarray(weights["ln_w"])
    r3[32] = np.asarray(weights["norm_w"])
    r3[64] = np.asarray(weights["ffn_b"])
    return r3


def make_inputs(cfg: Cfg, weights: dict, x_slab: np.ndarray, prev_slab: np.ndarray):
    c = cfg
    f32 = np.float32

    def asf(a):
        return np.ascontiguousarray(a, dtype=f32)

    wconv = np.asarray(weights["conv_w"])
    convc = np.zeros((128, c.XOT * c.K), f32)
    for xo in range(c.XOT):
        for k in range(c.K):
            convc[:, xo * c.K + k] = wconv[xo * 128:(xo + 1) * 128, k]
    mupcv = np.zeros((128, c.NOT), f32)
    mv = np.asarray(weights["mup_vector"])
    for o in range(c.NOT):
        w = min(128, c.PROJ - o * 128)
        mupcv[0:w, o] = mv[o * 128:o * 128 + w]
    A2d = np.asarray(weights["ffn_A"]).reshape(c.ER, c.D)
    Bt2d = np.asarray(weights["ffn_B"]).transpose(0, 2, 1).reshape(c.ER, c.D)
    wip = np.asarray(weights["in_proj_w"], f32)
    NBF = (c.NOT + 1) // 2 - (1 if c.NOT % 2 == 1 else 0)
    wipf = wip[:, :NBF * 256].reshape(c.NKQ, c.KQ, 128, NBF, 256)
    wipf = np.ascontiguousarray(wipf.transpose(0, 3, 2, 1, 4).reshape(
        c.NKQ, NBF, 128, c.KQ * 256))
    wop = np.asarray(weights["out_proj_w"], f32)
    wopt = np.ascontiguousarray(
        wop.reshape(c.NKQ, c.KQ, 128, c.D // 512, 512).transpose(0, 3, 2, 1, 4)
        .reshape(c.NKQ, c.D // 512, 128, c.KQ * 512))
    wff = np.ascontiguousarray(np.asarray(weights["ffn_W"], f32).T)
    wfft = np.ascontiguousarray(
        wff.reshape(c.NKQ, c.KQ, 128, c.D // 512, 512).transpose(0, 3, 2, 1, 4)
        .reshape(c.NKQ, c.D // 512, 128, c.KQ * 512))
    latt = np.ascontiguousarray(
        A2d.T.reshape(c.KT, 128, c.ER).transpose(1, 0, 2).reshape(128, c.KT * c.ER)).astype(f32)
    return {
        "x_in": asf(x_slab),
        "prev_in": asf(prev_slab),
        "w_ip_t": wipf,
        "w_ip_rag": asf(wip[:, NBF * 256:]),
        "w_convc": convc,
        "w_out_t": wopt,
        "w_ffn_t": wfft,
        "w_lat_t": latt,
        "w_lBt": asf(Bt2d),
        "probs_col": asf(np.repeat(np.asarray(weights["probs"]), c.R).reshape(c.ER // 128, 128).T),
        "mup_cols": mupcv,
        "rows3_in": (lambda r3: r3)(_mk_rows3(c, weights)),
        "alog_row": asf(np.asarray(weights["A_log"])[None, :]),
        "dtb_col": asf(np.asarray(weights["dt_bias"])[:, None]),
        "d_cols": asf(np.tile(np.asarray(weights["D"])[None, :], (128, 1))),
        "ident_bf": np.eye(128).astype(bf16),
        "ident_f": np.eye(64).astype(f32),
        "ut_bf": np.triu(np.ones((128, 128))).astype(bf16),
        "tri_bf": np.triu(np.ones((128, 128))).astype(bf16),
        "ones_bf": np.ones((1, 128), bf16),
        "ones65_bf": np.ones((65, 128), bf16),
        "ones_f": np.ones((1, 128), f32),
        "fconsts": np.tile(np.array([[1e-5, 1.0 / c.D, -0.5, -1.0, 1.0, 0.0, 3.0e37, 0.5]], f32), (128, 1)),
    }


from concourse.bass_utils import run_bass_kernel_spmd

_CFG = Cfg()
_NC_CACHE = {}


def _get_program():
    if "nc" not in _NC_CACHE:
        _NC_CACHE["nc"] = build_program(_CFG)
    return _NC_CACHE["nc"]


def kernel(**inputs):
    c = _CFG
    x = np.ascontiguousarray(np.asarray(inputs["L0_attn_hidden_states"], dtype=np.float32))
    B, L, D = x.shape
    assert (B, L, D) == (4, 2048, 4096)
    weights = {
        "in_proj_w": inputs["L0_attn_in_proj_w"],
        "conv_w": inputs["L0_attn_conv_w"],
        "out_proj_w": inputs["L0_attn_out_proj_w"],
        "ln_w": inputs["L0_attn_ln_w"],
        "norm_w": inputs["L0_attn_norm_w"],
        "mup_vector": inputs["L0_attn_mup_vector"],
        "A_log": inputs["L0_attn_A_log"],
        "dt_bias": inputs["L0_attn_dt_bias"],
        "D": inputs["L0_attn_D"],
        "ffn_W": inputs["L0_ffn_W"],
        "ffn_b": inputs["L0_ffn_b"],
        "ffn_A": inputs["L0_ffn_A"],
        "ffn_B": inputs["L0_ffn_B"],
        "probs": inputs["L0_ffn_probs"],
    }
    zeros_prev = np.zeros((c.HALO, c.D), np.float32)
    base = make_inputs(c, weights, x[0, :c.T], zeros_prev)
    nc = _get_program()
    in_maps = []
    for core in range(8):
        bt, hf = core // 2, core % 2
        m = dict(base)
        m["x_in"] = np.ascontiguousarray(x[bt, hf * c.T:(hf + 1) * c.T])
        m["prev_in"] = zeros_prev if hf == 0 else \
            np.ascontiguousarray(x[bt, c.T - c.HALO:c.T])
        in_maps.append(m)
    res = run_bass_kernel_spmd(nc, in_maps, core_ids=list(range(8)))
    out = np.empty((4, 2048, 4096), np.float32)
    for core in range(8):
        bt, hf = core // 2, core % 2
        out[bt, hf * c.T:(hf + 1) * c.T] = np.asarray(res.results[core]["out"])
    return out


# version 11
# speedup vs baseline: 1.3422x; 1.0102x over previous
import contextlib
import numpy as np
import ml_dtypes
import concourse.bass as bass
import concourse.mybir as mybir

F32 = mybir.dt.float32
BF16 = mybir.dt.bfloat16
AF = mybir.ActivationFunctionType
ALU = mybir.AluOpType
AX = mybir.AxisListType
bf16 = ml_dtypes.bfloat16

ENGINES = ("tensor", "vector", "scalar", "gpsimd", "sync")


class Cfg:
    def __init__(self, D=4096, S=128, NH=64, HD=64, K=4, T=1024, HALO=256,
                 E=16, R=16, debug=False, phases=6, bphase=9):
        self.D, self.S, self.NH, self.HD, self.K = D, S, NH, HD, K
        self.T, self.HALO, self.E, self.R = T, HALO, E, R
        self.debug = debug
        self.phases = phases
        self.bphase = bphase
        self.CONV = D + 2 * S
        self.PROJ = 2 * D + 2 * S + NH
        self.KT = D // 128
        self.ZOT = D // 128
        self.XOT = self.CONV // 128
        self.NC = T // 128
        self.HC = HALO // 128
        self.TH = T + HALO
        self.NG = NH // 8
        self.ER = E * R
        self.NOT = (self.PROJ + 127) // 128
        self.DTW = self.PROJ - 128 * (self.NOT - 1)
        self.KQ = max(1, self.KT // 4)
        self.NKQ = (self.KT + self.KQ - 1) // self.KQ
        self.ERT = self.ER // 128
        assert D % 128 == 0 and self.CONV % 128 == 0 and S == 128 and HD == 64
        assert T % 128 == 0 and HALO % 128 == 0 and NH % 8 == 0 and self.ER % 128 == 0


_LAST_MARKS = []
_LAST_B = None


class Sem:
    def __init__(self, h, step=1):
        self.h, self.step, self.n = h, step, 0

    def inc(self):
        self.n += self.step
        return self.n


class Build:
    def __init__(self, cfg):
        self.cfg = cfg
        self.nc = bass.Bass("TRN2", target_bir_lowering=False,
                            detect_race_conditions=False)
        self.ops = {e: [] for e in ENGINES}
        self._ctx = contextlib.ExitStack()

    def sem(self, name, step=1):
        sm = Sem(self._ctx.enter_context(self.nc.semaphore(name)), step)
        if not hasattr(self, "_allsems"):
            self._allsems = {}
        self._allsems[name] = sm
        return sm

    def sbuf(self, name, shape, dtype):
        return self._ctx.enter_context(self.nc.sbuf_tensor(name, shape, dtype))

    def psum(self, name, shape, dtype=F32):
        return self._ctx.enter_context(self.nc.psum_tensor(name, shape, dtype))

    def wait(self, e, sem, val):
        if val > 0:
            self.ops[e].append(lambda eng, s=sem.h, v=val: eng.wait_ge(s, v))

    def mark(self, name):
        if not hasattr(self, "marks"):
            self.marks = []
        self.marks.append((name, len(self.ops["tensor"])))

    def run(self, e, f, inc=None):
        if not hasattr(self, "nrun"):
            self.nrun = {}
        self.nrun[e] = self.nrun.get(e, 0) + 1
        incs = inc if isinstance(inc, (list, tuple)) else ([inc] if inc else [])
        tgts = [s.inc() for s in incs]

        def g(eng, f=f, incs=incs):
            ins = f(eng)
            for s in incs:
                ins = ins.then_inc(s.h, s.step)
        self.ops[e].append(g)
        return tgts[0] if len(tgts) == 1 else (tgts if tgts else None)


def _finalize(b, nc, c, dram):
    for name, sem_ in list(b._allsems.items()):
        b.wait("sync", sem_, sem_.n)
    with b._ctx:
        with nc.Block() as block:
            @block.tensor
            def _(eng):
                class _CE:
                    n = 0

                    def __getattr__(self, a):
                        v = getattr(eng, a)
                        if a in ("matmul", "transpose"):
                            def w(*ar, **kw):
                                _CE.n += 1
                                return v(*ar, **kw)
                            return w
                        return v
                ce = _CE()
                runmarks = {r: nm for nm, r in getattr(b, "marks", [])}
                mmarks = []
                nwait = 0
                for idx, f in enumerate(b.ops["tensor"]):
                    if idx in runmarks:
                        mmarks.append((runmarks[idx], _CE.n))
                    f(ce)
                b.mm_marks = mmarks

            @block.vector
            def _(eng):
                for f in b.ops["vector"]:
                    f(eng)

            @block.scalar
            def _(eng):
                for f in b.ops["scalar"]:
                    f(eng)

            @block.gpsimd
            def _(eng):
                for f in b.ops["gpsimd"]:
                    f(eng)

            @block.sync
            def _(eng):
                for f in b.ops["sync"]:
                    f(eng)
    return nc


def build_program(cfg: Cfg):
    b = Build(cfg)
    nc, c = b.nc, cfg
    D_, T_, H_, TH, KT, NH = c.D, c.T, c.HALO, c.TH, c.KT, c.NH
    NT, HT = T_ // 128, H_ // 128
    tsp = [(i * 512, min(512, T_ - i * 512)) for i in range((T_ + 511) // 512)]
    KQ, NKQ, ERT = c.KQ, c.NKQ, c.ERT
    dram = {}

    def din(name, shape, dtype):
        dram[name] = nc.dram_tensor(name, shape, dtype, kind="ExternalInput").ap()
        return dram[name]

    def dout(name, shape, dtype):
        dram[name] = nc.dram_tensor(name, shape, dtype, kind="ExternalOutput").ap()
        return dram[name]

    def dint(name, shape, dtype):
        dram[name] = nc.dram_tensor(name, shape, dtype, kind="Internal").ap()
        return dram[name]

    x_in = din("x_in", [T_, D_], F32)
    prev_in = din("prev_in", [H_, D_], F32)
    NBF = (c.NOT + 1) // 2 - (1 if c.NOT % 2 == 1 else 0)
    w_ip_t = din("w_ip_t", [c.NKQ, NBF, 128, c.KQ * 256], F32)
    w_ip_rag = din("w_ip_rag", [D_, c.PROJ - NBF * 256], F32)
    w_convc = din("w_convc", [128, c.XOT * c.K], F32)
    w_out_t = din("w_out_t", [c.NKQ, D_ // 512, 128, c.KQ * 512], F32)
    w_ffn_t = din("w_ffn_t", [c.NKQ, D_ // 512, 128, c.KQ * 512], F32)
    w_lat_t = din("w_lat_t", [128, KT * c.ER], F32)
    w_lBt = din("w_lBt", [c.ER, D_], F32)
    probs_col = din("probs_col", [128, c.ER // 128], F32)
    mup_cols = din("mup_cols", [128, c.NOT], F32)
    rows3_in = din("rows3_in", [65, D_], F32)
    alog_row = din("alog_row", [1, NH], F32)
    dtb_col = din("dtb_col", [NH, 1], F32)
    d_cols = din("d_cols", [128, NH], F32)
    ident_bf = din("ident_bf", [128, 128], BF16)
    ident_f = din("ident_f", [64, 64], F32)
    ut_bf = din("ut_bf", [128, 128], BF16)
    tri_bf = din("tri_bf", [128, 128], BF16)
    ones_bf = din("ones_bf", [1, 128], BF16)
    ones65_bf = din("ones65_bf", [65, 128], BF16)
    ones_f = din("ones_f", [1, 128], F32)
    fconsts = din("fconsts", [128, 8], F32)
    out = dout("out", [T_, D_], F32)

    ydump = dint("ydump", [T_, D_], F32)
    lnspill = dint("lnspill", [T_, D_], BF16)
    scspill = dint("scspill", [TH, D_], BF16)
    zspill = dint("zspill", [2 * D_ + 2 * c.S, TH], BF16)
    ffnspill = dint("ffnspill", [T_, D_], BF16)
    yspill = dint("yspill", [T_, D_], BF16)

    sb = b.sbuf
    poolA = sb("poolA", [128, KT * T_], BF16)
    PB = max(KT * H_ + 2 * (3 + TH),
             KT * ERT * 128 + 2 * ERT * 512 + ERT * T_,
             NH * 128 + 2 * D_ + 512,
             2 * 512)
    poolB = sb("poolB", [128, PB], BF16)
    poolF = sb("poolF", [128, max(D_, 4096, 2560 + TH, 2 * TH)], F32)
    wq = [sb(f"wq{i}", [128, KQ * 512], BF16) for i in range(2)]
    xs_tm = [sb(f"xs_tm{i}", [128, D_], BF16) for i in range(2)]
    sz_tm = sb("sz_tm", [128, D_], BF16)
    bcc = [sb(f"bcc{i}", [128, 2 * 128], BF16) for i in range(2)]

    ln_bf = sz_tm
    lnT = poolA
    lnTh = poolB
    xbc2 = [poolB[:, KT * H_ + i * (3 + TH):KT * H_ + (i + 1) * (3 + TH)]
            for i in range(2)]
    lAT = poolB[:, 0:KT * ERT * 128]
    lB0 = KT * ERT * 128
    lBtq = [[poolB[:, lB0 + (p * ERT + et) * 512: lB0 + (p * ERT + et) * 512 + 512]
             for et in range(ERT)] for p in range(2)]
    hprime = poolB[:, lB0 + 2 * ERT * 512:lB0 + 2 * ERT * 512 + ERT * T_]
    diagD = poolB[:, 0:NH * 128]
    nw_bc = poolB[:, NH * 128:NH * 128 + D_]
    yhat_bf = poolB[:, NH * 128 + D_:NH * 128 + 2 * D_]
    xw8 = poolB[:, NH * 128 + 2 * D_:NH * 128 + 2 * D_ + 512]
    gbufs = [poolB[:, i * 512:(i + 1) * 512] for i in range(2)]
    xa = poolF[:, 0:D_]
    dtscr_f = poolF[0:NH, 0:TH]
    dt_t2 = poolF[0:64, TH:2 * TH]
    yfull = poolF[:, 0:D_]
    xrb = [poolF[:, i * 512:(i + 1) * 512] for i in range(2)]
    ob = [poolF[:, 1024 + i * 512:1024 + (i + 1) * 512] for i in range(2)]
    gsil = poolF[:, 2048:2560]
    acc_cv = poolF[:, poolF.shape[1] - TH:]
    sqjunkA = xs_tm[0]
    sc_bf = xs_tm[1]

    ev_bf = sb("ev_bf", [128, max(TH, 1024)], BF16)
    convw = sb("convw", [128, c.XOT * c.K], F32)
    mupc = sb("mupc", [128, c.NOT], F32)
    dt_sp = sb("dt_sp", [NH, TH], F32)
    H = sb("H", [128, NH * 64], F32)
    H_bf = sb("H_bf", [128, NH * 64], BF16)
    dtT = sb("dtT", [128, NH], F32)
    logdtT = sb("logdtT", [128, NH], F32)
    dtAT = sb("dtAT", [128, NH], BF16)
    colst = sb("colst", [128, NH], F32)
    expcT = sb("expcT", [128, NH], F32)
    a_bc = sb("a_bc", [128, NH], F32)
    a_col = sb("a_col", [NH, 1], F32)
    atot_bc = sb("atot_bc", [128, NH], F32)
    wtilT = sb("wtilT", [128, NH], F32)
    rowbuf = sb("rowbuf", [1, 128], F32)
    dtsum = sb("dtsum", [NH, 2], F32)
    gsb = sb("gsb", [128, 128], F32)
    btile = sb("btile", [128, 128], BF16)
    e8 = sb("e8", [128, 8 * 128], BF16)
    mstat8 = sb("mstat8", [128, 8 * 128], BF16)
    t1buf = sb("t1buf", [128, 512], F32)
    dcol_all = sb("dcol_all", [128, NH], F32)
    ident_s = sb("ident_s", [128, 128], BF16)
    identf_s = sb("identf_s", [64, 64], F32)
    ut_s = sb("ut_s", [128, 128], BF16)
    tri_s = sb("tri_s", [128, 128], BF16)
    ones_s = sb("ones_s", [1, 128], BF16)
    ones65 = sb("ones65", [65, 128], BF16)
    onesf_s = sb("onesf_s", [1, 128], F32)
    dtb_s = sb("dtb_s", [NH, 1], F32)
    arow_s = sb("arow_s", [1, NH], F32)
    arow_bf = sb("arow_bf", [1, NH], BF16)
    rows3 = sb("rows3", [65, D_], BF16)
    probsc = sb("probsc", [128, max(ERT, 1)], F32)
    lncol = sb("lncol", [128, 8], F32)
    lncol2 = sb("lncol2", [128, 12 * (T_ // 128 + c.HALO // 128)], F32)
    lnwT = sb("lnwT", [128, KT], F32)
    fconst = sb("fconst", [128, 8], F32)
    scr = sb("scr", [128, 8], F32)

    pb = [b.psum(f"pb{i}", [128, 512], F32) for i in range(7)]
    pbf = b.psum("pbf", [128, 1024], BF16)

    s_te = b.sem("s_te")
    s_dve = b.sem("s_dve")
    s_act = b.sem("s_act")
    s_ldx = b.sem("s_ldx", 16)
    s_ldw = [b.sem(f"s_ldw{i}", 16) for i in range(2)]
    s_st = b.sem("s_st", 16)
    s_misc = b.sem("s_misc", 16)
    s_gp = b.sem("s_gp_eng")
    s_lb = [b.sem(f"s_lb{i}", 16) for i in range(2)]
    s_xa = b.sem("s_xa", 16)
    s_xb = b.sem("s_xb", 16)
    s_lnt = b.sem("s_lnt", 16)
    s_rel2 = [b.sem(f"s_rel2{i}", 16) for i in range(2)]
    s_sz = b.sem("s_sz", 16)
    s_gx = [b.sem(f"s_gx{i}", 16) for i in range(2)]
    s_gx4 = [b.sem(f"s_gx4{i}", 16) for i in range(4)]

    def ld_sync(dst, src, transpose=False, sem=None):
        return b.run("sync", lambda e, d=dst, s=src, t=transpose:
                     e.dma_start(out=d, in_=s, transpose=t), inc=(sem or s_ldx))

    def ld_w(dst, src, wi):
        return b.run("gpsimd", lambda e, d=dst, s=src: e.dma_start(out=d, in_=s),
                     inc=s_ldw[wi])

    def store(dst, src, gate_sem, gate_val):
        b.wait("scalar", gate_sem, gate_val)
        return b.run("scalar", lambda e, d=dst, s=src: e.dma_start(out=d, in_=s), inc=s_st)

    def rb_act(tile):
        p = tile.shape[0]
        b.wait("scalar", s_act, s_act.n)
        return b.run("scalar", lambda e, t=tile, p=p: e.activation(scr[0:p, 6:7], t, AF.Copy), inc=s_act)

    def rb_dve(tile):
        p = tile.shape[0]
        b.wait("vector", s_dve, s_dve.n)
        return b.run("vector", lambda e, t=tile, p=p: e.tensor_copy(scr[0:p, 7:8], t), inc=s_dve)

    def barrier():
        te0, dv0, ac0 = s_te.n, s_dve.n, s_act.n
        b.wait("tensor", s_dve, dv0)
        b.wait("tensor", s_act, ac0)
        b.wait("vector", s_te, te0)
        b.wait("vector", s_act, ac0)
        b.wait("scalar", s_te, te0)
        b.wait("scalar", s_dve, dv0)

    ld_sync(fconst[:], fconsts)
    for _i, _v in enumerate([1e-5, 1.0 / D_, -0.5, -1.0, 1.0, 0.0, 3.0e37, 0.5]):
        nc.const_aps.register(F32, float(_v), fconst[:, _i:_i + 1])
    ld_sync(ident_s[:], ident_bf)
    ld_sync(identf_s[:], ident_f)
    ld_sync(ut_s[:], ut_bf)
    ld_sync(tri_s[:], tri_bf)
    ld_sync(ones_s[:], ones_bf)
    ld_sync(ones65[:], ones65_bf)
    ld_sync(onesf_s[:], ones_f)
    ld_sync(convw[:], w_convc)
    ld_sync(mupc[:], mup_cols)
    ld_sync(dtb_s[:], dtb_col)
    ld_sync(arow_s[:], alog_row)
    ld_sync(dcol_all[:], d_cols)
    cst = ld_sync(probsc[:], probs_col)
    gw0 = b.run("gpsimd", lambda e: e.dma_start(out=rows3[:], in_=rows3_in), inc=s_misc)

    b.wait("vector", s_ldx, cst)
    b.run("vector", lambda e: e.memset(H[:, 0:NH * 64], 0.0))
    b.run("vector", lambda e: e.memset(H_bf[:, 0:NH * 64], 0.0), inc=s_dve)
    b.run("vector", lambda e: e.memset(lncol2[:], 0.0))
    b.run("vector", lambda e: e.memset(xbc2[0][:, 0:3], 0.0))
    zmv = b.run("vector", lambda e: e.memset(xbc2[1][:, 0:3], 0.0), inc=s_dve)

    b.wait("tensor", s_ldx, cst)
    b.wait("tensor", s_misc, gw0)
    for kk in range(KT):
        tw_ = b.run("tensor", lambda e, kk=kk: e.matmul(
            pb[6][:, kk:kk + 1], rows3[0:1, kk * 128:(kk + 1) * 128],
            ones_s[0:1, 0:1], start=True, stop=True),
            inc=(s_te if kk == KT - 1 else None))
    b.wait("vector", s_te, tw_)
    lwv = b.run("vector", lambda e: e.tensor_copy(lnwT[:], pb[6][:, 0:KT]), inc=s_dve)
    b.wait("scalar", s_ldx, cst)
    b.run("scalar", lambda e: e.activation(arow_s[:], arow_s[:], AF.Exp))
    b.run("scalar", lambda e: e.mul(arow_s[:], arow_s[:], -1.0))
    aa = b.run("scalar", lambda e: e.copy(arow_bf[:], arow_s[:]), inc=s_act)
    b.wait("tensor", s_act, aa)
    tt = b.run("tensor", lambda e: e.matmul(pb[2][:, 0:NH], ones_s[0:1, :],
                                            arow_bf[0:1, :], start=True, stop=True), inc=s_te)
    t2 = b.run("tensor", lambda e: e.transpose(pb[3][0:NH, 0:1], arow_s[0:1, 0:NH],
                                               identf_s[0:1, 0:1]), inc=s_te)
    b.wait("vector", s_te, t2)
    b.run("vector", lambda e: e.tensor_copy(a_bc[:], pb[2][:, 0:NH]))
    acopy = b.run("vector", lambda e: e.tensor_copy(a_col[:], pb[3][0:NH, 0:1]), inc=s_dve)

    pre_lw = [ld_w(wq[q][:, 0:KQ * 256], w_ip_t[q, 0], q) for q in range(2)]

    b.mark("A1")
    inv_d = 1.0 / D_
    HD2 = D_ // 2
    CH2 = KT // 2
    gp_a1_base = s_gp.n
    ap_prev = [0, 0]
    tr_prev = [0, 0]
    nst = [0, 0]
    ev_bank = [0, 0, 0, 0]
    rot = 0
    b.wait("tensor", s_dve, acopy)
    for i in range(NT + HT):
        halo = i >= NT
        q = 12 * i
        cl = [lncol2[:, q + k:q + k + 1] for k in range(12)]
        src = prev_in[(i - NT) * 128:(i - NT + 1) * 128, :] if halo else \
            x_in[i * 128:(i + 1) * 128, :]
        b.wait("sync", s_act, ap_prev[0])
        ldA = ld_sync(xa[:, 0:HD2], src[:, 0:HD2], sem=s_xa)
        b.wait("sync", s_act, ap_prev[1])
        ldB = ld_sync(xa[:, HD2:D_], src[:, HD2:D_], sem=s_xb)
        b.wait("vector", s_xa, ldA)
        b.run("vector", lambda e, d=cl[0]: e.tensor_reduce(d, xa[:, 0:HD2], AX.X, ALU.add))
        b.wait("vector", s_xb, ldB)
        vsum = b.run("vector", lambda e, d=cl[1]: e.tensor_reduce(d, xa[:, HD2:D_], AX.X, ALU.add), inc=s_dve)
        if i == 0:
            b.wait("scalar", s_dve, zmv)
        b.wait("scalar", s_xa, ldA)
        b.run("scalar", lambda e, d=cl[2]: e.activation(
            sqjunkA[:, 0:HD2], xa[:, 0:HD2], AF.Square, accum_out=d))
        b.wait("scalar", s_xb, ldB)
        aq = b.run("scalar", lambda e, d=cl[3]: e.activation(
            sqjunkA[:, HD2:D_], xa[:, HD2:D_], AF.Square, accum_out=d), inc=s_act)
        b.wait("vector", s_act, aq)
        b.wait("vector", s_dve, vsum)
        b.run("vector", lambda e, d=cl[4], a=cl[0], bb=cl[1]: e.tensor_tensor(d, a, bb, ALU.add))
        v = b.run("vector", lambda e, d=cl[5], a=cl[2], bb=cl[3]: e.tensor_tensor(d, a, bb, ALU.add), inc=s_dve)
        b.wait("vector", s_dve, v)
        b.run("vector", lambda e, d=cl[4]: e.tensor_scalar(d, d, inv_d, None, ALU.mult))
        v = b.run("vector", lambda e, d=cl[5]: e.tensor_scalar(d, d, inv_d, None, ALU.mult), inc=s_dve)
        b.wait("vector", s_dve, v)
        v = b.run("vector", lambda e, d=cl[11], a=cl[4]: e.tensor_tensor(d, a, a, ALU.mult), inc=s_dve)
        b.wait("vector", s_dve, v)
        vvar = b.run("vector", lambda e, d=cl[5], a=cl[11]: e.tensor_tensor(d, d, a, ALU.subtract), inc=s_dve)
        b.wait("scalar", s_dve, vvar)
        b.run("scalar", lambda e, d=cl[6], a=cl[5]: e.activation(d, a, AF.Ln, bias=1e-5))
        arstd = b.run("scalar", lambda e, d=cl[6]: e.activation(d, d, AF.Exp, scale=-0.5), inc=s_act)
        b.wait("vector", s_act, arstd)
        v = b.run("vector", lambda e, d=cl[11], a=cl[5], bb=cl[6]: e.tensor_tensor(d, a, bb, ALU.mult), inc=s_dve)
        b.wait("vector", s_dve, v)
        vr2 = b.run("vector", lambda e, d=cl[11], bb=cl[6]: e.tensor_tensor(d, d, bb, ALU.mult), inc=s_dve)
        b.wait("scalar", s_dve, vr2)
        b.run("scalar", lambda e, d=cl[7], a=cl[11]: e.activation(d, a, AF.Ln, bias=1e-5))
        ar = b.run("scalar", lambda e, d=cl[7]: e.activation(d, d, AF.Exp, scale=-0.5), inc=s_act)
        b.wait("vector", s_act, ar)
        b.run("vector", lambda e, d=cl[8], a=cl[4], bb=cl[6]: e.tensor_tensor(d, a, bb, ALU.mult))
        v = b.run("vector", lambda e, d=cl[9], a=cl[6], bb=cl[7]: e.tensor_tensor(d, a, bb, ALU.mult), inc=s_dve)
        b.wait("vector", s_dve, v)
        v = b.run("vector", lambda e, d=cl[8]: e.tensor_scalar(d, d, -1.0, None, ALU.mult), inc=s_dve)
        b.wait("vector", s_dve, v)
        b.run("vector", lambda e, d=cl[11], a=cl[6]: e.tensor_copy(d, a))
        vbias = b.run("vector", lambda e, d=cl[10], a=cl[8], bb=cl[7]: e.tensor_tensor(d, a, bb, ALU.mult), inc=s_dve)
        b.wait("scalar", s_dve, vbias)
        ap_new = [0, 0]
        for hf in range(2):
            c0, c1 = (0, HD2) if hf == 0 else (HD2, D_)
            b.wait("scalar", s_te, tr_prev[hf])
            if not halo:
                b.wait("scalar", s_st, nst[hf])
                b.run("scalar", lambda e, c0=c0, c1=c1, s=cl[11], bi=cl[8]: e.activation(
                    ln_bf[:, c0:c1], xa[:, c0:c1], AF.Identity, bias=bi, scale=s))
                ga_ = rb_act(ln_bf[:, c0:c0 + 1])
                nst[hf] = store(lnspill[i * 128:(i + 1) * 128, c0:c1],
                                ln_bf[:, c0:c1], s_act, ga_)
            ap_new[hf] = b.run("scalar", lambda e, c0=c0, c1=c1, s=cl[9], bi=cl[10]: e.activation(
                sc_bf[:, c0:c1], xa[:, c0:c1], AF.Identity, bias=bi, scale=s), inc=s_act)
            b.wait("tensor", s_act, ap_new[hf])
            for cb in range(hf * CH2, (hf + 1) * CH2):
                bank = rot % 4
                rot += 1
                b.wait("tensor", s_gp, ev_bank[bank])
                tr = b.run("tensor", lambda e, cb=cb, bank=bank: e.matmul(
                    pb[bank][:, 0:128], sc_bf[:, cb * 128:(cb + 1) * 128],
                    ident_s[:], start=True, stop=True), inc=s_te)
                b.wait("vector", s_te, tr)
                if halo:
                    dst = lnTh[:, cb * H_ + (i - NT) * 128:cb * H_ + (i - NT) * 128 + 128]
                else:
                    dst = lnT[:, cb * T_ + i * 128:cb * T_ + i * 128 + 128]
                ev_bank[bank] = b.run("vector", lambda e, dst=dst, bank=bank, cb=cb: e.tensor_scalar(
                    dst, pb[bank][:, 0:128], lnwT[:, cb:cb + 1], None, ALU.mult), inc=s_gp)
            tr_prev[hf] = s_te.n
        ap_prev = ap_new
    gp_a1_end = s_gp.n

    b.mark("A2")
    if c.phases < 2:
        return _finalize(b, nc, c, dram)
    n_blk = (c.NOT + 1) // 2
    pfree = {0: (None, 0), 1: (None, 0)}
    wfree = {0: 0, 1: 0}
    conv_free = {0: 0, 1: 0}
    evst = 0
    sil_prev = 0
    qi = 0
    b.wait("tensor", s_gp, gp_a1_end)
    for blk in range(n_blk):
        o0 = blk * 2
        ntile = min(2, c.NOT - o0)
        cols0 = o0 * 128
        cols1 = min((o0 + 2) * 128, c.PROJ)
        bw = cols1 - cols0
        psets = [[pb[0], pb[1], pb[2]], [pb[3], pb[4], pb[5]]]
        for oj in range(ntile):
            sfree, vfree = pfree[oj]
            if sfree is not None:
                b.wait("tensor", sfree, vfree)
        tdone = [0] * ntile
        for kq in range(NKQ):
            wi = qi % 2
            if qi < 2:
                lw = pre_lw[qi]
            else:
                b.wait("gpsimd", s_te, wfree[wi])
                if bw == 256:
                    lw = ld_w(wq[wi][:, 0:KQ * 256], w_ip_t[kq, blk], wi)
                else:
                    lw = ld_w(wq[wi][:, 0:KQ * bw].rearrange("p (k o) -> p k o", k=KQ),
                              w_ip_rag[kq * KQ * 128:min((kq + 1) * KQ * 128, D_), :]
                              .rearrange("(k p) o -> p k o", p=128), wi)
            qi += 1
            b.wait("tensor", s_ldw[wi], lw)
            for oj in range(ntile):
                o = o0 + oj
                is_dt = (o == c.NOT - 1)
                owid = c.DTW if is_dt else 128
                is_z = o < c.ZOT
                ps = psets[oj]

                def mms(eng, wi=wi, kq=kq, oj=oj, owid=owid, is_z=is_z, ps=ps, bw=bw):
                    ins = None
                    for kk in range(KQ):
                        k = kq * KQ + kk
                        if k >= KT:
                            break
                        st = wq[wi][:, kk * bw + oj * 128: kk * bw + oj * 128 + owid]
                        first = (k == 0)
                        last = (k == KT - 1)
                        for hi, (t0, tw) in enumerate(tsp):
                            ins = eng.matmul(ps[hi][0:owid, 0:tw], st,
                                             lnT[:, k * T_ + t0:k * T_ + t0 + tw],
                                             start=first, stop=last)
                        if not is_z:
                            ins = eng.matmul(ps[2][0:owid, 0:H_], st,
                                             lnTh[:, k * H_:(k + 1) * H_],
                                             start=first, stop=last)
                    return ins
                tdone[oj] = b.run("tensor", mms, inc=s_te)
            wfree[wi] = s_te.n
        conv_jobs = []
        for oj in range(ntile):
            o = o0 + oj
            is_dt = (o == c.NOT - 1)
            owid = c.DTW if is_dt else 128
            is_z = o < c.ZOT
            ps = psets[oj]
            mupcol = mupc[:, o:o + 1]
            if is_z:
                b.wait("scalar", s_te, tdone[oj])
                b.wait("scalar", s_st, evst)
                for hi, (t0, tw) in enumerate(tsp):
                    b.run("scalar", lambda e, hi=hi, t0=t0, tw=tw, ps=ps, mc=mupcol: e.activation(
                        ev_bf[:, t0:t0 + tw], ps[hi][:, 0:tw], AF.Silu, scale=mc))
                ga = rb_act(ev_bf[:, 0:1])
                pfree[oj] = (s_act, ga)
                evst = store(zspill[o * 128:(o + 1) * 128, H_:TH], ev_bf[:, 0:T_], s_act, ga)
            else:
                b.wait("vector", s_te, tdone[oj])
                b.wait("vector", s_dve, conv_free[oj])
                xb = xbc2[oj]
                for hi, (t0, tw) in enumerate(tsp):
                    b.run("vector", lambda e, hi=hi, t0=t0, tw=tw, ps=ps, mc=mupcol, w=owid, xb=xb: e.tensor_scalar(
                        xb[0:w, 3 + H_ + t0:3 + H_ + t0 + tw], ps[hi][0:w, 0:tw],
                        mc[0:w, :], None, ALU.mult))
                vv1 = b.run("vector", lambda e, ps=ps, mc=mupcol, w=owid, xb=xb: e.tensor_scalar(
                    xb[0:w, 3:3 + H_], ps[2][0:w, 0:H_], mc[0:w, :], None, ALU.mult), inc=s_dve)
                pfree[oj] = (s_dve, vv1)
                if is_dt:
                    b.run("vector", lambda e, w=owid, xb=xb: e.tensor_copy(
                        dtscr_f[0:w, 0:TH], xb[0:w, 3:3 + TH]))
                    rb_dve(dtscr_f[0:NH, 0:1])
                else:
                    conv_jobs.append((oj, o - c.ZOT, vv1))
        for (oj, xo, vv1) in conv_jobs:
            xb = xbc2[oj]
            b.wait("vector", s_act, sil_prev)
            b.wait("vector", s_dve, vv1)
            b.run("vector", lambda e, xo=xo, xb=xb: e.tensor_scalar(
                acc_cv[:, 0:TH], xb[:, 0:TH],
                convw[:, (xo * c.K):(xo * c.K) + 1], None, ALU.mult))
            for kk in range(1, c.K):
                gcv = b.run("vector", lambda e, xo=xo, kk=kk, xb=xb: e.scalar_tensor_tensor(
                    acc_cv[:, 0:TH], xb[:, kk:kk + TH],
                    convw[:, (xo * c.K + kk):(xo * c.K + kk + 1)],
                    acc_cv[:, 0:TH], ALU.mult, ALU.add),
                    inc=(s_dve if kk == c.K - 1 else None))
            conv_free[oj] = gcv
            b.wait("scalar", s_dve, gcv)
            b.wait("scalar", s_st, evst)
            sil = b.run("scalar", lambda e: e.activation(
                ev_bf[:, 0:TH], acc_cv[:, 0:TH], AF.Silu), inc=s_act)
            sil_prev = sil
            ga = rb_act(ev_bf[:, 0:1])
            evst = store(zspill[D_ + xo * 128:D_ + (xo + 1) * 128, :],
                         ev_bf[:, 0:TH], s_act, ga)
    a2_spills = s_st.n
    barrier()

    b.run("vector", lambda e: e.tensor_scalar(dtscr_f[0:NH, 0:TH], dtscr_f[0:NH, 0:TH],
                                              dtb_s[0:NH, :], None, ALU.add))
    vr = b.run("vector", lambda e: e.tensor_scalar(dt_sp[0:NH, :], dtscr_f[0:NH, 0:TH],
                                                   0.0, None, ALU.max), inc=s_dve)
    b.wait("scalar", s_dve, vr)
    b.run("scalar", lambda e: e.activation(dt_t2[0:NH, 0:TH], dtscr_f[0:NH, 0:TH], AF.Abs))
    b.run("scalar", lambda e: e.activation(dt_t2[0:NH, 0:TH], dt_t2[0:NH, 0:TH], AF.Exp, scale=-1.0))
    asp = b.run("scalar", lambda e: e.activation(dt_t2[0:NH, 0:TH], dt_t2[0:NH, 0:TH], AF.Ln, bias=1.0), inc=s_act)
    b.wait("vector", s_act, asp)
    vsp = b.run("vector", lambda e: e.tensor_tensor(dt_sp[0:NH, :], dt_sp[0:NH, :],
                                                    dt_t2[0:NH, 0:TH], ALU.add), inc=s_dve)

    def emit_reload(ci, slot, main):
        col0 = (H_ + ci * 128) if main else ci * 128
        b.wait("sync", s_st, a2_spills)
        b.wait("sync", s_te, s_te.n)
        b.wait("sync", s_dve, s_dve.n)
        b.wait("sync", s_act, s_act.n)
        ld_sync(xs_tm[slot][:], zspill[D_:2 * D_, col0:col0 + 128], transpose=True,
                sem=s_rel2[slot])
        return (slot, ld_sync(bcc[slot].rearrange("p (a t) -> p a t", a=2),
                              zspill[2 * D_:2 * D_ + 2 * c.S, col0:col0 + 128]
                              .rearrange("(a p) t -> p a t", p=128), sem=s_rel2[slot]))

    def chunk_prep(ci, main, slot, rel_tgt):
        mp = getattr(c, "microphase", 9)
        barrier()
        col0 = (H_ + ci * 128) if main else ci * 128
        if mp < 0.5:
            return dict(vcs=0, aat=0, aex=0, vbt=0, awt=0, vg=0)
        b.wait("tensor", s_dve, vsp)
        ta = b.run("tensor", lambda e, c0=col0: e.transpose(
            pb[2][:, 0:NH], dt_sp[0:NH, c0:c0 + 128], identf_s[0:NH, 0:NH]), inc=s_te)
        if mp < 1:
            tq = b.run("tensor", lambda e: e.matmul(
                pb[3][:, 0:128], ident_s[:], ut_s[:], start=True, stop=True), inc=s_te)
            b.wait("vector", s_te, tq)
            b.run("vector", lambda e: e.tensor_copy(gsb[:], pb[3][:, 0:128]), inc=s_dve)
            return dict(vcs=0, aat=0, aex=0, vbt=0, awt=0, vg=0)
        b.wait("vector", s_te, ta)
        vda = b.run("vector", lambda e: e.tensor_copy(dtT[:], pb[2][:, 0:NH]), inc=s_dve)
        if mp < 2:
            return dict(vcs=0, aat=0, aex=0, vbt=0, awt=0, vg=0)
        b.wait("scalar", s_dve, vda)
        ala = b.run("scalar", lambda e: e.activation(logdtT[:], dtT[:], AF.Ln), inc=s_act)
        if mp < 2.2:
            return dict(vcs=0, aat=0, aex=0, vbt=0, awt=0, vg=0)
        b.wait("vector", s_dve, vda)
        vab = b.run("vector", lambda e: e.tensor_tensor(dtAT[:], dtT[:], a_bc[:], ALU.mult), inc=s_dve)
        if mp < 2.4:
            return dict(vcs=0, aat=0, aex=0, vbt=0, awt=0, vg=0)
        b.wait("tensor", s_dve, vab)
        tb_ = b.run("tensor", lambda e: e.matmul(pb[2][:, 0:NH], ut_s[:], dtAT[:],
                                                 start=True, stop=True), inc=s_te)
        if mp < 2.6:
            return dict(vcs=0, aat=0, aex=0, vbt=0, awt=0, vg=0)
        b.wait("vector", s_te, tb_)
        b.wait("vector", s_act, ala)
        vcs = b.run("vector", lambda e: e.tensor_tensor(colst[:], logdtT[:], pb[2][:, 0:NH], ALU.subtract), inc=s_dve)
        if mp < 2.8:
            return dict(vcs=0, aat=0, aex=0, vbt=0, awt=0, vg=0)
        b.wait("scalar", s_te, tb_)
        b.wait("scalar", s_dve, vcs)
        aex = b.run("scalar", lambda e: e.activation(expcT[:], pb[2][:, 0:NH], AF.Exp), inc=s_act)
        if mp < 3:
            return dict(vcs=vcs, aat=0, aex=aex, vbt=0, awt=0, vg=0)
        vq2 = b.run("vector", lambda e, c0=col0: e.tensor_reduce(
            dtsum[0:NH, 0:1], dt_sp[0:NH, c0:c0 + 128], AX.X, ALU.add), inc=s_dve)
        b.wait("vector", s_dve, vq2)
        vds = b.run("vector", lambda e: e.tensor_tensor(dtsum[0:NH, 1:2], dtsum[0:NH, 0:1],
                                                        a_col[0:NH, :], ALU.mult), inc=s_dve)
        b.wait("tensor", s_dve, vds)
        b.wait("tensor", s_act, aex)
        b.wait("tensor", s_dve, vcs)
        if mp < 4:
            return dict(vcs=vcs, aat=0, aex=aex, vbt=0, awt=0, vg=0)
        tc_ = b.run("tensor", lambda e: e.transpose(pb[2][0:1, 0:NH], dtsum[0:NH, 1:2],
                                                    identf_s[0:NH, 0:NH]), inc=s_te)
        b.wait("vector", s_te, tc_)
        vrow = b.run("vector", lambda e: e.tensor_copy(rowbuf[0:1, 0:NH], pb[2][0:1, 0:NH]), inc=s_dve)
        b.wait("tensor", s_dve, vrow)
        if mp < 5:
            return dict(vcs=vcs, aat=0, aex=aex, vbt=0, awt=0, vg=0)
        td = b.run("tensor", lambda e: e.matmul(pb[2][:, 0:NH], onesf_s[0:1, :],
                                                rowbuf[0:1, 0:NH], start=True, stop=True), inc=s_te)
        b.wait("scalar", s_te, td)
        aat = b.run("scalar", lambda e: e.activation(atot_bc[:], pb[2][:, 0:NH], AF.Exp), inc=s_act)
        pr = dict(vcs=vcs, aat=aat, aex=aex)
        if not main:
            b.wait("vector", s_te, td)
            b.wait("vector", s_dve, vcs)
            b.wait("vector", s_act, aat)
            vwa = b.run("vector", lambda e: e.tensor_tensor(wtilT[:], colst[:], pb[2][:, 0:NH], ALU.add), inc=s_dve)
            b.wait("scalar", s_dve, vwa)
            pr["awt"] = b.run("scalar", lambda e: e.activation(wtilT[:], wtilT[:], AF.Exp), inc=s_act)
        b.wait("tensor", s_rel2[rel_tgt[0]], rel_tgt[1])
        te_ = b.run("tensor", lambda e, sl=slot: e.transpose(
            pbf[:, 0:128], bcc[sl][:, 0:128], ident_s[:]), inc=s_te)
        b.wait("vector", s_te, te_)
        pr["vbt"] = b.run("vector", lambda e: e.tensor_copy(btile[:], pbf[:, 0:128]), inc=s_dve)
        if main:
            b.wait("tensor", s_act, aat)
            tg = b.run("tensor", lambda e, sl=slot: e.matmul(
                pb[2][:, 0:128], bcc[sl][:, 0:128], bcc[sl][:, 128:256],
                start=True, stop=True), inc=s_te)
            b.wait("vector", s_te, tg)
            pr["vg"] = b.run("vector", lambda e: e.tensor_tensor(gsb[:], pb[2][:, 0:128],
                                                                 tri_s[:], ALU.mult), inc=s_dve)
        return pr

    def emit_state_update(g, slot, pr, main):
        if main:
            src = e8.rearrange("p (h t) -> p h t", h=8)[:, :, 127:128].broadcast_to([128, 8, 64])
            b.wait("vector", s_act, pr["e_done"])
        else:
            src = wtilT[:, g * 8:(g + 1) * 8].broadcast_to([128, 8, 64])
            b.wait("vector", s_act, pr["awt"])
        vxw = b.run("vector", lambda e, g=g, sl=slot, s=src: e.tensor_tensor(
            xw8.rearrange("p (h t) -> p h t", h=8),
            xs_tm[sl].rearrange("p (h t) -> p h t", h=NH)[:, g * 8:(g + 1) * 8, :],
            s, ALU.mult), inc=s_dve)
        b.wait("tensor", s_dve, max(vxw, pr["vbt"], pr.get("hfree", 0)))
        th = b.run("tensor", lambda e: e.matmul(pb[6][:, 0:512], btile[:], xw8,
                                                start=True, stop=True), inc=s_te)
        b.wait("vector", s_te, th)
        b.wait("vector", s_act, pr["aat"])
        b.run("vector", lambda e, g=g: e.tensor_tensor(
            H.rearrange("p (h t) -> p h t", h=NH)[:, g * 8:(g + 1) * 8, :],
            H.rearrange("p (h t) -> p h t", h=NH)[:, g * 8:(g + 1) * 8, :],
            atot_bc[:, g * 8:(g + 1) * 8].broadcast_to([128, 8, 64]), ALU.mult))
        b.run("vector", lambda e, g=g: e.tensor_tensor(
            H[:, g * 512:g * 512 + 512], H[:, g * 512:g * 512 + 512], pb[6][:, 0:512], ALU.add))
        vH = b.run("vector", lambda e, g=g: e.tensor_copy(
            H_bf[:, g * 512:g * 512 + 512], H[:, g * 512:g * 512 + 512]), inc=s_dve)
        return vH, vxw

    b.mark("A2.5")
    if c.phases < 2.2:
        return _finalize(b, nc, c, dram)
    relq = []
    rel0 = emit_reload(0, 0, main=False)
    if c.phases < 2.4:
        return _finalize(b, nc, c, dram)
    hf = 0
    for hc in range(c.HC):
        slot = hc % 2
        nrel = emit_reload(hc + 1, (hc + 1) % 2, main=False) if hc + 1 < c.HC else None
        pr = chunk_prep(hc, False, slot, rel0 if hc == 0 else relq.pop())
        if nrel is not None:
            relq.append(nrel)
        if c.phases < 2.6:
            return _finalize(b, nc, c, dram)
        ng_lim = c.NG if c.phases >= 3 else max(1, int(round((c.phases - 2.6) * 20)))
        for g in range(min(c.NG, ng_lim)):
            pr["hfree"] = hf
            hf = emit_state_update(g, slot, pr, main=False)[0]
        if c.phases < 3:
            return _finalize(b, nc, c, dram)
    barrier()

    b.mark("A3")
    if c.phases < 4:
        return _finalize(b, nc, c, dram)
    b.wait("sync", s_te, s_te.n)
    b.wait("sync", s_dve, s_dve.n)
    for k in range(KT):
        ld_sync(lnT[:, k * T_:(k + 1) * T_],
                lnspill[:, k * 128:(k + 1) * 128], transpose=True, sem=s_lnt)
    lnT2_ready = s_lnt.n
    b.wait("gpsimd", s_te, s_te.n)
    b.wait("gpsimd", s_dve, s_dve.n)
    b.wait("gpsimd", s_act, s_act.n)
    lg = b.run("gpsimd", lambda e: e.dma_start(out=lAT[:], in_=w_lat_t), inc=s_misc)
    b.wait("tensor", s_lnt, lnT2_ready)
    b.wait("tensor", s_misc, lg)
    h_ev = {0: 0, 1: 0}
    hq = 0
    for et in range(ERT):
        for hi, (t0, tw) in enumerate(tsp):
            bank = hq % 2
            hq += 1
            b.wait("tensor", s_dve, h_ev[bank])

            def hmm(eng, et=et, t0=t0, tw=tw, bank=bank):
                ins = None
                for k in range(KT):
                    ins = eng.matmul(pb[bank][:, 0:tw],
                                     lAT[:, (k * ERT + et) * 128:(k * ERT + et) * 128 + 128],
                                     lnT[:, k * T_ + t0:k * T_ + t0 + tw],
                                     start=(k == 0), stop=(k == KT - 1))
                return ins
            th = b.run("tensor", hmm, inc=s_te)
            b.wait("vector", s_te, th)
            h_ev[bank] = b.run("vector", lambda e, et=et, t0=t0, tw=tw, bank=bank: e.tensor_scalar(
                hprime[:, et * T_ + t0:et * T_ + t0 + tw], pb[bank][:, 0:tw],
                probsc[:, et:et + 1], None, ALU.mult), inc=s_dve)
    vhp = rb_dve(hprime[:, 0:1])
    b.wait("tensor", s_dve, vhp)

    NT4 = min(4, NT)
    ffn_ev = {i: 0 for i in range(NT4)}
    evstl = {0: 0, 1: 0}
    lb_t = {0: 0, 1: 0}
    for os_ in range(D_ // 512):
        lbw = os_ % 2
        b.wait("gpsimd", s_te, lb_t[lbw])
        lbv = None
        for et in range(ERT):
            lbv = b.run("gpsimd", lambda e, et=et, os_=os_, lbw=lbw: e.dma_start(
                out=lBtq[lbw][et],
                in_=w_lBt[et * 128:(et + 1) * 128, os_ * 512:(os_ + 1) * 512]), inc=s_lb[lbw])
        for tp in range(NT // NT4):
            tis = [tp * NT4 + i for i in range(NT4)]
            for i, ti in enumerate(tis):
                b.wait("tensor", s_dve, ffn_ev[i])
                b.run("tensor", lambda e, i=i, os_=os_: e.matmul(
                    pb[i][:, 0:512], ones65[64:65, :], rows3[64:65, os_ * 512:(os_ + 1) * 512],
                    start=True, stop=False))
            for kq in range(NKQ):
                wi = qi % 2
                qi += 1
                b.wait("gpsimd", s_te, wfree[wi])
                lw = ld_w(wq[wi][:], w_ffn_t[kq, os_], wi)
                b.wait("tensor", s_ldw[wi], lw)

                def fmm(eng, wi=wi, kq=kq, tis=tis):
                    ins = None
                    for kk in range(KQ):
                        k = kq * KQ + kk
                        for i, ti in enumerate(tis):
                            ins = eng.matmul(pb[i][:, 0:512],
                                             lnT[:, k * T_ + ti * 128:k * T_ + ti * 128 + 128],
                                             wq[wi][:, kk * 512:(kk + 1) * 512],
                                             start=False, stop=False)
                    return ins
                b.run("tensor", fmm, inc=s_te)
                wfree[wi] = s_te.n
            b.wait("tensor", s_lb[lbw], lbv)

            def lmm(eng, tis=tis, lbw=lbw):
                ins = None
                for et in range(ERT):
                    for i, ti in enumerate(tis):
                        ins = eng.matmul(pb[i][:, 0:512],
                                         hprime[:, et * T_ + ti * 128:et * T_ + ti * 128 + 128],
                                         lBtq[lbw][et],
                                         start=False, stop=(et == ERT - 1))
                return ins
            tl = b.run("tensor", lmm, inc=s_te)
            for i, ti in enumerate(tis):
                p = ti % 2
                b.wait("vector", s_te, tl)
                b.wait("vector", s_st, evstl[p])
                ve = b.run("vector", lambda e, i=i, p=p: e.tensor_copy(
                    ev_bf[:, p * 512:p * 512 + 512], pb[i][:, 0:512]), inc=s_dve)
                ffn_ev[i] = ve
                gv = rb_dve(ev_bf[:, p * 512:p * 512 + 1])
                evstl[p] = store(ffnspill[ti * 128:(ti + 1) * 128, os_ * 512:(os_ + 1) * 512],
                                 ev_bf[:, p * 512:p * 512 + 512], s_dve, gv)
        lb_t[lbw] = s_te.n
    barrier()

    b.mark("B")
    if c.phases < 5:
        return _finalize(b, nc, c, dram)
    for h in range(NH):
        b.run("vector", lambda e, h=h: e.tensor_scalar(
            diagD[:, h * 128:(h + 1) * 128], ident_s[:], dcol_all[:, h:h + 1], None, ALU.mult))
    rb_dve(diagD[:, 0:1])
    nwev = {0: 0, 1: 0}
    for j in range(D_ // 512):
        bk = 2 + (j % 2)
        b.wait("tensor", s_dve, nwev[j % 2])
        tt = b.run("tensor", lambda e, j=j, bk=bk: e.matmul(
            pb[bk][:, 0:512], ones65[32:33, :], rows3[32:33, j * 512:(j + 1) * 512],
            start=True, stop=True), inc=s_te)
        b.wait("vector", s_te, tt)
        nwev[j % 2] = b.run("vector", lambda e, j=j, bk=bk: e.tensor_copy(
            nw_bc[:, j * 512:(j + 1) * 512], pb[bk][:, 0:512]), inc=s_dve)
    vnw = rb_dve(nw_bc[:, 0:1])
    b.wait("tensor", s_dve, vnw)
    bp = getattr(c, "bphase", 9)
    if bp < 1:
        return _finalize(b, nc, c, dram)

    vH_last = hf
    vxw_last = 0
    vHg = {g: hf for g in range(c.NG)}
    vg1 = 0
    rel0 = emit_reload(0, 0, main=True)
    relq = [rel0]
    y_sp = []
    t1_free = {0: 0, 1: 0}
    for ci in range(c.NC):
        slot = ci % 2
        my_rel = relq.pop(0)
        if ci + 1 < c.NC:
            relq.append(emit_reload(ci + 1, (ci + 1) % 2, main=True))
        b.wait("sync", s_dve, vg1)
        szv = ld_sync(sz_tm[:], zspill[0:D_, H_ + ci * 128:H_ + ci * 128 + 128],
                      transpose=True, sem=s_sz)
        pr = chunk_prep(ci, True, slot, my_rel)
        if bp < 1.5:
            return _finalize(b, nc, c, dram)
        t1_free[0] = max(t1_free[0], pr["vg"])
        yev = {0: 0, 1: 0}
        m_done = {0: 0, 1: 0}
        bc_free = {0: 0, 1: 0}
        for g in range(c.NG):
            ypb = pb[g % 2]
            t1b = pb[2] if g % 2 == 0 else pb[4]
            b.wait("tensor", s_dve, max(vHg[g], t1_free[g % 2]))
            b.wait("tensor", s_rel2[my_rel[0]], my_rel[1])
            if g == 0:
                b.wait("tensor", s_act, pr["aat"])
            t1m = b.run("tensor", lambda e, sl=slot, g=g, t1b=t1b: e.matmul(
                t1b[:, 0:512], bcc[sl][:, 128:256], H_bf[:, g * 512:(g + 1) * 512],
                start=True, stop=True), inc=s_te)
            b.wait("vector", s_te, t1m)
            if g == 0:
                b.wait("vector", s_act, pr["aat"])
            vt1 = b.run("vector", lambda e, g=g, t1b=t1b: e.tensor_tensor(
                t1buf.rearrange("p (h t) -> p h t", h=8),
                t1b.rearrange("p (h t) -> p h t", h=8),
                expcT[:, g * 8:(g + 1) * 8].broadcast_to([128, 8, 64]), ALU.mult), inc=s_dve)
            t1_free[g % 2] = vt1
            if bp < 2:
                return _finalize(b, nc, c, dram)
            b.wait("tensor", s_dve, yev[g % 2])
            e_done = 0
            for hb in range(2):
                hls = [hb * 4 + j for j in range(4)]
                bcb = pb[3] if hb == 0 else pb[5]
                if bc_free[hb]:
                    b.wait("tensor", s_act, bc_free[hb])
                elif hb == 0:
                    b.wait("tensor", s_dve, pr["vg"])

                def bmm(eng, g=g, hls=hls, bcb=bcb):
                    ins = None
                    for j, hl in enumerate(hls):
                        h = g * 8 + hl
                        ins = eng.matmul(bcb[:, j * 128:(j + 1) * 128],
                                         dtAT[:, h:h + 1].broadcast_to([128, 128]),
                                         ut_s[:], start=True, stop=True)
                    return ins
                tb_ = b.run("tensor", bmm, inc=s_te)
                if bp < 2.2:
                    return _finalize(b, nc, c, dram)
                b.wait("scalar", s_te, tb_)
                b.wait("scalar", s_dve, pr["vcs"] if hb > 0
                       else max(pr["vcs"], vxw_last))
                for j, hl in enumerate(hls):
                    h = g * 8 + hl
                    ae = b.run("scalar", lambda e, h=h, hl=hl, j=j, bcb=bcb: e.activation(
                        e8[:, hl * 128:(hl + 1) * 128], bcb[:, j * 128:(j + 1) * 128],
                        AF.Exp, bias=colst[:, h:h + 1]),
                        inc=(s_act if j == 3 else None))
                bc_free[hb] = ae
                e_done = ae
                if bp < 2.4:
                    return _finalize(b, nc, c, dram)
                b.wait("vector", s_act, ae)
                b.wait("vector", s_dve, pr["vg"])
                b.wait("vector", s_te, m_done[hb])
                for j, hl in enumerate(hls):
                    vm = b.run("vector", lambda e, hl=hl: e.scalar_tensor_tensor(
                        mstat8[:, hl * 128:(hl + 1) * 128],
                        e8[:, hl * 128:(hl + 1) * 128], 3.0e37, gsb[:],
                        ALU.min, ALU.mult), inc=(s_dve if j == 3 else None))
                b.wait("tensor", s_dve, vm)
                if bp < 2.6:
                    return _finalize(b, nc, c, dram)

                def ymm(eng, g=g, hls=hls, sl=slot, yp=ypb):
                    ins = None
                    for hl in hls:
                        h = g * 8 + hl
                        eng.matmul(yp[:, hl * 64:hl * 64 + 64],
                                   mstat8[:, hl * 128:(hl + 1) * 128],
                                   xs_tm[sl][:, h * 64:h * 64 + 64],
                                   start=(hl == 0), stop=False)
                        ins = eng.matmul(yp[:, hl * 64:hl * 64 + 64],
                                         diagD[:, h * 128:(h + 1) * 128],
                                         xs_tm[sl][:, h * 64:h * 64 + 64],
                                         start=False, stop=(hl == 7))
                    return ins
                tym = b.run("tensor", ymm, inc=s_te)
                m_done[hb] = tym
                if bp < 2.8:
                    return _finalize(b, nc, c, dram)
            pr["e_done"] = e_done
            pr["hfree"] = vH_last
            vH_last, vxw_last = emit_state_update(g, slot, pr, main=True)
            vHg[g] = vH_last
            b.wait("vector", s_te, max(m_done[0], m_done[1]))
            vy = b.run("vector", lambda e, g=g, yp=ypb: e.tensor_tensor(
                yfull[:, g * 512:(g + 1) * 512], t1buf[:], yp[:, 0:512], ALU.add), inc=s_dve)
            yev[g % 2] = vy
            if bp < 3.5:
                return _finalize(b, nc, c, dram)
        b.wait("vector", s_dve, max(yev[0], yev[1]))
        if c.debug:
            ygate = rb_dve(yfull[:, 0:1])
            ystt = store(ydump[ci * 128:(ci + 1) * 128, :], yfull, s_dve, ygate)
            b.wait("vector", s_st, ystt)
        b.wait("vector", s_sz, szv)
        vg1a = b.run("vector", lambda e: e.tensor_tensor(yfull, yfull, sz_tm[:], ALU.mult), inc=s_dve)
        vg1 = b.run("vector", lambda e: e.memset(lncol[:, 0:1], 0.0), inc=s_dve)
        b.wait("scalar", s_dve, max(vg1, vg1a))
        asq = b.run("scalar", lambda e, sl=slot: e.activation(
            xs_tm[sl][:], yfull, AF.Square, accum_out=lncol[:, 0:1]), inc=s_act)
        b.wait("scalar", s_act, asq)
        b.run("scalar", lambda e: e.activation(lncol[:, 1:2], lncol[:, 0:1], AF.Ln,
                                               scale=inv_d, bias=1e-5))
        ar2 = b.run("scalar", lambda e: e.activation(lncol[:, 1:2], lncol[:, 1:2], AF.Exp,
                                                     scale=-0.5), inc=s_act)
        b.wait("vector", s_act, ar2)
        b.wait("vector", s_st, y_sp[-1] if y_sp else 0)
        b.run("vector", lambda e: e.tensor_scalar(yfull, yfull, lncol[:, 1:2], None, ALU.mult))
        vyh = b.run("vector", lambda e: e.tensor_tensor(yhat_bf, yfull, nw_bc, ALU.mult), inc=s_dve)
        gv = rb_dve(yhat_bf[:, 0:1])
        y_sp.append(store(yspill[ci * 128:(ci + 1) * 128, :], yhat_bf, s_dve, gv))
    b_spills = s_st.n
    barrier()

    b.mark("C")
    if c.phases < 6:
        return _finalize(b, nc, c, dram)
    b.wait("sync", s_st, b_spills)
    b.wait("sync", s_te, s_te.n)
    b.wait("sync", s_dve, s_dve.n)
    for k in range(KT):
        ld_sync(lnT[:, k * T_:(k + 1) * T_],
                yspill[:, k * 128:(k + 1) * 128], transpose=True, sem=s_lnt)
    ycm_ready = s_lnt.n
    ycm = lnT
    b.wait("tensor", s_lnt, ycm_ready)
    out_free = {i: 0 for i in range(NT4)}
    ld_free = {0: (0, 0), 1: (0, 0)}
    ost = {0: 0, 1: 0}
    gsil4 = [poolF[:, 2048 + j * 512:2048 + (j + 1) * 512] for j in range(4)]
    xrb4 = [poolF[:, j * 512:(j + 1) * 512] for j in range(4)]
    gbufs4 = [poolB[:, j * 512:(j + 1) * 512] for j in range(4)]
    gs_free = {i: 0 for i in range(NT4)}
    ag_free = {i: 0 for i in range(NT4)}
    xb_free = {i: 0 for i in range(NT4)}
    for os_ in range(D_ // 512):
        for tp in range(NT // NT4):
            tis = [tp * NT4 + i for i in range(NT4)]
            to = 0
            for kq in range(NKQ):
                wi = qi % 2
                qi += 1
                b.wait("gpsimd", s_te, wfree[wi])
                lw = ld_w(wq[wi][:], w_out_t[kq, os_], wi)
                b.wait("tensor", s_ldw[wi], lw)
                if kq == 0:
                    for i in range(NT4):
                        b.wait("tensor", s_dve, out_free[i])

                def omm(eng, wi=wi, kq=kq, tis=tis):
                    ins = None
                    for kk in range(KQ):
                        k = kq * KQ + kk
                        for i, ti in enumerate(tis):
                            ins = eng.matmul(pb[i][:, 0:512],
                                             ycm[:, k * T_ + ti * 128:k * T_ + ti * 128 + 128],
                                             wq[wi][:, kk * 512:(kk + 1) * 512],
                                             start=(k == 0), stop=(k == KT - 1))
                    return ins
                to = b.run("tensor", omm, inc=s_te)
                wfree[wi] = s_te.n
            consume = []
            for i, ti in enumerate(tis):
                p = ti % 2
                gb = gbufs4[i]
                xb_ = xrb4[i]
                gs = gsil4[i]
                b.wait("sync", s_act, ag_free[i])
                b.wait("sync", s_dve, xb_free[i])
                lgv = ld_sync(gb, ffnspill[ti * 128:(ti + 1) * 128, os_ * 512:(os_ + 1) * 512],
                              sem=s_gx4[i])
                lxv = ld_sync(xb_, x_in[ti * 128:(ti + 1) * 128, os_ * 512:(os_ + 1) * 512],
                              sem=s_gx4[i])
                b.wait("scalar", s_gx4[i], lxv)
                b.wait("scalar", s_st, gs_free[i])
                ag = b.run("scalar", lambda e, gb=gb, gs=gs: e.activation(gs, gb, AF.Silu), inc=s_act)
                ag_free[i] = ag
                consume.append((i, ti, p, gs, xb_, ag, lxv))
            b.wait("vector", s_te, to)
            for (i, ti, p, gs, xb_, ag, lxv) in consume:
                b.wait("vector", s_act, ag)
                b.wait("vector", s_gx4[i], lxv)
                vm = b.run("vector", lambda e, i=i, gs=gs: e.tensor_tensor(
                    gs, pb[i][:, 0:512], gs, ALU.mult), inc=s_dve)
                out_free[i] = vm
                vo = b.run("vector", lambda e, xb_=xb_, gs=gs: e.tensor_tensor(
                    gs, gs, xb_, ALU.add), inc=s_dve)
                xb_free[i] = vo
                gvo = rb_dve(gs[:, 0:1])
                gs_free[i] = store(out[ti * 128:(ti + 1) * 128, os_ * 512:(os_ + 1) * 512],
                                   gs, s_dve, gvo)

    if c.debug:
        for nm, t, shp in (("dbg_yd", ydump, [T_, D_]),
                           ("dbg_z", zspill, [2 * D_ + 2 * c.S, TH]),
                           ("dbg_ln", lnspill, [T_, D_]),
                           ("dbg_sc", scspill, [TH, D_]),
                           ("dbg_ffn", ffnspill, [T_, D_]),
                           ("dbg_y", yspill, [T_, D_])):
            o = dout(nm, shp, F32 if nm == "dbg_yd" else BF16)
            b.wait("sync", s_st, s_st.n)
            b.run("sync", lambda e, o=o, t=t: e.dma_start(out=o, in_=t), inc=s_ldx)
        oh = dout("dbg_h", [128, NH * 64], F32)
        b.wait("sync", s_dve, s_dve.n)
        b.run("sync", lambda e, o=oh: e.dma_start(out=o, in_=H[:]), inc=s_ldx)

    global _LAST_MARKS, _LAST_B
    _LAST_MARKS = getattr(b, "marks", [])
    _LAST_B = b
    for sem_ in (s_st, s_ldx, s_misc, s_xa, s_xb, s_lnt, s_sz, s_ldw[0], s_ldw[1],
                 s_lb[0], s_lb[1], s_rel2[0], s_rel2[1], s_gx[0], s_gx[1],
                 s_gx4[0], s_gx4[1], s_gx4[2], s_gx4[3]):
        b.wait("sync", sem_, sem_.n)

    with b._ctx:
        with nc.Block() as block:
            @block.tensor
            def _(eng):
                class _CE:
                    n = 0

                    def __getattr__(self, a):
                        v = getattr(eng, a)
                        if a in ("matmul", "transpose"):
                            def w(*ar, **kw):
                                _CE.n += 1
                                return v(*ar, **kw)
                            return w
                        return v
                ce = _CE()
                runmarks = {r: nm for nm, r in getattr(b, "marks", [])}
                mmarks = []
                nwait = 0
                for idx, f in enumerate(b.ops["tensor"]):
                    if idx in runmarks:
                        mmarks.append((runmarks[idx], _CE.n))
                    f(ce)
                b.mm_marks = mmarks

            @block.vector
            def _(eng):
                for f in b.ops["vector"]:
                    f(eng)

            @block.scalar
            def _(eng):
                for f in b.ops["scalar"]:
                    f(eng)

            @block.gpsimd
            def _(eng):
                for f in b.ops["gpsimd"]:
                    f(eng)

            @block.sync
            def _(eng):
                for f in b.ops["sync"]:
                    f(eng)
    return nc


def _mk_rows3(c, weights):
    r3 = np.zeros((65, c.D), np.float32)
    r3[0] = np.asarray(weights["ln_w"])
    r3[32] = np.asarray(weights["norm_w"])
    r3[64] = np.asarray(weights["ffn_b"])
    return r3


def make_inputs(cfg: Cfg, weights: dict, x_slab: np.ndarray, prev_slab: np.ndarray):
    c = cfg
    f32 = np.float32

    def asf(a):
        return np.ascontiguousarray(a, dtype=f32)

    wconv = np.asarray(weights["conv_w"])
    convc = np.zeros((128, c.XOT * c.K), f32)
    for xo in range(c.XOT):
        for k in range(c.K):
            convc[:, xo * c.K + k] = wconv[xo * 128:(xo + 1) * 128, k]
    mupcv = np.zeros((128, c.NOT), f32)
    mv = np.asarray(weights["mup_vector"])
    for o in range(c.NOT):
        w = min(128, c.PROJ - o * 128)
        mupcv[0:w, o] = mv[o * 128:o * 128 + w]
    A2d = np.asarray(weights["ffn_A"]).reshape(c.ER, c.D)
    Bt2d = np.asarray(weights["ffn_B"]).transpose(0, 2, 1).reshape(c.ER, c.D)
    wip = np.asarray(weights["in_proj_w"], f32)
    NBF = (c.NOT + 1) // 2 - (1 if c.NOT % 2 == 1 else 0)
    wipf = wip[:, :NBF * 256].reshape(c.NKQ, c.KQ, 128, NBF, 256)
    wipf = np.ascontiguousarray(wipf.transpose(0, 3, 2, 1, 4).reshape(
        c.NKQ, NBF, 128, c.KQ * 256))
    wop = np.asarray(weights["out_proj_w"], f32)
    wopt = np.ascontiguousarray(
        wop.reshape(c.NKQ, c.KQ, 128, c.D // 512, 512).transpose(0, 3, 2, 1, 4)
        .reshape(c.NKQ, c.D // 512, 128, c.KQ * 512))
    wff = np.ascontiguousarray(np.asarray(weights["ffn_W"], f32).T)
    wfft = np.ascontiguousarray(
        wff.reshape(c.NKQ, c.KQ, 128, c.D // 512, 512).transpose(0, 3, 2, 1, 4)
        .reshape(c.NKQ, c.D // 512, 128, c.KQ * 512))
    latt = np.ascontiguousarray(
        A2d.T.reshape(c.KT, 128, c.ER).transpose(1, 0, 2).reshape(128, c.KT * c.ER)).astype(f32)
    return {
        "x_in": asf(x_slab),
        "prev_in": asf(prev_slab),
        "w_ip_t": wipf,
        "w_ip_rag": asf(wip[:, NBF * 256:]),
        "w_convc": convc,
        "w_out_t": wopt,
        "w_ffn_t": wfft,
        "w_lat_t": latt,
        "w_lBt": asf(Bt2d),
        "probs_col": asf(np.repeat(np.asarray(weights["probs"]), c.R).reshape(c.ER // 128, 128).T),
        "mup_cols": mupcv,
        "rows3_in": (lambda r3: r3)(_mk_rows3(c, weights)),
        "alog_row": asf(np.asarray(weights["A_log"])[None, :]),
        "dtb_col": asf(np.asarray(weights["dt_bias"])[:, None]),
        "d_cols": asf(np.tile(np.asarray(weights["D"])[None, :], (128, 1))),
        "ident_bf": np.eye(128).astype(bf16),
        "ident_f": np.eye(64).astype(f32),
        "ut_bf": np.triu(np.ones((128, 128))).astype(bf16),
        "tri_bf": np.triu(np.ones((128, 128))).astype(bf16),
        "ones_bf": np.ones((1, 128), bf16),
        "ones65_bf": np.ones((65, 128), bf16),
        "ones_f": np.ones((1, 128), f32),
        "fconsts": np.tile(np.array([[1e-5, 1.0 / c.D, -0.5, -1.0, 1.0, 0.0, 3.0e37, 0.5]], f32), (128, 1)),
    }


from concourse.bass_utils import run_bass_kernel_spmd

_CFG = Cfg()
_NC_CACHE = {}


def _get_program():
    if "nc" not in _NC_CACHE:
        _NC_CACHE["nc"] = build_program(_CFG)
    return _NC_CACHE["nc"]


def kernel(**inputs):
    c = _CFG
    x = np.ascontiguousarray(np.asarray(inputs["L0_attn_hidden_states"], dtype=np.float32))
    B, L, D = x.shape
    assert (B, L, D) == (4, 2048, 4096)
    weights = {
        "in_proj_w": inputs["L0_attn_in_proj_w"],
        "conv_w": inputs["L0_attn_conv_w"],
        "out_proj_w": inputs["L0_attn_out_proj_w"],
        "ln_w": inputs["L0_attn_ln_w"],
        "norm_w": inputs["L0_attn_norm_w"],
        "mup_vector": inputs["L0_attn_mup_vector"],
        "A_log": inputs["L0_attn_A_log"],
        "dt_bias": inputs["L0_attn_dt_bias"],
        "D": inputs["L0_attn_D"],
        "ffn_W": inputs["L0_ffn_W"],
        "ffn_b": inputs["L0_ffn_b"],
        "ffn_A": inputs["L0_ffn_A"],
        "ffn_B": inputs["L0_ffn_B"],
        "probs": inputs["L0_ffn_probs"],
    }
    zeros_prev = np.zeros((c.HALO, c.D), np.float32)
    base = make_inputs(c, weights, x[0, :c.T], zeros_prev)
    nc = _get_program()
    in_maps = []
    for core in range(8):
        bt, hf = core // 2, core % 2
        m = dict(base)
        m["x_in"] = np.ascontiguousarray(x[bt, hf * c.T:(hf + 1) * c.T])
        m["prev_in"] = zeros_prev if hf == 0 else \
            np.ascontiguousarray(x[bt, c.T - c.HALO:c.T])
        in_maps.append(m)
    res = run_bass_kernel_spmd(nc, in_maps, core_ids=list(range(8)))
    out = np.empty((4, 2048, 4096), np.float32)
    for core in range(8):
        bt, hf = core // 2, core % 2
        out[bt, hf * c.T:(hf + 1) * c.T] = np.asarray(res.results[core]["out"])
    return out
